# version 09
# baseline (speedup 1.0000x reference)
import numpy as np

import concourse.bass as bass
import concourse.mybir as mybir
from concourse.bass_utils import run_bass_kernel_spmd
from concourse.bass_core_rust import IndirectOffsetOnAxis

F32 = mybir.dt.float32
BF16 = mybir.dt.bfloat16
I32 = mybir.dt.int32
AF = mybir.ActivationFunctionType
ALU = mybir.AluOpType
AX = mybir.AxisListType.X

NCORES = 8
B, T, H = 2, 8192, 1024
TL = 2048
NTC = TL // 128
NCH = H // 128
E = 8
FF = 4096
NFC = FF // 128
CGLOB = 8192
CL = 1024
NEG = -1e38
TRASH = E * CL


class SemCtr:
    def __init__(self, handle, name):
        self.h = handle
        self.v = 0
        self.name = name


class Plan:
    def __init__(self):
        self.ops = {k: [] for k in ("sync", "gpsimd", "vector", "scalar", "tensor")}
        self.hist = {}

    def emit(self, eng, fn):
        self.ops[eng].append(fn)

    def wait(self, eng, sem, val=None):
        if val is None:
            val = sem.v
        assert val <= sem.v, f"wait on future value of {sem.name}: {val} > {sem.v}"
        if val > 0:
            self.emit(eng, lambda e, s=sem.h, v=val: e.wait_ge(s, v))

    def dma(self, eng, out, in_, sem, **kw):
        sem.v += 16
        self.emit(eng, lambda e, o=out, i=in_, s=sem.h, kw=kw:
                  e.dma_start(out=o, in_=i, **kw).then_inc(s, 16))
        return sem.v

    def dmat(self, eng, out, in_, sem):
        sem.v += 16
        self.emit(eng, lambda e, o=out, i=in_, s=sem.h:
                  e.dma_start_transpose(o, i).then_inc(s, 16))
        return sem.v

    scalar_nop_fn = None

    def nop(self, eng):
        if eng == "scalar":
            self.emit(eng, self.scalar_nop_fn)
        else:
            self.emit(eng, lambda e: e.engine_nop())

    @staticmethod
    def _foot(a):
        try:
            t = a.tensor.name
            off = a.offset
            dims = list(a.ap)
        except Exception:
            return None
        if not dims:
            return (t, 0, 1, off, off + 1, 1)
        pitch, np_ = dims[0]
        if pitch <= 0:
            pitch = 1
        p0 = off // pitch
        c0 = off % pitch
        span = sum(st * (ct - 1) for st, ct in dims[1:]) + 1
        return (t, p0, p0 + np_, c0, c0 + span, span)

    @staticmethod
    def _ovl(f1, f2):
        return (f1[0] == f2[0] and f1[1] < f2[2] and f2[1] < f1[2]
                and f1[3] < f2[4] and f2[3] < f1[4])

    WINDOW = 8

    def _hazard(self, eng, writes, reads):
        hist = self.hist.setdefault(eng, [])
        need = 0
        for (pw, pr, psem, pval) in hist:
            conflict = False
            for f1 in pw:
                for f2 in writes + reads:
                    if self._ovl(f1, f2) and (f1[5] < 512 or f2[5] < 512):
                        conflict = True
            for f1 in pr:
                for f2 in writes:
                    if self._ovl(f1, f2) and (f1[5] < 512 or f2[5] < 512):
                        conflict = True
            if conflict:
                if psem is not None:
                    need = max(need, pval) if psem is self.def_sem[eng] else need
                    if psem is not self.def_sem[eng]:
                        self.wait(eng, psem, pval)
                else:
                    self.nop(eng)
                    self.nop(eng)
        if need:
            self.wait(eng, self.def_sem[eng], need)

    def op(self, eng, meth, *args, sem=None, inc=1, **kw):
        writes = reads = None
        if eng in ("vector", "scalar"):
            w = [self._foot(args[0])] if args else []
            w += [self._foot(kw["accum_out"])] if kw.get("accum_out") is not None else []
            r = [self._foot(a) for a in args[1:]]
            r += [self._foot(v) for k2, v in kw.items() if k2 != "accum_out"]
            writes = [f for f in w if f is not None]
            reads = [f for f in r if f is not None]
            self._hazard(eng, writes, reads)
            small = any(f[5] < 512 for f in writes)
            if sem is None and small:
                sem = self.def_sem[eng]
                inc = 1
        if sem is not None:
            sem.v += inc
            ret = sem.v
            self.emit(eng, lambda e, a=args, kw=kw, s=sem.h, n=inc:
                      getattr(e, meth)(*a, **kw).then_inc(s, n))
        else:
            ret = None
            self.emit(eng, lambda e, a=args, kw=kw: getattr(e, meth)(*a, **kw))
        if writes is not None:
            hist = self.hist.setdefault(eng, [])
            hist.append((writes, reads, sem, sem.v if sem is not None else None))
            if len(hist) > self.WINDOW:
                hist.pop(0)
        return ret


DEBUG = False


def build_nc():
    nc = bass.Bass("TRN2", target_bir_lowering=False)
    SCR = "ExternalOutput" if DEBUG else "Internal"

    x_pad = nc.dram_tensor("x_pad", [TL + 1, H], F32, kind="ExternalInput").ap()
    Wk = nc.dram_tensor("Wk", [H, H], F32, kind="ExternalInput").ap()
    Wv = nc.dram_tensor("Wv", [H, H], F32, kind="ExternalInput").ap()
    Wr = nc.dram_tensor("Wr", [H, H], F32, kind="ExternalInput").ap()
    Wo = nc.dram_tensor("Wo", [H, H], F32, kind="ExternalInput").ap()
    tdecay = nc.dram_tensor("tdecay", [H], F32, kind="ExternalInput").ap()
    tfirst = nc.dram_tensor("tfirst", [H], F32, kind="ExternalInput").ap()
    tmix = nc.dram_tensor("tmix", [3, H], F32, kind="ExternalInput").ap()
    Wg = nc.dram_tensor("Wg", [H, E], F32, kind="ExternalInput").ap()
    W1 = nc.dram_tensor("W1", [E, H, FF], F32, kind="ExternalInput").ap()
    b1 = nc.dram_tensor("b1", [E, FF], F32, kind="ExternalInput").ap()
    W2 = nc.dram_tensor("W2", [E, FF, H], F32, kind="ExternalInput").ap()
    b2 = nc.dram_tensor("b2", [E, H], F32, kind="ExternalInput").ap()
    cmask_mul = nc.dram_tensor("cmask_mul", [128, NCORES], F32, kind="ExternalInput").ap()
    cmask_add = nc.dram_tensor("cmask_add", [128, NCORES], F32, kind="ExternalInput").ap()
    premask = nc.dram_tensor("premask", [NCORES, 1], F32, kind="ExternalInput").ap()
    out = nc.dram_tensor("y", [TL, H], F32, kind="ExternalOutput").ap()

    k_sp = nc.dram_tensor("k_sp", [H, TL], F32, kind=SCR).ap()
    v_sp = nc.dram_tensor("v_sp", [H, TL], F32, kind=SCR).ap()
    r_sp = nc.dram_tensor("r_sp", [H, TL], F32, kind=SCR).ap()
    wb_d = [nc.dram_tensor(f"w{i}b_d", [H, H], BF16).ap() for i in range(4)]
    w1b_d = nc.dram_tensor("ffn_w1b", [E, H, FF], BF16).ap()
    w2b_d = nc.dram_tensor("ffn_w2b", [E, FF, H], BF16).ap()
    x1_sp = nc.dram_tensor("x1_sp", [TL, H], F32, kind=SCR).ap()
    xe_d = nc.dram_tensor("xe_d", [E * CL + 1, H], BF16, kind=SCR).ap()
    ye_d = nc.dram_tensor("ye_d", [E * CL + 1, H], F32, kind=SCR).ap()
    cc1_in = nc.dram_tensor("cc1_in", [128, 3 * NCH], F32).ap()
    cc1_out = nc.dram_tensor("cc1_out", [128 * NCORES, 3 * NCH], F32).ap()
    cc2_in = nc.dram_tensor("cc2_in", [1, 2 * E], F32).ap()
    cc2_out = nc.dram_tensor("cc2_out", [NCORES, 2 * E], F32).ap()

    from contextlib import ExitStack
    ctx = ExitStack()

    def sb(name, shape, dt=F32):
        return ctx.enter_context(nc.sbuf_tensor(name, shape, dt))

    def mksem(name):
        return SemCtr(ctx.enter_context(nc.semaphore(name)), name)

    NF = 22544
    NBQ = 50176
    ARF = sb("ARF", [128, NF])
    ARB = sb("ARB", [128, NBQ], BF16)

    def fv(a, b):
        return ARF[:, a:b]

    def bv(a, b):
        return ARB[:, a:b]

    def xTv(g, a, b):
        base = g * (TL + 1)
        return ARF[:, base + a: base + b]
    SPL0 = 16392
    KK, VV, RR = 0, 2048, 4096
    MINC, SINC, DINC = 6144, 8196, 10248
    B1, B2_, B3, B4 = 12300, 14348, 16396, 18444

    WT = [0, 8192, 16384, 24576]
    XMT = 32768
    RWO = 0
    X1B = 32768
    X1TC = 16384
    H1O = 0
    W2O = 16384
    W1O = 32768
    XEO = 40960
    B2O = 49152

    iota0 = sb("iota0", [128, TL + 1])
    iota8 = sb("iota8", [128, E])
    iota8p = sb("iota8p", [128, E])
    ident = sb("ident", [128, 128])
    identb = sb("identb", [128, 128], BF16)
    triu = sb("triu", [128, 128])
    ones_c = sb("ones_c", [128, 1])
    ones_r1 = sb("ones_r1", [1, 128])
    ones_r1b = sb("ones_r1b", [1, 128], BF16)
    w_sb = sb("w_sb", [128, NCH])
    negw_sb = sb("negw_sb", [128, NCH])
    w2048_sb = sb("w2048_sb", [128, NCH])
    u_sb = sb("u_sb", [128, NCH])
    tm_sb = sb("tm_sb", [128, 3 * NCH])
    tm1_sb = sb("tm1_sb", [128, 3 * NCH])
    cmaskm = sb("cmaskm", [128, NCORES])
    cmaska = sb("cmaska", [128, NCORES])
    premk = sb("premk", [NCORES, 1])
    wgb = sb("wgb", [128, NCH * E], BF16)
    b1_sb = sb("b1_sb", [128, E * NFC])
    agg = sb("agg", [128, 3 * NCH])
    aggT = sb("aggT", [128, 3 * NCH])
    carry = sb("carry", [128, 3 * NCH])
    sm1 = sb("sm1", [128, 3 * NCH])
    sm2 = sb("sm2", [128, 3 * NCH])
    sm3 = sb("sm3", [128, 3 * NCH])
    cc2_sb = sb("cc2_sb", [NCORES, 2 * E])
    offs = sb("offs", [1, 2 * E])
    offg = sb("offg", [1, E])
    loco = sb("loco", [1, 2 * E])
    logit = sb("logit", [128, NTC * E])
    mask1 = sb("mask1", [128, NTC * E])
    mask2 = sb("mask2", [128, NTC * E])
    kept1 = sb("kept1", [128, NTC * E])
    kept2 = sb("kept2", [128, NTC * E])
    pos1 = sb("pos1", [128, NTC])
    pos2 = sb("pos2", [128, NTC])
    keep1 = sb("keep1", [128, NTC])
    keep2 = sb("keep2", [128, NTC])
    w1k = sb("w1k", [128, NTC])
    w2k = sb("w2k", [128, NTC])
    slot1 = sb("slot1", [128, NTC])
    slot2 = sb("slot2", [128, NTC])
    dst1i = sb("dst1i", [128, NTC], I32)
    dst2i = sb("dst2i", [128, NTC], I32)
    rsc1 = sb("rsc1", [128, E])
    rsc2 = sb("rsc2", [128, E])
    rsc3 = sb("rsc3", [128, E])
    rcol1 = sb("rcol1", [128, 1])
    rcol2 = sb("rcol2", [128, 1])
    rcol3 = sb("rcol3", [128, 1])
    xck = sb("xck", [128, H])
    zb = sb("zb", [128, H], BF16)
    nopc = sb("nopc", [128, 1])

    def psumt(name, shape):
        return ctx.enter_context(nc.psum_tensor(name, shape, F32))

    pbank = [psumt(f"pb{i}", [128, 512]) for i in range(8)]

    s_ld = mksem("s_ld")
    s_gp = mksem("s_gp")
    s_gc = mksem("s_gc")
    s_sp = mksem("s_sp")
    s_pe = mksem("s_pe")
    s_ve = mksem("s_ve")
    s_sc = mksem("s_sc")
    s_cc = mksem("s_cc")
    s_ix = mksem("s_ix")
    s_ot = mksem("s_ot")
    s_l2 = mksem("s_l2")

    p = Plan()
    p.scalar_nop_fn = lambda e: e.copy(nopc[:], nopc[:])
    p.def_sem = {"vector": s_ve, "scalar": s_sc}
    bankg = {}

    def bank_wait(i):
        if i in bankg:
            sem, val = bankg[i]
            p.wait("tensor", sem, val)

    def bank_set(i, sem, val):
        bankg[i] = (sem, val)

    p.op("gpsimd", "iota", iota0[:], [[1, TL + 1]], base=-1, channel_multiplier=0,
         allow_small_or_imprecise_dtypes=True)
    p.op("gpsimd", "iota", iota8[:], [[1, E]], channel_multiplier=0,
         allow_small_or_imprecise_dtypes=True)
    p.op("gpsimd", "memset", ident[:], 0.0)
    p.op("gpsimd", "affine_select", out=ident[:], in_=ident[:],
         compare_op=ALU.not_equal, fill=1.0, base=0,
         pattern=[[-1, 128]], channel_multiplier=1)
    p.op("gpsimd", "memset", identb[:], 0.0)
    p.op("gpsimd", "affine_select", out=identb[:], in_=identb[:],
         compare_op=ALU.not_equal, fill=1.0, base=0,
         pattern=[[-1, 128]], channel_multiplier=1)
    p.op("gpsimd", "memset", triu[:], 0.0)
    p.op("gpsimd", "affine_select", out=triu[:], in_=triu[:],
         compare_op=ALU.is_ge, fill=1.0, base=0,
         pattern=[[-1, 128]], channel_multiplier=1)
    p.op("gpsimd", "memset", ones_c[:], 1.0)
    p.op("gpsimd", "memset", ones_r1[:], 1.0)
    p.op("gpsimd", "memset", ones_r1b[:], 1.0, sem=s_gc)
    gc_consts = s_gc.v

    iotav = iota0[:, 1:TL + 1]
    iotam1 = iota0[:, 0:TL]

    p.dma("sync", w_sb[:], tdecay.rearrange("(g p) -> p g", p=128), s_ld,
          allow_slow_non_contiguous=True)
    p.dma("sync", u_sb[:], tfirst.rearrange("(g p) -> p g", p=128), s_ld,
          allow_slow_non_contiguous=True)
    p.dma("sync", tm_sb.ap().rearrange("p (m g) -> p m g", g=NCH),
          tmix.rearrange("m (g p) -> p m g", p=128), s_ld,
          allow_slow_non_contiguous=True)
    p.dma("sync", cmaskm[:], cmask_mul, s_ld)
    p.dma("sync", cmaska[:], cmask_add, s_ld)
    p.dma("sync", premk[:], premask, s_ld)
    p.dma("sync", b1_sb.ap().rearrange("p (e c) -> p e c", c=NFC),
          b1.rearrange("e (c p) -> p e c", p=128), s_ld,
          allow_slow_non_contiguous=True)
    ld_params = s_ld.v

    p.wait("vector", s_ld, ld_params)
    p.wait("vector", s_gc, gc_consts)
    p.op("vector", "tensor_scalar", negw_sb[:], w_sb[:], -1.0, None, ALU.mult)
    p.op("vector", "tensor_scalar", w2048_sb[:], w_sb[:], float(TL), None, ALU.mult)
    p.op("vector", "tensor_scalar", tm1_sb[:], tm_sb[:], -1.0, 1.0, ALU.mult, ALU.add)
    p.op("vector", "tensor_scalar", iota8p[:], iota8[:], 16384.0, None, ALU.add,
         sem=s_ve)
    ve_params = s_ve.v

    for w_src, w_dst in ((Wk, wb_d[0]), (Wv, wb_d[1]), (Wr, wb_d[2]), (Wo, wb_d[3])):
        p.dma("gpsimd", w_dst, w_src, s_gp)
    gp_wcast = s_gp.v
    p.dma("gpsimd", wgb.ap().rearrange("p (g e) -> p g e", e=E),
          Wg.rearrange("(g p) e -> p g e", p=128), s_gp,
          allow_slow_non_contiguous=True)
    gp_small = s_gp.v

    gp_w1cast = {}
    gp_w2cast = {}
    for e in range(E):
        for piece in range(4):
            p.dma("gpsimd", w1b_d[e, piece * 256:(piece + 1) * 256, :],
                  W1[e, piece * 256:(piece + 1) * 256, :], s_gp)
        gp_w1cast[e] = s_gp.v
        for piece in range(4):
            p.dma("gpsimd", w2b_d[e, piece * (FF // 4):(piece + 1) * (FF // 4), :],
                  W2[e, piece * (FF // 4):(piece + 1) * (FF // 4), :], s_gp)
        gp_w2cast[e] = s_gp.v

    p.op("gpsimd", "memset", zb[:], 0.0, sem=s_gc)
    gc_zb = s_gc.v
    p.wait("sync", s_gc, gc_zb)
    for rr_ in range(0, E * CL + 1, 128):
        re_ = min(rr_ + 128, E * CL + 1)
        p.dma("sync", xe_d[rr_:re_, :], zb[0:(re_ - rr_), :], s_sp)
    p.wait("gpsimd", s_gc, gc_zb)
    sp_zero_ye = p.dma("gpsimd", ye_d[TRASH:TRASH + 1, :], zb[0:1, :], s_gp)
    sp_zero = s_sp.v

    tp_done = 0
    for c in range(NTC):
        if c > 0:
            p.wait("sync", s_pe, tp_done)
        ldx = p.dma("sync", xck[:], x_pad[c * 128:(c + 1) * 128, :], s_ld)
        p.wait("tensor", s_ld, ldx)
        if c == 0:
            p.wait("tensor", s_gc, gc_consts)
        for g in range(NCH):
            bk = g % 4
            bank_wait(bk)
            p.op("tensor", "transpose", pbank[bk][:, 0:128],
                 xck[:, g * 128:(g + 1) * 128], ident[:], sem=s_pe)
            p.wait("scalar", s_pe, s_pe.v)
            tgt = p.op("scalar", "copy", xTv(g, c * 128, (c + 1) * 128),
                       pbank[bk][:, 0:128], sem=s_sc)
            bank_set(bk, s_sc, tgt)
        tp_done = s_pe.v
    ld_xT_last = p.dma("sync",
                       ARF.ap()[:, 0:NCH * (TL + 1)].rearrange(
                           "p (g t) -> p g t", g=NCH)[:, :, TL:TL + 1],
                       x_pad[TL:TL + 1, :].rearrange("o (g p) -> p g o", p=128), s_ld,
                       allow_slow_non_contiguous=True)
    sc_xT = s_sc.v

    p.wait("sync", s_gp, gp_wcast)
    for i in range(4):
        for g in range(NCH):
            p.dmat("sync", bv(WT[i] + g * H, WT[i] + (g + 1) * H),
                   wb_d[i][:, g * 128:(g + 1) * 128], s_ld)
    ld_wT = s_ld.v

    MIX_SP = {0: k_sp, 1: v_sp, 2: r_sp}
    spl = fv(SPL0, SPL0 + 4096)
    atmp = fv(20496, 20496 + TL)
    sp_proj = {}
    pe_proj_done = 0
    for m in range(3):
        p.wait("vector", s_sc, sc_xT)
        p.wait("vector", s_ld, ld_xT_last)
        p.wait("vector", s_ve, ve_params)
        if m > 0:
            p.wait("vector", s_pe, pe_proj_done)
        for g in range(NCH):
            xr = xTv(g, 1, 1 + TL)
            xp = xTv(g, 0, TL)
            p.op("vector", "tensor_scalar", atmp, xr,
                 tm_sb[:, m * NCH + g: m * NCH + g + 1], None, ALU.mult)
            p.op("vector", "scalar_tensor_tensor", bv(XMT + g * TL, XMT + (g + 1) * TL),
                 xp, tm1_sb[:, m * NCH + g: m * NCH + g + 1], atmp, ALU.mult, ALU.add,
                 sem=s_ve)
        ve_mix = s_ve.v

        p.wait("tensor", s_ld, ld_wT)
        p.wait("tensor", s_ve, ve_mix)
        for oc in range(NCH):
            buf = oc % 2
            prev = sp_proj.get(m * NCH + oc - 2, 0)
            for tq in range(4):
                bk = 4 + (tq % 2)
                bank_wait(bk)
                for g in range(NCH):
                    p.op("tensor", "matmul", pbank[bk][:],
                         bv(WT[m] + g * H + oc * 128, WT[m] + g * H + (oc + 1) * 128),
                         bv(XMT + g * TL + tq * 512, XMT + g * TL + (tq + 1) * 512),
                         start=(g == 0), stop=(g == NCH - 1),
                         sem=(s_pe if g == NCH - 1 else None))
                p.wait("scalar", s_pe, s_pe.v)
                p.wait("scalar", s_sp, prev)
                func = AF.Sigmoid if m == 2 else AF.Copy
                tgt = p.op("scalar", "activation",
                           fv(SPL0 + buf * TL + tq * 512, SPL0 + buf * TL + (tq + 1) * 512),
                           pbank[bk][:], func, sem=s_sc)
                bank_set(bk, s_sc, tgt)
            p.wait("sync", s_sc, s_sc.v)
            sp_proj[m * NCH + oc] = p.dma(
                "sync", MIX_SP[m][oc * 128:(oc + 1) * 128, :],
                fv(SPL0 + buf * TL, SPL0 + (buf + 1) * TL), s_sp)
        pe_proj_done = s_pe.v
    sp_proj_all = s_sp.v

    if DEBUG:
        dbgv = fv(20496, 20512)
        d_dbgv = nc.dram_tensor("d_dbgv", [128, 16], F32, kind="ExternalOutput").ap()
    kk = fv(KK, KK + TL)
    vv = fv(VV, VV + TL)
    a2_done = 0
    for g in range(NCH):
        p.wait("sync", s_sp, sp_proj[0 * NCH + g])
        p.wait("sync", s_sp, sp_proj[1 * NCH + g])
        if g == 0:
            p.wait("sync", s_pe, pe_proj_done)
            p.wait("sync", s_ve, s_ve.v)
        else:
            p.wait("sync", s_ve, a2_done)
        p.dma("sync", kk, k_sp[g * 128:(g + 1) * 128, :], s_ld)
        ldv = p.dma("sync", vv, v_sp[g * 128:(g + 1) * 128, :], s_ld)
        p.wait("vector", s_ld, ldv)
        a_t = fv(B1, B1 + TL)
        e_t = fv(B2_, B2_ + TL)
        ev_t = fv(B3, B3 + TL)
        p.op("vector", "scalar_tensor_tensor", a_t, iotav,
             negw_sb[:, g:g + 1], kk, ALU.mult, ALU.add)
        p.op("vector", "reduce_max", rcol1[:], a_t, axis=AX, negate=True, sem=s_ve)
        p.wait("scalar", s_ve, s_ve.v)
        p.op("scalar", "activation", e_t, a_t, AF.Exp,
             bias=rcol1[:, 0:1], scale=1.0,
             accum_out=agg[:, 2 * NCH + g: 2 * NCH + g + 1], sem=s_sc)
        p.wait("vector", s_sc, s_sc.v)
        p.op("vector", "tensor_mul", ev_t, e_t, vv)
        p.op("vector", "reduce_sum", agg[:, NCH + g:NCH + g + 1], ev_t, axis=AX)
        p.op("vector", "tensor_scalar", rcol2[:], rcol1[:], -1.0, None, ALU.mult)
        if DEBUG and g == 0:
            p.op("vector", "tensor_copy", dbgv[0:128, 0:1], rcol1[:])
            p.op("vector", "tensor_copy", dbgv[:, 1:2], rcol2[:])
            p.op("vector", "tensor_copy", dbgv[:, 2:3], w2048_sb[:, 0:1])
            p.op("vector", "tensor_copy", dbgv[:, 3:4], w_sb[:, 0:1])
            p.op("vector", "tensor_copy", dbgv[:, 4:5], negw_sb[:, 0:1])
            p.op("vector", "tensor_copy", dbgv[:, 5:6], u_sb[:, 0:1])
        p.op("vector", "tensor_tensor", rcol2[:], rcol2[:], w2048_sb[:, g:g + 1], ALU.add)
        if DEBUG and g == 0:
            p.op("vector", "tensor_copy", dbgv[:, 6:7], rcol2[:])
        p.op("vector", "tensor_tensor", agg[:, g:g + 1], rcol2[:], w_sb[:, g:g + 1],
             ALU.subtract, sem=s_ve)
        if DEBUG and g == 0:
            p.op("vector", "tensor_copy", dbgv[:, 7:8], agg[:, 0:1], sem=s_ve)
        a2_done = s_ve.v
    ve_agg = s_ve.v

    p.wait("sync", s_ve, ve_agg)
    cc_pub = p.dma("sync", cc1_in, agg[:], s_sp)
    p.wait("gpsimd", s_sp, cc_pub)
    p.op("gpsimd", "collective_compute", "AllGather", ALU.bypass,
         replica_groups=[list(range(NCORES))],
         ins=[cc1_in.opt()], outs=[cc1_out.opt()], sem=s_cc)
    cc1_done = s_cc.v
    p.op("vector", "memset", carry[:, 0:NCH], NEG)
    p.op("vector", "memset", carry[:, NCH:3 * NCH], 0.0, sem=s_ve)
    p.wait("gpsimd", s_cc, cc1_done)
    for cp in range(NCORES):
        if cp > 0:
            p.wait("gpsimd", s_ve, fold_done)
        ldt = p.dma("gpsimd", aggT[:], cc1_out[cp * 128:(cp + 1) * 128, :], s_gp)
        p.wait("vector", s_gp, ldt)
        p.op("vector", "tensor_scalar", sm1[:, 0:NCH], aggT[:, 0:NCH],
             cmaskm[:, cp:cp + 1], cmaska[:, cp:cp + 1], ALU.mult, ALU.add)
        p.op("vector", "tensor_scalar", sm1[:, NCH:3 * NCH], aggT[:, NCH:3 * NCH],
             cmaskm[:, cp:cp + 1], None, ALU.mult)
        p.op("vector", "tensor_scalar", sm2[:, 0:NCH], w2048_sb[:],
             cmaskm[:, cp:cp + 1], None, ALU.mult)
        p.nop("vector")
        p.op("vector", "tensor_tensor", sm2[:, 0:NCH], carry[:, 0:NCH],
             sm2[:, 0:NCH], ALU.add)
        p.nop("vector")
        p.op("vector", "tensor_tensor", sm3[:, 0:NCH], sm2[:, 0:NCH], sm1[:, 0:NCH],
             ALU.max)
        p.nop("vector")
        p.op("vector", "tensor_sub", sm2[:, NCH:2 * NCH], sm2[:, 0:NCH], sm3[:, 0:NCH])
        p.op("vector", "tensor_sub", sm2[:, 2 * NCH:3 * NCH], sm1[:, 0:NCH],
             sm3[:, 0:NCH], sem=s_ve)
        p.wait("scalar", s_ve, s_ve.v)
        p.op("scalar", "activation", sm2[:, NCH:3 * NCH], sm2[:, NCH:3 * NCH], AF.Exp,
             sem=s_sc)
        p.wait("vector", s_sc, s_sc.v)
        p.op("vector", "tensor_mul", sm3[:, NCH:2 * NCH], carry[:, NCH:2 * NCH],
             sm2[:, NCH:2 * NCH])
        p.op("vector", "tensor_mul", sm3[:, 2 * NCH:3 * NCH], carry[:, 2 * NCH:3 * NCH],
             sm2[:, NCH:2 * NCH])
        p.op("vector", "tensor_mul", sm1[:, NCH:2 * NCH], sm1[:, NCH:2 * NCH],
             sm2[:, 2 * NCH:3 * NCH])
        p.op("vector", "tensor_mul", sm1[:, 2 * NCH:3 * NCH], sm1[:, 2 * NCH:3 * NCH],
             sm2[:, 2 * NCH:3 * NCH])
        p.nop("vector")
        p.op("vector", "tensor_add", carry[:, NCH:3 * NCH], sm3[:, NCH:3 * NCH],
             sm1[:, NCH:3 * NCH])
        p.op("vector", "tensor_copy", carry[:, 0:NCH], sm3[:, 0:NCH], sem=s_ve)
        fold_done = s_ve.v
        if DEBUG and cp == 1:
            p.op("vector", "tensor_copy", dbgv[:, 8:9], sm1[:, 0:1], sem=s_ve)
            p.op("vector", "tensor_copy", dbgv[:, 9:10], sm2[:, 0:1], sem=s_ve)
            p.op("vector", "tensor_copy", dbgv[:, 10:11], sm3[:, 0:1], sem=s_ve)
            p.op("vector", "tensor_copy", dbgv[:, 11:12], sm2[:, NCH:NCH + 1], sem=s_ve)
            p.op("vector", "tensor_copy", dbgv[:, 12:13], sm2[:, 2 * NCH:2 * NCH + 1], sem=s_ve)
            p.op("vector", "tensor_copy", dbgv[:, 13:14], cmaskm[:, 1:2], sem=s_ve)
            p.op("vector", "tensor_copy", dbgv[:, 14:15], cmaska[:, 1:2], sem=s_ve)
            p.op("vector", "tensor_copy", dbgv[:, 15:16], carry[:, 0:1], sem=s_ve)
    ve_carry = s_ve.v

    rr_v = fv(RR, RR + TL)
    minc = fv(MINC, MINC + TL + 1)
    sinc = fv(SINC, SINC + TL + 1)
    dinc = fv(DINC, DINC + TL + 1)
    b1v = fv(B1, B1 + TL)
    b2v = fv(B2_, B2_ + TL)
    b3v = fv(B3, B3 + TL)
    b4v = fv(B4, B4 + TL)
    pav = fv(MINC, MINC + TL)
    ve_rw_prev = ve_agg
    for g in range(NCH):
        p.wait("sync", s_sp, sp_proj_all)
        p.wait("sync", s_ve, ve_rw_prev)
        p.dma("sync", kk, k_sp[g * 128:(g + 1) * 128, :], s_ld)
        p.dma("sync", vv, v_sp[g * 128:(g + 1) * 128, :], s_ld)
        p.dma("sync", rr_v, r_sp[g * 128:(g + 1) * 128, :], s_ld)
        ld_tile = s_ld.v

        p.wait("vector", s_ld, ld_tile)
        p.wait("vector", s_ve, ve_carry)
        p.op("vector", "scalar_tensor_tensor", b1v, iotav,
             negw_sb[:, g:g + 1], kk, ALU.mult, ALU.add)
        p.op("vector", "tensor_tensor", minc[:, 0:1], carry[:, g:g + 1],
             w_sb[:, g:g + 1], ALU.add)
        p.nop("vector")
        p.op("vector", "tensor_tensor_scan", minc[:, 1:TL + 1], b1v, b1v,
             minc[:, 0:1], ALU.max, ALU.max)
        p.op("vector", "tensor_sub", b2v, minc[:, 0:TL], minc[:, 1:TL + 1])
        p.op("vector", "tensor_sub", b3v, b1v, minc[:, 1:TL + 1], sem=s_ve)
        p.wait("scalar", s_ve, s_ve.v)
        p.op("scalar", "activation", b2v, b2v, AF.Exp)
        p.op("scalar", "activation", b3v, b3v, AF.Exp, sem=s_sc)
        p.wait("vector", s_sc, s_sc.v)
        p.op("vector", "tensor_mul", b4v, b3v, vv)
        p.op("vector", "tensor_copy", sinc[:, 0:1], carry[:, NCH + g:NCH + g + 1])
        p.op("vector", "tensor_copy", dinc[:, 0:1], carry[:, 2 * NCH + g:2 * NCH + g + 1])
        p.nop("vector")
        p.op("vector", "tensor_tensor_scan", sinc[:, 1:TL + 1], b2v, b4v,
             sinc[:, 0:1], ALU.mult, ALU.add)
        p.op("vector", "tensor_tensor_scan", dinc[:, 1:TL + 1], b2v, b3v,
             dinc[:, 0:1], ALU.mult, ALU.add)
        p.op("vector", "scalar_tensor_tensor", b1v, iotam1,
             w_sb[:, g:g + 1], minc[:, 0:TL], ALU.mult, ALU.add)
        p.op("vector", "scalar_tensor_tensor", pav, kk,
             u_sb[:, g:g + 1], b1v, ALU.add, ALU.max)
        p.op("vector", "tensor_sub", b1v, b1v, pav)
        p.op("vector", "tensor_sub", b3v, kk, pav, sem=s_ve)
        p.wait("scalar", s_ve, s_ve.v)
        p.op("scalar", "activation", b1v, b1v, AF.Exp)
        p.op("scalar", "activation", b3v, b3v, AF.Exp, bias=u_sb[:, g:g + 1],
             scale=1.0, sem=s_sc)
        p.wait("vector", s_sc, s_sc.v)
        p.op("vector", "tensor_mul", b2v, b1v, sinc[:, 0:TL])
        p.op("vector", "tensor_mul", b4v, b3v, vv)
        p.op("vector", "tensor_add", b2v, b2v, b4v)
        p.op("vector", "tensor_mul", pav, b1v, dinc[:, 0:TL])
        p.op("vector", "scalar_tensor_tensor", pav, pav, 1e-9, b3v,
             ALU.add, ALU.add, sem=s_ve)
        p.wait("scalar", s_ve, s_ve.v)
        p.op("scalar", "activation", pav, pav, AF.Ln)
        p.op("scalar", "activation", pav, pav, AF.Exp, scale=-1.0, sem=s_sc)
        p.wait("vector", s_sc, s_sc.v)
        p.op("vector", "tensor_mul", b2v, b2v, pav)
        p.op("vector", "tensor_mul", bv(RWO + g * TL, RWO + (g + 1) * TL),
             b2v, rr_v, sem=s_ve)
        ve_rw_prev = s_ve.v
    ve_rw_all = s_ve.v

    if DEBUG:
        d_rw = nc.dram_tensor("d_rw", [128, NCH * TL], BF16, kind="ExternalOutput").ap()
        d_carry = nc.dram_tensor("d_carry", [128, 3 * NCH], F32, kind="ExternalOutput").ap()
        d_agg = nc.dram_tensor("d_agg", [128, 3 * NCH], F32, kind="ExternalOutput").ap()
        d_aggall = nc.dram_tensor("d_aggall", [128 * NCORES, 3 * NCH], F32, kind="ExternalOutput").ap()
        p.wait("sync", s_ve, ve_rw_all)
        p.dma("sync", d_rw, bv(RWO, RWO + NCH * TL), s_sp)
        p.dma("sync", d_carry, carry[:], s_sp)
        p.dma("sync", d_agg, agg[:], s_sp)
        p.dma("sync", d_dbgv, dbgv[:], s_sp)
        p.dma("sync", d_aggall, cc1_out, s_sp)

    c1gC = fv(0, H)
    youtC = fv(H, 2 * H)
    if DEBUG:
        d_y1 = nc.dram_tensor("d_y1", [TL, H], F32, kind="ExternalOutput").ap()
    p.wait("tensor", s_ve, ve_rw_all)
    p.wait("tensor", s_ld, ld_wT)
    sp_x1_last = 0
    for c in range(NTC):
        ldx = p.dma("sync", xck[:], x_pad[1 + c * 128: 1 + (c + 1) * 128, :], s_ld)
        for oh in range(2):
            bk = oh
            bank_wait(bk)
            for g in range(NCH):
                p.op("tensor", "matmul", pbank[bk][:],
                     bv(RWO + g * TL + c * 128, RWO + g * TL + (c + 1) * 128),
                     bv(WT[3] + g * H + oh * 512, WT[3] + g * H + (oh + 1) * 512),
                     start=(g == 0), stop=(g == NCH - 1),
                     sem=(s_pe if g == NCH - 1 else None))
            p.wait("vector", s_pe, s_pe.v)
            p.wait("vector", s_ld, ldx)
            p.wait("vector", s_sp, sp_x1_last)
            tgt = p.op("vector", "tensor_add", youtC[:, oh * 512:(oh + 1) * 512]
                       if False else fv(H + oh * 512, H + (oh + 1) * 512),
                       pbank[bk][:], xck[:, oh * 512:(oh + 1) * 512], sem=s_ve)
            bank_set(bk, s_ve, tgt)
        if DEBUG:
            p.wait("sync", s_ve, s_ve.v)
            p.dma("sync", d_y1[c * 128:(c + 1) * 128, :], youtC, s_sp)
        y = youtC
        p.op("vector", "reduce_sum", rcol1[:], y, axis=AX, negate=True)
        p.nop("vector")
        p.op("vector", "tensor_scalar", rcol1[:], rcol1[:], 1.0 / H, None, ALU.mult)
        p.nop("vector")
        p.op("vector", "tensor_scalar", c1gC, y, rcol1[:, 0:1], None, ALU.add, sem=s_ve)
        p.wait("scalar", s_ve, s_ve.v)
        p.op("scalar", "activation", fv(2 * H, 3 * H), c1gC, AF.Square,
             accum_out=rcol2[:, 0:1], sem=s_sc)
        p.wait("vector", s_sc, s_sc.v)
        p.op("vector", "tensor_scalar", rcol2[:], rcol2[:], 1.0 / H, 1e-5,
             ALU.mult, ALU.add, sem=s_ve)
        p.wait("scalar", s_ve, s_ve.v)
        p.op("scalar", "activation", rcol3[:], rcol2[:], AF.Ln)
        p.nop("scalar")
        p.op("scalar", "activation", rcol3[:], rcol3[:], AF.Exp, scale=-0.5, sem=s_sc)
        p.wait("vector", s_sc, s_sc.v)
        p.op("vector", "tensor_scalar", c1gC, c1gC, rcol3[:, 0:1], None, ALU.mult)
        p.op("vector", "tensor_copy", bv(X1B + c * H, X1B + (c + 1) * H), c1gC, sem=s_ve)
        p.wait("sync", s_ve, s_ve.v)
        sp_x1_last = p.dma("sync", x1_sp[c * 128:(c + 1) * 128, :], c1gC, s_sp)
    sp_x1 = s_sp.v
    ve_x1b = s_ve.v

    x1Tc = bv(X1TC, X1TC + H)
    if DEBUG:
        dbg2 = fv(3 * H, 3 * H + 16)
        d_dbg2 = nc.dram_tensor("d_dbg2", [128, 16], F32, kind="ExternalOutput").ap()
    p.wait("tensor", s_ve, ve_x1b)
    p.wait("tensor", s_gp, gp_small)
    logit_mm_done = 0
    for c in range(NTC):
        p.wait("sync", s_pe, logit_mm_done)
        p.wait("sync", s_sp, sp_x1)
        ldxc = p.dma("sync", xck[:], x1_sp[c * 128:(c + 1) * 128, :], s_ld)
        p.wait("tensor", s_ld, ldxc)
        for g in range(NCH):
            bk = 4 + (g % 2)
            bank_wait(bk)
            p.op("tensor", "transpose", pbank[bk][:, 0:128],
                 xck[:, g * 128:(g + 1) * 128], ident[:], sem=s_pe)
            p.wait("scalar", s_pe, s_pe.v)
            p.wait("scalar", s_pe, logit_mm_done)
            tgt = p.op("scalar", "copy", x1Tc[:, g * 128:(g + 1) * 128],
                       pbank[bk][:, 0:128], sem=s_sc)
            bank_set(bk, s_sc, tgt)
        bank_wait(6)
        p.wait("tensor", s_sc, s_sc.v)
        for g in range(NCH):
            p.op("tensor", "matmul", pbank[6][:, 0:E],
                 x1Tc[:, g * 128:(g + 1) * 128], wgb[:, g * E:(g + 1) * E],
                 start=(g == 0), stop=(g == NCH - 1),
                 sem=(s_pe if g == NCH - 1 else None))
        logit_mm_done = s_pe.v
        p.wait("vector", s_pe, s_pe.v)
        tgt = p.op("vector", "tensor_copy", logit[:, c * E:(c + 1) * E],
                   pbank[6][:, 0:E], sem=s_ve)
        bank_set(6, s_ve, tgt)

        L = logit[:, c * E:(c + 1) * E]
        p.op("vector", "reduce_max", rcol1[:], L, axis=AX, negate=True)
        p.nop("vector")
        p.op("vector", "tensor_scalar", rsc1[:], L, rcol1[:, 0:1], 0.0, ALU.add, ALU.is_ge)
        p.nop("vector")
        p.op("vector", "scalar_tensor_tensor", rsc2[:], rsc1[:], 16384.0, iota8p[:],
             ALU.mult, ALU.subtract)
        p.nop("vector")
        p.op("vector", "reduce_max", rcol2[:], rsc2[:], axis=AX)
        p.nop("vector")
        p.op("vector", "tensor_scalar", mask1[:, c * E:(c + 1) * E], iota8[:],
             rcol2[:, 0:1], 0.0, ALU.add, ALU.is_equal)
        if DEBUG and c == 0:
            p.op("vector", "tensor_copy", dbg2[:, 0:8], rsc1[:], sem=s_ve)
            p.op("vector", "tensor_copy", dbg2[:, 8:9], rcol1[:], sem=s_ve)
            p.op("vector", "tensor_copy", dbg2[:, 9:10], rcol2[:], sem=s_ve)
        p.nop("vector")
        p.op("vector", "scalar_tensor_tensor", rsc3[:], mask1[:, c * E:(c + 1) * E],
             -3e38, L, ALU.mult, ALU.add)
        p.nop("vector")
        p.op("vector", "reduce_max", rcol3[:], rsc3[:], axis=AX, negate=True)
        p.nop("vector")
        p.op("vector", "tensor_scalar", rsc1[:], rsc3[:], rcol3[:, 0:1], 0.0,
             ALU.add, ALU.is_ge)
        p.nop("vector")
        p.op("vector", "scalar_tensor_tensor", rsc2[:], rsc1[:], 16384.0, iota8p[:],
             ALU.mult, ALU.subtract)
        p.nop("vector")
        p.op("vector", "reduce_max", rcol2[:], rsc2[:], axis=AX)
        p.nop("vector")
        p.op("vector", "tensor_scalar", mask2[:, c * E:(c + 1) * E], iota8[:],
             rcol2[:, 0:1], 0.0, ALU.add, ALU.is_equal, sem=s_ve)
        p.wait("scalar", s_ve, s_ve.v)
        p.op("scalar", "activation", rsc3[:], L, AF.Exp, bias=rcol1[:, 0:1], scale=1.0,
             accum_out=rcol3[:, 0:1], sem=s_sc)
        p.wait("vector", s_sc, s_sc.v)
        p.op("vector", "tensor_mul", rsc1[:], rsc3[:], mask1[:, c * E:(c + 1) * E])
        p.nop("vector")
        p.op("vector", "reduce_sum", rcol2[:], rsc1[:], axis=AX)
        p.nop("vector")
        p.op("vector", "tensor_mul", rsc1[:], rsc3[:], mask2[:, c * E:(c + 1) * E])
        p.nop("vector")
        p.op("vector", "reduce_sum", rcol1[:], rsc1[:], axis=AX)
        p.nop("vector")
        p.op("vector", "tensor_add", rsc2[:, 0:1], rcol2[:], rcol1[:])
        p.op("vector", "tensor_scalar", rsc2[:, 1:2], rcol3[:, 0:1], 1e-9, None, ALU.mult)
        p.nop("vector")
        p.op("vector", "tensor_add", rsc2[:, 0:1], rsc2[:, 0:1], rsc2[:, 1:2])
        p.nop("vector")
        p.op("vector", "reciprocal", rsc2[:, 2:3], rsc2[:, 0:1])
        p.nop("vector")
        p.op("vector", "tensor_mul", w1k[:, c:c + 1], rcol2[:], rsc2[:, 2:3])
        p.op("vector", "tensor_mul", w2k[:, c:c + 1], rcol1[:], rsc2[:, 2:3], sem=s_ve)
    ve_route1 = s_ve.v

    p.wait("tensor", s_ve, ve_route1)
    bank_wait(7)
    for c in range(NTC):
        p.op("tensor", "matmul", pbank[7][0:1, 0:E], ones_c[:],
             mask1[:, c * E:(c + 1) * E], start=(c == 0), stop=(c == NTC - 1),
             sem=(s_pe if c == NTC - 1 else None))
    bank_wait(6)
    for c in range(NTC):
        p.op("tensor", "matmul", pbank[6][0:1, 0:E], ones_c[:],
             mask2[:, c * E:(c + 1) * E], start=(c == 0), stop=(c == NTC - 1),
             sem=(s_pe if c == NTC - 1 else None))
    p.wait("vector", s_pe, s_pe.v)
    p.op("vector", "tensor_copy", sm1[0:1, 0:E], pbank[7][0:1, 0:E])
    tgt = p.op("vector", "tensor_copy", sm1[0:1, E:2 * E], pbank[6][0:1, 0:E], sem=s_ve)
    bank_set(6, s_ve, tgt)
    bank_set(7, s_ve, tgt)
    p.wait("sync", s_ve, s_ve.v)
    cc2_pub = p.dma("sync", cc2_in, sm1[0:1, 0:2 * E], s_sp)
    p.wait("gpsimd", s_sp, cc2_pub)
    p.op("gpsimd", "collective_compute", "AllGather", ALU.bypass,
         replica_groups=[list(range(NCORES))],
         ins=[cc2_in.opt()], outs=[cc2_out.opt()], sem=s_cc)
    cc2_done = s_cc.v
    p.wait("gpsimd", s_cc, cc2_done)
    ld_cc2 = p.dma("gpsimd", cc2_sb[:], cc2_out, s_gp)
    p.wait("tensor", s_gp, ld_cc2)
    bank_wait(7)
    p.op("tensor", "matmul", pbank[7][0:1, 0:2 * E], premk[:], cc2_sb[:],
         start=True, stop=True)
    bank_wait(5)
    p.op("tensor", "matmul", pbank[5][0:1, 0:E], ones_c[0:E, 0:1],
         cc2_sb[:, 0:E], start=True, stop=True, sem=s_pe)
    p.wait("vector", s_pe, s_pe.v)
    p.op("vector", "tensor_copy", offs[:], pbank[7][0:1, 0:2 * E])
    tgt = p.op("vector", "tensor_copy", offg[:], pbank[5][0:1, 0:E], sem=s_ve)
    bank_set(5, s_ve, tgt)
    bank_set(7, s_ve, tgt)
    ve_offs = s_ve.v

    def cumsum_pass(msk, possb, keepsb, locol):
        for c in range(NTC):
            bank_wait(5)
            p.wait("tensor", s_ve, s_ve.v)
            p.op("tensor", "matmul", pbank[5][:, 0:E], triu[:],
                 msk[:, c * E:(c + 1) * E], start=True, stop=False)
            p.op("tensor", "matmul", pbank[5][:, 0:E], ones_r1[:], locol,
                 start=False, stop=True, sem=s_pe)
            p.wait("vector", s_pe, s_pe.v)
            tgt_ = p.op("vector", "tensor_mul", rsc1[:], pbank[5][:, 0:E],
                        msk[:, c * E:(c + 1) * E], sem=s_ve)
            bank_set(5, s_ve, tgt_)
            p.nop("vector")
            p.op("vector", "reduce_sum", possb[:, c:c + 1], rsc1[:], axis=AX)
            if keepsb is not None:
                p.nop("vector")
                p.op("vector", "tensor_scalar", keepsb[:, c:c + 1], possb[:, c:c + 1],
                     float(CGLOB), 0.0, ALU.subtract, ALU.is_lt)
            bank_wait(6)
            p.op("tensor", "matmul", pbank[6][0:1, 0:E], ones_c[:],
                 msk[:, c * E:(c + 1) * E], start=True, stop=True, sem=s_pe)
            p.wait("vector", s_pe, s_pe.v)
            tgt_ = p.op("vector", "tensor_add", locol, locol, pbank[6][0:1, 0:E],
                        sem=s_ve)
            bank_set(6, s_ve, tgt_)

    p.wait("vector", s_ve, ve_offs)
    p.op("vector", "tensor_copy", loco[0:1, 0:E], offs[0:1, 0:E], sem=s_ve)
    cumsum_pass(mask1, pos1, keep1, loco[0:1, 0:E])
    p.op("vector", "tensor_add", loco[0:1, E:2 * E], offs[0:1, E:2 * E],
         offg[0:1, 0:E], sem=s_ve)
    cumsum_pass(mask2, pos2, keep2, loco[0:1, E:2 * E])

    for c in range(NTC):
        p.op("vector", "tensor_scalar", kept1[:, c * E:(c + 1) * E],
             mask1[:, c * E:(c + 1) * E], keep1[:, c:c + 1], None, ALU.mult)
        p.op("vector", "tensor_scalar", kept2[:, c * E:(c + 1) * E],
             mask2[:, c * E:(c + 1) * E], keep2[:, c:c + 1], None, ALU.mult)
        p.op("vector", "tensor_mul", w1k[:, c:c + 1], w1k[:, c:c + 1], keep1[:, c:c + 1])
        p.op("vector", "tensor_mul", w2k[:, c:c + 1], w2k[:, c:c + 1], keep2[:, c:c + 1],
             sem=s_ve)

    p.op("vector", "memset", loco[0:1, 0:E], 0.0, sem=s_ve)
    cumsum_pass(kept1, slot1, None, loco[0:1, 0:E])
    p.op("vector", "tensor_copy", loco[0:1, E:2 * E], loco[0:1, 0:E], sem=s_ve)
    cumsum_pass(kept2, slot2, None, loco[0:1, E:2 * E])

    for c in range(NTC):
        for (msk, kp, slotsb, dstsi) in ((mask1, keep1, slot1, dst1i),
                                         (mask2, keep2, slot2, dst2i)):
            p.op("vector", "tensor_mul", rsc1[:], iota8[:], msk[:, c * E:(c + 1) * E])
            p.nop("vector")
            p.op("vector", "reduce_sum", rcol1[:], rsc1[:], axis=AX)
            p.op("vector", "tensor_scalar", rcol2[:], slotsb[:, c:c + 1],
                 float(CL), 0.0, ALU.subtract, ALU.is_lt)
            p.nop("vector")
            p.op("vector", "tensor_mul", rcol2[:], rcol2[:], kp[:, c:c + 1])
            p.op("vector", "tensor_scalar", rcol1[:], rcol1[:], float(CL), None, ALU.mult)
            p.nop("vector")
            p.op("vector", "tensor_tensor", rcol1[:], rcol1[:], slotsb[:, c:c + 1], ALU.add)
            p.nop("vector")
            p.op("vector", "tensor_scalar", rcol1[:], rcol1[:], float(TRASH), None,
                 ALU.subtract)
            p.nop("vector")
            p.op("vector", "tensor_mul", rcol1[:], rcol1[:], rcol2[:])
            p.nop("vector")
            p.op("vector", "tensor_scalar", rcol3[:], rcol1[:], float(TRASH), None, ALU.add)
            p.nop("vector")
            p.op("vector", "tensor_copy", dstsi[:, c:c + 1], rcol3[:], sem=s_ve)
    ve_dests = s_ve.v

    if DEBUG:
        d_rt = nc.dram_tensor("d_rt", [128, 6 * NTC], F32, kind="ExternalOutput").ap()
        d_rti = nc.dram_tensor("d_rti", [128, 2 * NTC], I32, kind="ExternalOutput").ap()
        d_msk = nc.dram_tensor("d_msk", [128, 3 * NTC * E], F32, kind="ExternalOutput").ap()
        p.wait("sync", s_ve, ve_dests)
        p.dma("sync", d_rt[:, 0 * NTC:1 * NTC], pos1[:], s_sp)
        p.dma("sync", d_rt[:, 1 * NTC:2 * NTC], pos2[:], s_sp)
        p.dma("sync", d_rt[:, 2 * NTC:3 * NTC], slot1[:], s_sp)
        p.dma("sync", d_rt[:, 3 * NTC:4 * NTC], slot2[:], s_sp)
        p.dma("sync", d_rt[:, 4 * NTC:5 * NTC], w1k[:], s_sp)
        p.dma("sync", d_rt[:, 5 * NTC:6 * NTC], w2k[:], s_sp)
        p.dma("sync", d_rti[:, 0:NTC], dst1i[:], s_sp)
        p.dma("sync", d_rti[:, NTC:2 * NTC], dst2i[:], s_sp)
        p.dma("sync", d_msk[:, 0:NTC * E], logit[:], s_sp)
        p.dma("sync", d_msk[:, NTC * E:2 * NTC * E], mask1[:], s_sp)
        p.dma("sync", d_msk[:, 2 * NTC * E:3 * NTC * E], mask2[:], s_sp)
        p.dma("sync", d_dbg2, dbg2, s_sp)

    p.wait("gpsimd", s_ve, ve_dests)
    p.wait("gpsimd", s_sp, sp_zero)
    for c in range(NTC):
        for dsts in (dst1i, dst2i):
            p.op("gpsimd", "indirect_dma_start",
                 out=xe_d, out_offset=IndirectOffsetOnAxis(dsts[:, c:c + 1], 0),
                 in_=bv(X1B + c * H, X1B + (c + 1) * H), in_offset=None,
                 sem=s_ix, inc=16)
    ix_scatter = s_ix.v

    if DEBUG:
        d_h1 = [nc.dram_tensor(f"d_h1{i}", [128, 16 * CL], BF16, kind="ExternalOutput").ap()
                for i in range(2)]
        d_yep = nc.dram_tensor("d_yep", [128, E * H], F32, kind="ExternalOutput").ap()
    yev = fv(0, 2 * H)
    ye_part = fv(2 * H, 2 * H + E * H)
    b2e = ARB[0:1, B2O:B2O + H]
    NHF = NFC // 2
    pe_w1_done = {}
    pe_w2_done = {}
    sp_ye = {}
    w1sl_guard = {}
    p.wait("scalar", s_sp, sp_zero)
    for e in range(E):
        p.wait("sync", s_ix, ix_scatter)
        if e >= 1:
            p.wait("sync", s_pe, pe_w1_done[e - 1])
        for gg in range(NCH):
            p.dmat("sync", bv(XEO + gg * CL, XEO + (gg + 1) * CL),
                   xe_d[e * CL:(e + 1) * CL, gg * 128:(gg + 1) * 128], s_ld)
        ld_xeT = s_ld.v

        for hf in range(2):
            if hf == 0:
                if e >= 1:
                    p.wait("gpsimd", s_pe, pe_w2_done.get((e - 1, 1), 0))
                p.dma("gpsimd", b2e, b2[e:e + 1, :], s_gp)
                gp_b2e = s_gp.v
            f0 = hf * (FF // 2)
            p.wait("sync", s_gp, gp_w2cast[e])
            if e + hf >= 1:
                p.wait("sync", s_pe, pe_w2_done.get((e, hf - 1),
                                                    pe_w2_done.get((e - 1, 1), 0)))
            for piece in range(2):
                q0 = f0 + piece * (FF // 4)
                p.dma("sync",
                      bv(W2O + piece * 8 * H, W2O + (piece + 1) * 8 * H).rearrange(
                          "p (t d) -> p t d", d=H),
                      w2b_d[e, q0:q0 + FF // 4, :].rearrange("(t p) d -> p t d", p=128),
                      s_l2)
            gp_w2 = s_l2.v

            if e + hf >= 1:
                p.wait("scalar", s_pe, pe_w2_done.get((e, hf - 1),
                                                      pe_w2_done.get((e - 1, 1), 0)))
            for fs in range(4):
                buf = fs % 2
                p.wait("scalar", s_pe, w1sl_guard.get(buf, 0))
                if fs == 0 and hf == 0:
                    if e == 0:
                        p.wait("scalar", s_ix, ix_scatter)
                    p.wait("scalar", s_gp, gp_w1cast[e])
                p.dma("scalar",
                      bv(W1O + buf * NCH * 512, W1O + (buf + 1) * NCH * 512).rearrange(
                          "p (t f) -> p t f", f=512),
                      w1b_d[e, :, f0 + fs * 512: f0 + fs * 512 + 512].rearrange(
                          "(t p) f -> p t f", p=128),
                      s_l2)
                gp_w1 = s_l2.v
                p.wait("tensor", s_l2, gp_w1)
                p.wait("tensor", s_ld, ld_xeT)
                for fc in range(4):
                    fcl = fs * 4 + fc
                    fcg = hf * NHF + fcl
                    for sh in range(2):
                        bk = sh
                        bank_wait(bk)
                        for dt in range(NCH):
                            p.op("tensor", "matmul", pbank[bk][:],
                                 bv(W1O + buf * NCH * 512 + dt * 512 + fc * 128,
                                    W1O + buf * NCH * 512 + dt * 512 + (fc + 1) * 128),
                                 bv(XEO + dt * CL + sh * 512, XEO + dt * CL + (sh + 1) * 512),
                                 start=(dt == 0), stop=(dt == NCH - 1),
                                 sem=(s_pe if dt == NCH - 1 else None))
                        p.wait("scalar", s_pe, s_pe.v)
                        tgt = p.op("scalar", "activation",
                                   bv(H1O + fcl * CL + sh * 512,
                                      H1O + fcl * CL + (sh + 1) * 512),
                                   pbank[bk][:], AF.Gelu,
                                   bias=b1_sb[:, e * NFC + fcg: e * NFC + fcg + 1],
                                   scale=1.0, sem=s_sc)
                        bank_set(bk, s_sc, tgt)
                w1sl_guard[buf] = s_pe.v
            sc_h1_done = s_sc.v
            if DEBUG and e == 0:
                p.wait("sync", s_sc, sc_h1_done)
                p.dma("sync", d_h1[hf], bv(H1O, H1O + 16 * CL), s_sp)

            if DEBUG and e == 0 and hf == 1:
                p.wait("sync", s_sc, s_sc.v)
                p.dma("sync", d_yep, ye_part, s_sp)
            p.wait("tensor", s_l2, gp_w2)
            if hf == 1:
                p.wait("tensor", s_gp, gp_b2e)
            p.wait("tensor", s_sc, sc_h1_done)
            for sc8 in range(8):
                ybuf = sc8 % 2
                for dh in range(2):
                    bk = 2 + ((sc8 * 2 + dh) % 4)
                    bank_wait(bk)
                    for fcl in range(NHF):
                        last = (hf == 0 and fcl == NHF - 1)
                        p.op("tensor", "matmul", pbank[bk][:],
                             bv(H1O + fcl * CL + sc8 * 128, H1O + fcl * CL + (sc8 + 1) * 128),
                             bv(W2O + fcl * H + dh * 512, W2O + fcl * H + (dh + 1) * 512),
                             start=(fcl == 0), stop=last,
                             sem=(s_pe if last else None))
                    if hf == 1:
                        p.op("tensor", "matmul", pbank[bk][:],
                             ones_r1b[0:1, 0:128],
                             b2e[0:1, dh * 512:(dh + 1) * 512],
                             start=False, stop=True, sem=s_pe)
                    if hf == 0:
                        p.wait("scalar", s_pe, s_pe.v)
                        tgt = p.op("scalar", "copy",
                                   fv(2 * H + sc8 * H + dh * 512,
                                      2 * H + sc8 * H + (dh + 1) * 512),
                                   pbank[bk][:], sem=s_sc)
                        bank_set(bk, s_sc, tgt)
                    else:
                        p.wait("vector", s_pe, s_pe.v)
                        p.wait("vector", s_sp, sp_ye.get((e, sc8 - 2),
                                                         sp_ye.get((e - 1, 6 + sc8), 0)))
                        tgt = p.op("vector", "tensor_add",
                                   fv(ybuf * H + dh * 512, ybuf * H + (dh + 1) * 512),
                                   pbank[bk][:],
                                   fv(2 * H + sc8 * H + dh * 512,
                                      2 * H + sc8 * H + (dh + 1) * 512), sem=s_ve)
                        bank_set(bk, s_ve, tgt)
                if hf == 1:
                    p.wait("sync", s_ve, s_ve.v)
                    sp_ye[(e, sc8)] = p.dma(
                        "sync", ye_d[e * CL + sc8 * 128: e * CL + (sc8 + 1) * 128, :],
                        fv(ybuf * H, (ybuf + 1) * H), s_sp)
            pe_w2_done[(e, hf)] = s_pe.v
        pe_w1_done[e] = s_pe.v
    sp_ye_all = s_sp.v

    ot_done = {}
    ve_e_last = {}
    for c in range(NTC):
        half = c % 2
        if c == 0:
            p.wait("gpsimd", s_sp, sp_ye_all)
        if c >= 2:
            p.wait("gpsimd", s_ot, ot_done[c - 2])
        p.op("gpsimd", "indirect_dma_start",
             out=fv(half * H, (half + 1) * H), out_offset=None,
             in_=ye_d, in_offset=IndirectOffsetOnAxis(dst1i[:, c:c + 1], 0),
             sem=s_ix, inc=16)
        p.op("gpsimd", "indirect_dma_start",
             out=fv(2 * H + half * H, 2 * H + (half + 1) * H), out_offset=None,
             in_=ye_d, in_offset=IndirectOffsetOnAxis(dst2i[:, c:c + 1], 0),
             sem=s_ix, inc=16)
        ixg = s_ix.v
        if c >= 2:
            p.wait("sync", s_ot, ot_done[c - 2])
        ldx1 = p.dma("sync", fv(4 * H + half * H, 4 * H + (half + 1) * H),
                     x1_sp[c * 128:(c + 1) * 128, :], s_ld)
        p.wait("vector", s_ix, ixg)
        p.wait("vector", s_ld, ldx1)
        yy = fv(6 * H + half * H, 6 * H + (half + 1) * H)
        p.op("vector", "tensor_scalar", yy, fv(half * H, (half + 1) * H),
             w1k[:, c:c + 1], None, ALU.mult)
        p.op("vector", "scalar_tensor_tensor", yy, fv(2 * H + half * H, 2 * H + (half + 1) * H),
             w2k[:, c:c + 1], yy, ALU.mult, ALU.add)
        p.op("vector", "tensor_add", yy, yy, fv(4 * H + half * H, 4 * H + (half + 1) * H))
        p.op("vector", "reduce_sum", rcol1[:], yy, axis=AX, negate=True)
        p.nop("vector")
        p.op("vector", "tensor_scalar", rcol1[:], rcol1[:], 1.0 / H, None, ALU.mult)
        p.nop("vector")
        p.op("vector", "tensor_scalar", yy, yy, rcol1[:, 0:1], None, ALU.add, sem=s_ve)
        p.wait("scalar", s_ve, s_ve.v)
        p.op("scalar", "activation", fv(8 * H, 9 * H), yy, AF.Square,
             accum_out=rcol2[:, 0:1], sem=s_sc)
        p.wait("vector", s_sc, s_sc.v)
        p.op("vector", "tensor_scalar", rcol2[:], rcol2[:], 1.0 / H, 1e-5,
             ALU.mult, ALU.add, sem=s_ve)
        p.wait("scalar", s_ve, s_ve.v)
        p.op("scalar", "activation", rcol3[:], rcol2[:], AF.Ln)
        p.nop("scalar")
        p.op("scalar", "activation", rcol3[:], rcol3[:], AF.Exp, scale=-0.5, sem=s_sc)
        p.wait("vector", s_sc, s_sc.v)
        p.op("vector", "tensor_scalar", yy, yy, rcol3[:, 0:1], None, ALU.mult, sem=s_ve)
        ve_e_last[c] = s_ve.v
        p.wait("sync", s_ve, s_ve.v)
        ot_done[c] = p.dma("sync", out[c * 128:(c + 1) * 128, :], yy, s_ot)
    p.wait("sync", s_ot, s_ot.v)

    with nc.Block() as block:
        block.sync(lambda e: [fn(e) for fn in p.ops["sync"]])
        block.gpsimd(lambda e: [fn(e) for fn in p.ops["gpsimd"]])
        block.vector(lambda e: [fn(e) for fn in p.ops["vector"]])
        block.scalar(lambda e: [fn(e) for fn in p.ops["scalar"]])
        block.tensor(lambda e: [fn(e) for fn in p.ops["tensor"]])
    ctx.close()
    return nc


_BUILT = None


def _get_built():
    global _BUILT
    if _BUILT is None:
        _BUILT = build_nc()
    return _BUILT


def _host_inputs(inputs):
    x = np.ascontiguousarray(np.asarray(inputs["L0_attn_x"], np.float32).reshape(B * T, H))
    tmix = np.ascontiguousarray(np.stack([
        np.asarray(inputs["L0_attn_time_mix_k"], np.float32).reshape(H),
        np.asarray(inputs["L0_attn_time_mix_v"], np.float32).reshape(H),
        np.asarray(inputs["L0_attn_time_mix_r"], np.float32).reshape(H),
    ]))
    common = {
        "Wk": np.ascontiguousarray(np.asarray(inputs["L0_attn_W_k"], np.float32)),
        "Wv": np.ascontiguousarray(np.asarray(inputs["L0_attn_W_v"], np.float32)),
        "Wr": np.ascontiguousarray(np.asarray(inputs["L0_attn_W_r"], np.float32)),
        "Wo": np.ascontiguousarray(np.asarray(inputs["L0_attn_W_o"], np.float32)),
        "tdecay": np.ascontiguousarray(np.asarray(inputs["L0_attn_time_decay"], np.float32)),
        "tfirst": np.ascontiguousarray(np.asarray(inputs["L0_attn_time_first"], np.float32)),
        "tmix": tmix,
        "Wg": np.ascontiguousarray(np.asarray(inputs["L0_ffn_Wg"], np.float32)),
        "W1": np.ascontiguousarray(np.asarray(inputs["L0_ffn_W1"], np.float32)),
        "b1": np.ascontiguousarray(np.asarray(inputs["L0_ffn_b1"], np.float32)),
        "W2": np.ascontiguousarray(np.asarray(inputs["L0_ffn_W2"], np.float32)),
        "b2": np.ascontiguousarray(np.asarray(inputs["L0_ffn_b2"], np.float32)),
    }
    in_maps = []
    for c in range(NCORES):
        t0 = c * TL
        if (t0 % T) == 0:
            xp = np.vstack([np.zeros((1, H), np.float32), x[t0:t0 + TL]])
        else:
            xp = x[t0 - 1:t0 + TL]
        mm = np.zeros((128, NCORES), np.float32)
        ma = np.full((128, NCORES), NEG, np.float32)
        pm = np.zeros((NCORES, 1), np.float32)
        for cp in range(c - (c % 4), c):
            mm[:, cp] = 1.0
            ma[:, cp] = 0.0
        for cp in range(c):
            pm[cp, 0] = 1.0
        m = dict(common)
        m["x_pad"] = np.ascontiguousarray(xp)
        m["cmask_mul"] = mm
        m["cmask_add"] = ma
        m["premask"] = pm
        in_maps.append(m)
    return in_maps


def kernel(**inputs):
    nc = _get_built()
    in_maps = _host_inputs(inputs)
    res = run_bass_kernel_spmd(nc, in_maps, core_ids=list(range(NCORES)))
    outs = [res.results[c]["y"] for c in range(NCORES)]
    return np.concatenate(outs, axis=0).reshape(B, T, H).astype(np.float32)




# version 10
# speedup vs baseline: 1.3609x; 1.3609x over previous
import numpy as np

import concourse.bass as bass
import concourse.mybir as mybir
from concourse.bass_utils import run_bass_kernel_spmd
from concourse.bass_core_rust import IndirectOffsetOnAxis

F32 = mybir.dt.float32
BF16 = mybir.dt.bfloat16
I32 = mybir.dt.int32
AF = mybir.ActivationFunctionType
ALU = mybir.AluOpType
AX = mybir.AxisListType.X

NCORES = 8
B, T, H = 2, 8192, 1024
TL = 2048
NTC = TL // 128
NCH = H // 128
E = 8
FF = 4096
NFC = FF // 128
CGLOB = 8192
CL = 1024
NEG = -1e38
TRASH = E * CL


class SemCtr:
    def __init__(self, handle, name):
        self.h = handle
        self.v = 0
        self.name = name


class Plan:
    def __init__(self):
        self.ops = {k: [] for k in ("sync", "gpsimd", "vector", "scalar", "tensor")}
        self.hist = {}

    def emit(self, eng, fn):
        self.ops[eng].append(fn)

    def wait(self, eng, sem, val=None):
        if val is None:
            val = sem.v
        assert val <= sem.v, f"wait on future value of {sem.name}: {val} > {sem.v}"
        if val > 0:
            self.emit(eng, lambda e, s=sem.h, v=val: e.wait_ge(s, v))

    def dma(self, eng, out, in_, sem, **kw):
        sem.v += 16
        self.emit(eng, lambda e, o=out, i=in_, s=sem.h, kw=kw:
                  e.dma_start(out=o, in_=i, **kw).then_inc(s, 16))
        return sem.v

    def dmat(self, eng, out, in_, sem):
        sem.v += 16
        self.emit(eng, lambda e, o=out, i=in_, s=sem.h:
                  e.dma_start_transpose(o, i).then_inc(s, 16))
        return sem.v

    scalar_nop_fn = None

    def nop(self, eng):
        if eng == "scalar":
            self.emit(eng, self.scalar_nop_fn)
        else:
            self.emit(eng, lambda e: e.engine_nop())

    @staticmethod
    def _foot(a):
        try:
            t = a.tensor.name
            off = a.offset
            dims = list(a.ap)
        except Exception:
            return None
        if not dims:
            return (t, 0, 1, off, off + 1, 1)
        pitch, np_ = dims[0]
        if pitch <= 0:
            pitch = 1
        p0 = off // pitch
        c0 = off % pitch
        span = sum(st * (ct - 1) for st, ct in dims[1:]) + 1
        return (t, p0, p0 + np_, c0, c0 + span, span)

    @staticmethod
    def _ovl(f1, f2):
        return (f1[0] == f2[0] and f1[1] < f2[2] and f2[1] < f1[2]
                and f1[3] < f2[4] and f2[3] < f1[4])

    WINDOW = 8

    def _hazard(self, eng, writes, reads):
        hist = self.hist.setdefault(eng, [])
        need = 0
        for (pw, pr, psem, pval) in hist:
            conflict = False
            for f1 in pw:
                for f2 in writes + reads:
                    if self._ovl(f1, f2) and (f1[5] < 512 or f2[5] < 512):
                        conflict = True
            for f1 in pr:
                for f2 in writes:
                    if self._ovl(f1, f2) and (f1[5] < 512 or f2[5] < 512):
                        conflict = True
            if conflict:
                if psem is not None:
                    need = max(need, pval) if psem is self.def_sem[eng] else need
                    if psem is not self.def_sem[eng]:
                        self.wait(eng, psem, pval)
                else:
                    self.nop(eng)
                    self.nop(eng)
        if need:
            self.wait(eng, self.def_sem[eng], need)

    def op(self, eng, meth, *args, sem=None, inc=1, **kw):
        writes = reads = None
        if eng in ("vector", "scalar"):
            w = [self._foot(args[0])] if args else []
            w += [self._foot(kw["accum_out"])] if kw.get("accum_out") is not None else []
            r = [self._foot(a) for a in args[1:]]
            r += [self._foot(v) for k2, v in kw.items() if k2 != "accum_out"]
            writes = [f for f in w if f is not None]
            reads = [f for f in r if f is not None]
            self._hazard(eng, writes, reads)
            small = any(f[5] < 512 for f in writes)
            if sem is None and small:
                sem = self.def_sem[eng]
                inc = 1
        if sem is not None:
            sem.v += inc
            ret = sem.v
            self.emit(eng, lambda e, a=args, kw=kw, s=sem.h, n=inc:
                      getattr(e, meth)(*a, **kw).then_inc(s, n))
        else:
            ret = None
            self.emit(eng, lambda e, a=args, kw=kw: getattr(e, meth)(*a, **kw))
        if writes is not None:
            hist = self.hist.setdefault(eng, [])
            hist.append((writes, reads, sem, sem.v if sem is not None else None))
            if len(hist) > self.WINDOW:
                hist.pop(0)
        return ret


DEBUG = False


def build_nc():
    nc = bass.Bass("TRN2", target_bir_lowering=False)
    SCR = "ExternalOutput" if DEBUG else "Internal"

    x_pad = nc.dram_tensor("x_pad", [TL + 1, H], F32, kind="ExternalInput").ap()
    Wk = nc.dram_tensor("Wk", [H, H], F32, kind="ExternalInput").ap()
    Wv = nc.dram_tensor("Wv", [H, H], F32, kind="ExternalInput").ap()
    Wr = nc.dram_tensor("Wr", [H, H], F32, kind="ExternalInput").ap()
    Wo = nc.dram_tensor("Wo", [H, H], F32, kind="ExternalInput").ap()
    tdecay = nc.dram_tensor("tdecay", [H], F32, kind="ExternalInput").ap()
    tfirst = nc.dram_tensor("tfirst", [H], F32, kind="ExternalInput").ap()
    tmix = nc.dram_tensor("tmix", [3, H], F32, kind="ExternalInput").ap()
    Wg = nc.dram_tensor("Wg", [H, E], F32, kind="ExternalInput").ap()
    W1 = nc.dram_tensor("W1", [E, H, FF], F32, kind="ExternalInput").ap()
    b1 = nc.dram_tensor("b1", [E, FF], F32, kind="ExternalInput").ap()
    W2 = nc.dram_tensor("W2", [E, FF, H], F32, kind="ExternalInput").ap()
    b2 = nc.dram_tensor("b2", [E, H], F32, kind="ExternalInput").ap()
    cmask_mul = nc.dram_tensor("cmask_mul", [128, NCORES], F32, kind="ExternalInput").ap()
    cmask_add = nc.dram_tensor("cmask_add", [128, NCORES], F32, kind="ExternalInput").ap()
    premask = nc.dram_tensor("premask", [NCORES, 1], F32, kind="ExternalInput").ap()
    out = nc.dram_tensor("y", [TL, H], F32, kind="ExternalOutput").ap()

    k_sp = nc.dram_tensor("k_sp", [H, TL], F32, kind=SCR).ap()
    v_sp = nc.dram_tensor("v_sp", [H, TL], F32, kind=SCR).ap()
    r_sp = nc.dram_tensor("r_sp", [H, TL], F32, kind=SCR).ap()
    wb_d = [nc.dram_tensor(f"w{i}b_d", [H, H], BF16).ap() for i in range(4)]
    x1_sp = nc.dram_tensor("x1_sp", [TL, H], F32, kind=SCR).ap()
    xe_d = nc.dram_tensor("xe_d", [E * CL + 1, H], BF16, kind=SCR).ap()
    ye_d = nc.dram_tensor("ye_d", [E * CL + 1, H], F32, kind=SCR).ap()
    cc1_in = nc.dram_tensor("cc1_in", [128, 3 * NCH], F32).ap()
    cc1_out = nc.dram_tensor("cc1_out", [128 * NCORES, 3 * NCH], F32).ap()
    cc2_in = nc.dram_tensor("cc2_in", [1, 2 * E], F32).ap()
    cc2_out = nc.dram_tensor("cc2_out", [NCORES, 2 * E], F32).ap()

    from contextlib import ExitStack
    ctx = ExitStack()

    def sb(name, shape, dt=F32):
        return ctx.enter_context(nc.sbuf_tensor(name, shape, dt))

    def mksem(name):
        return SemCtr(ctx.enter_context(nc.semaphore(name)), name)

    NF = 22544
    NBQ = 50176
    ARF = sb("ARF", [128, NF])
    ARB = sb("ARB", [128, NBQ], BF16)

    def fv(a, b):
        return ARF[:, a:b]

    def bv(a, b):
        return ARB[:, a:b]

    def xTv(g, a, b):
        base = g * (TL + 1)
        return ARF[:, base + a: base + b]
    SPL0 = 16392
    KK, VV, RR = 0, 2048, 4096
    MINC, SINC, DINC = 6144, 8196, 10248
    B1, B2_, B3, B4 = 12300, 14348, 16396, 18444

    WT = [0, 8192, 16384, 24576]
    XMT = 32768
    RWO = 0
    X1B = 32768
    X1TC = 16384
    H1O = 0
    W2O = 16384
    W1O = 32768
    XEO = 40960
    B2O = 49152

    iota0 = sb("iota0", [128, TL + 1])
    iota8 = sb("iota8", [128, E])
    iota8p = sb("iota8p", [128, E])
    ident = sb("ident", [128, 128])
    identb = sb("identb", [128, 128], BF16)
    triu = sb("triu", [128, 128])
    ones_c = sb("ones_c", [128, 1])
    ones_r1 = sb("ones_r1", [1, 128])
    ones_r1b = sb("ones_r1b", [1, 128], BF16)
    w_sb = sb("w_sb", [128, NCH])
    negw_sb = sb("negw_sb", [128, NCH])
    w2048_sb = sb("w2048_sb", [128, NCH])
    u_sb = sb("u_sb", [128, NCH])
    tm_sb = sb("tm_sb", [128, 3 * NCH])
    tm1_sb = sb("tm1_sb", [128, 3 * NCH])
    cmaskm = sb("cmaskm", [128, NCORES])
    cmaska = sb("cmaska", [128, NCORES])
    premk = sb("premk", [NCORES, 1])
    wgb = sb("wgb", [128, NCH * E], BF16)
    b1_sb = sb("b1_sb", [128, E * NFC])
    agg = sb("agg", [128, 3 * NCH])
    aggT = sb("aggT", [128, 3 * NCH])
    carry = sb("carry", [128, 3 * NCH])
    sm1 = sb("sm1", [128, 3 * NCH])
    sm2 = sb("sm2", [128, 3 * NCH])
    sm3 = sb("sm3", [128, 3 * NCH])
    cc2_sb = sb("cc2_sb", [NCORES, 2 * E])
    offs = sb("offs", [1, 2 * E])
    offg = sb("offg", [1, E])
    loco = sb("loco", [1, 2 * E])
    logit = sb("logit", [128, NTC * E])
    mask1 = sb("mask1", [128, NTC * E])
    mask2 = sb("mask2", [128, NTC * E])
    kept1 = sb("kept1", [128, NTC * E])
    kept2 = sb("kept2", [128, NTC * E])
    pos1 = sb("pos1", [128, NTC])
    pos2 = sb("pos2", [128, NTC])
    keep1 = sb("keep1", [128, NTC])
    keep2 = sb("keep2", [128, NTC])
    w1k = sb("w1k", [128, NTC])
    w2k = sb("w2k", [128, NTC])
    slot1 = sb("slot1", [128, NTC])
    slot2 = sb("slot2", [128, NTC])
    dst1i = sb("dst1i", [128, NTC], I32)
    dst2i = sb("dst2i", [128, NTC], I32)
    rsc1 = sb("rsc1", [128, E])
    rsc2 = sb("rsc2", [128, E])
    rsc3 = sb("rsc3", [128, E])
    rcol1 = sb("rcol1", [128, 1])
    rcol2 = sb("rcol2", [128, 1])
    rcol3 = sb("rcol3", [128, 1])
    xck = sb("xck", [128, H])
    zb = sb("zb", [128, H], BF16)
    nopc = sb("nopc", [128, 1])

    def psumt(name, shape):
        return ctx.enter_context(nc.psum_tensor(name, shape, F32))

    pbank = [psumt(f"pb{i}", [128, 512]) for i in range(8)]

    s_ld = mksem("s_ld")
    s_gp = mksem("s_gp")
    s_gc = mksem("s_gc")
    s_sp = mksem("s_sp")
    s_pe = mksem("s_pe")
    s_ve = mksem("s_ve")
    s_sc = mksem("s_sc")
    s_cc = mksem("s_cc")
    s_ix = mksem("s_ix")
    s_ot = mksem("s_ot")

    p = Plan()
    p.scalar_nop_fn = lambda e: e.copy(nopc[:], nopc[:])
    p.def_sem = {"vector": s_ve, "scalar": s_sc}
    bankg = {}

    def bank_wait(i):
        if i in bankg:
            sem, val = bankg[i]
            p.wait("tensor", sem, val)

    def bank_set(i, sem, val):
        bankg[i] = (sem, val)

    p.op("gpsimd", "iota", iota0[:], [[1, TL + 1]], base=-1, channel_multiplier=0,
         allow_small_or_imprecise_dtypes=True)
    p.op("gpsimd", "iota", iota8[:], [[1, E]], channel_multiplier=0,
         allow_small_or_imprecise_dtypes=True)
    p.op("gpsimd", "memset", ident[:], 0.0)
    p.op("gpsimd", "affine_select", out=ident[:], in_=ident[:],
         compare_op=ALU.not_equal, fill=1.0, base=0,
         pattern=[[-1, 128]], channel_multiplier=1)
    p.op("gpsimd", "memset", identb[:], 0.0)
    p.op("gpsimd", "affine_select", out=identb[:], in_=identb[:],
         compare_op=ALU.not_equal, fill=1.0, base=0,
         pattern=[[-1, 128]], channel_multiplier=1)
    p.op("gpsimd", "memset", triu[:], 0.0)
    p.op("gpsimd", "affine_select", out=triu[:], in_=triu[:],
         compare_op=ALU.is_ge, fill=1.0, base=0,
         pattern=[[-1, 128]], channel_multiplier=1)
    p.op("gpsimd", "memset", ones_c[:], 1.0)
    p.op("gpsimd", "memset", ones_r1[:], 1.0)
    p.op("gpsimd", "memset", ones_r1b[:], 1.0, sem=s_gc)
    gc_consts = s_gc.v

    iotav = iota0[:, 1:TL + 1]
    iotam1 = iota0[:, 0:TL]

    p.dma("sync", w_sb[:], tdecay.rearrange("(g p) -> p g", p=128), s_ld,
          allow_slow_non_contiguous=True)
    p.dma("sync", u_sb[:], tfirst.rearrange("(g p) -> p g", p=128), s_ld,
          allow_slow_non_contiguous=True)
    p.dma("sync", tm_sb.ap().rearrange("p (m g) -> p m g", g=NCH),
          tmix.rearrange("m (g p) -> p m g", p=128), s_ld,
          allow_slow_non_contiguous=True)
    p.dma("sync", cmaskm[:], cmask_mul, s_ld)
    p.dma("sync", cmaska[:], cmask_add, s_ld)
    p.dma("sync", premk[:], premask, s_ld)
    p.dma("sync", b1_sb.ap().rearrange("p (e c) -> p e c", c=NFC),
          b1.rearrange("e (c p) -> p e c", p=128), s_ld,
          allow_slow_non_contiguous=True)
    ld_params = s_ld.v

    p.wait("vector", s_ld, ld_params)
    p.wait("vector", s_gc, gc_consts)
    p.op("vector", "tensor_scalar", negw_sb[:], w_sb[:], -1.0, None, ALU.mult)
    p.op("vector", "tensor_scalar", w2048_sb[:], w_sb[:], float(TL), None, ALU.mult)
    p.op("vector", "tensor_scalar", tm1_sb[:], tm_sb[:], -1.0, 1.0, ALU.mult, ALU.add)
    p.op("vector", "tensor_scalar", iota8p[:], iota8[:], 16384.0, None, ALU.add,
         sem=s_ve)
    ve_params = s_ve.v

    for w_src, w_dst in ((Wk, wb_d[0]), (Wv, wb_d[1]), (Wr, wb_d[2]), (Wo, wb_d[3])):
        p.dma("gpsimd", w_dst, w_src, s_gp)
    gp_wcast = s_gp.v
    p.dma("gpsimd", wgb.ap().rearrange("p (g e) -> p g e", e=E),
          Wg.rearrange("(g p) e -> p g e", p=128), s_gp,
          allow_slow_non_contiguous=True)
    gp_small = s_gp.v

    tp_done = 0
    for c in range(NTC):
        if c > 0:
            p.wait("sync", s_pe, tp_done)
        ldx = p.dma("sync", xck[:], x_pad[c * 128:(c + 1) * 128, :], s_ld)
        p.wait("tensor", s_ld, ldx)
        if c == 0:
            p.wait("tensor", s_gc, gc_consts)
        for g in range(NCH):
            bk = g % 4
            bank_wait(bk)
            p.op("tensor", "transpose", pbank[bk][:, 0:128],
                 xck[:, g * 128:(g + 1) * 128], ident[:], sem=s_pe)
            p.wait("scalar", s_pe, s_pe.v)
            tgt = p.op("scalar", "copy", xTv(g, c * 128, (c + 1) * 128),
                       pbank[bk][:, 0:128], sem=s_sc)
            bank_set(bk, s_sc, tgt)
        tp_done = s_pe.v
    ld_xT_last = p.dma("sync",
                       ARF.ap()[:, 0:NCH * (TL + 1)].rearrange(
                           "p (g t) -> p g t", g=NCH)[:, :, TL:TL + 1],
                       x_pad[TL:TL + 1, :].rearrange("o (g p) -> p g o", p=128), s_ld,
                       allow_slow_non_contiguous=True)
    sc_xT = s_sc.v

    p.wait("sync", s_gp, gp_wcast)
    for i in range(4):
        for g in range(NCH):
            p.dmat("sync", bv(WT[i] + g * H, WT[i] + (g + 1) * H),
                   wb_d[i][:, g * 128:(g + 1) * 128], s_ld)
    ld_wT = s_ld.v

    p.op("gpsimd", "memset", zb[:], 0.0, sem=s_gc)
    gc_zb = s_gc.v
    p.wait("sync", s_gc, gc_zb)
    for rr_ in range(0, E * CL + 1, 128):
        re_ = min(rr_ + 128, E * CL + 1)
        p.dma("sync", xe_d[rr_:re_, :], zb[0:(re_ - rr_), :], s_sp)
    p.wait("gpsimd", s_gc, gc_zb)
    sp_zero_ye = p.dma("gpsimd", ye_d[TRASH:TRASH + 1, :], zb[0:1, :], s_gp)
    sp_zero = s_sp.v

    MIX_SP = {0: k_sp, 1: v_sp, 2: r_sp}
    spl = fv(SPL0, SPL0 + 4096)
    atmp = fv(20496, 20496 + TL)
    sp_proj = {}
    pe_proj_done = 0
    for m in range(3):
        p.wait("vector", s_sc, sc_xT)
        p.wait("vector", s_ld, ld_xT_last)
        p.wait("vector", s_ve, ve_params)
        if m > 0:
            p.wait("vector", s_pe, pe_proj_done)
        for g in range(NCH):
            xr = xTv(g, 1, 1 + TL)
            xp = xTv(g, 0, TL)
            p.op("vector", "tensor_scalar", atmp, xr,
                 tm_sb[:, m * NCH + g: m * NCH + g + 1], None, ALU.mult)
            p.op("vector", "scalar_tensor_tensor", bv(XMT + g * TL, XMT + (g + 1) * TL),
                 xp, tm1_sb[:, m * NCH + g: m * NCH + g + 1], atmp, ALU.mult, ALU.add,
                 sem=s_ve)
        ve_mix = s_ve.v

        p.wait("tensor", s_ld, ld_wT)
        p.wait("tensor", s_ve, ve_mix)
        for oc in range(NCH):
            buf = oc % 2
            prev = sp_proj.get(m * NCH + oc - 2, 0)
            for tq in range(4):
                bk = 4 + (tq % 2)
                bank_wait(bk)
                for g in range(NCH):
                    p.op("tensor", "matmul", pbank[bk][:],
                         bv(WT[m] + g * H + oc * 128, WT[m] + g * H + (oc + 1) * 128),
                         bv(XMT + g * TL + tq * 512, XMT + g * TL + (tq + 1) * 512),
                         start=(g == 0), stop=(g == NCH - 1),
                         sem=(s_pe if g == NCH - 1 else None))
                p.wait("scalar", s_pe, s_pe.v)
                p.wait("scalar", s_sp, prev)
                func = AF.Sigmoid if m == 2 else AF.Copy
                tgt = p.op("scalar", "activation",
                           fv(SPL0 + buf * TL + tq * 512, SPL0 + buf * TL + (tq + 1) * 512),
                           pbank[bk][:], func, sem=s_sc)
                bank_set(bk, s_sc, tgt)
            p.wait("sync", s_sc, s_sc.v)
            sp_proj[m * NCH + oc] = p.dma(
                "sync", MIX_SP[m][oc * 128:(oc + 1) * 128, :],
                fv(SPL0 + buf * TL, SPL0 + (buf + 1) * TL), s_sp)
        pe_proj_done = s_pe.v
    sp_proj_all = s_sp.v

    if DEBUG:
        dbgv = fv(20496, 20512)
        d_dbgv = nc.dram_tensor("d_dbgv", [128, 16], F32, kind="ExternalOutput").ap()
    kk = fv(KK, KK + TL)
    vv = fv(VV, VV + TL)
    a2_done = 0
    for g in range(NCH):
        p.wait("sync", s_sp, sp_proj[0 * NCH + g])
        p.wait("sync", s_sp, sp_proj[1 * NCH + g])
        if g == 0:
            p.wait("sync", s_pe, pe_proj_done)
            p.wait("sync", s_ve, s_ve.v)
        else:
            p.wait("sync", s_ve, a2_done)
        p.dma("sync", kk, k_sp[g * 128:(g + 1) * 128, :], s_ld)
        ldv = p.dma("sync", vv, v_sp[g * 128:(g + 1) * 128, :], s_ld)
        p.wait("vector", s_ld, ldv)
        a_t = fv(B1, B1 + TL)
        e_t = fv(B2_, B2_ + TL)
        ev_t = fv(B3, B3 + TL)
        p.op("vector", "scalar_tensor_tensor", a_t, iotav,
             negw_sb[:, g:g + 1], kk, ALU.mult, ALU.add)
        p.op("vector", "reduce_max", rcol1[:], a_t, axis=AX, negate=True, sem=s_ve)
        p.wait("scalar", s_ve, s_ve.v)
        p.op("scalar", "activation", e_t, a_t, AF.Exp,
             bias=rcol1[:, 0:1], scale=1.0,
             accum_out=agg[:, 2 * NCH + g: 2 * NCH + g + 1], sem=s_sc)
        p.wait("vector", s_sc, s_sc.v)
        p.op("vector", "tensor_mul", ev_t, e_t, vv)
        p.op("vector", "reduce_sum", agg[:, NCH + g:NCH + g + 1], ev_t, axis=AX)
        p.op("vector", "tensor_scalar", rcol2[:], rcol1[:], -1.0, None, ALU.mult)
        if DEBUG and g == 0:
            p.op("vector", "tensor_copy", dbgv[0:128, 0:1], rcol1[:])
            p.op("vector", "tensor_copy", dbgv[:, 1:2], rcol2[:])
            p.op("vector", "tensor_copy", dbgv[:, 2:3], w2048_sb[:, 0:1])
            p.op("vector", "tensor_copy", dbgv[:, 3:4], w_sb[:, 0:1])
            p.op("vector", "tensor_copy", dbgv[:, 4:5], negw_sb[:, 0:1])
            p.op("vector", "tensor_copy", dbgv[:, 5:6], u_sb[:, 0:1])
        p.op("vector", "tensor_tensor", rcol2[:], rcol2[:], w2048_sb[:, g:g + 1], ALU.add)
        if DEBUG and g == 0:
            p.op("vector", "tensor_copy", dbgv[:, 6:7], rcol2[:])
        p.op("vector", "tensor_tensor", agg[:, g:g + 1], rcol2[:], w_sb[:, g:g + 1],
             ALU.subtract, sem=s_ve)
        if DEBUG and g == 0:
            p.op("vector", "tensor_copy", dbgv[:, 7:8], agg[:, 0:1], sem=s_ve)
        a2_done = s_ve.v
    ve_agg = s_ve.v

    p.wait("sync", s_ve, ve_agg)
    cc_pub = p.dma("sync", cc1_in, agg[:], s_sp)
    p.wait("gpsimd", s_sp, cc_pub)
    p.op("gpsimd", "collective_compute", "AllGather", ALU.bypass,
         replica_groups=[list(range(NCORES))],
         ins=[cc1_in.opt()], outs=[cc1_out.opt()], sem=s_cc)
    cc1_done = s_cc.v
    p.op("vector", "memset", carry[:, 0:NCH], NEG)
    p.op("vector", "memset", carry[:, NCH:3 * NCH], 0.0, sem=s_ve)
    p.wait("gpsimd", s_cc, cc1_done)
    for cp in range(NCORES):
        if cp > 0:
            p.wait("gpsimd", s_ve, fold_done)
        ldt = p.dma("gpsimd", aggT[:], cc1_out[cp * 128:(cp + 1) * 128, :], s_gp)
        p.wait("vector", s_gp, ldt)
        p.op("vector", "tensor_scalar", sm1[:, 0:NCH], aggT[:, 0:NCH],
             cmaskm[:, cp:cp + 1], cmaska[:, cp:cp + 1], ALU.mult, ALU.add)
        p.op("vector", "tensor_scalar", sm1[:, NCH:3 * NCH], aggT[:, NCH:3 * NCH],
             cmaskm[:, cp:cp + 1], None, ALU.mult)
        p.op("vector", "tensor_scalar", sm2[:, 0:NCH], w2048_sb[:],
             cmaskm[:, cp:cp + 1], None, ALU.mult)
        p.nop("vector")
        p.op("vector", "tensor_tensor", sm2[:, 0:NCH], carry[:, 0:NCH],
             sm2[:, 0:NCH], ALU.add)
        p.nop("vector")
        p.op("vector", "tensor_tensor", sm3[:, 0:NCH], sm2[:, 0:NCH], sm1[:, 0:NCH],
             ALU.max)
        p.nop("vector")
        p.op("vector", "tensor_sub", sm2[:, NCH:2 * NCH], sm2[:, 0:NCH], sm3[:, 0:NCH])
        p.op("vector", "tensor_sub", sm2[:, 2 * NCH:3 * NCH], sm1[:, 0:NCH],
             sm3[:, 0:NCH], sem=s_ve)
        p.wait("scalar", s_ve, s_ve.v)
        p.op("scalar", "activation", sm2[:, NCH:3 * NCH], sm2[:, NCH:3 * NCH], AF.Exp,
             sem=s_sc)
        p.wait("vector", s_sc, s_sc.v)
        p.op("vector", "tensor_mul", sm3[:, NCH:2 * NCH], carry[:, NCH:2 * NCH],
             sm2[:, NCH:2 * NCH])
        p.op("vector", "tensor_mul", sm3[:, 2 * NCH:3 * NCH], carry[:, 2 * NCH:3 * NCH],
             sm2[:, NCH:2 * NCH])
        p.op("vector", "tensor_mul", sm1[:, NCH:2 * NCH], sm1[:, NCH:2 * NCH],
             sm2[:, 2 * NCH:3 * NCH])
        p.op("vector", "tensor_mul", sm1[:, 2 * NCH:3 * NCH], sm1[:, 2 * NCH:3 * NCH],
             sm2[:, 2 * NCH:3 * NCH])
        p.nop("vector")
        p.op("vector", "tensor_add", carry[:, NCH:3 * NCH], sm3[:, NCH:3 * NCH],
             sm1[:, NCH:3 * NCH])
        p.op("vector", "tensor_copy", carry[:, 0:NCH], sm3[:, 0:NCH], sem=s_ve)
        fold_done = s_ve.v
        if DEBUG and cp == 1:
            p.op("vector", "tensor_copy", dbgv[:, 8:9], sm1[:, 0:1], sem=s_ve)
            p.op("vector", "tensor_copy", dbgv[:, 9:10], sm2[:, 0:1], sem=s_ve)
            p.op("vector", "tensor_copy", dbgv[:, 10:11], sm3[:, 0:1], sem=s_ve)
            p.op("vector", "tensor_copy", dbgv[:, 11:12], sm2[:, NCH:NCH + 1], sem=s_ve)
            p.op("vector", "tensor_copy", dbgv[:, 12:13], sm2[:, 2 * NCH:2 * NCH + 1], sem=s_ve)
            p.op("vector", "tensor_copy", dbgv[:, 13:14], cmaskm[:, 1:2], sem=s_ve)
            p.op("vector", "tensor_copy", dbgv[:, 14:15], cmaska[:, 1:2], sem=s_ve)
            p.op("vector", "tensor_copy", dbgv[:, 15:16], carry[:, 0:1], sem=s_ve)
    ve_carry = s_ve.v

    rr_v = fv(RR, RR + TL)
    minc = fv(MINC, MINC + TL + 1)
    sinc = fv(SINC, SINC + TL + 1)
    dinc = fv(DINC, DINC + TL + 1)
    b1v = fv(B1, B1 + TL)
    b2v = fv(B2_, B2_ + TL)
    b3v = fv(B3, B3 + TL)
    b4v = fv(B4, B4 + TL)
    pav = fv(MINC, MINC + TL)
    ve_rw_prev = ve_agg
    for g in range(NCH):
        p.wait("sync", s_sp, sp_proj_all)
        p.wait("sync", s_ve, ve_rw_prev)
        p.dma("sync", kk, k_sp[g * 128:(g + 1) * 128, :], s_ld)
        p.dma("sync", vv, v_sp[g * 128:(g + 1) * 128, :], s_ld)
        p.dma("sync", rr_v, r_sp[g * 128:(g + 1) * 128, :], s_ld)
        ld_tile = s_ld.v

        p.wait("vector", s_ld, ld_tile)
        p.wait("vector", s_ve, ve_carry)
        p.op("vector", "scalar_tensor_tensor", b1v, iotav,
             negw_sb[:, g:g + 1], kk, ALU.mult, ALU.add)
        p.op("vector", "tensor_tensor", minc[:, 0:1], carry[:, g:g + 1],
             w_sb[:, g:g + 1], ALU.add)
        p.nop("vector")
        p.op("vector", "tensor_tensor_scan", minc[:, 1:TL + 1], b1v, b1v,
             minc[:, 0:1], ALU.max, ALU.max)
        p.op("vector", "tensor_sub", b2v, minc[:, 0:TL], minc[:, 1:TL + 1])
        p.op("vector", "tensor_sub", b3v, b1v, minc[:, 1:TL + 1], sem=s_ve)
        p.wait("scalar", s_ve, s_ve.v)
        p.op("scalar", "activation", b2v, b2v, AF.Exp)
        p.op("scalar", "activation", b3v, b3v, AF.Exp, sem=s_sc)
        p.wait("vector", s_sc, s_sc.v)
        p.op("vector", "tensor_mul", b4v, b3v, vv)
        p.op("vector", "tensor_copy", sinc[:, 0:1], carry[:, NCH + g:NCH + g + 1])
        p.op("vector", "tensor_copy", dinc[:, 0:1], carry[:, 2 * NCH + g:2 * NCH + g + 1])
        p.nop("vector")
        p.op("vector", "tensor_tensor_scan", sinc[:, 1:TL + 1], b2v, b4v,
             sinc[:, 0:1], ALU.mult, ALU.add)
        p.op("vector", "tensor_tensor_scan", dinc[:, 1:TL + 1], b2v, b3v,
             dinc[:, 0:1], ALU.mult, ALU.add)
        p.op("vector", "scalar_tensor_tensor", b1v, iotam1,
             w_sb[:, g:g + 1], minc[:, 0:TL], ALU.mult, ALU.add)
        p.op("vector", "scalar_tensor_tensor", pav, kk,
             u_sb[:, g:g + 1], b1v, ALU.add, ALU.max)
        p.op("vector", "tensor_sub", b1v, b1v, pav)
        p.op("vector", "tensor_sub", b3v, kk, pav, sem=s_ve)
        p.wait("scalar", s_ve, s_ve.v)
        p.op("scalar", "activation", b1v, b1v, AF.Exp)
        p.op("scalar", "activation", b3v, b3v, AF.Exp, bias=u_sb[:, g:g + 1],
             scale=1.0, sem=s_sc)
        p.wait("vector", s_sc, s_sc.v)
        p.op("vector", "tensor_mul", b2v, b1v, sinc[:, 0:TL])
        p.op("vector", "tensor_mul", b4v, b3v, vv)
        p.op("vector", "tensor_add", b2v, b2v, b4v)
        p.op("vector", "tensor_mul", pav, b1v, dinc[:, 0:TL])
        p.op("vector", "scalar_tensor_tensor", pav, pav, 1e-9, b3v,
             ALU.add, ALU.add, sem=s_ve)
        p.wait("scalar", s_ve, s_ve.v)
        p.op("scalar", "activation", pav, pav, AF.Ln)
        p.op("scalar", "activation", pav, pav, AF.Exp, scale=-1.0, sem=s_sc)
        p.wait("vector", s_sc, s_sc.v)
        p.op("vector", "tensor_mul", b2v, b2v, pav)
        p.op("vector", "tensor_mul", bv(RWO + g * TL, RWO + (g + 1) * TL),
             b2v, rr_v, sem=s_ve)
        ve_rw_prev = s_ve.v
    ve_rw_all = s_ve.v

    if DEBUG:
        d_rw = nc.dram_tensor("d_rw", [128, NCH * TL], BF16, kind="ExternalOutput").ap()
        d_carry = nc.dram_tensor("d_carry", [128, 3 * NCH], F32, kind="ExternalOutput").ap()
        d_agg = nc.dram_tensor("d_agg", [128, 3 * NCH], F32, kind="ExternalOutput").ap()
        d_aggall = nc.dram_tensor("d_aggall", [128 * NCORES, 3 * NCH], F32, kind="ExternalOutput").ap()
        p.wait("sync", s_ve, ve_rw_all)
        p.dma("sync", d_rw, bv(RWO, RWO + NCH * TL), s_sp)
        p.dma("sync", d_carry, carry[:], s_sp)
        p.dma("sync", d_agg, agg[:], s_sp)
        p.dma("sync", d_dbgv, dbgv[:], s_sp)
        p.dma("sync", d_aggall, cc1_out, s_sp)

    c1gC = fv(0, H)
    youtC = fv(H, 2 * H)
    if DEBUG:
        d_y1 = nc.dram_tensor("d_y1", [TL, H], F32, kind="ExternalOutput").ap()
    p.wait("tensor", s_ve, ve_rw_all)
    p.wait("tensor", s_ld, ld_wT)
    sp_x1_last = 0
    for c in range(NTC):
        ldx = p.dma("sync", xck[:], x_pad[1 + c * 128: 1 + (c + 1) * 128, :], s_ld)
        for oh in range(2):
            bk = oh
            bank_wait(bk)
            for g in range(NCH):
                p.op("tensor", "matmul", pbank[bk][:],
                     bv(RWO + g * TL + c * 128, RWO + g * TL + (c + 1) * 128),
                     bv(WT[3] + g * H + oh * 512, WT[3] + g * H + (oh + 1) * 512),
                     start=(g == 0), stop=(g == NCH - 1),
                     sem=(s_pe if g == NCH - 1 else None))
            p.wait("vector", s_pe, s_pe.v)
            p.wait("vector", s_ld, ldx)
            p.wait("vector", s_sp, sp_x1_last)
            tgt = p.op("vector", "tensor_add", youtC[:, oh * 512:(oh + 1) * 512]
                       if False else fv(H + oh * 512, H + (oh + 1) * 512),
                       pbank[bk][:], xck[:, oh * 512:(oh + 1) * 512], sem=s_ve)
            bank_set(bk, s_ve, tgt)
        if DEBUG:
            p.wait("sync", s_ve, s_ve.v)
            p.dma("sync", d_y1[c * 128:(c + 1) * 128, :], youtC, s_sp)
        y = youtC
        p.op("vector", "reduce_sum", rcol1[:], y, axis=AX, negate=True)
        p.nop("vector")
        p.op("vector", "tensor_scalar", rcol1[:], rcol1[:], 1.0 / H, None, ALU.mult)
        p.nop("vector")
        p.op("vector", "tensor_scalar", c1gC, y, rcol1[:, 0:1], None, ALU.add, sem=s_ve)
        p.wait("scalar", s_ve, s_ve.v)
        p.op("scalar", "activation", fv(2 * H, 3 * H), c1gC, AF.Square,
             accum_out=rcol2[:, 0:1], sem=s_sc)
        p.wait("vector", s_sc, s_sc.v)
        p.op("vector", "tensor_scalar", rcol2[:], rcol2[:], 1.0 / H, 1e-5,
             ALU.mult, ALU.add, sem=s_ve)
        p.wait("scalar", s_ve, s_ve.v)
        p.op("scalar", "activation", rcol3[:], rcol2[:], AF.Ln)
        p.nop("scalar")
        p.op("scalar", "activation", rcol3[:], rcol3[:], AF.Exp, scale=-0.5, sem=s_sc)
        p.wait("vector", s_sc, s_sc.v)
        p.op("vector", "tensor_scalar", c1gC, c1gC, rcol3[:, 0:1], None, ALU.mult)
        p.op("vector", "tensor_copy", bv(X1B + c * H, X1B + (c + 1) * H), c1gC, sem=s_ve)
        p.wait("sync", s_ve, s_ve.v)
        sp_x1_last = p.dma("sync", x1_sp[c * 128:(c + 1) * 128, :], c1gC, s_sp)
    sp_x1 = s_sp.v
    ve_x1b = s_ve.v

    x1Tc = bv(X1TC, X1TC + H)
    if DEBUG:
        dbg2 = fv(3 * H, 3 * H + 16)
        d_dbg2 = nc.dram_tensor("d_dbg2", [128, 16], F32, kind="ExternalOutput").ap()
    p.wait("tensor", s_ve, ve_x1b)
    p.wait("tensor", s_gp, gp_small)
    logit_mm_done = 0
    for c in range(NTC):
        p.wait("sync", s_pe, logit_mm_done)
        p.wait("sync", s_sp, sp_x1)
        ldxc = p.dma("sync", xck[:], x1_sp[c * 128:(c + 1) * 128, :], s_ld)
        p.wait("tensor", s_ld, ldxc)
        for g in range(NCH):
            bk = 4 + (g % 2)
            bank_wait(bk)
            p.op("tensor", "transpose", pbank[bk][:, 0:128],
                 xck[:, g * 128:(g + 1) * 128], ident[:], sem=s_pe)
            p.wait("scalar", s_pe, s_pe.v)
            p.wait("scalar", s_pe, logit_mm_done)
            tgt = p.op("scalar", "copy", x1Tc[:, g * 128:(g + 1) * 128],
                       pbank[bk][:, 0:128], sem=s_sc)
            bank_set(bk, s_sc, tgt)
        bank_wait(6)
        p.wait("tensor", s_sc, s_sc.v)
        for g in range(NCH):
            p.op("tensor", "matmul", pbank[6][:, 0:E],
                 x1Tc[:, g * 128:(g + 1) * 128], wgb[:, g * E:(g + 1) * E],
                 start=(g == 0), stop=(g == NCH - 1),
                 sem=(s_pe if g == NCH - 1 else None))
        logit_mm_done = s_pe.v
        p.wait("vector", s_pe, s_pe.v)
        tgt = p.op("vector", "tensor_copy", logit[:, c * E:(c + 1) * E],
                   pbank[6][:, 0:E], sem=s_ve)
        bank_set(6, s_ve, tgt)

        L = logit[:, c * E:(c + 1) * E]
        p.op("vector", "reduce_max", rcol1[:], L, axis=AX, negate=True)
        p.nop("vector")
        p.op("vector", "tensor_scalar", rsc1[:], L, rcol1[:, 0:1], 0.0, ALU.add, ALU.is_ge)
        p.nop("vector")
        p.op("vector", "scalar_tensor_tensor", rsc2[:], rsc1[:], 16384.0, iota8p[:],
             ALU.mult, ALU.subtract)
        p.nop("vector")
        p.op("vector", "reduce_max", rcol2[:], rsc2[:], axis=AX)
        p.nop("vector")
        p.op("vector", "tensor_scalar", mask1[:, c * E:(c + 1) * E], iota8[:],
             rcol2[:, 0:1], 0.0, ALU.add, ALU.is_equal)
        if DEBUG and c == 0:
            p.op("vector", "tensor_copy", dbg2[:, 0:8], rsc1[:], sem=s_ve)
            p.op("vector", "tensor_copy", dbg2[:, 8:9], rcol1[:], sem=s_ve)
            p.op("vector", "tensor_copy", dbg2[:, 9:10], rcol2[:], sem=s_ve)
        p.nop("vector")
        p.op("vector", "scalar_tensor_tensor", rsc3[:], mask1[:, c * E:(c + 1) * E],
             -3e38, L, ALU.mult, ALU.add)
        p.nop("vector")
        p.op("vector", "reduce_max", rcol3[:], rsc3[:], axis=AX, negate=True)
        p.nop("vector")
        p.op("vector", "tensor_scalar", rsc1[:], rsc3[:], rcol3[:, 0:1], 0.0,
             ALU.add, ALU.is_ge)
        p.nop("vector")
        p.op("vector", "scalar_tensor_tensor", rsc2[:], rsc1[:], 16384.0, iota8p[:],
             ALU.mult, ALU.subtract)
        p.nop("vector")
        p.op("vector", "reduce_max", rcol2[:], rsc2[:], axis=AX)
        p.nop("vector")
        p.op("vector", "tensor_scalar", mask2[:, c * E:(c + 1) * E], iota8[:],
             rcol2[:, 0:1], 0.0, ALU.add, ALU.is_equal, sem=s_ve)
        p.wait("scalar", s_ve, s_ve.v)
        p.op("scalar", "activation", rsc3[:], L, AF.Exp, bias=rcol1[:, 0:1], scale=1.0,
             accum_out=rcol3[:, 0:1], sem=s_sc)
        p.wait("vector", s_sc, s_sc.v)
        p.op("vector", "tensor_mul", rsc1[:], rsc3[:], mask1[:, c * E:(c + 1) * E])
        p.nop("vector")
        p.op("vector", "reduce_sum", rcol2[:], rsc1[:], axis=AX)
        p.nop("vector")
        p.op("vector", "tensor_mul", rsc1[:], rsc3[:], mask2[:, c * E:(c + 1) * E])
        p.nop("vector")
        p.op("vector", "reduce_sum", rcol1[:], rsc1[:], axis=AX)
        p.nop("vector")
        p.op("vector", "tensor_add", rsc2[:, 0:1], rcol2[:], rcol1[:])
        p.op("vector", "tensor_scalar", rsc2[:, 1:2], rcol3[:, 0:1], 1e-9, None, ALU.mult)
        p.nop("vector")
        p.op("vector", "tensor_add", rsc2[:, 0:1], rsc2[:, 0:1], rsc2[:, 1:2])
        p.nop("vector")
        p.op("vector", "reciprocal", rsc2[:, 2:3], rsc2[:, 0:1])
        p.nop("vector")
        p.op("vector", "tensor_mul", w1k[:, c:c + 1], rcol2[:], rsc2[:, 2:3])
        p.op("vector", "tensor_mul", w2k[:, c:c + 1], rcol1[:], rsc2[:, 2:3], sem=s_ve)
    ve_route1 = s_ve.v

    p.wait("tensor", s_ve, ve_route1)
    bank_wait(7)
    for c in range(NTC):
        p.op("tensor", "matmul", pbank[7][0:1, 0:E], ones_c[:],
             mask1[:, c * E:(c + 1) * E], start=(c == 0), stop=(c == NTC - 1),
             sem=(s_pe if c == NTC - 1 else None))
    bank_wait(6)
    for c in range(NTC):
        p.op("tensor", "matmul", pbank[6][0:1, 0:E], ones_c[:],
             mask2[:, c * E:(c + 1) * E], start=(c == 0), stop=(c == NTC - 1),
             sem=(s_pe if c == NTC - 1 else None))
    p.wait("vector", s_pe, s_pe.v)
    p.op("vector", "tensor_copy", sm1[0:1, 0:E], pbank[7][0:1, 0:E])
    tgt = p.op("vector", "tensor_copy", sm1[0:1, E:2 * E], pbank[6][0:1, 0:E], sem=s_ve)
    bank_set(6, s_ve, tgt)
    bank_set(7, s_ve, tgt)
    p.wait("sync", s_ve, s_ve.v)
    cc2_pub = p.dma("sync", cc2_in, sm1[0:1, 0:2 * E], s_sp)
    p.wait("gpsimd", s_sp, cc2_pub)
    p.op("gpsimd", "collective_compute", "AllGather", ALU.bypass,
         replica_groups=[list(range(NCORES))],
         ins=[cc2_in.opt()], outs=[cc2_out.opt()], sem=s_cc)
    cc2_done = s_cc.v
    p.wait("gpsimd", s_cc, cc2_done)
    ld_cc2 = p.dma("gpsimd", cc2_sb[:], cc2_out, s_gp)
    p.wait("tensor", s_gp, ld_cc2)
    bank_wait(7)
    p.op("tensor", "matmul", pbank[7][0:1, 0:2 * E], premk[:], cc2_sb[:],
         start=True, stop=True)
    bank_wait(5)
    p.op("tensor", "matmul", pbank[5][0:1, 0:E], ones_c[0:E, 0:1],
         cc2_sb[:, 0:E], start=True, stop=True, sem=s_pe)
    p.wait("vector", s_pe, s_pe.v)
    p.op("vector", "tensor_copy", offs[:], pbank[7][0:1, 0:2 * E])
    tgt = p.op("vector", "tensor_copy", offg[:], pbank[5][0:1, 0:E], sem=s_ve)
    bank_set(5, s_ve, tgt)
    bank_set(7, s_ve, tgt)
    ve_offs = s_ve.v

    def cumsum_pass(msk, possb, keepsb, locol):
        for c in range(NTC):
            bank_wait(5)
            p.wait("tensor", s_ve, s_ve.v)
            p.op("tensor", "matmul", pbank[5][:, 0:E], triu[:],
                 msk[:, c * E:(c + 1) * E], start=True, stop=False)
            p.op("tensor", "matmul", pbank[5][:, 0:E], ones_r1[:], locol,
                 start=False, stop=True, sem=s_pe)
            p.wait("vector", s_pe, s_pe.v)
            tgt_ = p.op("vector", "tensor_mul", rsc1[:], pbank[5][:, 0:E],
                        msk[:, c * E:(c + 1) * E], sem=s_ve)
            bank_set(5, s_ve, tgt_)
            p.nop("vector")
            p.op("vector", "reduce_sum", possb[:, c:c + 1], rsc1[:], axis=AX)
            if keepsb is not None:
                p.nop("vector")
                p.op("vector", "tensor_scalar", keepsb[:, c:c + 1], possb[:, c:c + 1],
                     float(CGLOB), 0.0, ALU.subtract, ALU.is_lt)
            bank_wait(6)
            p.op("tensor", "matmul", pbank[6][0:1, 0:E], ones_c[:],
                 msk[:, c * E:(c + 1) * E], start=True, stop=True, sem=s_pe)
            p.wait("vector", s_pe, s_pe.v)
            tgt_ = p.op("vector", "tensor_add", locol, locol, pbank[6][0:1, 0:E],
                        sem=s_ve)
            bank_set(6, s_ve, tgt_)

    p.wait("vector", s_ve, ve_offs)
    p.op("vector", "tensor_copy", loco[0:1, 0:E], offs[0:1, 0:E], sem=s_ve)
    cumsum_pass(mask1, pos1, keep1, loco[0:1, 0:E])
    p.op("vector", "tensor_add", loco[0:1, E:2 * E], offs[0:1, E:2 * E],
         offg[0:1, 0:E], sem=s_ve)
    cumsum_pass(mask2, pos2, keep2, loco[0:1, E:2 * E])

    for c in range(NTC):
        p.op("vector", "tensor_scalar", kept1[:, c * E:(c + 1) * E],
             mask1[:, c * E:(c + 1) * E], keep1[:, c:c + 1], None, ALU.mult)
        p.op("vector", "tensor_scalar", kept2[:, c * E:(c + 1) * E],
             mask2[:, c * E:(c + 1) * E], keep2[:, c:c + 1], None, ALU.mult)
        p.op("vector", "tensor_mul", w1k[:, c:c + 1], w1k[:, c:c + 1], keep1[:, c:c + 1])
        p.op("vector", "tensor_mul", w2k[:, c:c + 1], w2k[:, c:c + 1], keep2[:, c:c + 1],
             sem=s_ve)

    p.op("vector", "memset", loco[0:1, 0:E], 0.0, sem=s_ve)
    cumsum_pass(kept1, slot1, None, loco[0:1, 0:E])
    p.op("vector", "tensor_copy", loco[0:1, E:2 * E], loco[0:1, 0:E], sem=s_ve)
    cumsum_pass(kept2, slot2, None, loco[0:1, E:2 * E])

    for c in range(NTC):
        for (msk, kp, slotsb, dstsi) in ((mask1, keep1, slot1, dst1i),
                                         (mask2, keep2, slot2, dst2i)):
            p.op("vector", "tensor_mul", rsc1[:], iota8[:], msk[:, c * E:(c + 1) * E])
            p.nop("vector")
            p.op("vector", "reduce_sum", rcol1[:], rsc1[:], axis=AX)
            p.op("vector", "tensor_scalar", rcol2[:], slotsb[:, c:c + 1],
                 float(CL), 0.0, ALU.subtract, ALU.is_lt)
            p.nop("vector")
            p.op("vector", "tensor_mul", rcol2[:], rcol2[:], kp[:, c:c + 1])
            p.op("vector", "tensor_scalar", rcol1[:], rcol1[:], float(CL), None, ALU.mult)
            p.nop("vector")
            p.op("vector", "tensor_tensor", rcol1[:], rcol1[:], slotsb[:, c:c + 1], ALU.add)
            p.nop("vector")
            p.op("vector", "tensor_scalar", rcol1[:], rcol1[:], float(TRASH), None,
                 ALU.subtract)
            p.nop("vector")
            p.op("vector", "tensor_mul", rcol1[:], rcol1[:], rcol2[:])
            p.nop("vector")
            p.op("vector", "tensor_scalar", rcol3[:], rcol1[:], float(TRASH), None, ALU.add)
            p.nop("vector")
            p.op("vector", "tensor_copy", dstsi[:, c:c + 1], rcol3[:], sem=s_ve)
    ve_dests = s_ve.v

    if DEBUG:
        d_rt = nc.dram_tensor("d_rt", [128, 6 * NTC], F32, kind="ExternalOutput").ap()
        d_rti = nc.dram_tensor("d_rti", [128, 2 * NTC], I32, kind="ExternalOutput").ap()
        d_msk = nc.dram_tensor("d_msk", [128, 3 * NTC * E], F32, kind="ExternalOutput").ap()
        p.wait("sync", s_ve, ve_dests)
        p.dma("sync", d_rt[:, 0 * NTC:1 * NTC], pos1[:], s_sp)
        p.dma("sync", d_rt[:, 1 * NTC:2 * NTC], pos2[:], s_sp)
        p.dma("sync", d_rt[:, 2 * NTC:3 * NTC], slot1[:], s_sp)
        p.dma("sync", d_rt[:, 3 * NTC:4 * NTC], slot2[:], s_sp)
        p.dma("sync", d_rt[:, 4 * NTC:5 * NTC], w1k[:], s_sp)
        p.dma("sync", d_rt[:, 5 * NTC:6 * NTC], w2k[:], s_sp)
        p.dma("sync", d_rti[:, 0:NTC], dst1i[:], s_sp)
        p.dma("sync", d_rti[:, NTC:2 * NTC], dst2i[:], s_sp)
        p.dma("sync", d_msk[:, 0:NTC * E], logit[:], s_sp)
        p.dma("sync", d_msk[:, NTC * E:2 * NTC * E], mask1[:], s_sp)
        p.dma("sync", d_msk[:, 2 * NTC * E:3 * NTC * E], mask2[:], s_sp)
        p.dma("sync", d_dbg2, dbg2, s_sp)

    p.wait("gpsimd", s_ve, ve_dests)
    p.wait("gpsimd", s_sp, sp_zero)
    for c in range(NTC):
        for dsts in (dst1i, dst2i):
            p.op("gpsimd", "indirect_dma_start",
                 out=xe_d, out_offset=IndirectOffsetOnAxis(dsts[:, c:c + 1], 0),
                 in_=bv(X1B + c * H, X1B + (c + 1) * H), in_offset=None,
                 sem=s_ix, inc=16)
    ix_scatter = s_ix.v

    if DEBUG:
        d_h1 = [nc.dram_tensor(f"d_h1{i}", [128, 16 * CL], BF16, kind="ExternalOutput").ap()
                for i in range(2)]
        d_yep = nc.dram_tensor("d_yep", [128, E * H], F32, kind="ExternalOutput").ap()
    yev = fv(0, 2 * H)
    ye_part = fv(2 * H, 2 * H + E * H)
    b2e = ARB[0:1, B2O:B2O + H]
    NHF = NFC // 2
    pe_w1_done = {}
    pe_w2_done = {}
    sp_ye = {}
    w1sl_guard = {}
    p.wait("scalar", s_sp, sp_zero)
    for e in range(E):
        p.wait("sync", s_ix, ix_scatter)
        if e >= 1:
            p.wait("sync", s_pe, pe_w1_done[e - 1])
        for gg in range(NCH):
            p.dmat("sync", bv(XEO + gg * CL, XEO + (gg + 1) * CL),
                   xe_d[e * CL:(e + 1) * CL, gg * 128:(gg + 1) * 128], s_ld)
        ld_xeT = s_ld.v

        for hf in range(2):
            if e + hf >= 1:
                p.wait("gpsimd", s_pe, pe_w2_done.get((e, hf - 1),
                                                      pe_w2_done.get((e - 1, 1), 0)))
            if hf == 0:
                p.dma("gpsimd", b2e, b2[e:e + 1, :], s_gp)
                gp_b2e = s_gp.v
            f0 = hf * (FF // 2)

            if e + hf >= 1:
                p.wait("scalar", s_pe, pe_w2_done.get((e, hf - 1),
                                                      pe_w2_done.get((e - 1, 1), 0)))
            for fs in range(4):
                buf = fs % 2
                p.wait("gpsimd", s_pe, w1sl_guard.get(buf, 0))
                p.dma("gpsimd",
                      bv(W1O + buf * NCH * 512, W1O + (buf + 1) * NCH * 512).rearrange(
                          "p (t f) -> p t f", f=512),
                      W1[e, :, f0 + fs * 512: f0 + fs * 512 + 512].rearrange(
                          "(t p) f -> p t f", p=128),
                      s_gp)
                gp_w1 = s_gp.v
                p.wait("tensor", s_gp, gp_w1)
                p.wait("tensor", s_ld, ld_xeT)
                for fc in range(4):
                    fcl = fs * 4 + fc
                    fcg = hf * NHF + fcl
                    for sh in range(2):
                        bk = sh
                        bank_wait(bk)
                        for dt in range(NCH):
                            p.op("tensor", "matmul", pbank[bk][:],
                                 bv(W1O + buf * NCH * 512 + dt * 512 + fc * 128,
                                    W1O + buf * NCH * 512 + dt * 512 + (fc + 1) * 128),
                                 bv(XEO + dt * CL + sh * 512, XEO + dt * CL + (sh + 1) * 512),
                                 start=(dt == 0), stop=(dt == NCH - 1),
                                 sem=(s_pe if dt == NCH - 1 else None))
                        p.wait("scalar", s_pe, s_pe.v)
                        tgt = p.op("scalar", "activation",
                                   bv(H1O + fcl * CL + sh * 512,
                                      H1O + fcl * CL + (sh + 1) * 512),
                                   pbank[bk][:], AF.Gelu,
                                   bias=b1_sb[:, e * NFC + fcg: e * NFC + fcg + 1],
                                   scale=1.0, sem=s_sc)
                        bank_set(bk, s_sc, tgt)
                w1sl_guard[buf] = s_pe.v
            sc_h1_done = s_sc.v
            if DEBUG and e == 0:
                p.wait("sync", s_sc, sc_h1_done)
                p.dma("sync", d_h1[hf], bv(H1O, H1O + 16 * CL), s_sp)

            for piece in range(2):
                q0 = f0 + piece * (FF // 4)
                p.dma("gpsimd",
                      bv(W2O + piece * 8 * H, W2O + (piece + 1) * 8 * H).rearrange(
                          "p (t d) -> p t d", d=H),
                      W2[e, q0:q0 + FF // 4, :].rearrange("(t p) d -> p t d", p=128),
                      s_gp)
            gp_w2 = s_gp.v
            if DEBUG and e == 0 and hf == 1:
                p.wait("sync", s_sc, s_sc.v)
                p.dma("sync", d_yep, ye_part, s_sp)
            p.wait("tensor", s_gp, gp_w2)
            if hf == 1:
                p.wait("tensor", s_gp, gp_b2e)
            p.wait("tensor", s_sc, sc_h1_done)
            for sc8 in range(8):
                ybuf = sc8 % 2
                for dh in range(2):
                    bk = 2 + ((sc8 * 2 + dh) % 4)
                    bank_wait(bk)
                    for fcl in range(NHF):
                        last = (hf == 0 and fcl == NHF - 1)
                        p.op("tensor", "matmul", pbank[bk][:],
                             bv(H1O + fcl * CL + sc8 * 128, H1O + fcl * CL + (sc8 + 1) * 128),
                             bv(W2O + fcl * H + dh * 512, W2O + fcl * H + (dh + 1) * 512),
                             start=(fcl == 0), stop=last,
                             sem=(s_pe if last else None))
                    if hf == 1:
                        p.op("tensor", "matmul", pbank[bk][:],
                             ones_r1b[0:1, 0:128],
                             b2e[0:1, dh * 512:(dh + 1) * 512],
                             start=False, stop=True, sem=s_pe)
                    if hf == 0:
                        p.wait("scalar", s_pe, s_pe.v)
                        tgt = p.op("scalar", "copy",
                                   fv(2 * H + sc8 * H + dh * 512,
                                      2 * H + sc8 * H + (dh + 1) * 512),
                                   pbank[bk][:], sem=s_sc)
                        bank_set(bk, s_sc, tgt)
                    else:
                        p.wait("vector", s_pe, s_pe.v)
                        p.wait("vector", s_sp, sp_ye.get((e, sc8 - 2),
                                                         sp_ye.get((e - 1, 6 + sc8), 0)))
                        tgt = p.op("vector", "tensor_add",
                                   fv(ybuf * H + dh * 512, ybuf * H + (dh + 1) * 512),
                                   pbank[bk][:],
                                   fv(2 * H + sc8 * H + dh * 512,
                                      2 * H + sc8 * H + (dh + 1) * 512), sem=s_ve)
                        bank_set(bk, s_ve, tgt)
                if hf == 1:
                    p.wait("sync", s_ve, s_ve.v)
                    sp_ye[(e, sc8)] = p.dma(
                        "sync", ye_d[e * CL + sc8 * 128: e * CL + (sc8 + 1) * 128, :],
                        fv(ybuf * H, (ybuf + 1) * H), s_sp)
            pe_w2_done[(e, hf)] = s_pe.v
        pe_w1_done[e] = s_pe.v
    sp_ye_all = s_sp.v

    ot_done = {}
    ve_e_last = {}
    for c in range(NTC):
        half = c % 2
        if c == 0:
            p.wait("gpsimd", s_sp, sp_ye_all)
        if c >= 2:
            p.wait("gpsimd", s_ot, ot_done[c - 2])
        p.op("gpsimd", "indirect_dma_start",
             out=fv(half * H, (half + 1) * H), out_offset=None,
             in_=ye_d, in_offset=IndirectOffsetOnAxis(dst1i[:, c:c + 1], 0),
             sem=s_ix, inc=16)
        p.op("gpsimd", "indirect_dma_start",
             out=fv(2 * H + half * H, 2 * H + (half + 1) * H), out_offset=None,
             in_=ye_d, in_offset=IndirectOffsetOnAxis(dst2i[:, c:c + 1], 0),
             sem=s_ix, inc=16)
        ixg = s_ix.v
        if c >= 2:
            p.wait("sync", s_ot, ot_done[c - 2])
        ldx1 = p.dma("sync", fv(4 * H + half * H, 4 * H + (half + 1) * H),
                     x1_sp[c * 128:(c + 1) * 128, :], s_ld)
        p.wait("vector", s_ix, ixg)
        p.wait("vector", s_ld, ldx1)
        yy = fv(6 * H + half * H, 6 * H + (half + 1) * H)
        p.op("vector", "tensor_scalar", yy, fv(half * H, (half + 1) * H),
             w1k[:, c:c + 1], None, ALU.mult)
        p.op("vector", "scalar_tensor_tensor", yy, fv(2 * H + half * H, 2 * H + (half + 1) * H),
             w2k[:, c:c + 1], yy, ALU.mult, ALU.add)
        p.op("vector", "tensor_add", yy, yy, fv(4 * H + half * H, 4 * H + (half + 1) * H))
        p.op("vector", "reduce_sum", rcol1[:], yy, axis=AX, negate=True)
        p.nop("vector")
        p.op("vector", "tensor_scalar", rcol1[:], rcol1[:], 1.0 / H, None, ALU.mult)
        p.nop("vector")
        p.op("vector", "tensor_scalar", yy, yy, rcol1[:, 0:1], None, ALU.add, sem=s_ve)
        p.wait("scalar", s_ve, s_ve.v)
        p.op("scalar", "activation", fv(8 * H, 9 * H), yy, AF.Square,
             accum_out=rcol2[:, 0:1], sem=s_sc)
        p.wait("vector", s_sc, s_sc.v)
        p.op("vector", "tensor_scalar", rcol2[:], rcol2[:], 1.0 / H, 1e-5,
             ALU.mult, ALU.add, sem=s_ve)
        p.wait("scalar", s_ve, s_ve.v)
        p.op("scalar", "activation", rcol3[:], rcol2[:], AF.Ln)
        p.nop("scalar")
        p.op("scalar", "activation", rcol3[:], rcol3[:], AF.Exp, scale=-0.5, sem=s_sc)
        p.wait("vector", s_sc, s_sc.v)
        p.op("vector", "tensor_scalar", yy, yy, rcol3[:, 0:1], None, ALU.mult, sem=s_ve)
        ve_e_last[c] = s_ve.v
        p.wait("sync", s_ve, s_ve.v)
        ot_done[c] = p.dma("sync", out[c * 128:(c + 1) * 128, :], yy, s_ot)
    p.wait("sync", s_ot, s_ot.v)

    with nc.Block() as block:
        block.sync(lambda e: [fn(e) for fn in p.ops["sync"]])
        block.gpsimd(lambda e: [fn(e) for fn in p.ops["gpsimd"]])
        block.vector(lambda e: [fn(e) for fn in p.ops["vector"]])
        block.scalar(lambda e: [fn(e) for fn in p.ops["scalar"]])
        block.tensor(lambda e: [fn(e) for fn in p.ops["tensor"]])
    ctx.close()
    return nc


_BUILT = None


def _get_built():
    global _BUILT
    if _BUILT is None:
        _BUILT = build_nc()
    return _BUILT


def _host_inputs(inputs):
    x = np.ascontiguousarray(np.asarray(inputs["L0_attn_x"], np.float32).reshape(B * T, H))
    tmix = np.ascontiguousarray(np.stack([
        np.asarray(inputs["L0_attn_time_mix_k"], np.float32).reshape(H),
        np.asarray(inputs["L0_attn_time_mix_v"], np.float32).reshape(H),
        np.asarray(inputs["L0_attn_time_mix_r"], np.float32).reshape(H),
    ]))
    common = {
        "Wk": np.ascontiguousarray(np.asarray(inputs["L0_attn_W_k"], np.float32)),
        "Wv": np.ascontiguousarray(np.asarray(inputs["L0_attn_W_v"], np.float32)),
        "Wr": np.ascontiguousarray(np.asarray(inputs["L0_attn_W_r"], np.float32)),
        "Wo": np.ascontiguousarray(np.asarray(inputs["L0_attn_W_o"], np.float32)),
        "tdecay": np.ascontiguousarray(np.asarray(inputs["L0_attn_time_decay"], np.float32)),
        "tfirst": np.ascontiguousarray(np.asarray(inputs["L0_attn_time_first"], np.float32)),
        "tmix": tmix,
        "Wg": np.ascontiguousarray(np.asarray(inputs["L0_ffn_Wg"], np.float32)),
        "W1": np.ascontiguousarray(np.asarray(inputs["L0_ffn_W1"], np.float32)),
        "b1": np.ascontiguousarray(np.asarray(inputs["L0_ffn_b1"], np.float32)),
        "W2": np.ascontiguousarray(np.asarray(inputs["L0_ffn_W2"], np.float32)),
        "b2": np.ascontiguousarray(np.asarray(inputs["L0_ffn_b2"], np.float32)),
    }
    in_maps = []
    for c in range(NCORES):
        t0 = c * TL
        if (t0 % T) == 0:
            xp = np.vstack([np.zeros((1, H), np.float32), x[t0:t0 + TL]])
        else:
            xp = x[t0 - 1:t0 + TL]
        mm = np.zeros((128, NCORES), np.float32)
        ma = np.full((128, NCORES), NEG, np.float32)
        pm = np.zeros((NCORES, 1), np.float32)
        for cp in range(c - (c % 4), c):
            mm[:, cp] = 1.0
            ma[:, cp] = 0.0
        for cp in range(c):
            pm[cp, 0] = 1.0
        m = dict(common)
        m["x_pad"] = np.ascontiguousarray(xp)
        m["cmask_mul"] = mm
        m["cmask_add"] = ma
        m["premask"] = pm
        in_maps.append(m)
    return in_maps


def kernel(**inputs):
    nc = _get_built()
    in_maps = _host_inputs(inputs)
    res = run_bass_kernel_spmd(nc, in_maps, core_ids=list(range(NCORES)))
    outs = [res.results[c]["y"] for c in range(NCORES)]
    return np.concatenate(outs, axis=0).reshape(B, T, H).astype(np.float32)




# version 11
# speedup vs baseline: 1.4508x; 1.0660x over previous
import numpy as np

import concourse.bass as bass
import concourse.mybir as mybir
from concourse.bass_utils import run_bass_kernel_spmd
from concourse.bass_core_rust import IndirectOffsetOnAxis

F32 = mybir.dt.float32
BF16 = mybir.dt.bfloat16
I32 = mybir.dt.int32
AF = mybir.ActivationFunctionType
ALU = mybir.AluOpType
AX = mybir.AxisListType.X

NCORES = 8
B, T, H = 2, 8192, 1024
TL = 2048
NTC = TL // 128
NCH = H // 128
E = 8
FF = 4096
NFC = FF // 128
CGLOB = 8192
CL = 1024
NEG = -1e38
TRASH = E * CL


class SemCtr:
    def __init__(self, handle, name):
        self.h = handle
        self.v = 0
        self.name = name


class Plan:
    def __init__(self):
        self.ops = {k: [] for k in ("sync", "gpsimd", "vector", "scalar", "tensor")}
        self.hist = {}

    def emit(self, eng, fn):
        self.ops[eng].append(fn)

    def wait(self, eng, sem, val=None):
        if val is None:
            val = sem.v
        assert val <= sem.v, f"wait on future value of {sem.name}: {val} > {sem.v}"
        if val > 0:
            self.emit(eng, lambda e, s=sem.h, v=val: e.wait_ge(s, v))

    def dma(self, eng, out, in_, sem, **kw):
        sem.v += 16
        self.emit(eng, lambda e, o=out, i=in_, s=sem.h, kw=kw:
                  e.dma_start(out=o, in_=i, **kw).then_inc(s, 16))
        return sem.v

    def dmat(self, eng, out, in_, sem):
        sem.v += 16
        self.emit(eng, lambda e, o=out, i=in_, s=sem.h:
                  e.dma_start_transpose(o, i).then_inc(s, 16))
        return sem.v

    scalar_nop_fn = None

    def nop(self, eng):
        if eng == "scalar":
            self.emit(eng, self.scalar_nop_fn)
        else:
            self.emit(eng, lambda e: e.engine_nop())

    @staticmethod
    def _foot(a):
        try:
            t = a.tensor.name
            off = a.offset
            dims = list(a.ap)
        except Exception:
            return None
        if not dims:
            return (t, 0, 1, off, off + 1, 1)
        pitch, np_ = dims[0]
        if pitch <= 0:
            pitch = 1
        p0 = off // pitch
        c0 = off % pitch
        span = sum(st * (ct - 1) for st, ct in dims[1:]) + 1
        return (t, p0, p0 + np_, c0, c0 + span, span)

    @staticmethod
    def _ovl(f1, f2):
        return (f1[0] == f2[0] and f1[1] < f2[2] and f2[1] < f1[2]
                and f1[3] < f2[4] and f2[3] < f1[4])

    WINDOW = 8

    def _hazard(self, eng, writes, reads):
        hist = self.hist.setdefault(eng, [])
        need = 0
        for (pw, pr, psem, pval) in hist:
            conflict = False
            for f1 in pw:
                for f2 in writes + reads:
                    if self._ovl(f1, f2) and (f1[5] < 512 or f2[5] < 512):
                        conflict = True
            for f1 in pr:
                for f2 in writes:
                    if self._ovl(f1, f2) and (f1[5] < 512 or f2[5] < 512):
                        conflict = True
            if conflict:
                if psem is not None:
                    need = max(need, pval) if psem is self.def_sem[eng] else need
                    if psem is not self.def_sem[eng]:
                        self.wait(eng, psem, pval)
                else:
                    self.nop(eng)
                    self.nop(eng)
        if need:
            self.wait(eng, self.def_sem[eng], need)

    def op(self, eng, meth, *args, sem=None, inc=1, **kw):
        writes = reads = None
        if eng in ("vector", "scalar"):
            w = [self._foot(args[0])] if args else []
            w += [self._foot(kw["accum_out"])] if kw.get("accum_out") is not None else []
            r = [self._foot(a) for a in args[1:]]
            r += [self._foot(v) for k2, v in kw.items() if k2 != "accum_out"]
            writes = [f for f in w if f is not None]
            reads = [f for f in r if f is not None]
            self._hazard(eng, writes, reads)
            small = any(f[5] < 512 for f in writes)
            if sem is None and small:
                sem = self.def_sem[eng]
                inc = 1
        if sem is not None:
            sem.v += inc
            ret = sem.v
            self.emit(eng, lambda e, a=args, kw=kw, s=sem.h, n=inc:
                      getattr(e, meth)(*a, **kw).then_inc(s, n))
        else:
            ret = None
            self.emit(eng, lambda e, a=args, kw=kw: getattr(e, meth)(*a, **kw))
        if writes is not None:
            hist = self.hist.setdefault(eng, [])
            hist.append((writes, reads, sem, sem.v if sem is not None else None))
            if len(hist) > self.WINDOW:
                hist.pop(0)
        return ret


DEBUG = False


def build_nc():
    nc = bass.Bass("TRN2", target_bir_lowering=False)
    SCR = "ExternalOutput" if DEBUG else "Internal"

    x_pad = nc.dram_tensor("x_pad", [TL + 1, H], F32, kind="ExternalInput").ap()
    Wk = nc.dram_tensor("Wk", [H, H], F32, kind="ExternalInput").ap()
    Wv = nc.dram_tensor("Wv", [H, H], F32, kind="ExternalInput").ap()
    Wr = nc.dram_tensor("Wr", [H, H], F32, kind="ExternalInput").ap()
    Wo = nc.dram_tensor("Wo", [H, H], F32, kind="ExternalInput").ap()
    tdecay = nc.dram_tensor("tdecay", [H], F32, kind="ExternalInput").ap()
    tfirst = nc.dram_tensor("tfirst", [H], F32, kind="ExternalInput").ap()
    tmix = nc.dram_tensor("tmix", [3, H], F32, kind="ExternalInput").ap()
    Wg = nc.dram_tensor("Wg", [H, E], F32, kind="ExternalInput").ap()
    W1 = nc.dram_tensor("W1", [E, H, FF], F32, kind="ExternalInput").ap()
    b1 = nc.dram_tensor("b1", [E, FF], F32, kind="ExternalInput").ap()
    W2 = nc.dram_tensor("W2", [E, FF, H], F32, kind="ExternalInput").ap()
    b2 = nc.dram_tensor("b2", [E, H], F32, kind="ExternalInput").ap()
    cmask_mul = nc.dram_tensor("cmask_mul", [128, NCORES], F32, kind="ExternalInput").ap()
    cmask_add = nc.dram_tensor("cmask_add", [128, NCORES], F32, kind="ExternalInput").ap()
    premask = nc.dram_tensor("premask", [NCORES, 1], F32, kind="ExternalInput").ap()
    out = nc.dram_tensor("y", [TL, H], F32, kind="ExternalOutput").ap()

    k_sp = nc.dram_tensor("k_sp", [H, TL], F32, kind=SCR).ap()
    v_sp = nc.dram_tensor("v_sp", [H, TL], F32, kind=SCR).ap()
    r_sp = nc.dram_tensor("r_sp", [H, TL], F32, kind=SCR).ap()
    xw_sp = nc.dram_tensor("xw_sp", [H, TL], F32, kind=SCR).ap()
    ss_sp = nc.dram_tensor("ss_sp", [H, TL], F32, kind=SCR).ap()
    dd_sp = nc.dram_tensor("dd_sp", [H, TL], F32, kind=SCR).ap()
    wb_d = [nc.dram_tensor(f"w{i}b_d", [H, H], BF16).ap() for i in range(4)]
    x1_sp = nc.dram_tensor("x1_sp", [TL, H], F32, kind=SCR).ap()
    xe_d = nc.dram_tensor("xe_d", [E * CL + 1, H], BF16, kind=SCR).ap()
    ye_d = nc.dram_tensor("ye_d", [E * CL + 1, H], F32, kind=SCR).ap()
    cc1_in = nc.dram_tensor("cc1_in", [128, 3 * NCH], F32).ap()
    cc1_out = nc.dram_tensor("cc1_out", [128 * NCORES, 3 * NCH], F32).ap()
    cc2_in = nc.dram_tensor("cc2_in", [1, 2 * E], F32).ap()
    cc2_out = nc.dram_tensor("cc2_out", [NCORES, 2 * E], F32).ap()

    from contextlib import ExitStack
    ctx = ExitStack()

    def sb(name, shape, dt=F32):
        return ctx.enter_context(nc.sbuf_tensor(name, shape, dt))

    def mksem(name):
        return SemCtr(ctx.enter_context(nc.semaphore(name)), name)

    NF = 22544
    NBQ = 50176
    ARF = sb("ARF", [128, NF])
    ARB = sb("ARB", [128, NBQ], BF16)

    def fv(a, b):
        return ARF[:, a:b]

    def bv(a, b):
        return ARB[:, a:b]

    def xTv(g, a, b):
        base = g * (TL + 1)
        return ARF[:, base + a: base + b]
    SPL0 = 16392
    SKA, SKB, SVV = 0, 2048, 4096
    SA_, SL_, SG_ = 6144, 8192, 10240
    SM_, SS_, SD_ = 12288, 14337, 20480
    KA, KB, XA, XB = 0, 2048, 4096, 6144
    SA2, SB2, DA2 = 8192, 10240, 12288
    CVV, CRR, CX3, CP_ = 14336, 16384, 18432, 20480

    WT = [0, 8192, 16384, 24576]
    XMT = 32768
    RWO = 0
    X1B = 32768
    X1TC = 16384
    H1O = 0
    W2O = 16384
    W1O = 32768
    XEO = 40960
    B2O = 49152

    iota0 = sb("iota0", [128, TL + 1])
    iota8 = sb("iota8", [128, E])
    iota8p = sb("iota8p", [128, E])
    ident = sb("ident", [128, 128])
    identb = sb("identb", [128, 128], BF16)
    triu = sb("triu", [128, 128])
    ones_c = sb("ones_c", [128, 1])
    ones_r1 = sb("ones_r1", [1, 128])
    ones_r1b = sb("ones_r1b", [1, 128], BF16)
    w_sb = sb("w_sb", [128, NCH])
    negw_sb = sb("negw_sb", [128, NCH])
    w2048_sb = sb("w2048_sb", [128, NCH])
    u_sb = sb("u_sb", [128, NCH])
    tm_sb = sb("tm_sb", [128, 3 * NCH])
    tm1_sb = sb("tm1_sb", [128, 3 * NCH])
    cmaskm = sb("cmaskm", [128, NCORES])
    cmaska = sb("cmaska", [128, NCORES])
    premk = sb("premk", [NCORES, 1])
    wgb = sb("wgb", [128, NCH * E], BF16)
    b1_sb = sb("b1_sb", [128, E * NFC])
    agg = sb("agg", [128, 3 * NCH])
    aggT = sb("aggT", [128, 3 * NCH])
    carry = sb("carry", [128, 3 * NCH])
    sm1 = sb("sm1", [128, 3 * NCH])
    sm2 = sb("sm2", [128, 3 * NCH])
    sm3 = sb("sm3", [128, 3 * NCH])
    cc2_sb = sb("cc2_sb", [NCORES, 2 * E])
    offs = sb("offs", [1, 2 * E])
    offg = sb("offg", [1, E])
    loco = sb("loco", [1, 2 * E])
    logit = sb("logit", [128, NTC * E])
    mask1 = sb("mask1", [128, NTC * E])
    mask2 = sb("mask2", [128, NTC * E])
    kept1 = sb("kept1", [128, NTC * E])
    kept2 = sb("kept2", [128, NTC * E])
    pos1 = sb("pos1", [128, NTC])
    pos2 = sb("pos2", [128, NTC])
    keep1 = sb("keep1", [128, NTC])
    keep2 = sb("keep2", [128, NTC])
    w1k = sb("w1k", [128, NTC])
    w2k = sb("w2k", [128, NTC])
    slot1 = sb("slot1", [128, NTC])
    slot2 = sb("slot2", [128, NTC])
    dst1i = sb("dst1i", [128, NTC], I32)
    dst2i = sb("dst2i", [128, NTC], I32)
    rsc1 = sb("rsc1", [128, E])
    rsc2 = sb("rsc2", [128, E])
    rsc3 = sb("rsc3", [128, E])
    rcol1 = sb("rcol1", [128, 1])
    rcol2 = sb("rcol2", [128, 1])
    rcol3 = sb("rcol3", [128, 1])
    xck = sb("xck", [128, H])
    zb = sb("zb", [128, H], BF16)
    nopc = sb("nopc", [128, 1])

    def psumt(name, shape):
        return ctx.enter_context(nc.psum_tensor(name, shape, F32))

    pbank = [psumt(f"pb{i}", [128, 512]) for i in range(8)]

    s_ld = mksem("s_ld")
    s_gp = mksem("s_gp")
    s_gc = mksem("s_gc")
    s_sp = mksem("s_sp")
    s_pe = mksem("s_pe")
    s_ve = mksem("s_ve")
    s_sc = mksem("s_sc")
    s_cc = mksem("s_cc")
    s_ix = mksem("s_ix")
    s_ot = mksem("s_ot")

    p = Plan()
    p.scalar_nop_fn = lambda e: e.copy(nopc[:], nopc[:])
    p.def_sem = {"vector": s_ve, "scalar": s_sc}
    bankg = {}

    def bank_wait(i):
        if i in bankg:
            sem, val = bankg[i]
            p.wait("tensor", sem, val)

    def bank_set(i, sem, val):
        bankg[i] = (sem, val)

    p.op("gpsimd", "iota", iota0[:], [[1, TL + 1]], base=-1, channel_multiplier=0,
         allow_small_or_imprecise_dtypes=True)
    p.op("gpsimd", "iota", iota8[:], [[1, E]], channel_multiplier=0,
         allow_small_or_imprecise_dtypes=True)
    p.op("gpsimd", "memset", ident[:], 0.0)
    p.op("gpsimd", "affine_select", out=ident[:], in_=ident[:],
         compare_op=ALU.not_equal, fill=1.0, base=0,
         pattern=[[-1, 128]], channel_multiplier=1)
    p.op("gpsimd", "memset", identb[:], 0.0)
    p.op("gpsimd", "affine_select", out=identb[:], in_=identb[:],
         compare_op=ALU.not_equal, fill=1.0, base=0,
         pattern=[[-1, 128]], channel_multiplier=1)
    p.op("gpsimd", "memset", triu[:], 0.0)
    p.op("gpsimd", "affine_select", out=triu[:], in_=triu[:],
         compare_op=ALU.is_ge, fill=1.0, base=0,
         pattern=[[-1, 128]], channel_multiplier=1)
    p.op("gpsimd", "memset", ones_c[:], 1.0)
    p.op("gpsimd", "memset", ones_r1[:], 1.0)
    p.op("gpsimd", "memset", ones_r1b[:], 1.0, sem=s_gc)
    gc_consts = s_gc.v

    iotav = iota0[:, 1:TL + 1]
    iotam1 = iota0[:, 0:TL]

    p.dma("sync", w_sb[:], tdecay.rearrange("(g p) -> p g", p=128), s_ld,
          allow_slow_non_contiguous=True)
    p.dma("sync", u_sb[:], tfirst.rearrange("(g p) -> p g", p=128), s_ld,
          allow_slow_non_contiguous=True)
    p.dma("sync", tm_sb.ap().rearrange("p (m g) -> p m g", g=NCH),
          tmix.rearrange("m (g p) -> p m g", p=128), s_ld,
          allow_slow_non_contiguous=True)
    p.dma("sync", cmaskm[:], cmask_mul, s_ld)
    p.dma("sync", cmaska[:], cmask_add, s_ld)
    p.dma("sync", premk[:], premask, s_ld)
    p.dma("sync", b1_sb.ap().rearrange("p (e c) -> p e c", c=NFC),
          b1.rearrange("e (c p) -> p e c", p=128), s_ld,
          allow_slow_non_contiguous=True)
    ld_params = s_ld.v

    p.wait("vector", s_ld, ld_params)
    p.wait("vector", s_gc, gc_consts)
    p.op("vector", "tensor_scalar", negw_sb[:], w_sb[:], -1.0, None, ALU.mult)
    p.op("vector", "tensor_scalar", w2048_sb[:], w_sb[:], float(TL), None, ALU.mult)
    p.op("vector", "tensor_scalar", tm1_sb[:], tm_sb[:], -1.0, 1.0, ALU.mult, ALU.add)
    p.op("vector", "tensor_scalar", iota8p[:], iota8[:], 16384.0, None, ALU.add,
         sem=s_ve)
    ve_params = s_ve.v

    for w_src, w_dst in ((Wk, wb_d[0]), (Wv, wb_d[1]), (Wr, wb_d[2]), (Wo, wb_d[3])):
        p.dma("gpsimd", w_dst, w_src, s_gp)
    gp_wcast = s_gp.v
    p.dma("gpsimd", wgb.ap().rearrange("p (g e) -> p g e", e=E),
          Wg.rearrange("(g p) e -> p g e", p=128), s_gp,
          allow_slow_non_contiguous=True)
    gp_small = s_gp.v

    tp_done = 0
    for c in range(NTC):
        if c > 0:
            p.wait("sync", s_pe, tp_done)
        ldx = p.dma("sync", xck[:], x_pad[c * 128:(c + 1) * 128, :], s_ld)
        p.wait("tensor", s_ld, ldx)
        if c == 0:
            p.wait("tensor", s_gc, gc_consts)
        for g in range(NCH):
            bk = g % 4
            bank_wait(bk)
            p.op("tensor", "transpose", pbank[bk][:, 0:128],
                 xck[:, g * 128:(g + 1) * 128], ident[:], sem=s_pe)
            p.wait("scalar", s_pe, s_pe.v)
            tgt = p.op("scalar", "copy", xTv(g, c * 128, (c + 1) * 128),
                       pbank[bk][:, 0:128], sem=s_sc)
            bank_set(bk, s_sc, tgt)
        tp_done = s_pe.v
    ld_xT_last = p.dma("sync",
                       ARF.ap()[:, 0:NCH * (TL + 1)].rearrange(
                           "p (g t) -> p g t", g=NCH)[:, :, TL:TL + 1],
                       x_pad[TL:TL + 1, :].rearrange("o (g p) -> p g o", p=128), s_ld,
                       allow_slow_non_contiguous=True)
    sc_xT = s_sc.v

    p.wait("sync", s_gp, gp_wcast)
    for i in range(4):
        for g in range(NCH):
            p.dmat("sync", bv(WT[i] + g * H, WT[i] + (g + 1) * H),
                   wb_d[i][:, g * 128:(g + 1) * 128], s_ld)
    ld_wT = s_ld.v

    p.op("gpsimd", "memset", zb[:], 0.0, sem=s_gc)
    gc_zb = s_gc.v
    p.wait("gpsimd", s_gc, gc_zb)
    sp_zero_ye = p.dma("gpsimd", ye_d[TRASH:TRASH + 1, :], zb[0:1, :], s_gp)
    sp_zero = s_sp.v

    MIX_SP = {0: k_sp, 1: v_sp, 2: r_sp}
    spl = fv(SPL0, SPL0 + 4096)
    atmp = fv(20496, 20496 + TL)

    kbuf = [fv(SKA, SKA + TL), fv(SKB, SKB + TL)]
    svv = fv(SVV, SVV + TL)
    sa = fv(SA_, SA_ + TL)
    sl = fv(SL_, SL_ + TL)
    sg = fv(SG_, SG_ + TL)
    sminc = ARF[:, SM_:SM_ + TL + 1]
    ssinc = ARF[:, SS_:SS_ + TL + 1]
    sdinc = ARF[:, SD_:SD_ + TL + 1]
    sp_proj = {}
    mix2_sems = {}
    ld_k = {}
    ld_kv = {}
    sa_done = {}
    gv_done = {}
    scans_done = {}

    def scan_loads(g):
        p.wait("gpsimd", s_sp, sp_proj[0 * NCH + g])
        p.wait("gpsimd", s_sp, sp_proj[1 * NCH + g])
        if g >= 2:
            p.wait("gpsimd", s_ve, sa_done[g - 2])
        else:
            p.wait("gpsimd", s_ve, mix2_sems[min(g + 1, NCH - 1)])
        p.dma("gpsimd", kbuf[g % 2], k_sp[g * 128:(g + 1) * 128, :], s_gp)
        ld_k[g] = s_gp.v
        if g >= 1:
            p.wait("gpsimd", s_ve, gv_done[g - 1])
        else:
            p.wait("gpsimd", s_ve, mix2_sems[2])
        ld_kv[g] = p.dma("gpsimd", svv, v_sp[g * 128:(g + 1) * 128, :], s_gp)

    def scan_tile(g):
        kk_ = kbuf[g % 2]
        p.wait("vector", s_gp, ld_k[g])
        p.op("vector", "scalar_tensor_tensor", sa, iotav,
             negw_sb[:, g:g + 1], kk_, ALU.mult, ALU.add, sem=s_ve)
        sa_done[g] = s_ve.v
        p.op("vector", "memset", sminc[:, 0:1], NEG)
        p.op("vector", "tensor_tensor_scan", sminc[:, 1:TL + 1], sa, sa,
             sminc[:, 0:1], ALU.max, ALU.max)
        p.op("vector", "tensor_sub", sl, sminc[:, 0:TL], sminc[:, 1:TL + 1])
        p.op("vector", "tensor_sub", sa, sa, sminc[:, 1:TL + 1], sem=s_ve)
        ve_gin = s_ve.v
        p.wait("scalar", s_ve, ve_gin)
        p.op("scalar", "activation", sl, sl, AF.Exp)
        p.op("scalar", "activation", sa, sa, AF.Exp, sem=s_sc)
        sc_exp = s_sc.v
        p.wait("vector", s_sc, sc_exp)
        p.wait("vector", s_gp, ld_kv[g])
        p.op("vector", "tensor_mul", sg, sa, svv, sem=s_ve)
        gv_done[g] = s_ve.v
        if g + 1 < NCH:
            scan_loads(g + 1)
        p.op("vector", "memset", ssinc[:, 0:1], 0.0)
        p.op("vector", "memset", sdinc[:, 0:1], 0.0)
        p.op("vector", "tensor_tensor_scan", ssinc[:, 1:TL + 1], sl, sg,
             ssinc[:, 0:1], ALU.mult, ALU.add)
        p.op("vector", "tensor_tensor_scan", sdinc[:, 1:TL + 1], sl, sa,
             sdinc[:, 0:1], ALU.mult, ALU.add)
        p.op("vector", "scalar_tensor_tensor", sl, iotam1,
             w_sb[:, g:g + 1], sminc[:, 0:TL], ALU.mult, ALU.add)
        p.op("vector", "tensor_scalar", agg[:, g:g + 1], sminc[:, TL:TL + 1],
             w2048_sb[:, g:g + 1], w_sb[:, g:g + 1], ALU.add, ALU.subtract)
        p.op("vector", "tensor_copy", agg[:, NCH + g:NCH + g + 1],
             ssinc[:, TL:TL + 1])
        p.op("vector", "tensor_copy", agg[:, 2 * NCH + g:2 * NCH + g + 1],
             sdinc[:, TL:TL + 1], sem=s_ve)
        scans_done[g] = s_ve.v
        p.wait("sync", s_ve, scans_done[g])
        p.dma("sync", xw_sp[g * 128:(g + 1) * 128, :], sl, s_sp)
        p.dma("sync", dd_sp[g * 128:(g + 1) * 128, :], sdinc[:, 0:TL], s_sp)
        p.wait("gpsimd", s_ve, scans_done[g])
        p.dma("gpsimd", ss_sp[g * 128:(g + 1) * 128, :], ssinc[:, 0:TL], s_gp)

    pe_proj_done = 0
    for m in range(3):
        p.wait("vector", s_sc, sc_xT)
        p.wait("vector", s_ld, ld_xT_last)
        p.wait("vector", s_ve, ve_params)
        if m > 0:
            p.wait("vector", s_pe, pe_proj_done)
        for g in range(NCH):
            xr = xTv(g, 1, 1 + TL)
            xp = xTv(g, 0, TL)
            p.op("vector", "tensor_scalar", atmp, xr,
                 tm_sb[:, m * NCH + g: m * NCH + g + 1], None, ALU.mult)
            p.op("vector", "scalar_tensor_tensor", bv(XMT + g * TL, XMT + (g + 1) * TL),
                 xp, tm1_sb[:, m * NCH + g: m * NCH + g + 1], atmp, ALU.mult, ALU.add,
                 sem=s_ve)
            if m == 2:
                mix2_sems[g] = s_ve.v
        ve_mix = s_ve.v

        if m == 2:
            scan_loads(0)
        p.wait("tensor", s_ld, ld_wT)
        p.wait("tensor", s_ve, ve_mix)
        for oc in range(NCH):
            buf = oc % 2
            prev = sp_proj.get(m * NCH + oc - 2, 0)
            for tq in range(4):
                bk = 4 + (tq % 2)
                bank_wait(bk)
                for g in range(NCH):
                    p.op("tensor", "matmul", pbank[bk][:],
                         bv(WT[m] + g * H + oc * 128, WT[m] + g * H + (oc + 1) * 128),
                         bv(XMT + g * TL + tq * 512, XMT + g * TL + (tq + 1) * 512),
                         start=(g == 0), stop=(g == NCH - 1),
                         sem=(s_pe if g == NCH - 1 else None))
                p.wait("scalar", s_pe, s_pe.v)
                p.wait("scalar", s_sp, prev)
                func = AF.Sigmoid if m == 2 else AF.Copy
                tgt = p.op("scalar", "activation",
                           fv(SPL0 + buf * TL + tq * 512, SPL0 + buf * TL + (tq + 1) * 512),
                           pbank[bk][:], func, sem=s_sc)
                bank_set(bk, s_sc, tgt)
            p.wait("sync", s_sc, s_sc.v)
            sp_proj[m * NCH + oc] = p.dma(
                "sync", MIX_SP[m][oc * 128:(oc + 1) * 128, :],
                fv(SPL0 + buf * TL, SPL0 + (buf + 1) * TL), s_sp)
            if m == 2:
                scan_tile(oc)
        pe_proj_done = s_pe.v
    sp_proj_all = s_sp.v
    gp_scan_all = s_gp.v
    ve_agg = scans_done[NCH - 1]

    if DEBUG:
        dbgv = fv(20496, 20512)
        d_dbgv = nc.dram_tensor("d_dbgv", [128, 16], F32, kind="ExternalOutput").ap()

    p.wait("sync", s_ve, ve_agg)
    cc_pub = p.dma("sync", cc1_in, agg[:], s_sp)
    p.wait("gpsimd", s_sp, cc_pub)
    p.op("gpsimd", "collective_compute", "AllGather", ALU.bypass,
         replica_groups=[list(range(NCORES))],
         ins=[cc1_in.opt()], outs=[cc1_out.opt()], sem=s_cc)
    cc1_done = s_cc.v
    p.op("vector", "memset", carry[:, 0:NCH], NEG)
    p.op("vector", "memset", carry[:, NCH:3 * NCH], 0.0, sem=s_ve)
    p.wait("gpsimd", s_cc, cc1_done)
    for cp in range(NCORES):
        if cp > 0:
            p.wait("gpsimd", s_ve, fold_done)
        ldt = p.dma("gpsimd", aggT[:], cc1_out[cp * 128:(cp + 1) * 128, :], s_gp)
        p.wait("vector", s_gp, ldt)
        p.op("vector", "tensor_scalar", sm1[:, 0:NCH], aggT[:, 0:NCH],
             cmaskm[:, cp:cp + 1], cmaska[:, cp:cp + 1], ALU.mult, ALU.add)
        p.op("vector", "tensor_scalar", sm1[:, NCH:3 * NCH], aggT[:, NCH:3 * NCH],
             cmaskm[:, cp:cp + 1], None, ALU.mult)
        p.op("vector", "tensor_scalar", sm2[:, 0:NCH], w2048_sb[:],
             cmaskm[:, cp:cp + 1], None, ALU.mult)
        p.nop("vector")
        p.op("vector", "tensor_tensor", sm2[:, 0:NCH], carry[:, 0:NCH],
             sm2[:, 0:NCH], ALU.add)
        p.nop("vector")
        p.op("vector", "tensor_tensor", sm3[:, 0:NCH], sm2[:, 0:NCH], sm1[:, 0:NCH],
             ALU.max)
        p.nop("vector")
        p.op("vector", "tensor_sub", sm2[:, NCH:2 * NCH], sm2[:, 0:NCH], sm3[:, 0:NCH])
        p.op("vector", "tensor_sub", sm2[:, 2 * NCH:3 * NCH], sm1[:, 0:NCH],
             sm3[:, 0:NCH], sem=s_ve)
        p.wait("scalar", s_ve, s_ve.v)
        p.op("scalar", "activation", sm2[:, NCH:3 * NCH], sm2[:, NCH:3 * NCH], AF.Exp,
             sem=s_sc)
        p.wait("vector", s_sc, s_sc.v)
        p.op("vector", "tensor_mul", sm3[:, NCH:2 * NCH], carry[:, NCH:2 * NCH],
             sm2[:, NCH:2 * NCH])
        p.op("vector", "tensor_mul", sm3[:, 2 * NCH:3 * NCH], carry[:, 2 * NCH:3 * NCH],
             sm2[:, NCH:2 * NCH])
        p.op("vector", "tensor_mul", sm1[:, NCH:2 * NCH], sm1[:, NCH:2 * NCH],
             sm2[:, 2 * NCH:3 * NCH])
        p.op("vector", "tensor_mul", sm1[:, 2 * NCH:3 * NCH], sm1[:, 2 * NCH:3 * NCH],
             sm2[:, 2 * NCH:3 * NCH])
        p.nop("vector")
        p.op("vector", "tensor_add", carry[:, NCH:3 * NCH], sm3[:, NCH:3 * NCH],
             sm1[:, NCH:3 * NCH])
        p.op("vector", "tensor_copy", carry[:, 0:NCH], sm3[:, 0:NCH], sem=s_ve)
        fold_done = s_ve.v
        if DEBUG and cp == 1:
            p.op("vector", "tensor_copy", dbgv[:, 8:9], sm1[:, 0:1], sem=s_ve)
            p.op("vector", "tensor_copy", dbgv[:, 9:10], sm2[:, 0:1], sem=s_ve)
            p.op("vector", "tensor_copy", dbgv[:, 10:11], sm3[:, 0:1], sem=s_ve)
            p.op("vector", "tensor_copy", dbgv[:, 11:12], sm2[:, NCH:NCH + 1], sem=s_ve)
            p.op("vector", "tensor_copy", dbgv[:, 12:13], sm2[:, 2 * NCH:2 * NCH + 1], sem=s_ve)
            p.op("vector", "tensor_copy", dbgv[:, 13:14], cmaskm[:, 1:2], sem=s_ve)
            p.op("vector", "tensor_copy", dbgv[:, 14:15], cmaska[:, 1:2], sem=s_ve)
            p.op("vector", "tensor_copy", dbgv[:, 15:16], carry[:, 0:1], sem=s_ve)
    ve_carry = s_ve.v

    ckk = [fv(KA, KA + TL), fv(KB, KB + TL)]
    cxw = [fv(XA, XA + TL), fv(XB, XB + TL)]
    css = [fv(SA2, SA2 + TL), fv(SB2, SB2 + TL)]
    cdd_ = fv(DA2, DA2 + TL)
    cvv = fv(CVV, CVV + TL)
    crr = fv(CRR, CRR + TL)
    cx3 = fv(CX3, CX3 + TL)
    cpb = fv(CP_, CP_ + TL)

    p.wait("sync", s_ve, ve_agg)
    p.wait("sync", s_sp, sp_proj_all)
    p.wait("sync", s_gp, gp_scan_all)
    p.wait("gpsimd", s_ve, ve_agg)
    p.wait("gpsimd", s_sp, sp_proj_all)
    p.wait("vector", s_sp, sp_proj_all)
    p.wait("vector", s_gp, gp_scan_all)
    cld = {}
    cld[("k", 0)] = p.dma("sync", ckk[0], k_sp[0:128, :], s_ld)
    cld[("v", 0)] = p.dma("sync", cvv, v_sp[0:128, :], s_ld)
    cld[("r", 0)] = p.dma("sync", crr, r_sp[0:128, :], s_ld)
    cld[("k", 1)] = p.dma("sync", ckk[1], k_sp[128:256, :], s_ld)
    cld[("d", 0)] = p.dma("gpsimd", cdd_, dd_sp[0:128, :], s_gp)
    for g0 in range(2):
        cld[("x", g0)] = p.dma("gpsimd", cxw[g0],
                               xw_sp[g0 * 128:(g0 + 1) * 128, :], s_gp)
        cld[("s", g0)] = p.dma("gpsimd", css[g0],
                               ss_sp[g0 * 128:(g0 + 1) * 128, :], s_gp)

    ve_rw_all = 0
    for g in range(NCH):
        kk_ = ckk[g % 2]
        xw_ = cxw[g % 2]
        ss_ = css[g % 2]
        dd_ = cdd_
        if g == 0:
            p.wait("vector", s_ve, ve_carry)
        p.wait("vector", s_gp, cld[("x", g)])
        p.op("vector", "tensor_scalar", cx3, iotav, w_sb[:, g:g + 1],
             carry[:, g:g + 1], ALU.mult, ALU.add)
        p.wait("vector", s_ld, cld[("k", g)])
        p.op("vector", "scalar_tensor_tensor", cpb, kk_, u_sb[:, g:g + 1],
             xw_, ALU.add, ALU.max)
        p.op("vector", "tensor_tensor", cpb, cpb, cx3, ALU.max)
        p.op("vector", "tensor_sub", xw_, xw_, cpb, sem=s_ve)
        v4 = s_ve.v
        p.op("vector", "tensor_sub", kk_, kk_, cpb, sem=s_ve)
        v5 = s_ve.v
        p.op("vector", "tensor_sub", cx3, cx3, cpb, sem=s_ve)
        v6 = s_ve.v
        p.wait("scalar", s_ve, v4)
        p.op("scalar", "activation", xw_, xw_, AF.Exp, sem=s_sc)
        e1d = s_sc.v
        p.wait("scalar", s_ve, v5)
        p.op("scalar", "activation", kk_, kk_, AF.Exp, bias=u_sb[:, g:g + 1],
             scale=1.0, sem=s_sc)
        e2d = s_sc.v
        p.wait("scalar", s_ve, v6)
        p.op("scalar", "activation", cx3, cx3, AF.Exp, sem=s_sc)
        e3d = s_sc.v
        p.wait("vector", s_sc, e1d)
        p.wait("vector", s_gp, cld[("s", g)])
        p.op("vector", "tensor_mul", ss_, ss_, xw_)
        p.wait("vector", s_gp, cld[("d", g)])
        p.op("vector", "tensor_mul", dd_, dd_, xw_, sem=s_ve)
        v11 = s_ve.v
        p.wait("vector", s_sc, e2d)
        p.wait("vector", s_ld, cld[("v", g)])
        p.op("vector", "tensor_mul", cvv, cvv, kk_)
        p.op("vector", "scalar_tensor_tensor", dd_, kk_, 1e-9, dd_,
             ALU.add, ALU.add, sem=s_ve)
        v12 = s_ve.v
        p.op("vector", "tensor_add", ss_, ss_, cvv, sem=s_ve)
        v9 = s_ve.v
        p.wait("vector", s_sc, e3d)
        p.op("vector", "scalar_tensor_tensor", dd_, cx3,
             carry[:, 2 * NCH + g:2 * NCH + g + 1], dd_, ALU.mult, ALU.add,
             sem=s_ve)
        v13 = s_ve.v
        p.op("vector", "scalar_tensor_tensor", ss_, cx3,
             carry[:, NCH + g:NCH + g + 1], ss_, ALU.mult, ALU.add)
        p.wait("scalar", s_ve, v13)
        p.op("scalar", "activation", dd_, dd_, AF.Ln)
        p.op("scalar", "activation", dd_, dd_, AF.Exp, scale=-1.0, sem=s_sc)
        recd = s_sc.v
        p.wait("vector", s_sc, recd)
        p.op("vector", "tensor_mul", ss_, ss_, dd_, sem=s_ve)
        v14 = s_ve.v
        p.wait("vector", s_ld, cld[("r", g)])
        p.op("vector", "tensor_mul", bv(RWO + g * TL, RWO + (g + 1) * TL),
             ss_, crr, sem=s_ve)
        ve_rw_all = s_ve.v

        if g + 1 < NCH:
            p.wait("sync", s_ve, v9)
            cld[("v", g + 1)] = p.dma("sync", cvv,
                                      v_sp[(g + 1) * 128:(g + 2) * 128, :], s_ld)
            p.wait("sync", s_ve, ve_rw_all)
            cld[("r", g + 1)] = p.dma("sync", crr,
                                      r_sp[(g + 1) * 128:(g + 2) * 128, :], s_ld)
            p.wait("gpsimd", s_ve, v14)
            cld[("d", g + 1)] = p.dma("gpsimd", cdd_,
                                      dd_sp[(g + 1) * 128:(g + 2) * 128, :], s_gp)
        if g + 2 < NCH:
            p.wait("sync", s_ve, v12)
            cld[("k", g + 2)] = p.dma("sync", ckk[g % 2],
                                      k_sp[(g + 2) * 128:(g + 3) * 128, :], s_ld)
            p.wait("gpsimd", s_ve, v11)
            cld[("x", g + 2)] = p.dma("gpsimd", cxw[g % 2],
                                      xw_sp[(g + 2) * 128:(g + 3) * 128, :], s_gp)
            p.wait("gpsimd", s_ve, ve_rw_all)
            cld[("s", g + 2)] = p.dma("gpsimd", css[g % 2],
                                      ss_sp[(g + 2) * 128:(g + 3) * 128, :], s_gp)

    if DEBUG:
        d_rw = nc.dram_tensor("d_rw", [128, NCH * TL], BF16, kind="ExternalOutput").ap()
        d_carry = nc.dram_tensor("d_carry", [128, 3 * NCH], F32, kind="ExternalOutput").ap()
        d_agg = nc.dram_tensor("d_agg", [128, 3 * NCH], F32, kind="ExternalOutput").ap()
        d_aggall = nc.dram_tensor("d_aggall", [128 * NCORES, 3 * NCH], F32, kind="ExternalOutput").ap()
        p.wait("sync", s_ve, ve_rw_all)
        p.dma("sync", d_rw, bv(RWO, RWO + NCH * TL), s_sp)
        p.dma("sync", d_carry, carry[:], s_sp)
        p.dma("sync", d_agg, agg[:], s_sp)
        p.dma("sync", d_dbgv, dbgv[:], s_sp)
        p.dma("sync", d_aggall, cc1_out, s_sp)

    c1gC = fv(0, H)
    youtC = fv(H, 2 * H)
    if DEBUG:
        d_y1 = nc.dram_tensor("d_y1", [TL, H], F32, kind="ExternalOutput").ap()
    p.wait("tensor", s_ve, ve_rw_all)
    p.wait("tensor", s_ld, ld_wT)
    sp_x1_last = 0
    for c in range(NTC):
        ldx = p.dma("sync", xck[:], x_pad[1 + c * 128: 1 + (c + 1) * 128, :], s_ld)
        for oh in range(2):
            bk = oh
            bank_wait(bk)
            for g in range(NCH):
                p.op("tensor", "matmul", pbank[bk][:],
                     bv(RWO + g * TL + c * 128, RWO + g * TL + (c + 1) * 128),
                     bv(WT[3] + g * H + oh * 512, WT[3] + g * H + (oh + 1) * 512),
                     start=(g == 0), stop=(g == NCH - 1),
                     sem=(s_pe if g == NCH - 1 else None))
            p.wait("vector", s_pe, s_pe.v)
            p.wait("vector", s_ld, ldx)
            p.wait("vector", s_sp, sp_x1_last)
            tgt = p.op("vector", "tensor_add", youtC[:, oh * 512:(oh + 1) * 512]
                       if False else fv(H + oh * 512, H + (oh + 1) * 512),
                       pbank[bk][:], xck[:, oh * 512:(oh + 1) * 512], sem=s_ve)
            bank_set(bk, s_ve, tgt)
        if DEBUG:
            p.wait("sync", s_ve, s_ve.v)
            p.dma("sync", d_y1[c * 128:(c + 1) * 128, :], youtC, s_sp)
        y = youtC
        p.op("vector", "reduce_sum", rcol1[:], y, axis=AX, negate=True)
        p.nop("vector")
        p.op("vector", "tensor_scalar", rcol1[:], rcol1[:], 1.0 / H, None, ALU.mult)
        p.nop("vector")
        p.op("vector", "tensor_scalar", c1gC, y, rcol1[:, 0:1], None, ALU.add, sem=s_ve)
        p.wait("scalar", s_ve, s_ve.v)
        p.op("scalar", "activation", fv(2 * H, 3 * H), c1gC, AF.Square,
             accum_out=rcol2[:, 0:1], sem=s_sc)
        p.wait("vector", s_sc, s_sc.v)
        p.op("vector", "tensor_scalar", rcol2[:], rcol2[:], 1.0 / H, 1e-5,
             ALU.mult, ALU.add, sem=s_ve)
        p.wait("scalar", s_ve, s_ve.v)
        p.op("scalar", "activation", rcol3[:], rcol2[:], AF.Ln)
        p.nop("scalar")
        p.op("scalar", "activation", rcol3[:], rcol3[:], AF.Exp, scale=-0.5, sem=s_sc)
        p.wait("vector", s_sc, s_sc.v)
        p.op("vector", "tensor_scalar", c1gC, c1gC, rcol3[:, 0:1], None, ALU.mult)
        p.op("vector", "tensor_copy", bv(X1B + c * H, X1B + (c + 1) * H), c1gC, sem=s_ve)
        p.wait("sync", s_ve, s_ve.v)
        sp_x1_last = p.dma("sync", x1_sp[c * 128:(c + 1) * 128, :], c1gC, s_sp)
    sp_x1 = s_sp.v
    ve_x1b = s_ve.v

    x1Tc = bv(X1TC, X1TC + H)
    if DEBUG:
        dbg2 = fv(3 * H, 3 * H + 16)
        d_dbg2 = nc.dram_tensor("d_dbg2", [128, 16], F32, kind="ExternalOutput").ap()
    p.wait("tensor", s_ve, ve_x1b)
    p.wait("tensor", s_gp, gp_small)
    logit_mm_done = 0
    for c in range(NTC):
        p.wait("sync", s_pe, logit_mm_done)
        p.wait("sync", s_sp, sp_x1)
        ldxc = p.dma("sync", xck[:], x1_sp[c * 128:(c + 1) * 128, :], s_ld)
        p.wait("tensor", s_ld, ldxc)
        for g in range(NCH):
            bk = 4 + (g % 2)
            bank_wait(bk)
            p.op("tensor", "transpose", pbank[bk][:, 0:128],
                 xck[:, g * 128:(g + 1) * 128], ident[:], sem=s_pe)
            p.wait("scalar", s_pe, s_pe.v)
            p.wait("scalar", s_pe, logit_mm_done)
            tgt = p.op("scalar", "copy", x1Tc[:, g * 128:(g + 1) * 128],
                       pbank[bk][:, 0:128], sem=s_sc)
            bank_set(bk, s_sc, tgt)
        bank_wait(6)
        p.wait("tensor", s_sc, s_sc.v)
        for g in range(NCH):
            p.op("tensor", "matmul", pbank[6][:, 0:E],
                 x1Tc[:, g * 128:(g + 1) * 128], wgb[:, g * E:(g + 1) * E],
                 start=(g == 0), stop=(g == NCH - 1),
                 sem=(s_pe if g == NCH - 1 else None))
        logit_mm_done = s_pe.v
        p.wait("vector", s_pe, s_pe.v)
        tgt = p.op("vector", "tensor_copy", logit[:, c * E:(c + 1) * E],
                   pbank[6][:, 0:E], sem=s_ve)
        bank_set(6, s_ve, tgt)

        L = logit[:, c * E:(c + 1) * E]
        p.op("vector", "reduce_max", rcol1[:], L, axis=AX, negate=True)
        p.nop("vector")
        p.op("vector", "tensor_scalar", rsc1[:], L, rcol1[:, 0:1], 0.0, ALU.add, ALU.is_ge)
        p.nop("vector")
        p.op("vector", "scalar_tensor_tensor", rsc2[:], rsc1[:], 16384.0, iota8p[:],
             ALU.mult, ALU.subtract)
        p.nop("vector")
        p.op("vector", "reduce_max", rcol2[:], rsc2[:], axis=AX)
        p.nop("vector")
        p.op("vector", "tensor_scalar", mask1[:, c * E:(c + 1) * E], iota8[:],
             rcol2[:, 0:1], 0.0, ALU.add, ALU.is_equal)
        if DEBUG and c == 0:
            p.op("vector", "tensor_copy", dbg2[:, 0:8], rsc1[:], sem=s_ve)
            p.op("vector", "tensor_copy", dbg2[:, 8:9], rcol1[:], sem=s_ve)
            p.op("vector", "tensor_copy", dbg2[:, 9:10], rcol2[:], sem=s_ve)
        p.nop("vector")
        p.op("vector", "scalar_tensor_tensor", rsc3[:], mask1[:, c * E:(c + 1) * E],
             -3e38, L, ALU.mult, ALU.add)
        p.nop("vector")
        p.op("vector", "reduce_max", rcol3[:], rsc3[:], axis=AX, negate=True)
        p.nop("vector")
        p.op("vector", "tensor_scalar", rsc1[:], rsc3[:], rcol3[:, 0:1], 0.0,
             ALU.add, ALU.is_ge)
        p.nop("vector")
        p.op("vector", "scalar_tensor_tensor", rsc2[:], rsc1[:], 16384.0, iota8p[:],
             ALU.mult, ALU.subtract)
        p.nop("vector")
        p.op("vector", "reduce_max", rcol2[:], rsc2[:], axis=AX)
        p.nop("vector")
        p.op("vector", "tensor_scalar", mask2[:, c * E:(c + 1) * E], iota8[:],
             rcol2[:, 0:1], 0.0, ALU.add, ALU.is_equal, sem=s_ve)
        p.wait("scalar", s_ve, s_ve.v)
        p.op("scalar", "activation", rsc3[:], L, AF.Exp, bias=rcol1[:, 0:1], scale=1.0,
             accum_out=rcol3[:, 0:1], sem=s_sc)
        p.wait("vector", s_sc, s_sc.v)
        p.op("vector", "tensor_mul", rsc1[:], rsc3[:], mask1[:, c * E:(c + 1) * E])
        p.nop("vector")
        p.op("vector", "reduce_sum", rcol2[:], rsc1[:], axis=AX)
        p.nop("vector")
        p.op("vector", "tensor_mul", rsc1[:], rsc3[:], mask2[:, c * E:(c + 1) * E])
        p.nop("vector")
        p.op("vector", "reduce_sum", rcol1[:], rsc1[:], axis=AX)
        p.nop("vector")
        p.op("vector", "tensor_add", rsc2[:, 0:1], rcol2[:], rcol1[:])
        p.op("vector", "tensor_scalar", rsc2[:, 1:2], rcol3[:, 0:1], 1e-9, None, ALU.mult)
        p.nop("vector")
        p.op("vector", "tensor_add", rsc2[:, 0:1], rsc2[:, 0:1], rsc2[:, 1:2])
        p.nop("vector")
        p.op("vector", "reciprocal", rsc2[:, 2:3], rsc2[:, 0:1])
        p.nop("vector")
        p.op("vector", "tensor_mul", w1k[:, c:c + 1], rcol2[:], rsc2[:, 2:3])
        p.op("vector", "tensor_mul", w2k[:, c:c + 1], rcol1[:], rsc2[:, 2:3], sem=s_ve)
    ve_route1 = s_ve.v

    p.wait("tensor", s_ve, ve_route1)
    bank_wait(7)
    for c in range(NTC):
        p.op("tensor", "matmul", pbank[7][0:1, 0:E], ones_c[:],
             mask1[:, c * E:(c + 1) * E], start=(c == 0), stop=(c == NTC - 1),
             sem=(s_pe if c == NTC - 1 else None))
    bank_wait(6)
    for c in range(NTC):
        p.op("tensor", "matmul", pbank[6][0:1, 0:E], ones_c[:],
             mask2[:, c * E:(c + 1) * E], start=(c == 0), stop=(c == NTC - 1),
             sem=(s_pe if c == NTC - 1 else None))
    p.wait("vector", s_pe, s_pe.v)
    p.op("vector", "tensor_copy", sm1[0:1, 0:E], pbank[7][0:1, 0:E])
    tgt = p.op("vector", "tensor_copy", sm1[0:1, E:2 * E], pbank[6][0:1, 0:E], sem=s_ve)
    bank_set(6, s_ve, tgt)
    bank_set(7, s_ve, tgt)
    p.wait("sync", s_ve, s_ve.v)
    cc2_pub = p.dma("sync", cc2_in, sm1[0:1, 0:2 * E], s_sp)
    p.wait("gpsimd", s_sp, cc2_pub)
    p.op("gpsimd", "collective_compute", "AllGather", ALU.bypass,
         replica_groups=[list(range(NCORES))],
         ins=[cc2_in.opt()], outs=[cc2_out.opt()], sem=s_cc)
    cc2_done = s_cc.v
    p.wait("gpsimd", s_cc, cc2_done)
    ld_cc2 = p.dma("gpsimd", cc2_sb[:], cc2_out, s_gp)
    p.wait("tensor", s_gp, ld_cc2)
    bank_wait(7)
    p.op("tensor", "matmul", pbank[7][0:1, 0:2 * E], premk[:], cc2_sb[:],
         start=True, stop=True)
    bank_wait(5)
    p.op("tensor", "matmul", pbank[5][0:1, 0:E], ones_c[0:E, 0:1],
         cc2_sb[:, 0:E], start=True, stop=True, sem=s_pe)
    p.wait("vector", s_pe, s_pe.v)
    p.op("vector", "tensor_copy", offs[:], pbank[7][0:1, 0:2 * E])
    tgt = p.op("vector", "tensor_copy", offg[:], pbank[5][0:1, 0:E], sem=s_ve)
    bank_set(5, s_ve, tgt)
    bank_set(7, s_ve, tgt)
    ve_offs = s_ve.v

    def cumsum_pass(msk, possb, keepsb, locol):
        for c in range(NTC):
            bank_wait(5)
            p.wait("tensor", s_ve, s_ve.v)
            p.op("tensor", "matmul", pbank[5][:, 0:E], triu[:],
                 msk[:, c * E:(c + 1) * E], start=True, stop=False)
            p.op("tensor", "matmul", pbank[5][:, 0:E], ones_r1[:], locol,
                 start=False, stop=True, sem=s_pe)
            p.wait("vector", s_pe, s_pe.v)
            tgt_ = p.op("vector", "tensor_mul", rsc1[:], pbank[5][:, 0:E],
                        msk[:, c * E:(c + 1) * E], sem=s_ve)
            bank_set(5, s_ve, tgt_)
            p.nop("vector")
            p.op("vector", "reduce_sum", possb[:, c:c + 1], rsc1[:], axis=AX)
            if keepsb is not None:
                p.nop("vector")
                p.op("vector", "tensor_scalar", keepsb[:, c:c + 1], possb[:, c:c + 1],
                     float(CGLOB), 0.0, ALU.subtract, ALU.is_lt)
            bank_wait(6)
            p.op("tensor", "matmul", pbank[6][0:1, 0:E], ones_c[:],
                 msk[:, c * E:(c + 1) * E], start=True, stop=True, sem=s_pe)
            p.wait("vector", s_pe, s_pe.v)
            tgt_ = p.op("vector", "tensor_add", locol, locol, pbank[6][0:1, 0:E],
                        sem=s_ve)
            bank_set(6, s_ve, tgt_)

    p.wait("vector", s_ve, ve_offs)
    p.op("vector", "tensor_copy", loco[0:1, 0:E], offs[0:1, 0:E], sem=s_ve)
    cumsum_pass(mask1, pos1, keep1, loco[0:1, 0:E])
    p.op("vector", "tensor_add", loco[0:1, E:2 * E], offs[0:1, E:2 * E],
         offg[0:1, 0:E], sem=s_ve)
    cumsum_pass(mask2, pos2, keep2, loco[0:1, E:2 * E])

    for c in range(NTC):
        p.op("vector", "tensor_scalar", kept1[:, c * E:(c + 1) * E],
             mask1[:, c * E:(c + 1) * E], keep1[:, c:c + 1], None, ALU.mult)
        p.op("vector", "tensor_scalar", kept2[:, c * E:(c + 1) * E],
             mask2[:, c * E:(c + 1) * E], keep2[:, c:c + 1], None, ALU.mult)
        p.op("vector", "tensor_mul", w1k[:, c:c + 1], w1k[:, c:c + 1], keep1[:, c:c + 1])
        p.op("vector", "tensor_mul", w2k[:, c:c + 1], w2k[:, c:c + 1], keep2[:, c:c + 1],
             sem=s_ve)

    p.op("vector", "memset", loco[0:1, 0:E], 0.0, sem=s_ve)
    cumsum_pass(kept1, slot1, None, loco[0:1, 0:E])
    p.op("vector", "tensor_copy", loco[0:1, E:2 * E], loco[0:1, 0:E], sem=s_ve)
    cumsum_pass(kept2, slot2, None, loco[0:1, E:2 * E])

    for c in range(NTC):
        for (msk, kp, slotsb, dstsi) in ((mask1, keep1, slot1, dst1i),
                                         (mask2, keep2, slot2, dst2i)):
            p.op("vector", "tensor_mul", rsc1[:], iota8[:], msk[:, c * E:(c + 1) * E])
            p.nop("vector")
            p.op("vector", "reduce_sum", rcol1[:], rsc1[:], axis=AX)
            p.op("vector", "tensor_scalar", rcol2[:], slotsb[:, c:c + 1],
                 float(CL), 0.0, ALU.subtract, ALU.is_lt)
            p.nop("vector")
            p.op("vector", "tensor_mul", rcol2[:], rcol2[:], kp[:, c:c + 1])
            p.op("vector", "tensor_scalar", rcol1[:], rcol1[:], float(CL), None, ALU.mult)
            p.nop("vector")
            p.op("vector", "tensor_tensor", rcol1[:], rcol1[:], slotsb[:, c:c + 1], ALU.add)
            p.nop("vector")
            p.op("vector", "tensor_scalar", rcol1[:], rcol1[:], float(TRASH), None,
                 ALU.subtract)
            p.nop("vector")
            p.op("vector", "tensor_mul", rcol1[:], rcol1[:], rcol2[:])
            p.nop("vector")
            p.op("vector", "tensor_scalar", rcol3[:], rcol1[:], float(TRASH), None, ALU.add)
            p.nop("vector")
            p.op("vector", "tensor_copy", dstsi[:, c:c + 1], rcol3[:], sem=s_ve)
    ve_dests = s_ve.v

    if DEBUG:
        d_rt = nc.dram_tensor("d_rt", [128, 6 * NTC], F32, kind="ExternalOutput").ap()
        d_rti = nc.dram_tensor("d_rti", [128, 2 * NTC], I32, kind="ExternalOutput").ap()
        d_msk = nc.dram_tensor("d_msk", [128, 3 * NTC * E], F32, kind="ExternalOutput").ap()
        p.wait("sync", s_ve, ve_dests)
        p.dma("sync", d_rt[:, 0 * NTC:1 * NTC], pos1[:], s_sp)
        p.dma("sync", d_rt[:, 1 * NTC:2 * NTC], pos2[:], s_sp)
        p.dma("sync", d_rt[:, 2 * NTC:3 * NTC], slot1[:], s_sp)
        p.dma("sync", d_rt[:, 3 * NTC:4 * NTC], slot2[:], s_sp)
        p.dma("sync", d_rt[:, 4 * NTC:5 * NTC], w1k[:], s_sp)
        p.dma("sync", d_rt[:, 5 * NTC:6 * NTC], w2k[:], s_sp)
        p.dma("sync", d_rti[:, 0:NTC], dst1i[:], s_sp)
        p.dma("sync", d_rti[:, NTC:2 * NTC], dst2i[:], s_sp)
        p.dma("sync", d_msk[:, 0:NTC * E], logit[:], s_sp)
        p.dma("sync", d_msk[:, NTC * E:2 * NTC * E], mask1[:], s_sp)
        p.dma("sync", d_msk[:, 2 * NTC * E:3 * NTC * E], mask2[:], s_sp)
        p.dma("sync", d_dbg2, dbg2, s_sp)

    p.wait("gpsimd", s_ve, ve_dests)
    p.wait("gpsimd", s_sp, sp_zero)
    for c in range(NTC):
        for dsts in (dst1i, dst2i):
            p.op("gpsimd", "indirect_dma_start",
                 out=xe_d, out_offset=IndirectOffsetOnAxis(dsts[:, c:c + 1], 0),
                 in_=bv(X1B + c * H, X1B + (c + 1) * H), in_offset=None,
                 sem=s_ix, inc=16)
    ix_scatter = s_ix.v

    if DEBUG:
        d_h1 = [nc.dram_tensor(f"d_h1{i}", [128, 16 * CL], BF16, kind="ExternalOutput").ap()
                for i in range(2)]
        d_yep = nc.dram_tensor("d_yep", [128, E * H], F32, kind="ExternalOutput").ap()
    yev = fv(0, 2 * H)
    ye_part = fv(2 * H, 2 * H + E * H)
    b2e = ARB[0:1, B2O:B2O + H]
    NHF = NFC // 2
    pe_w1_done = {}
    pe_w2_done = {}
    sp_ye = {}
    w1sl_guard = {}
    p.wait("scalar", s_sp, sp_zero)
    for e in range(E):
        p.wait("sync", s_ix, ix_scatter)
        if e >= 1:
            p.wait("sync", s_pe, pe_w1_done[e - 1])
        for gg in range(NCH):
            p.dmat("sync", bv(XEO + gg * CL, XEO + (gg + 1) * CL),
                   xe_d[e * CL:(e + 1) * CL, gg * 128:(gg + 1) * 128], s_ld)
        ld_xeT = s_ld.v

        for hf in range(2):
            if e + hf >= 1:
                p.wait("gpsimd", s_pe, pe_w2_done.get((e, hf - 1),
                                                      pe_w2_done.get((e - 1, 1), 0)))
            if hf == 0:
                p.dma("gpsimd", b2e, b2[e:e + 1, :], s_gp)
                gp_b2e = s_gp.v
            f0 = hf * (FF // 2)

            if e + hf >= 1:
                p.wait("scalar", s_pe, pe_w2_done.get((e, hf - 1),
                                                      pe_w2_done.get((e - 1, 1), 0)))
            for fs in range(4):
                buf = fs % 2
                p.wait("gpsimd", s_pe, w1sl_guard.get(buf, 0))
                p.dma("gpsimd",
                      bv(W1O + buf * NCH * 512, W1O + (buf + 1) * NCH * 512).rearrange(
                          "p (t f) -> p t f", f=512),
                      W1[e, :, f0 + fs * 512: f0 + fs * 512 + 512].rearrange(
                          "(t p) f -> p t f", p=128),
                      s_gp)
                gp_w1 = s_gp.v
                p.wait("tensor", s_gp, gp_w1)
                p.wait("tensor", s_ld, ld_xeT)
                for fc in range(4):
                    fcl = fs * 4 + fc
                    fcg = hf * NHF + fcl
                    for sh in range(2):
                        bk = sh
                        bank_wait(bk)
                        for dt in range(NCH):
                            p.op("tensor", "matmul", pbank[bk][:],
                                 bv(W1O + buf * NCH * 512 + dt * 512 + fc * 128,
                                    W1O + buf * NCH * 512 + dt * 512 + (fc + 1) * 128),
                                 bv(XEO + dt * CL + sh * 512, XEO + dt * CL + (sh + 1) * 512),
                                 start=(dt == 0), stop=(dt == NCH - 1),
                                 sem=(s_pe if dt == NCH - 1 else None))
                        p.wait("scalar", s_pe, s_pe.v)
                        tgt = p.op("scalar", "activation",
                                   bv(H1O + fcl * CL + sh * 512,
                                      H1O + fcl * CL + (sh + 1) * 512),
                                   pbank[bk][:], AF.Gelu,
                                   bias=b1_sb[:, e * NFC + fcg: e * NFC + fcg + 1],
                                   scale=1.0, sem=s_sc)
                        bank_set(bk, s_sc, tgt)
                w1sl_guard[buf] = s_pe.v
            sc_h1_done = s_sc.v
            if DEBUG and e == 0:
                p.wait("sync", s_sc, sc_h1_done)
                p.dma("sync", d_h1[hf], bv(H1O, H1O + 16 * CL), s_sp)

            for piece in range(2):
                q0 = f0 + piece * (FF // 4)
                p.dma("gpsimd",
                      bv(W2O + piece * 8 * H, W2O + (piece + 1) * 8 * H).rearrange(
                          "p (t d) -> p t d", d=H),
                      W2[e, q0:q0 + FF // 4, :].rearrange("(t p) d -> p t d", p=128),
                      s_gp)
            gp_w2 = s_gp.v
            if DEBUG and e == 0 and hf == 1:
                p.wait("sync", s_sc, s_sc.v)
                p.dma("sync", d_yep, ye_part, s_sp)
            p.wait("tensor", s_gp, gp_w2)
            if hf == 1:
                p.wait("tensor", s_gp, gp_b2e)
            p.wait("tensor", s_sc, sc_h1_done)
            for sc8 in range(8):
                ybuf = sc8 % 2
                for dh in range(2):
                    bk = 2 + ((sc8 * 2 + dh) % 4)
                    bank_wait(bk)
                    for fcl in range(NHF):
                        last = (hf == 0 and fcl == NHF - 1)
                        p.op("tensor", "matmul", pbank[bk][:],
                             bv(H1O + fcl * CL + sc8 * 128, H1O + fcl * CL + (sc8 + 1) * 128),
                             bv(W2O + fcl * H + dh * 512, W2O + fcl * H + (dh + 1) * 512),
                             start=(fcl == 0), stop=last,
                             sem=(s_pe if last else None))
                    if hf == 1:
                        p.op("tensor", "matmul", pbank[bk][:],
                             ones_r1b[0:1, 0:128],
                             b2e[0:1, dh * 512:(dh + 1) * 512],
                             start=False, stop=True, sem=s_pe)
                    if hf == 0:
                        p.wait("scalar", s_pe, s_pe.v)
                        tgt = p.op("scalar", "copy",
                                   fv(2 * H + sc8 * H + dh * 512,
                                      2 * H + sc8 * H + (dh + 1) * 512),
                                   pbank[bk][:], sem=s_sc)
                        bank_set(bk, s_sc, tgt)
                    else:
                        p.wait("vector", s_pe, s_pe.v)
                        p.wait("vector", s_sp, sp_ye.get((e, sc8 - 2),
                                                         sp_ye.get((e - 1, 6 + sc8), 0)))
                        tgt = p.op("vector", "tensor_add",
                                   fv(ybuf * H + dh * 512, ybuf * H + (dh + 1) * 512),
                                   pbank[bk][:],
                                   fv(2 * H + sc8 * H + dh * 512,
                                      2 * H + sc8 * H + (dh + 1) * 512), sem=s_ve)
                        bank_set(bk, s_ve, tgt)
                if hf == 1:
                    p.wait("sync", s_ve, s_ve.v)
                    sp_ye[(e, sc8)] = p.dma(
                        "sync", ye_d[e * CL + sc8 * 128: e * CL + (sc8 + 1) * 128, :],
                        fv(ybuf * H, (ybuf + 1) * H), s_sp)
            pe_w2_done[(e, hf)] = s_pe.v
        pe_w1_done[e] = s_pe.v
    sp_ye_all = s_sp.v

    ot_done = {}
    ve_e_last = {}
    for c in range(NTC):
        half = c % 2
        if c == 0:
            p.wait("gpsimd", s_sp, sp_ye_all)
        if c >= 2:
            p.wait("gpsimd", s_ot, ot_done[c - 2])
        p.op("gpsimd", "indirect_dma_start",
             out=fv(half * H, (half + 1) * H), out_offset=None,
             in_=ye_d, in_offset=IndirectOffsetOnAxis(dst1i[:, c:c + 1], 0),
             sem=s_ix, inc=16)
        p.op("gpsimd", "indirect_dma_start",
             out=fv(2 * H + half * H, 2 * H + (half + 1) * H), out_offset=None,
             in_=ye_d, in_offset=IndirectOffsetOnAxis(dst2i[:, c:c + 1], 0),
             sem=s_ix, inc=16)
        ixg = s_ix.v
        if c >= 2:
            p.wait("sync", s_ot, ot_done[c - 2])
        ldx1 = p.dma("sync", fv(4 * H + half * H, 4 * H + (half + 1) * H),
                     x1_sp[c * 128:(c + 1) * 128, :], s_ld)
        p.wait("vector", s_ix, ixg)
        p.wait("vector", s_ld, ldx1)
        yy = fv(6 * H + half * H, 6 * H + (half + 1) * H)
        p.op("vector", "tensor_scalar", yy, fv(half * H, (half + 1) * H),
             w1k[:, c:c + 1], None, ALU.mult)
        p.op("vector", "scalar_tensor_tensor", yy, fv(2 * H + half * H, 2 * H + (half + 1) * H),
             w2k[:, c:c + 1], yy, ALU.mult, ALU.add)
        p.op("vector", "tensor_add", yy, yy, fv(4 * H + half * H, 4 * H + (half + 1) * H))
        p.op("vector", "reduce_sum", rcol1[:], yy, axis=AX, negate=True)
        p.nop("vector")
        p.op("vector", "tensor_scalar", rcol1[:], rcol1[:], 1.0 / H, None, ALU.mult)
        p.nop("vector")
        p.op("vector", "tensor_scalar", yy, yy, rcol1[:, 0:1], None, ALU.add, sem=s_ve)
        p.wait("scalar", s_ve, s_ve.v)
        p.op("scalar", "activation", fv(8 * H, 9 * H), yy, AF.Square,
             accum_out=rcol2[:, 0:1], sem=s_sc)
        p.wait("vector", s_sc, s_sc.v)
        p.op("vector", "tensor_scalar", rcol2[:], rcol2[:], 1.0 / H, 1e-5,
             ALU.mult, ALU.add, sem=s_ve)
        p.wait("scalar", s_ve, s_ve.v)
        p.op("scalar", "activation", rcol3[:], rcol2[:], AF.Ln)
        p.nop("scalar")
        p.op("scalar", "activation", rcol3[:], rcol3[:], AF.Exp, scale=-0.5, sem=s_sc)
        p.wait("vector", s_sc, s_sc.v)
        p.op("vector", "tensor_scalar", yy, yy, rcol3[:, 0:1], None, ALU.mult, sem=s_ve)
        ve_e_last[c] = s_ve.v
        p.wait("sync", s_ve, s_ve.v)
        ot_done[c] = p.dma("sync", out[c * 128:(c + 1) * 128, :], yy, s_ot)
    p.wait("sync", s_ot, s_ot.v)

    with nc.Block() as block:
        block.sync(lambda e: [fn(e) for fn in p.ops["sync"]])
        block.gpsimd(lambda e: [fn(e) for fn in p.ops["gpsimd"]])
        block.vector(lambda e: [fn(e) for fn in p.ops["vector"]])
        block.scalar(lambda e: [fn(e) for fn in p.ops["scalar"]])
        block.tensor(lambda e: [fn(e) for fn in p.ops["tensor"]])
    ctx.close()
    return nc


_BUILT = None


def _get_built():
    global _BUILT
    if _BUILT is None:
        _BUILT = build_nc()
    return _BUILT


def _host_inputs(inputs):
    x = np.ascontiguousarray(np.asarray(inputs["L0_attn_x"], np.float32).reshape(B * T, H))
    tmix = np.ascontiguousarray(np.stack([
        np.asarray(inputs["L0_attn_time_mix_k"], np.float32).reshape(H),
        np.asarray(inputs["L0_attn_time_mix_v"], np.float32).reshape(H),
        np.asarray(inputs["L0_attn_time_mix_r"], np.float32).reshape(H),
    ]))
    common = {
        "Wk": np.ascontiguousarray(np.asarray(inputs["L0_attn_W_k"], np.float32)),
        "Wv": np.ascontiguousarray(np.asarray(inputs["L0_attn_W_v"], np.float32)),
        "Wr": np.ascontiguousarray(np.asarray(inputs["L0_attn_W_r"], np.float32)),
        "Wo": np.ascontiguousarray(np.asarray(inputs["L0_attn_W_o"], np.float32)),
        "tdecay": np.ascontiguousarray(np.asarray(inputs["L0_attn_time_decay"], np.float32)),
        "tfirst": np.ascontiguousarray(np.asarray(inputs["L0_attn_time_first"], np.float32)),
        "tmix": tmix,
        "Wg": np.ascontiguousarray(np.asarray(inputs["L0_ffn_Wg"], np.float32)),
        "W1": np.ascontiguousarray(np.asarray(inputs["L0_ffn_W1"], np.float32)),
        "b1": np.ascontiguousarray(np.asarray(inputs["L0_ffn_b1"], np.float32)),
        "W2": np.ascontiguousarray(np.asarray(inputs["L0_ffn_W2"], np.float32)),
        "b2": np.ascontiguousarray(np.asarray(inputs["L0_ffn_b2"], np.float32)),
    }
    in_maps = []
    for c in range(NCORES):
        t0 = c * TL
        if (t0 % T) == 0:
            xp = np.vstack([np.zeros((1, H), np.float32), x[t0:t0 + TL]])
        else:
            xp = x[t0 - 1:t0 + TL]
        mm = np.zeros((128, NCORES), np.float32)
        ma = np.full((128, NCORES), NEG, np.float32)
        pm = np.zeros((NCORES, 1), np.float32)
        for cp in range(c - (c % 4), c):
            mm[:, cp] = 1.0
            ma[:, cp] = 0.0
        for cp in range(c):
            pm[cp, 0] = 1.0
        m = dict(common)
        m["x_pad"] = np.ascontiguousarray(xp)
        m["cmask_mul"] = mm
        m["cmask_add"] = ma
        m["premask"] = pm
        in_maps.append(m)
    return in_maps


def kernel(**inputs):
    nc = _get_built()
    in_maps = _host_inputs(inputs)
    res = run_bass_kernel_spmd(nc, in_maps, core_ids=list(range(NCORES)))
    outs = [res.results[c]["y"] for c in range(NCORES)]
    return np.concatenate(outs, axis=0).reshape(B, T, H).astype(np.float32)




# version 14
# speedup vs baseline: 1.4622x; 1.0078x over previous
import numpy as np

import concourse.bass as bass
import concourse.mybir as mybir
from concourse.bass_utils import run_bass_kernel_spmd
from concourse.bass_core_rust import IndirectOffsetOnAxis

F32 = mybir.dt.float32
BF16 = mybir.dt.bfloat16
I32 = mybir.dt.int32
AF = mybir.ActivationFunctionType
ALU = mybir.AluOpType
AX = mybir.AxisListType.X

NCORES = 8
B, T, H = 2, 8192, 1024
TL = 2048
NTC = TL // 128
NCH = H // 128
E = 8
FF = 4096
NFC = FF // 128
CGLOB = 8192
CL = 1024
NEG = -1e38
TRASH = E * CL


class SemCtr:
    def __init__(self, handle, name):
        self.h = handle
        self.v = 0
        self.name = name


class Plan:
    def __init__(self):
        self.ops = {k: [] for k in ("sync", "gpsimd", "vector", "scalar", "tensor")}
        self.hist = {}

    def emit(self, eng, fn):
        self.ops[eng].append(fn)

    def wait(self, eng, sem, val=None):
        if val is None:
            val = sem.v
        assert val <= sem.v, f"wait on future value of {sem.name}: {val} > {sem.v}"
        if val > 0:
            self.emit(eng, lambda e, s=sem.h, v=val: e.wait_ge(s, v))

    def dma(self, eng, out, in_, sem, **kw):
        sem.v += 16
        self.emit(eng, lambda e, o=out, i=in_, s=sem.h, kw=kw:
                  e.dma_start(out=o, in_=i, **kw).then_inc(s, 16))
        return sem.v

    def dmat(self, eng, out, in_, sem):
        sem.v += 16
        self.emit(eng, lambda e, o=out, i=in_, s=sem.h:
                  e.dma_start_transpose(o, i).then_inc(s, 16))
        return sem.v

    scalar_nop_fn = None

    def nop(self, eng):
        if eng == "scalar":
            self.emit(eng, self.scalar_nop_fn)
        else:
            self.emit(eng, lambda e: e.engine_nop())

    @staticmethod
    def _foot(a):
        try:
            t = a.tensor.name
            off = a.offset
            dims = list(a.ap)
        except Exception:
            return None
        if not dims:
            return (t, 0, 1, off, off + 1, 1)
        pitch, np_ = dims[0]
        if pitch <= 0:
            pitch = 1
        p0 = off // pitch
        c0 = off % pitch
        span = sum(st * (ct - 1) for st, ct in dims[1:]) + 1
        return (t, p0, p0 + np_, c0, c0 + span, span)

    @staticmethod
    def _ovl(f1, f2):
        return (f1[0] == f2[0] and f1[1] < f2[2] and f2[1] < f1[2]
                and f1[3] < f2[4] and f2[3] < f1[4])

    WINDOW = 8

    def _hazard(self, eng, writes, reads):
        hist = self.hist.setdefault(eng, [])
        need = 0
        for (pw, pr, psem, pval) in hist:
            conflict = False
            for f1 in pw:
                for f2 in writes + reads:
                    if self._ovl(f1, f2) and (f1[5] < 512 or f2[5] < 512):
                        conflict = True
            for f1 in pr:
                for f2 in writes:
                    if self._ovl(f1, f2) and (f1[5] < 512 or f2[5] < 512):
                        conflict = True
            if conflict:
                if psem is not None:
                    need = max(need, pval) if psem is self.def_sem[eng] else need
                    if psem is not self.def_sem[eng]:
                        self.wait(eng, psem, pval)
                else:
                    self.nop(eng)
                    self.nop(eng)
        if need:
            self.wait(eng, self.def_sem[eng], need)

    def op(self, eng, meth, *args, sem=None, inc=1, **kw):
        writes = reads = None
        if eng in ("vector", "scalar"):
            w = [self._foot(args[0])] if args else []
            w += [self._foot(kw["accum_out"])] if kw.get("accum_out") is not None else []
            r = [self._foot(a) for a in args[1:]]
            r += [self._foot(v) for k2, v in kw.items() if k2 != "accum_out"]
            writes = [f for f in w if f is not None]
            reads = [f for f in r if f is not None]
            self._hazard(eng, writes, reads)
            small = any(f[5] < 512 for f in writes)
            if sem is None and small:
                sem = self.def_sem[eng]
                inc = 1
        if sem is not None:
            sem.v += inc
            ret = sem.v
            self.emit(eng, lambda e, a=args, kw=kw, s=sem.h, n=inc:
                      getattr(e, meth)(*a, **kw).then_inc(s, n))
        else:
            ret = None
            self.emit(eng, lambda e, a=args, kw=kw: getattr(e, meth)(*a, **kw))
        if writes is not None:
            hist = self.hist.setdefault(eng, [])
            hist.append((writes, reads, sem, sem.v if sem is not None else None))
            if len(hist) > self.WINDOW:
                hist.pop(0)
        return ret


DEBUG = False
FFN_SKIP = False


def build_nc():
    nc = bass.Bass("TRN2", target_bir_lowering=False)
    SCR = "ExternalOutput" if DEBUG else "Internal"

    x_pad = nc.dram_tensor("x_pad", [TL + 1, H], F32, kind="ExternalInput").ap()
    Wk = nc.dram_tensor("Wk", [H, H], F32, kind="ExternalInput").ap()
    Wv = nc.dram_tensor("Wv", [H, H], F32, kind="ExternalInput").ap()
    Wr = nc.dram_tensor("Wr", [H, H], F32, kind="ExternalInput").ap()
    Wo = nc.dram_tensor("Wo", [H, H], F32, kind="ExternalInput").ap()
    tdecay = nc.dram_tensor("tdecay", [H], F32, kind="ExternalInput").ap()
    tfirst = nc.dram_tensor("tfirst", [H], F32, kind="ExternalInput").ap()
    tmix = nc.dram_tensor("tmix", [3, H], F32, kind="ExternalInput").ap()
    Wg = nc.dram_tensor("Wg", [H, E], F32, kind="ExternalInput").ap()
    W1 = nc.dram_tensor("W1", [E, H, FF], F32, kind="ExternalInput").ap()
    b1 = nc.dram_tensor("b1", [E, FF], F32, kind="ExternalInput").ap()
    W2 = nc.dram_tensor("W2", [E, FF, H], F32, kind="ExternalInput").ap()
    b2 = nc.dram_tensor("b2", [E, H], F32, kind="ExternalInput").ap()
    cmask_mul = nc.dram_tensor("cmask_mul", [128, NCORES], F32, kind="ExternalInput").ap()
    cmask_add = nc.dram_tensor("cmask_add", [128, NCORES], F32, kind="ExternalInput").ap()
    premask = nc.dram_tensor("premask", [NCORES, 1], F32, kind="ExternalInput").ap()
    out = nc.dram_tensor("y", [TL, H], F32, kind="ExternalOutput").ap()

    k_sp = nc.dram_tensor("k_sp", [H, TL], F32, kind=SCR).ap()
    v_sp = nc.dram_tensor("v_sp", [H, TL], F32, kind=SCR).ap()
    r_sp = nc.dram_tensor("r_sp", [H, TL], F32, kind=SCR).ap()
    xw_sp = nc.dram_tensor("xw_sp", [H, TL], F32, kind=SCR).ap()
    ss_sp = nc.dram_tensor("ss_sp", [H, TL], F32, kind=SCR).ap()
    dd_sp = nc.dram_tensor("dd_sp", [H, TL], F32, kind=SCR).ap()
    wb_d = [nc.dram_tensor(f"w{i}b_d", [H, H], BF16).ap() for i in range(4)]
    x1_sp = nc.dram_tensor("x1_sp", [TL, H], F32, kind=SCR).ap()
    xe_d = nc.dram_tensor("xe_d", [E * CL + 1, H], BF16, kind=SCR).ap()
    ye_d = nc.dram_tensor("ye_d", [E * CL + 1, H], F32, kind=SCR).ap()
    cc1_in = nc.dram_tensor("cc1_in", [128, 3 * NCH], F32).ap()
    cc1_out = nc.dram_tensor("cc1_out", [128 * NCORES, 3 * NCH], F32).ap()
    cc2_in = nc.dram_tensor("cc2_in", [1, 2 * E], F32).ap()
    cc2_out = nc.dram_tensor("cc2_out", [NCORES, 2 * E], F32).ap()

    from contextlib import ExitStack
    ctx = ExitStack()

    def sb(name, shape, dt=F32):
        return ctx.enter_context(nc.sbuf_tensor(name, shape, dt))

    def mksem(name):
        return SemCtr(ctx.enter_context(nc.semaphore(name)), name)

    NF = 22537
    NBQ = 50176
    ARF = sb("ARF", [128, NF])
    ARB = sb("ARB", [128, NBQ], BF16)

    def fv(a, b):
        return ARF[:, a:b]

    def bv(a, b):
        return ARB[:, a:b]

    def xTv(g, a, b):
        base = g * (TL + 1)
        return ARF[:, base + a: base + b]
    SPL0 = 16392
    SKA, SKB, SVV = 0, 2048, 4096
    SA_, SL_, SG_ = 6144, 8192, 10240
    SM_, SS_, SD_ = 12288, 14337, 20488
    KA, KB, XA, XB = 0, 2048, 4096, 6144
    SA2, SB2, DA2 = 8192, 10240, 12288
    CVV, CRR, CX3, CP_ = 14336, 16384, 18432, 20480

    WT = [0, 8192, 16384, 24576]
    XMT = 32768
    RWO = 0
    X1B = 32768
    X1TC = 16384
    H1O = 0
    W2O = 16384
    W1O = 32768
    XEO = 40960
    B2O = 49152

    iota0 = sb("iota0", [128, TL + 1])
    iota8 = sb("iota8", [128, E])
    iota8p = sb("iota8p", [128, E])
    ident = sb("ident", [128, 128])
    identb = sb("identb", [128, 128], BF16)
    triu = sb("triu", [128, 128])
    ones_c = sb("ones_c", [128, 1])
    ones_r1 = sb("ones_r1", [1, 128])
    ones_r1b = sb("ones_r1b", [1, 128], BF16)
    w_sb = sb("w_sb", [128, NCH])
    negw_sb = sb("negw_sb", [128, NCH])
    w2048_sb = sb("w2048_sb", [128, NCH])
    u_sb = sb("u_sb", [128, NCH])
    tm_sb = sb("tm_sb", [128, 3 * NCH])
    tm1_sb = sb("tm1_sb", [128, 3 * NCH])
    cmaskm = sb("cmaskm", [128, NCORES])
    cmaska = sb("cmaska", [128, NCORES])
    premk = sb("premk", [NCORES, 1])
    wgb = sb("wgb", [128, NCH * E], BF16)
    b1_sb = sb("b1_sb", [128, E * NFC])
    agg = sb("agg", [128, 3 * NCH])
    aggT = sb("aggT", [128, 3 * NCH])
    carry = sb("carry", [128, 3 * NCH])
    sm1 = sb("sm1", [128, 3 * NCH])
    sm2 = sb("sm2", [128, 3 * NCH])
    sm3 = sb("sm3", [128, 3 * NCH])
    cc2_sb = sb("cc2_sb", [NCORES, 2 * E])
    offs = sb("offs", [1, 2 * E])
    offg = sb("offg", [1, E])
    loco = sb("loco", [1, 2 * E])
    logit = sb("logit", [128, NTC * E])
    mask1 = sb("mask1", [128, NTC * E])
    mask2 = sb("mask2", [128, NTC * E])
    kept1 = sb("kept1", [128, NTC * E])
    kept2 = sb("kept2", [128, NTC * E])
    pos1 = sb("pos1", [128, NTC])
    pos2 = sb("pos2", [128, NTC])
    keep1 = sb("keep1", [128, NTC])
    keep2 = sb("keep2", [128, NTC])
    w1k = sb("w1k", [128, NTC])
    w2k = sb("w2k", [128, NTC])
    slot1 = sb("slot1", [128, NTC])
    slot2 = sb("slot2", [128, NTC])
    dst1i = sb("dst1i", [128, NTC], I32)
    dst2i = sb("dst2i", [128, NTC], I32)
    rsc1 = sb("rsc1", [128, E])
    rsc2 = sb("rsc2", [128, E])
    rsc3 = sb("rsc3", [128, E])
    rcol1 = sb("rcol1", [128, 1])
    rcol2 = sb("rcol2", [128, 1])
    rcol3 = sb("rcol3", [128, 1])
    xck = sb("xck", [128, H])
    nopc = sb("nopc", [128, 1])
    cnt_i32 = sb("cnt_i32", [1, E], I32)

    def psumt(name, shape):
        return ctx.enter_context(nc.psum_tensor(name, shape, F32))

    pbank = [psumt(f"pb{i}", [128, 512]) for i in range(8)]

    s_ld = mksem("s_ld")
    s_gp = mksem("s_gp")
    s_gc = mksem("s_gc")
    s_sp = mksem("s_sp")
    s_pe = mksem("s_pe")
    s_ve = mksem("s_ve")
    s_sc = mksem("s_sc")
    s_cc = mksem("s_cc")
    s_ix = mksem("s_ix")
    s_ot = mksem("s_ot")

    p = Plan()
    p.scalar_nop_fn = lambda e: e.copy(nopc[:], nopc[:])
    p.def_sem = {"vector": s_ve, "scalar": s_sc}
    bankg = {}

    def bank_wait(i):
        if i in bankg:
            sem, val = bankg[i]
            p.wait("tensor", sem, val)

    def bank_set(i, sem, val):
        bankg[i] = (sem, val)

    p.op("gpsimd", "iota", iota0[:], [[1, TL + 1]], base=-1, channel_multiplier=0,
         allow_small_or_imprecise_dtypes=True)
    p.op("gpsimd", "iota", iota8[:], [[1, E]], channel_multiplier=0,
         allow_small_or_imprecise_dtypes=True)
    p.op("gpsimd", "memset", ident[:], 0.0)
    p.op("gpsimd", "affine_select", out=ident[:], in_=ident[:],
         compare_op=ALU.not_equal, fill=1.0, base=0,
         pattern=[[-1, 128]], channel_multiplier=1)
    p.op("gpsimd", "memset", identb[:], 0.0)
    p.op("gpsimd", "affine_select", out=identb[:], in_=identb[:],
         compare_op=ALU.not_equal, fill=1.0, base=0,
         pattern=[[-1, 128]], channel_multiplier=1)
    p.op("gpsimd", "memset", triu[:], 0.0)
    p.op("gpsimd", "affine_select", out=triu[:], in_=triu[:],
         compare_op=ALU.is_ge, fill=1.0, base=0,
         pattern=[[-1, 128]], channel_multiplier=1)
    p.op("gpsimd", "memset", ones_c[:], 1.0)
    p.op("gpsimd", "memset", ones_r1[:], 1.0)
    p.op("gpsimd", "memset", ones_r1b[:], 1.0, sem=s_gc)
    gc_consts = s_gc.v

    iotav = iota0[:, 1:TL + 1]
    iotam1 = iota0[:, 0:TL]

    p.dma("sync", w_sb[:], tdecay.rearrange("(g p) -> p g", p=128), s_ld,
          allow_slow_non_contiguous=True)
    p.dma("sync", u_sb[:], tfirst.rearrange("(g p) -> p g", p=128), s_ld,
          allow_slow_non_contiguous=True)
    p.dma("sync", tm_sb.ap().rearrange("p (m g) -> p m g", g=NCH),
          tmix.rearrange("m (g p) -> p m g", p=128), s_ld,
          allow_slow_non_contiguous=True)
    p.dma("sync", cmaskm[:], cmask_mul, s_ld)
    p.dma("sync", cmaska[:], cmask_add, s_ld)
    p.dma("sync", premk[:], premask, s_ld)
    p.dma("sync", b1_sb.ap().rearrange("p (e c) -> p e c", c=NFC),
          b1.rearrange("e (c p) -> p e c", p=128), s_ld,
          allow_slow_non_contiguous=True)
    ld_params = s_ld.v

    p.wait("vector", s_ld, ld_params)
    p.wait("vector", s_gc, gc_consts)
    p.op("vector", "tensor_scalar", negw_sb[:], w_sb[:], -1.0, None, ALU.mult)
    p.op("vector", "tensor_scalar", w2048_sb[:], w_sb[:], float(TL), None, ALU.mult)
    p.op("vector", "tensor_scalar", tm1_sb[:], tm_sb[:], -1.0, 1.0, ALU.mult, ALU.add)
    p.op("vector", "tensor_scalar", iota8p[:], iota8[:], 16384.0, None, ALU.add,
         sem=s_ve)
    ve_params = s_ve.v

    for w_src, w_dst in ((Wk, wb_d[0]), (Wv, wb_d[1]), (Wr, wb_d[2]), (Wo, wb_d[3])):
        p.dma("gpsimd", w_dst, w_src, s_gp)
    gp_wcast = s_gp.v
    p.dma("gpsimd", wgb.ap().rearrange("p (g e) -> p g e", e=E),
          Wg.rearrange("(g p) e -> p g e", p=128), s_gp,
          allow_slow_non_contiguous=True)
    gp_small = s_gp.v

    tp_done = 0
    for c in range(NTC):
        if c > 0:
            p.wait("sync", s_pe, tp_done)
        ldx = p.dma("sync", xck[:], x_pad[c * 128:(c + 1) * 128, :], s_ld)
        p.wait("tensor", s_ld, ldx)
        if c == 0:
            p.wait("tensor", s_gc, gc_consts)
        for g in range(NCH):
            bk = g % 4
            bank_wait(bk)
            p.op("tensor", "transpose", pbank[bk][:, 0:128],
                 xck[:, g * 128:(g + 1) * 128], ident[:], sem=s_pe)
            p.wait("scalar", s_pe, s_pe.v)
            tgt = p.op("scalar", "copy", xTv(g, c * 128, (c + 1) * 128),
                       pbank[bk][:, 0:128], sem=s_sc)
            bank_set(bk, s_sc, tgt)
        tp_done = s_pe.v
    ld_xT_last = p.dma("sync",
                       ARF.ap()[:, 0:NCH * (TL + 1)].rearrange(
                           "p (g t) -> p g t", g=NCH)[:, :, TL:TL + 1],
                       x_pad[TL:TL + 1, :].rearrange("o (g p) -> p g o", p=128), s_ld,
                       allow_slow_non_contiguous=True)
    sc_xT = s_sc.v

    p.wait("sync", s_gp, gp_wcast)
    for i in range(4):
        for g in range(NCH):
            p.dmat("sync", bv(WT[i] + g * H, WT[i] + (g + 1) * H),
                   wb_d[i][:, g * 128:(g + 1) * 128], s_ld)
    ld_wT = s_ld.v

    sp_zero = s_sp.v

    MIX_SP = {0: k_sp, 1: v_sp, 2: r_sp}
    spl = fv(SPL0, SPL0 + 4096)
    atmp = fv(20488, 20488 + TL)

    kbuf = [fv(SKA, SKA + TL), fv(SKB, SKB + TL)]
    svv = fv(SVV, SVV + TL)
    sa = fv(SA_, SA_ + TL)
    sl = fv(SL_, SL_ + TL)
    sg = fv(SG_, SG_ + TL)
    sminc = ARF[:, SM_:SM_ + TL + 1]
    ssinc = ARF[:, SS_:SS_ + TL + 1]
    sdinc = ARF[:, SD_:SD_ + TL + 1]
    sp_proj = {}
    mix2_sems = {}
    ld_k = {}
    ld_kv = {}
    sa_done = {}
    gv_done = {}
    scans_done = {}

    def scan_loads(g):
        p.wait("gpsimd", s_sp, sp_proj[0 * NCH + g])
        p.wait("gpsimd", s_sp, sp_proj[1 * NCH + g])
        if g >= 2:
            p.wait("gpsimd", s_ve, sa_done[g - 2])
        else:
            p.wait("gpsimd", s_ve, mix2_sems[min(g + 1, NCH - 1)])
        p.dma("gpsimd", kbuf[g % 2], k_sp[g * 128:(g + 1) * 128, :], s_gp)
        ld_k[g] = s_gp.v
        if g >= 1:
            p.wait("gpsimd", s_ve, gv_done[g - 1])
        else:
            p.wait("gpsimd", s_ve, mix2_sems[2])
        ld_kv[g] = p.dma("gpsimd", svv, v_sp[g * 128:(g + 1) * 128, :], s_gp)

    def scan_tile(g):
        kk_ = kbuf[g % 2]
        p.wait("vector", s_gp, ld_k[g])
        p.op("vector", "scalar_tensor_tensor", sa, iotav,
             negw_sb[:, g:g + 1], kk_, ALU.mult, ALU.add, sem=s_ve)
        sa_done[g] = s_ve.v
        p.op("vector", "memset", sminc[:, 0:1], NEG)
        p.op("vector", "tensor_tensor_scan", sminc[:, 1:TL + 1], sa, sa,
             sminc[:, 0:1], ALU.max, ALU.max)
        p.op("vector", "tensor_sub", sl, sminc[:, 0:TL], sminc[:, 1:TL + 1])
        p.op("vector", "tensor_sub", sa, sa, sminc[:, 1:TL + 1], sem=s_ve)
        ve_gin = s_ve.v
        p.wait("scalar", s_ve, ve_gin)
        p.op("scalar", "activation", sl, sl, AF.Exp)
        p.op("scalar", "activation", sa, sa, AF.Exp, sem=s_sc)
        sc_exp = s_sc.v
        p.wait("vector", s_sc, sc_exp)
        p.wait("vector", s_gp, ld_kv[g])
        p.op("vector", "tensor_mul", sg, sa, svv, sem=s_ve)
        gv_done[g] = s_ve.v
        if g + 1 < NCH:
            scan_loads(g + 1)
        p.op("vector", "memset", ssinc[:, 0:1], 0.0)
        p.op("vector", "memset", sdinc[:, 0:1], 0.0)
        p.op("vector", "tensor_tensor_scan", ssinc[:, 1:TL + 1], sl, sg,
             ssinc[:, 0:1], ALU.mult, ALU.add)
        p.op("vector", "tensor_tensor_scan", sdinc[:, 1:TL + 1], sl, sa,
             sdinc[:, 0:1], ALU.mult, ALU.add)
        p.op("vector", "scalar_tensor_tensor", sl, iotam1,
             w_sb[:, g:g + 1], sminc[:, 0:TL], ALU.mult, ALU.add)
        p.op("vector", "tensor_scalar", agg[:, g:g + 1], sminc[:, TL:TL + 1],
             w2048_sb[:, g:g + 1], w_sb[:, g:g + 1], ALU.add, ALU.subtract)
        p.op("vector", "tensor_copy", agg[:, NCH + g:NCH + g + 1],
             ssinc[:, TL:TL + 1])
        p.op("vector", "tensor_copy", agg[:, 2 * NCH + g:2 * NCH + g + 1],
             sdinc[:, TL:TL + 1], sem=s_ve)
        scans_done[g] = s_ve.v
        p.wait("sync", s_ve, scans_done[g])
        p.dma("sync", xw_sp[g * 128:(g + 1) * 128, :], sl, s_sp)
        p.dma("sync", dd_sp[g * 128:(g + 1) * 128, :], sdinc[:, 0:TL], s_sp)
        p.wait("gpsimd", s_ve, scans_done[g])
        p.dma("gpsimd", ss_sp[g * 128:(g + 1) * 128, :], ssinc[:, 0:TL], s_gp)

    pe_proj_done = 0
    for m in range(3):
        p.wait("vector", s_sc, sc_xT)
        p.wait("vector", s_ld, ld_xT_last)
        p.wait("vector", s_ve, ve_params)
        if m > 0:
            p.wait("vector", s_pe, pe_proj_done)
        for g in range(NCH):
            xr = xTv(g, 1, 1 + TL)
            xp = xTv(g, 0, TL)
            p.op("vector", "tensor_scalar", atmp, xr,
                 tm_sb[:, m * NCH + g: m * NCH + g + 1], None, ALU.mult)
            p.op("vector", "scalar_tensor_tensor", bv(XMT + g * TL, XMT + (g + 1) * TL),
                 xp, tm1_sb[:, m * NCH + g: m * NCH + g + 1], atmp, ALU.mult, ALU.add,
                 sem=s_ve)
            if m == 2:
                mix2_sems[g] = s_ve.v
        ve_mix = s_ve.v

        if m == 2:
            scan_loads(0)
        p.wait("tensor", s_ld, ld_wT)
        p.wait("tensor", s_ve, ve_mix)
        for oc in range(NCH):
            buf = oc % 2
            prev = sp_proj.get(m * NCH + oc - 2, 0)
            for tq in range(4):
                bk = 4 + (tq % 2)
                bank_wait(bk)
                for g in range(NCH):
                    p.op("tensor", "matmul", pbank[bk][:],
                         bv(WT[m] + g * H + oc * 128, WT[m] + g * H + (oc + 1) * 128),
                         bv(XMT + g * TL + tq * 512, XMT + g * TL + (tq + 1) * 512),
                         start=(g == 0), stop=(g == NCH - 1),
                         sem=(s_pe if g == NCH - 1 else None))
                p.wait("scalar", s_pe, s_pe.v)
                p.wait("scalar", s_sp, prev)
                func = AF.Sigmoid if m == 2 else AF.Copy
                tgt = p.op("scalar", "activation",
                           fv(SPL0 + buf * TL + tq * 512, SPL0 + buf * TL + (tq + 1) * 512),
                           pbank[bk][:], func, sem=s_sc)
                bank_set(bk, s_sc, tgt)
            p.wait("sync", s_sc, s_sc.v)
            sp_proj[m * NCH + oc] = p.dma(
                "sync", MIX_SP[m][oc * 128:(oc + 1) * 128, :],
                fv(SPL0 + buf * TL, SPL0 + (buf + 1) * TL), s_sp)
            if m == 2:
                scan_tile(oc)
        pe_proj_done = s_pe.v
    sp_proj_all = s_sp.v
    gp_scan_all = s_gp.v
    ve_agg = scans_done[NCH - 1]

    p.wait("gpsimd", s_pe, pe_proj_done)
    p.op("gpsimd", "memset", bv(XMT, XMT + H), 0.0, sem=s_gc)
    p.wait("gpsimd", s_gc, s_gc.v)
    sp_zero_ye = p.dma("gpsimd", ye_d[TRASH:TRASH + 1, :], ARB[0:1, XMT:XMT + H],
                       s_gp)

    if DEBUG:
        dbgv = fv(20496, 20512)
        d_dbgv = nc.dram_tensor("d_dbgv", [128, 16], F32, kind="ExternalOutput").ap()

    p.wait("sync", s_ve, ve_agg)
    cc_pub = p.dma("sync", cc1_in, agg[:], s_sp)
    p.wait("gpsimd", s_sp, cc_pub)
    p.op("gpsimd", "collective_compute", "AllGather", ALU.bypass,
         replica_groups=[list(range(NCORES))],
         ins=[cc1_in.opt()], outs=[cc1_out.opt()], sem=s_cc)
    cc1_done = s_cc.v
    p.op("vector", "memset", carry[:, 0:NCH], NEG)
    p.op("vector", "memset", carry[:, NCH:3 * NCH], 0.0, sem=s_ve)
    p.wait("gpsimd", s_cc, cc1_done)
    for cp in range(NCORES):
        if cp > 0:
            p.wait("gpsimd", s_ve, fold_done)
        ldt = p.dma("gpsimd", aggT[:], cc1_out[cp * 128:(cp + 1) * 128, :], s_gp)
        p.wait("vector", s_gp, ldt)
        p.op("vector", "tensor_scalar", sm1[:, 0:NCH], aggT[:, 0:NCH],
             cmaskm[:, cp:cp + 1], cmaska[:, cp:cp + 1], ALU.mult, ALU.add)
        p.op("vector", "tensor_scalar", sm1[:, NCH:3 * NCH], aggT[:, NCH:3 * NCH],
             cmaskm[:, cp:cp + 1], None, ALU.mult)
        p.op("vector", "tensor_scalar", sm2[:, 0:NCH], w2048_sb[:],
             cmaskm[:, cp:cp + 1], None, ALU.mult)
        p.nop("vector")
        p.op("vector", "tensor_tensor", sm2[:, 0:NCH], carry[:, 0:NCH],
             sm2[:, 0:NCH], ALU.add)
        p.nop("vector")
        p.op("vector", "tensor_tensor", sm3[:, 0:NCH], sm2[:, 0:NCH], sm1[:, 0:NCH],
             ALU.max)
        p.nop("vector")
        p.op("vector", "tensor_sub", sm2[:, NCH:2 * NCH], sm2[:, 0:NCH], sm3[:, 0:NCH])
        p.op("vector", "tensor_sub", sm2[:, 2 * NCH:3 * NCH], sm1[:, 0:NCH],
             sm3[:, 0:NCH], sem=s_ve)
        p.wait("scalar", s_ve, s_ve.v)
        p.op("scalar", "activation", sm2[:, NCH:3 * NCH], sm2[:, NCH:3 * NCH], AF.Exp,
             sem=s_sc)
        p.wait("vector", s_sc, s_sc.v)
        p.op("vector", "tensor_mul", sm3[:, NCH:2 * NCH], carry[:, NCH:2 * NCH],
             sm2[:, NCH:2 * NCH])
        p.op("vector", "tensor_mul", sm3[:, 2 * NCH:3 * NCH], carry[:, 2 * NCH:3 * NCH],
             sm2[:, NCH:2 * NCH])
        p.op("vector", "tensor_mul", sm1[:, NCH:2 * NCH], sm1[:, NCH:2 * NCH],
             sm2[:, 2 * NCH:3 * NCH])
        p.op("vector", "tensor_mul", sm1[:, 2 * NCH:3 * NCH], sm1[:, 2 * NCH:3 * NCH],
             sm2[:, 2 * NCH:3 * NCH])
        p.nop("vector")
        p.op("vector", "tensor_add", carry[:, NCH:3 * NCH], sm3[:, NCH:3 * NCH],
             sm1[:, NCH:3 * NCH])
        p.op("vector", "tensor_copy", carry[:, 0:NCH], sm3[:, 0:NCH], sem=s_ve)
        fold_done = s_ve.v
        if DEBUG and cp == 1:
            p.op("vector", "tensor_copy", dbgv[:, 8:9], sm1[:, 0:1], sem=s_ve)
            p.op("vector", "tensor_copy", dbgv[:, 9:10], sm2[:, 0:1], sem=s_ve)
            p.op("vector", "tensor_copy", dbgv[:, 10:11], sm3[:, 0:1], sem=s_ve)
            p.op("vector", "tensor_copy", dbgv[:, 11:12], sm2[:, NCH:NCH + 1], sem=s_ve)
            p.op("vector", "tensor_copy", dbgv[:, 12:13], sm2[:, 2 * NCH:2 * NCH + 1], sem=s_ve)
            p.op("vector", "tensor_copy", dbgv[:, 13:14], cmaskm[:, 1:2], sem=s_ve)
            p.op("vector", "tensor_copy", dbgv[:, 14:15], cmaska[:, 1:2], sem=s_ve)
            p.op("vector", "tensor_copy", dbgv[:, 15:16], carry[:, 0:1], sem=s_ve)
    ve_carry = s_ve.v

    ckk = [fv(KA, KA + TL), fv(KB, KB + TL)]
    cxw = [fv(XA, XA + TL), fv(XB, XB + TL)]
    css = [fv(SA2, SA2 + TL), fv(SB2, SB2 + TL)]
    cdd_ = fv(DA2, DA2 + TL)
    cvv = fv(CVV, CVV + TL)
    crr = fv(CRR, CRR + TL)
    cx3 = fv(CX3, CX3 + TL)
    cpb = fv(CP_, CP_ + TL)

    p.wait("sync", s_ve, ve_agg)
    p.wait("sync", s_sp, sp_proj_all)
    p.wait("sync", s_gp, gp_scan_all)
    p.wait("gpsimd", s_ve, ve_agg)
    p.wait("gpsimd", s_sp, sp_proj_all)
    p.wait("vector", s_sp, sp_proj_all)
    p.wait("vector", s_gp, gp_scan_all)
    cld = {}
    cld[("k", 0)] = p.dma("sync", ckk[0], k_sp[0:128, :], s_ld)
    cld[("v", 0)] = p.dma("sync", cvv, v_sp[0:128, :], s_ld)
    cld[("r", 0)] = p.dma("sync", crr, r_sp[0:128, :], s_ld)
    cld[("k", 1)] = p.dma("sync", ckk[1], k_sp[128:256, :], s_ld)
    cld[("d", 0)] = p.dma("gpsimd", cdd_, dd_sp[0:128, :], s_gp)
    for g0 in range(2):
        cld[("x", g0)] = p.dma("gpsimd", cxw[g0],
                               xw_sp[g0 * 128:(g0 + 1) * 128, :], s_gp)
        cld[("s", g0)] = p.dma("gpsimd", css[g0],
                               ss_sp[g0 * 128:(g0 + 1) * 128, :], s_gp)

    ve_rw_all = 0
    for g in range(NCH):
        kk_ = ckk[g % 2]
        xw_ = cxw[g % 2]
        ss_ = css[g % 2]
        dd_ = cdd_
        if g == 0:
            p.wait("vector", s_ve, ve_carry)
        p.wait("vector", s_gp, cld[("x", g)])
        p.op("vector", "tensor_scalar", cx3, iotav, w_sb[:, g:g + 1],
             carry[:, g:g + 1], ALU.mult, ALU.add)
        p.wait("vector", s_ld, cld[("k", g)])
        p.op("vector", "scalar_tensor_tensor", cpb, kk_, u_sb[:, g:g + 1],
             xw_, ALU.add, ALU.max)
        p.op("vector", "tensor_tensor", cpb, cpb, cx3, ALU.max)
        p.op("vector", "tensor_sub", xw_, xw_, cpb, sem=s_ve)
        v4 = s_ve.v
        p.op("vector", "tensor_sub", kk_, kk_, cpb, sem=s_ve)
        v5 = s_ve.v
        p.op("vector", "tensor_sub", cx3, cx3, cpb, sem=s_ve)
        v6 = s_ve.v
        p.wait("scalar", s_ve, v4)
        p.op("scalar", "activation", xw_, xw_, AF.Exp, sem=s_sc)
        e1d = s_sc.v
        p.wait("scalar", s_ve, v5)
        p.op("scalar", "activation", kk_, kk_, AF.Exp, bias=u_sb[:, g:g + 1],
             scale=1.0, sem=s_sc)
        e2d = s_sc.v
        p.wait("scalar", s_ve, v6)
        p.op("scalar", "activation", cx3, cx3, AF.Exp, sem=s_sc)
        e3d = s_sc.v
        p.wait("vector", s_sc, e1d)
        p.wait("vector", s_gp, cld[("s", g)])
        p.op("vector", "tensor_mul", ss_, ss_, xw_)
        p.wait("vector", s_gp, cld[("d", g)])
        p.op("vector", "tensor_mul", dd_, dd_, xw_, sem=s_ve)
        v11 = s_ve.v
        p.wait("vector", s_sc, e2d)
        p.wait("vector", s_ld, cld[("v", g)])
        p.op("vector", "tensor_mul", cvv, cvv, kk_)
        p.op("vector", "scalar_tensor_tensor", dd_, kk_, 1e-9, dd_,
             ALU.add, ALU.add, sem=s_ve)
        v12 = s_ve.v
        p.op("vector", "tensor_add", ss_, ss_, cvv, sem=s_ve)
        v9 = s_ve.v
        p.wait("vector", s_sc, e3d)
        p.op("vector", "scalar_tensor_tensor", dd_, cx3,
             carry[:, 2 * NCH + g:2 * NCH + g + 1], dd_, ALU.mult, ALU.add,
             sem=s_ve)
        v13 = s_ve.v
        p.op("vector", "scalar_tensor_tensor", ss_, cx3,
             carry[:, NCH + g:NCH + g + 1], ss_, ALU.mult, ALU.add)
        p.wait("scalar", s_ve, v13)
        p.op("scalar", "activation", dd_, dd_, AF.Ln)
        p.op("scalar", "activation", dd_, dd_, AF.Exp, scale=-1.0, sem=s_sc)
        recd = s_sc.v
        p.wait("vector", s_sc, recd)
        p.op("vector", "tensor_mul", ss_, ss_, dd_, sem=s_ve)
        v14 = s_ve.v
        p.wait("vector", s_ld, cld[("r", g)])
        p.op("vector", "tensor_mul", bv(RWO + g * TL, RWO + (g + 1) * TL),
             ss_, crr, sem=s_ve)
        ve_rw_all = s_ve.v

        if g + 1 < NCH:
            p.wait("sync", s_ve, v9)
            cld[("v", g + 1)] = p.dma("sync", cvv,
                                      v_sp[(g + 1) * 128:(g + 2) * 128, :], s_ld)
            p.wait("sync", s_ve, ve_rw_all)
            cld[("r", g + 1)] = p.dma("sync", crr,
                                      r_sp[(g + 1) * 128:(g + 2) * 128, :], s_ld)
            p.wait("gpsimd", s_ve, v14)
            cld[("d", g + 1)] = p.dma("gpsimd", cdd_,
                                      dd_sp[(g + 1) * 128:(g + 2) * 128, :], s_gp)
        if g + 2 < NCH:
            p.wait("sync", s_ve, v12)
            cld[("k", g + 2)] = p.dma("sync", ckk[g % 2],
                                      k_sp[(g + 2) * 128:(g + 3) * 128, :], s_ld)
            p.wait("gpsimd", s_ve, v11)
            cld[("x", g + 2)] = p.dma("gpsimd", cxw[g % 2],
                                      xw_sp[(g + 2) * 128:(g + 3) * 128, :], s_gp)
            p.wait("gpsimd", s_ve, ve_rw_all)
            cld[("s", g + 2)] = p.dma("gpsimd", css[g % 2],
                                      ss_sp[(g + 2) * 128:(g + 3) * 128, :], s_gp)

    if DEBUG:
        d_rw = nc.dram_tensor("d_rw", [128, NCH * TL], BF16, kind="ExternalOutput").ap()
        d_carry = nc.dram_tensor("d_carry", [128, 3 * NCH], F32, kind="ExternalOutput").ap()
        d_agg = nc.dram_tensor("d_agg", [128, 3 * NCH], F32, kind="ExternalOutput").ap()
        d_aggall = nc.dram_tensor("d_aggall", [128 * NCORES, 3 * NCH], F32, kind="ExternalOutput").ap()
        p.wait("sync", s_ve, ve_rw_all)
        p.dma("sync", d_rw, bv(RWO, RWO + NCH * TL), s_sp)
        p.dma("sync", d_carry, carry[:], s_sp)
        p.dma("sync", d_agg, agg[:], s_sp)
        p.dma("sync", d_dbgv, dbgv[:], s_sp)
        p.dma("sync", d_aggall, cc1_out, s_sp)

    c1gC = fv(0, H)
    youtC = fv(H, 2 * H)
    x1Tc = bv(X1TC, X1TC + H)
    if DEBUG:
        d_y1 = nc.dram_tensor("d_y1", [TL, H], F32, kind="ExternalOutput").ap()
        dbg2 = fv(3 * H, 3 * H + 16)
        d_dbg2 = nc.dram_tensor("d_dbg2", [128, 16], F32, kind="ExternalOutput").ap()
    p.wait("tensor", s_ve, ve_rw_all)
    p.wait("tensor", s_ld, ld_wT)
    p.wait("tensor", s_gp, gp_small)
    p.wait("vector", s_gp, sp_zero_ye)
    sp_x1_last = 0
    logit_mm_done = 0
    tpc_prev = 0
    for c in range(NTC):
        ldx = p.dma("sync", xck[:], x_pad[1 + c * 128: 1 + (c + 1) * 128, :], s_ld)
        for oh in range(2):
            bk = oh
            bank_wait(bk)
            for g in range(NCH):
                p.op("tensor", "matmul", pbank[bk][:],
                     bv(RWO + g * TL + c * 128, RWO + g * TL + (c + 1) * 128),
                     bv(WT[3] + g * H + oh * 512, WT[3] + g * H + (oh + 1) * 512),
                     start=(g == 0), stop=(g == NCH - 1),
                     sem=(s_pe if g == NCH - 1 else None))
            p.wait("vector", s_pe, s_pe.v)
            p.wait("vector", s_ld, ldx)
            p.wait("vector", s_sp, sp_x1_last)
            tgt = p.op("vector", "tensor_add", fv(H + oh * 512, H + (oh + 1) * 512),
                       pbank[bk][:], xck[:, oh * 512:(oh + 1) * 512], sem=s_ve)
            bank_set(bk, s_ve, tgt)
        if DEBUG:
            p.wait("sync", s_ve, s_ve.v)
            p.dma("sync", d_y1[c * 128:(c + 1) * 128, :], youtC, s_sp)
        y = youtC
        p.op("vector", "reduce_sum", rcol1[:], y, axis=AX, negate=True)
        p.nop("vector")
        p.op("vector", "tensor_scalar", rcol1[:], rcol1[:], 1.0 / H, None, ALU.mult)
        p.nop("vector")
        p.wait("vector", s_pe, tpc_prev)
        p.op("vector", "tensor_scalar", c1gC, y, rcol1[:, 0:1], None, ALU.add, sem=s_ve)
        p.wait("scalar", s_ve, s_ve.v)
        p.op("scalar", "activation", fv(2 * H, 3 * H), c1gC, AF.Square,
             accum_out=rcol2[:, 0:1], sem=s_sc)
        p.wait("vector", s_sc, s_sc.v)
        p.op("vector", "tensor_scalar", rcol2[:], rcol2[:], 1.0 / H, 1e-5,
             ALU.mult, ALU.add, sem=s_ve)
        p.wait("scalar", s_ve, s_ve.v)
        p.op("scalar", "activation", rcol3[:], rcol2[:], AF.Ln)
        p.nop("scalar")
        p.op("scalar", "activation", rcol3[:], rcol3[:], AF.Exp, scale=-0.5, sem=s_sc)
        p.wait("vector", s_sc, s_sc.v)
        p.op("vector", "tensor_scalar", c1gC, c1gC, rcol3[:, 0:1], None, ALU.mult)
        p.op("vector", "tensor_copy", bv(X1B + c * H, X1B + (c + 1) * H), c1gC, sem=s_ve)
        ve_c1 = s_ve.v
        p.wait("sync", s_ve, ve_c1)
        sp_x1_last = p.dma("sync", x1_sp[c * 128:(c + 1) * 128, :], c1gC, s_sp)

        p.wait("tensor", s_ve, ve_c1)
        for g in range(NCH):
            bk = 4 + (g % 2)
            bank_wait(bk)
            p.op("tensor", "transpose", pbank[bk][:, 0:128],
                 c1gC[:, g * 128:(g + 1) * 128], ident[:], sem=s_pe)
            p.wait("scalar", s_pe, s_pe.v)
            p.wait("scalar", s_pe, logit_mm_done)
            tgt = p.op("scalar", "copy", x1Tc[:, g * 128:(g + 1) * 128],
                       pbank[bk][:, 0:128], sem=s_sc)
            bank_set(bk, s_sc, tgt)
        tpc_prev = s_pe.v
        bank_wait(6)
        p.wait("tensor", s_sc, s_sc.v)
        for g in range(NCH):
            p.op("tensor", "matmul", pbank[6][:, 0:E],
                 x1Tc[:, g * 128:(g + 1) * 128], wgb[:, g * E:(g + 1) * E],
                 start=(g == 0), stop=(g == NCH - 1),
                 sem=(s_pe if g == NCH - 1 else None))
        logit_mm_done = s_pe.v
        p.wait("vector", s_pe, s_pe.v)
        tgt = p.op("vector", "tensor_copy", logit[:, c * E:(c + 1) * E],
                   pbank[6][:, 0:E], sem=s_ve)
        bank_set(6, s_ve, tgt)

        L = logit[:, c * E:(c + 1) * E]
        p.op("vector", "reduce_max", rcol1[:], L, axis=AX, negate=True)
        p.nop("vector")
        p.op("vector", "tensor_scalar", rsc1[:], L, rcol1[:, 0:1], 0.0, ALU.add, ALU.is_ge)
        p.nop("vector")
        p.op("vector", "scalar_tensor_tensor", rsc2[:], rsc1[:], 16384.0, iota8p[:],
             ALU.mult, ALU.subtract)
        p.nop("vector")
        p.op("vector", "reduce_max", rcol2[:], rsc2[:], axis=AX)
        p.nop("vector")
        p.op("vector", "tensor_scalar", mask1[:, c * E:(c + 1) * E], iota8[:],
             rcol2[:, 0:1], 0.0, ALU.add, ALU.is_equal)
        if DEBUG and c == 0:
            p.op("vector", "tensor_copy", dbg2[:, 0:8], rsc1[:], sem=s_ve)
            p.op("vector", "tensor_copy", dbg2[:, 8:9], rcol1[:], sem=s_ve)
            p.op("vector", "tensor_copy", dbg2[:, 9:10], rcol2[:], sem=s_ve)
        p.nop("vector")
        p.op("vector", "scalar_tensor_tensor", rsc3[:], mask1[:, c * E:(c + 1) * E],
             -3e38, L, ALU.mult, ALU.add)
        p.nop("vector")
        p.op("vector", "reduce_max", rcol3[:], rsc3[:], axis=AX, negate=True)
        p.nop("vector")
        p.op("vector", "tensor_scalar", rsc1[:], rsc3[:], rcol3[:, 0:1], 0.0,
             ALU.add, ALU.is_ge)
        p.nop("vector")
        p.op("vector", "scalar_tensor_tensor", rsc2[:], rsc1[:], 16384.0, iota8p[:],
             ALU.mult, ALU.subtract)
        p.nop("vector")
        p.op("vector", "reduce_max", rcol2[:], rsc2[:], axis=AX)
        p.nop("vector")
        p.op("vector", "tensor_scalar", mask2[:, c * E:(c + 1) * E], iota8[:],
             rcol2[:, 0:1], 0.0, ALU.add, ALU.is_equal, sem=s_ve)
        p.wait("scalar", s_ve, s_ve.v)
        p.op("scalar", "activation", rsc3[:], L, AF.Exp, bias=rcol1[:, 0:1], scale=1.0,
             accum_out=rcol3[:, 0:1], sem=s_sc)
        p.wait("vector", s_sc, s_sc.v)
        p.op("vector", "tensor_mul", rsc1[:], rsc3[:], mask1[:, c * E:(c + 1) * E])
        p.nop("vector")
        p.op("vector", "reduce_sum", rcol2[:], rsc1[:], axis=AX)
        p.nop("vector")
        p.op("vector", "tensor_mul", rsc1[:], rsc3[:], mask2[:, c * E:(c + 1) * E])
        p.nop("vector")
        p.op("vector", "reduce_sum", rcol1[:], rsc1[:], axis=AX)
        p.nop("vector")
        p.op("vector", "tensor_add", rsc2[:, 0:1], rcol2[:], rcol1[:])
        p.op("vector", "tensor_scalar", rsc2[:, 1:2], rcol3[:, 0:1], 1e-9, None, ALU.mult)
        p.nop("vector")
        p.op("vector", "tensor_add", rsc2[:, 0:1], rsc2[:, 0:1], rsc2[:, 1:2])
        p.nop("vector")
        p.op("vector", "reciprocal", rsc2[:, 2:3], rsc2[:, 0:1])
        p.nop("vector")
        p.op("vector", "tensor_mul", w1k[:, c:c + 1], rcol2[:], rsc2[:, 2:3])
        p.op("vector", "tensor_mul", w2k[:, c:c + 1], rcol1[:], rsc2[:, 2:3], sem=s_ve)
    sp_x1 = s_sp.v
    ve_x1b = s_ve.v
    ve_route1 = s_ve.v

    p.wait("tensor", s_ve, ve_route1)
    bank_wait(7)
    for c in range(NTC):
        p.op("tensor", "matmul", pbank[7][0:1, 0:E], ones_c[:],
             mask1[:, c * E:(c + 1) * E], start=(c == 0), stop=(c == NTC - 1),
             sem=(s_pe if c == NTC - 1 else None))
    bank_wait(6)
    for c in range(NTC):
        p.op("tensor", "matmul", pbank[6][0:1, 0:E], ones_c[:],
             mask2[:, c * E:(c + 1) * E], start=(c == 0), stop=(c == NTC - 1),
             sem=(s_pe if c == NTC - 1 else None))
    p.wait("vector", s_pe, s_pe.v)
    p.op("vector", "tensor_copy", sm1[0:1, 0:E], pbank[7][0:1, 0:E])
    tgt = p.op("vector", "tensor_copy", sm1[0:1, E:2 * E], pbank[6][0:1, 0:E], sem=s_ve)
    bank_set(6, s_ve, tgt)
    bank_set(7, s_ve, tgt)
    p.wait("sync", s_ve, s_ve.v)
    cc2_pub = p.dma("sync", cc2_in, sm1[0:1, 0:2 * E], s_sp)
    p.wait("gpsimd", s_sp, cc2_pub)
    p.op("gpsimd", "collective_compute", "AllGather", ALU.bypass,
         replica_groups=[list(range(NCORES))],
         ins=[cc2_in.opt()], outs=[cc2_out.opt()], sem=s_cc)
    cc2_done = s_cc.v
    p.wait("gpsimd", s_cc, cc2_done)
    ld_cc2 = p.dma("gpsimd", cc2_sb[:], cc2_out, s_gp)
    p.wait("tensor", s_gp, ld_cc2)
    bank_wait(7)
    p.op("tensor", "matmul", pbank[7][0:1, 0:2 * E], premk[:], cc2_sb[:],
         start=True, stop=True)
    bank_wait(5)
    p.op("tensor", "matmul", pbank[5][0:1, 0:E], ones_c[0:E, 0:1],
         cc2_sb[:, 0:E], start=True, stop=True, sem=s_pe)
    p.wait("vector", s_pe, s_pe.v)
    p.op("vector", "tensor_copy", offs[:], pbank[7][0:1, 0:2 * E])
    tgt = p.op("vector", "tensor_copy", offg[:], pbank[5][0:1, 0:E], sem=s_ve)
    bank_set(5, s_ve, tgt)
    bank_set(7, s_ve, tgt)
    ve_offs = s_ve.v

    def cumsum_pass(msk, possb, keepsb, locol):
        for c in range(NTC):
            bank_wait(5)
            p.wait("tensor", s_ve, s_ve.v)
            p.op("tensor", "matmul", pbank[5][:, 0:E], triu[:],
                 msk[:, c * E:(c + 1) * E], start=True, stop=False)
            p.op("tensor", "matmul", pbank[5][:, 0:E], ones_r1[:], locol,
                 start=False, stop=True, sem=s_pe)
            p.wait("vector", s_pe, s_pe.v)
            tgt_ = p.op("vector", "tensor_mul", rsc1[:], pbank[5][:, 0:E],
                        msk[:, c * E:(c + 1) * E], sem=s_ve)
            bank_set(5, s_ve, tgt_)
            p.nop("vector")
            p.op("vector", "reduce_sum", possb[:, c:c + 1], rsc1[:], axis=AX)
            if keepsb is not None:
                p.nop("vector")
                p.op("vector", "tensor_scalar", keepsb[:, c:c + 1], possb[:, c:c + 1],
                     float(CGLOB), 0.0, ALU.subtract, ALU.is_lt)
            bank_wait(6)
            p.op("tensor", "matmul", pbank[6][0:1, 0:E], ones_c[:],
                 msk[:, c * E:(c + 1) * E], start=True, stop=True, sem=s_pe)
            p.wait("vector", s_pe, s_pe.v)
            tgt_ = p.op("vector", "tensor_add", locol, locol, pbank[6][0:1, 0:E],
                        sem=s_ve)
            bank_set(6, s_ve, tgt_)

    p.wait("vector", s_ve, ve_offs)
    p.op("vector", "tensor_copy", loco[0:1, 0:E], offs[0:1, 0:E], sem=s_ve)
    cumsum_pass(mask1, pos1, keep1, loco[0:1, 0:E])
    p.op("vector", "tensor_add", loco[0:1, E:2 * E], offs[0:1, E:2 * E],
         offg[0:1, 0:E], sem=s_ve)
    cumsum_pass(mask2, pos2, keep2, loco[0:1, E:2 * E])

    for c in range(NTC):
        p.op("vector", "tensor_scalar", kept1[:, c * E:(c + 1) * E],
             mask1[:, c * E:(c + 1) * E], keep1[:, c:c + 1], None, ALU.mult)
        p.op("vector", "tensor_scalar", kept2[:, c * E:(c + 1) * E],
             mask2[:, c * E:(c + 1) * E], keep2[:, c:c + 1], None, ALU.mult)
        p.op("vector", "tensor_mul", w1k[:, c:c + 1], w1k[:, c:c + 1], keep1[:, c:c + 1])
        p.op("vector", "tensor_mul", w2k[:, c:c + 1], w2k[:, c:c + 1], keep2[:, c:c + 1],
             sem=s_ve)

    p.op("vector", "memset", loco[0:1, 0:E], 0.0, sem=s_ve)
    cumsum_pass(kept1, slot1, None, loco[0:1, 0:E])
    p.op("vector", "tensor_copy", loco[0:1, E:2 * E], loco[0:1, 0:E], sem=s_ve)
    cumsum_pass(kept2, slot2, None, loco[0:1, E:2 * E])
    p.op("vector", "tensor_copy", cnt_i32[0:1, 0:E], loco[0:1, E:2 * E], sem=s_ve)

    for c in range(NTC):
        for (msk, kp, slotsb, dstsi) in ((mask1, keep1, slot1, dst1i),
                                         (mask2, keep2, slot2, dst2i)):
            p.op("vector", "tensor_mul", rsc1[:], iota8[:], msk[:, c * E:(c + 1) * E])
            p.nop("vector")
            p.op("vector", "reduce_sum", rcol1[:], rsc1[:], axis=AX)
            p.op("vector", "tensor_scalar", rcol2[:], slotsb[:, c:c + 1],
                 float(CL), 0.0, ALU.subtract, ALU.is_lt)
            p.nop("vector")
            p.op("vector", "tensor_mul", rcol2[:], rcol2[:], kp[:, c:c + 1])
            p.op("vector", "tensor_scalar", rcol1[:], rcol1[:], float(CL), None, ALU.mult)
            p.nop("vector")
            p.op("vector", "tensor_tensor", rcol1[:], rcol1[:], slotsb[:, c:c + 1], ALU.add)
            p.nop("vector")
            p.op("vector", "tensor_scalar", rcol1[:], rcol1[:], float(TRASH), None,
                 ALU.subtract)
            p.nop("vector")
            p.op("vector", "tensor_mul", rcol1[:], rcol1[:], rcol2[:])
            p.nop("vector")
            p.op("vector", "tensor_scalar", rcol3[:], rcol1[:], float(TRASH), None, ALU.add)
            p.nop("vector")
            p.op("vector", "tensor_copy", dstsi[:, c:c + 1], rcol3[:], sem=s_ve)
    ve_dests = s_ve.v

    if DEBUG:
        d_rt = nc.dram_tensor("d_rt", [128, 6 * NTC], F32, kind="ExternalOutput").ap()
        d_rti = nc.dram_tensor("d_rti", [128, 2 * NTC], I32, kind="ExternalOutput").ap()
        d_msk = nc.dram_tensor("d_msk", [128, 3 * NTC * E], F32, kind="ExternalOutput").ap()
        p.wait("sync", s_ve, ve_dests)
        p.dma("sync", d_rt[:, 0 * NTC:1 * NTC], pos1[:], s_sp)
        p.dma("sync", d_rt[:, 1 * NTC:2 * NTC], pos2[:], s_sp)
        p.dma("sync", d_rt[:, 2 * NTC:3 * NTC], slot1[:], s_sp)
        p.dma("sync", d_rt[:, 3 * NTC:4 * NTC], slot2[:], s_sp)
        p.dma("sync", d_rt[:, 4 * NTC:5 * NTC], w1k[:], s_sp)
        p.dma("sync", d_rt[:, 5 * NTC:6 * NTC], w2k[:], s_sp)
        p.dma("sync", d_rti[:, 0:NTC], dst1i[:], s_sp)
        p.dma("sync", d_rti[:, NTC:2 * NTC], dst2i[:], s_sp)
        p.dma("sync", d_msk[:, 0:NTC * E], logit[:], s_sp)
        p.dma("sync", d_msk[:, NTC * E:2 * NTC * E], mask1[:], s_sp)
        p.dma("sync", d_msk[:, 2 * NTC * E:3 * NTC * E], mask2[:], s_sp)
        p.dma("sync", d_dbg2, dbg2, s_sp)

    p.wait("gpsimd", s_ve, ve_dests)
    p.wait("gpsimd", s_sp, sp_zero)
    for c in range(NTC):
        for dsts in (dst1i, dst2i):
            p.op("gpsimd", "indirect_dma_start",
                 out=xe_d, out_offset=IndirectOffsetOnAxis(dsts[:, c:c + 1], 0),
                 in_=bv(X1B + c * H, X1B + (c + 1) * H), in_offset=None,
                 sem=s_ix, inc=16)
    ix_scatter = s_ix.v

    if DEBUG:
        d_h1 = [nc.dram_tensor(f"d_h1{i}", [128, 16 * CL], BF16, kind="ExternalOutput").ap()
                for i in range(2)]
        d_yep = nc.dram_tensor("d_yep", [128, E * H], F32, kind="ExternalOutput").ap()
    yev = fv(0, 2 * H)
    ye_part = fv(2 * H, 2 * H + E * H)
    b2e = ARB[0:1, B2O:B2O + H]
    NHF = NFC // 2
    pe_w1_done = {}
    pe_w2_done = {}
    sp_ye = {}
    w1sl_guard = {}
    rtv = {}
    p.wait("tensor", s_ix, ix_scatter)
    for e in range(E):
        p.wait("sync", s_ix, ix_scatter)
        if e >= 1:
            p.wait("sync", s_pe, pe_w1_done[e - 1])
        for gg in range(NCH):
            p.dmat("sync", bv(XEO + gg * CL, XEO + (gg + 1) * CL),
                   xe_d[e * CL:(e + 1) * CL, gg * 128:(gg + 1) * 128], s_ld)
        ld_xeT = s_ld.v

        def fload(e_, ee=e):
            if not FFN_SKIP:
                return
            _, (v_,) = nc.values_load_multi_w_load_instructions(
                cnt_i32[0:1, ee:ee + 1], engines=[mybir.EngineType.PE],
                min_val=0, max_val=2 * TL, skip_runtime_bounds_check=True)
            rtv[ee] = v_
        p.emit("tensor", fload)

        for hf in range(2):
            if e + hf >= 1:
                p.wait("gpsimd", s_pe, pe_w2_done.get((e, hf - 1),
                                                      pe_w2_done.get((e - 1, 1), 0)))
            if hf == 0:
                p.dma("gpsimd", b2e, b2[e:e + 1, :], s_gp)
                gp_b2e = s_gp.v
            f0 = hf * (FF // 2)

            if e + hf >= 1:
                p.wait("scalar", s_pe, pe_w2_done.get((e, hf - 1),
                                                      pe_w2_done.get((e - 1, 1), 0)))
            for fs in range(4):
                buf = fs % 2
                p.wait("gpsimd", s_pe, w1sl_guard.get(buf, 0))
                p.dma("gpsimd",
                      bv(W1O + buf * NCH * 512, W1O + (buf + 1) * NCH * 512).rearrange(
                          "p (t f) -> p t f", f=512),
                      W1[e, :, f0 + fs * 512: f0 + fs * 512 + 512].rearrange(
                          "(t p) f -> p t f", p=128),
                      s_gp)
                gp_w1 = s_gp.v
                p.wait("tensor", s_gp, gp_w1)
                p.wait("tensor", s_ld, ld_xeT)
                for fc in range(4):
                    fcl = fs * 4 + fc
                    fcg = hf * NHF + fcl
                    for sh in range(2):
                        bk = sh
                        bank_wait(bk)
                        s_pe.v += 1

                        def w1body(e_, buf=buf, fc=fc, sh=sh, bk=bk):
                            for dt in range(NCH):
                                mm = e_.matmul(
                                    pbank[bk][:],
                                    bv(W1O + buf * NCH * 512 + dt * 512 + fc * 128,
                                       W1O + buf * NCH * 512 + dt * 512 + (fc + 1) * 128),
                                    bv(XEO + dt * CL + sh * 512,
                                       XEO + dt * CL + (sh + 1) * 512),
                                    start=(dt == 0), stop=(dt == NCH - 1))
                            mm.then_inc(s_pe.h, 1)

                        def w1grp(e_, body=w1body, sh=sh, ee=e):
                            if not FFN_SKIP:
                                body(e_)
                                return
                            with e_.If(rtv[ee] > sh * 512):
                                body(e_)
                            with e_.Else():
                                e_.sem_inc(s_pe.h, 1)
                        p.emit("tensor", w1grp)
                        p.wait("scalar", s_pe, s_pe.v)
                        tgt = p.op("scalar", "activation",
                                   bv(H1O + fcl * CL + sh * 512,
                                      H1O + fcl * CL + (sh + 1) * 512),
                                   pbank[bk][:], AF.Gelu,
                                   bias=b1_sb[:, e * NFC + fcg: e * NFC + fcg + 1],
                                   scale=1.0, sem=s_sc)
                        bank_set(bk, s_sc, tgt)
                w1sl_guard[buf] = s_pe.v
            sc_h1_done = s_sc.v
            if DEBUG and e == 0:
                p.wait("sync", s_sc, sc_h1_done)
                p.dma("sync", d_h1[hf], bv(H1O, H1O + 16 * CL), s_sp)

            for piece in range(2):
                q0 = f0 + piece * (FF // 4)
                p.dma("gpsimd",
                      bv(W2O + piece * 8 * H, W2O + (piece + 1) * 8 * H).rearrange(
                          "p (t d) -> p t d", d=H),
                      W2[e, q0:q0 + FF // 4, :].rearrange("(t p) d -> p t d", p=128),
                      s_gp)
            gp_w2 = s_gp.v
            if DEBUG and e == 0 and hf == 1:
                p.wait("sync", s_sc, s_sc.v)
                p.dma("sync", d_yep, ye_part, s_sp)
            p.wait("tensor", s_gp, gp_w2)
            if hf == 1:
                p.wait("tensor", s_gp, gp_b2e)
            p.wait("tensor", s_sc, sc_h1_done)
            for sc8 in range(8):
                ybuf = sc8 % 2
                for dh in range(2):
                    bk = 2 + ((sc8 * 2 + dh) % 4)
                    bank_wait(bk)
                    s_pe.v += 1

                    def w2body(e_, sc8=sc8, dh=dh, bk=bk, hf=hf):
                        for fcl in range(NHF):
                            last = (hf == 0 and fcl == NHF - 1)
                            mm = e_.matmul(
                                pbank[bk][:],
                                bv(H1O + fcl * CL + sc8 * 128,
                                   H1O + fcl * CL + (sc8 + 1) * 128),
                                bv(W2O + fcl * H + dh * 512,
                                   W2O + fcl * H + (dh + 1) * 512),
                                start=(fcl == 0), stop=last)
                        if hf == 1:
                            mm = e_.matmul(
                                pbank[bk][:], ones_r1b[0:1, 0:128],
                                b2e[0:1, dh * 512:(dh + 1) * 512],
                                start=False, stop=True)
                        mm.then_inc(s_pe.h, 1)

                    def w2grp(e_, body=w2body, sc8=sc8, ee=e):
                        if not FFN_SKIP:
                            body(e_)
                            return
                        with e_.If(rtv[ee] > sc8 * 128):
                            body(e_)
                        with e_.Else():
                            e_.sem_inc(s_pe.h, 1)
                    p.emit("tensor", w2grp)
                    if hf == 0:
                        p.wait("scalar", s_pe, s_pe.v)
                        tgt = p.op("scalar", "copy",
                                   fv(2 * H + sc8 * H + dh * 512,
                                      2 * H + sc8 * H + (dh + 1) * 512),
                                   pbank[bk][:], sem=s_sc)
                        bank_set(bk, s_sc, tgt)
                    else:
                        p.wait("vector", s_pe, s_pe.v)
                        p.wait("vector", s_sp, sp_ye.get((e, sc8 - 2),
                                                         sp_ye.get((e - 1, 6 + sc8), 0)))
                        tgt = p.op("vector", "tensor_add",
                                   fv(ybuf * H + dh * 512, ybuf * H + (dh + 1) * 512),
                                   pbank[bk][:],
                                   fv(2 * H + sc8 * H + dh * 512,
                                      2 * H + sc8 * H + (dh + 1) * 512), sem=s_ve)
                        bank_set(bk, s_ve, tgt)
                if hf == 1:
                    p.wait("sync", s_ve, s_ve.v)
                    sp_ye[(e, sc8)] = p.dma(
                        "sync", ye_d[e * CL + sc8 * 128: e * CL + (sc8 + 1) * 128, :],
                        fv(ybuf * H, (ybuf + 1) * H), s_sp)
            pe_w2_done[(e, hf)] = s_pe.v
        pe_w1_done[e] = s_pe.v
    sp_ye_all = s_sp.v

    ot_done = {}
    ve_e_last = {}
    for c in range(NTC):
        half = c % 2
        if c == 0:
            p.wait("gpsimd", s_sp, sp_ye_all)
        if c >= 2:
            p.wait("gpsimd", s_ot, ot_done[c - 2])
        p.op("gpsimd", "indirect_dma_start",
             out=fv(half * H, (half + 1) * H), out_offset=None,
             in_=ye_d, in_offset=IndirectOffsetOnAxis(dst1i[:, c:c + 1], 0),
             sem=s_ix, inc=16)
        p.op("gpsimd", "indirect_dma_start",
             out=fv(2 * H + half * H, 2 * H + (half + 1) * H), out_offset=None,
             in_=ye_d, in_offset=IndirectOffsetOnAxis(dst2i[:, c:c + 1], 0),
             sem=s_ix, inc=16)
        ixg = s_ix.v
        if c >= 2:
            p.wait("sync", s_ot, ot_done[c - 2])
        ldx1 = p.dma("sync", fv(4 * H + half * H, 4 * H + (half + 1) * H),
                     x1_sp[c * 128:(c + 1) * 128, :], s_ld)
        p.wait("vector", s_ix, ixg)
        p.wait("vector", s_ld, ldx1)
        yy = fv(6 * H + half * H, 6 * H + (half + 1) * H)
        p.op("vector", "tensor_scalar", yy, fv(half * H, (half + 1) * H),
             w1k[:, c:c + 1], None, ALU.mult)
        p.op("vector", "scalar_tensor_tensor", yy, fv(2 * H + half * H, 2 * H + (half + 1) * H),
             w2k[:, c:c + 1], yy, ALU.mult, ALU.add)
        p.op("vector", "tensor_add", yy, yy, fv(4 * H + half * H, 4 * H + (half + 1) * H))
        p.op("vector", "reduce_sum", rcol1[:], yy, axis=AX, negate=True)
        p.nop("vector")
        p.op("vector", "tensor_scalar", rcol1[:], rcol1[:], 1.0 / H, None, ALU.mult)
        p.nop("vector")
        p.op("vector", "tensor_scalar", yy, yy, rcol1[:, 0:1], None, ALU.add, sem=s_ve)
        p.wait("scalar", s_ve, s_ve.v)
        p.op("scalar", "activation", fv(8 * H, 9 * H), yy, AF.Square,
             accum_out=rcol2[:, 0:1], sem=s_sc)
        p.wait("vector", s_sc, s_sc.v)
        p.op("vector", "tensor_scalar", rcol2[:], rcol2[:], 1.0 / H, 1e-5,
             ALU.mult, ALU.add, sem=s_ve)
        p.wait("scalar", s_ve, s_ve.v)
        p.op("scalar", "activation", rcol3[:], rcol2[:], AF.Ln)
        p.nop("scalar")
        p.op("scalar", "activation", rcol3[:], rcol3[:], AF.Exp, scale=-0.5, sem=s_sc)
        p.wait("vector", s_sc, s_sc.v)
        p.op("vector", "tensor_scalar", yy, yy, rcol3[:, 0:1], None, ALU.mult, sem=s_ve)
        ve_e_last[c] = s_ve.v
        p.wait("sync", s_ve, s_ve.v)
        ot_done[c] = p.dma("sync", out[c * 128:(c + 1) * 128, :], yy, s_ot)
    p.wait("sync", s_ot, s_ot.v)

    with nc.Block() as block:
        block.sync(lambda e: [fn(e) for fn in p.ops["sync"]])
        block.gpsimd(lambda e: [fn(e) for fn in p.ops["gpsimd"]])
        block.vector(lambda e: [fn(e) for fn in p.ops["vector"]])
        block.scalar(lambda e: [fn(e) for fn in p.ops["scalar"]])
        block.tensor(lambda e: [fn(e) for fn in p.ops["tensor"]])
    ctx.close()
    return nc


_BUILT = None


def _get_built():
    global _BUILT
    if _BUILT is None:
        _BUILT = build_nc()
    return _BUILT


def _host_inputs(inputs):
    x = np.ascontiguousarray(np.asarray(inputs["L0_attn_x"], np.float32).reshape(B * T, H))
    tmix = np.ascontiguousarray(np.stack([
        np.asarray(inputs["L0_attn_time_mix_k"], np.float32).reshape(H),
        np.asarray(inputs["L0_attn_time_mix_v"], np.float32).reshape(H),
        np.asarray(inputs["L0_attn_time_mix_r"], np.float32).reshape(H),
    ]))
    common = {
        "Wk": np.ascontiguousarray(np.asarray(inputs["L0_attn_W_k"], np.float32)),
        "Wv": np.ascontiguousarray(np.asarray(inputs["L0_attn_W_v"], np.float32)),
        "Wr": np.ascontiguousarray(np.asarray(inputs["L0_attn_W_r"], np.float32)),
        "Wo": np.ascontiguousarray(np.asarray(inputs["L0_attn_W_o"], np.float32)),
        "tdecay": np.ascontiguousarray(np.asarray(inputs["L0_attn_time_decay"], np.float32)),
        "tfirst": np.ascontiguousarray(np.asarray(inputs["L0_attn_time_first"], np.float32)),
        "tmix": tmix,
        "Wg": np.ascontiguousarray(np.asarray(inputs["L0_ffn_Wg"], np.float32)),
        "W1": np.ascontiguousarray(np.asarray(inputs["L0_ffn_W1"], np.float32)),
        "b1": np.ascontiguousarray(np.asarray(inputs["L0_ffn_b1"], np.float32)),
        "W2": np.ascontiguousarray(np.asarray(inputs["L0_ffn_W2"], np.float32)),
        "b2": np.ascontiguousarray(np.asarray(inputs["L0_ffn_b2"], np.float32)),
    }
    in_maps = []
    for c in range(NCORES):
        t0 = c * TL
        if (t0 % T) == 0:
            xp = np.vstack([np.zeros((1, H), np.float32), x[t0:t0 + TL]])
        else:
            xp = x[t0 - 1:t0 + TL]
        mm = np.zeros((128, NCORES), np.float32)
        ma = np.full((128, NCORES), NEG, np.float32)
        pm = np.zeros((NCORES, 1), np.float32)
        for cp in range(c - (c % 4), c):
            mm[:, cp] = 1.0
            ma[:, cp] = 0.0
        for cp in range(c):
            pm[cp, 0] = 1.0
        m = dict(common)
        m["x_pad"] = np.ascontiguousarray(xp)
        m["cmask_mul"] = mm
        m["cmask_add"] = ma
        m["premask"] = pm
        in_maps.append(m)
    return in_maps


def kernel(**inputs):
    nc = _get_built()
    in_maps = _host_inputs(inputs)
    res = run_bass_kernel_spmd(nc, in_maps, core_ids=list(range(NCORES)))
    outs = [res.results[c]["y"] for c in range(NCORES)]
    return np.concatenate(outs, axis=0).reshape(B, T, H).astype(np.float32)




# version 18
# speedup vs baseline: 1.5674x; 1.0720x over previous
import numpy as np

import concourse.bass as bass
import concourse.mybir as mybir
from concourse.bass_utils import run_bass_kernel_spmd
from concourse.bass_core_rust import IndirectOffsetOnAxis

F32 = mybir.dt.float32
BF16 = mybir.dt.bfloat16
I32 = mybir.dt.int32
AF = mybir.ActivationFunctionType
ALU = mybir.AluOpType
AX = mybir.AxisListType.X

NCORES = 8
B, T, H = 2, 8192, 1024
TL = 2048
NTC = TL // 128
NCH = H // 128
E = 8
FF = 4096
NFC = FF // 128
CGLOB = 8192
CL = 1024
NEG = -1e38
TRASH = E * CL


class SemCtr:
    def __init__(self, handle, name):
        self.h = handle
        self.v = 0
        self.name = name


class Plan:
    def __init__(self):
        self.ops = {k: [] for k in ("sync", "gpsimd", "vector", "scalar", "tensor")}
        self.hist = {}

    def emit(self, eng, fn):
        self.ops[eng].append(fn)

    def wait(self, eng, sem, val=None):
        if val is None:
            val = sem.v
        assert val <= sem.v, f"wait on future value of {sem.name}: {val} > {sem.v}"
        if val > 0:
            self.emit(eng, lambda e, s=sem.h, v=val: e.wait_ge(s, v))

    def dma(self, eng, out, in_, sem, **kw):
        sem.v += 16
        self.emit(eng, lambda e, o=out, i=in_, s=sem.h, kw=kw:
                  e.dma_start(out=o, in_=i, **kw).then_inc(s, 16))
        return sem.v

    def dmat(self, eng, out, in_, sem):
        sem.v += 16
        self.emit(eng, lambda e, o=out, i=in_, s=sem.h:
                  e.dma_start_transpose(o, i).then_inc(s, 16))
        return sem.v

    scalar_nop_fn = None

    def nop(self, eng):
        if eng == "scalar":
            self.emit(eng, self.scalar_nop_fn)
        else:
            self.emit(eng, lambda e: e.engine_nop())

    @staticmethod
    def _foot(a):
        try:
            t = a.tensor.name
            off = a.offset
            dims = list(a.ap)
        except Exception:
            return None
        if not dims:
            return (t, 0, 1, off, off + 1, 1)
        pitch, np_ = dims[0]
        if pitch <= 0:
            pitch = 1
        p0 = off // pitch
        c0 = off % pitch
        span = sum(st * (ct - 1) for st, ct in dims[1:]) + 1
        return (t, p0, p0 + np_, c0, c0 + span, span)

    @staticmethod
    def _ovl(f1, f2):
        return (f1[0] == f2[0] and f1[1] < f2[2] and f2[1] < f1[2]
                and f1[3] < f2[4] and f2[3] < f1[4])

    WINDOW = 8

    def _hazard(self, eng, writes, reads):
        hist = self.hist.setdefault(eng, [])
        need = 0
        for (pw, pr, psem, pval) in hist:
            conflict = False
            for f1 in pw:
                for f2 in writes + reads:
                    if self._ovl(f1, f2) and (f1[5] < 512 or f2[5] < 512):
                        conflict = True
            for f1 in pr:
                for f2 in writes:
                    if self._ovl(f1, f2) and (f1[5] < 512 or f2[5] < 512):
                        conflict = True
            if conflict:
                if psem is not None:
                    need = max(need, pval) if psem is self.def_sem[eng] else need
                    if psem is not self.def_sem[eng]:
                        self.wait(eng, psem, pval)
                else:
                    self.nop(eng)
                    self.nop(eng)
        if need:
            self.wait(eng, self.def_sem[eng], need)

    def op(self, eng, meth, *args, sem=None, inc=1, **kw):
        writes = reads = None
        if eng in ("vector", "scalar"):
            w = [self._foot(args[0])] if args else []
            w += [self._foot(kw["accum_out"])] if kw.get("accum_out") is not None else []
            r = [self._foot(a) for a in args[1:]]
            r += [self._foot(v) for k2, v in kw.items() if k2 != "accum_out"]
            writes = [f for f in w if f is not None]
            reads = [f for f in r if f is not None]
            self._hazard(eng, writes, reads)
            small = any(f[5] < 512 for f in writes)
            if sem is None and small:
                sem = self.def_sem[eng]
                inc = 1
        if sem is not None:
            sem.v += inc
            ret = sem.v
            self.emit(eng, lambda e, a=args, kw=kw, s=sem.h, n=inc:
                      getattr(e, meth)(*a, **kw).then_inc(s, n))
        else:
            ret = None
            self.emit(eng, lambda e, a=args, kw=kw: getattr(e, meth)(*a, **kw))
        if writes is not None:
            hist = self.hist.setdefault(eng, [])
            hist.append((writes, reads, sem, sem.v if sem is not None else None))
            if len(hist) > self.WINDOW:
                hist.pop(0)
        return ret


DEBUG = False
FFN_SKIP = True
FFN_SKIP_W1 = True


def build_nc():
    nc = bass.Bass("TRN2", target_bir_lowering=False)
    SCR = "ExternalOutput" if DEBUG else "Internal"

    x_pad = nc.dram_tensor("x_pad", [TL + 1, H], F32, kind="ExternalInput").ap()
    Wk = nc.dram_tensor("Wk", [H, H], F32, kind="ExternalInput").ap()
    Wv = nc.dram_tensor("Wv", [H, H], F32, kind="ExternalInput").ap()
    Wr = nc.dram_tensor("Wr", [H, H], F32, kind="ExternalInput").ap()
    Wo = nc.dram_tensor("Wo", [H, H], F32, kind="ExternalInput").ap()
    tdecay = nc.dram_tensor("tdecay", [H], F32, kind="ExternalInput").ap()
    tfirst = nc.dram_tensor("tfirst", [H], F32, kind="ExternalInput").ap()
    tmix = nc.dram_tensor("tmix", [3, H], F32, kind="ExternalInput").ap()
    Wg = nc.dram_tensor("Wg", [H, E], F32, kind="ExternalInput").ap()
    W1 = nc.dram_tensor("W1", [E, H, FF], F32, kind="ExternalInput").ap()
    b1 = nc.dram_tensor("b1", [E, FF], F32, kind="ExternalInput").ap()
    W2 = nc.dram_tensor("W2", [E, FF, H], F32, kind="ExternalInput").ap()
    b2 = nc.dram_tensor("b2", [E, H], F32, kind="ExternalInput").ap()
    cmask_mul = nc.dram_tensor("cmask_mul", [128, NCORES], F32, kind="ExternalInput").ap()
    cmask_add = nc.dram_tensor("cmask_add", [128, NCORES], F32, kind="ExternalInput").ap()
    premask = nc.dram_tensor("premask", [NCORES, 1], F32, kind="ExternalInput").ap()
    out = nc.dram_tensor("y", [TL, H], F32, kind="ExternalOutput").ap()

    k_sp = nc.dram_tensor("k_sp", [H, TL], F32, kind=SCR).ap()
    v_sp = nc.dram_tensor("v_sp", [H, TL], F32, kind=SCR).ap()
    r_sp = nc.dram_tensor("r_sp", [H, TL], F32, kind=SCR).ap()
    xw_sp = nc.dram_tensor("xw_sp", [H, TL], F32, kind=SCR).ap()
    ss_sp = nc.dram_tensor("ss_sp", [H, TL], F32, kind=SCR).ap()
    dd_sp = nc.dram_tensor("dd_sp", [H, TL], F32, kind=SCR).ap()
    wb_d = [nc.dram_tensor(f"w{i}b_d", [H, H], BF16).ap() for i in range(4)]
    x1_sp = nc.dram_tensor("x1_sp", [TL, H], F32, kind=SCR).ap()
    xe_d = nc.dram_tensor("xe_d", [E * CL + 1, H], BF16, kind=SCR).ap()
    ye_d = nc.dram_tensor("ye_d", [E * CL + 1, H], F32, kind=SCR).ap()
    cc1_in = nc.dram_tensor("cc1_in", [128, 3 * NCH], F32).ap()
    cc1_out = nc.dram_tensor("cc1_out", [128 * NCORES, 3 * NCH], F32).ap()
    cc2_in = nc.dram_tensor("cc2_in", [1, 2 * E], F32).ap()
    cc2_out = nc.dram_tensor("cc2_out", [NCORES, 2 * E], F32).ap()

    from contextlib import ExitStack
    ctx = ExitStack()

    def sb(name, shape, dt=F32):
        return ctx.enter_context(nc.sbuf_tensor(name, shape, dt))

    def mksem(name):
        return SemCtr(ctx.enter_context(nc.semaphore(name)), name)

    NF = 22537
    NBQ = 50176
    ARF = sb("ARF", [128, NF])
    ARB = sb("ARB", [128, NBQ], BF16)

    def fv(a, b):
        return ARF[:, a:b]

    def bv(a, b):
        return ARB[:, a:b]

    def xTv(g, a, b):
        base = g * (TL + 1)
        return ARF[:, base + a: base + b]
    SPL0 = 16392
    SKA, SKB, SVV = 0, 2048, 4096
    SA_, SL_, SG_ = 6144, 8192, 10240
    SM_, SS_, SD_ = 12288, 14337, 20488
    KA, KB, XA, XB = 0, 2048, 4096, 6144
    SA2, SB2, DA2 = 8192, 10240, 12288
    CVV, CRR, CX3, CP_ = 14336, 16384, 18432, 20480

    WT = [0, 8192, 16384, 24576]
    XMT = 32768
    RWO = 0
    X1B = 32768
    X1TC = 16384
    H1O = 0
    W2O = 16384
    W1O = 32768
    XEO = 40960
    B2O = 49152

    iota0 = sb("iota0", [128, TL + 1])
    iota8 = sb("iota8", [128, E])
    iota8p = sb("iota8p", [128, E])
    ident = sb("ident", [128, 128])
    identb = sb("identb", [128, 128], BF16)
    triu = sb("triu", [128, 128])
    ones_c = sb("ones_c", [128, 1])
    ones_r1 = sb("ones_r1", [1, 128])
    ones_r1b = sb("ones_r1b", [1, 128], BF16)
    w_sb = sb("w_sb", [128, NCH])
    negw_sb = sb("negw_sb", [128, NCH])
    w2048_sb = sb("w2048_sb", [128, NCH])
    u_sb = sb("u_sb", [128, NCH])
    tm_sb = sb("tm_sb", [128, 3 * NCH])
    tm1_sb = sb("tm1_sb", [128, 3 * NCH])
    cmaskm = sb("cmaskm", [128, NCORES])
    cmaska = sb("cmaska", [128, NCORES])
    premk = sb("premk", [NCORES, 1])
    wgb = sb("wgb", [128, NCH * E], BF16)
    b1_sb = sb("b1_sb", [128, E * NFC])
    agg = sb("agg", [128, 3 * NCH])
    aggT = sb("aggT", [128, 3 * NCH])
    carry = sb("carry", [128, 3 * NCH])
    sm1 = sb("sm1", [128, 3 * NCH])
    sm2 = sb("sm2", [128, 3 * NCH])
    sm3 = sb("sm3", [128, 3 * NCH])
    cc2_sb = sb("cc2_sb", [NCORES, 2 * E])
    offs = sb("offs", [1, 2 * E])
    offg = sb("offg", [1, E])
    loco = sb("loco", [1, 2 * E])
    logit = sb("logit", [128, NTC * E])
    mask1 = sb("mask1", [128, NTC * E])
    mask2 = sb("mask2", [128, NTC * E])
    kept1 = sb("kept1", [128, NTC * E])
    kept2 = sb("kept2", [128, NTC * E])
    pos1 = sb("pos1", [128, NTC])
    pos2 = sb("pos2", [128, NTC])
    keep1 = sb("keep1", [128, NTC])
    keep2 = sb("keep2", [128, NTC])
    w1k = sb("w1k", [128, NTC])
    w2k = sb("w2k", [128, NTC])
    slot1 = sb("slot1", [128, NTC])
    slot2 = sb("slot2", [128, NTC])
    dst1i = sb("dst1i", [128, NTC], I32)
    dst2i = sb("dst2i", [128, NTC], I32)
    rsc1 = sb("rsc1", [128, E])
    rsc2 = sb("rsc2", [128, E])
    rsc3 = sb("rsc3", [128, E])
    rcol1 = sb("rcol1", [128, 1])
    rcol2 = sb("rcol2", [128, 1])
    rcol3 = sb("rcol3", [128, 1])
    xck = sb("xck", [128, H])
    nopc = sb("nopc", [128, 1])
    cnt_i32 = sb("cnt_i32", [1, E], I32)

    def psumt(name, shape):
        return ctx.enter_context(nc.psum_tensor(name, shape, F32))

    pbank = [psumt(f"pb{i}", [128, 512]) for i in range(8)]

    s_ld = mksem("s_ld")
    s_gp = mksem("s_gp")
    s_gc = mksem("s_gc")
    s_sp = mksem("s_sp")
    s_pe = mksem("s_pe")
    s_ve = mksem("s_ve")
    s_sc = mksem("s_sc")
    s_cc = mksem("s_cc")
    s_ix = mksem("s_ix")
    s_ot = mksem("s_ot")

    p = Plan()
    p.scalar_nop_fn = lambda e: e.copy(nopc[:], nopc[:])
    p.def_sem = {"vector": s_ve, "scalar": s_sc}
    bankg = {}

    def bank_wait(i):
        if i in bankg:
            sem, val = bankg[i]
            p.wait("tensor", sem, val)

    def bank_set(i, sem, val):
        bankg[i] = (sem, val)

    p.op("gpsimd", "iota", iota0[:], [[1, TL + 1]], base=-1, channel_multiplier=0,
         allow_small_or_imprecise_dtypes=True)
    p.op("gpsimd", "iota", iota8[:], [[1, E]], channel_multiplier=0,
         allow_small_or_imprecise_dtypes=True)
    p.op("gpsimd", "memset", ident[:], 0.0)
    p.op("gpsimd", "affine_select", out=ident[:], in_=ident[:],
         compare_op=ALU.not_equal, fill=1.0, base=0,
         pattern=[[-1, 128]], channel_multiplier=1)
    p.op("gpsimd", "memset", identb[:], 0.0)
    p.op("gpsimd", "affine_select", out=identb[:], in_=identb[:],
         compare_op=ALU.not_equal, fill=1.0, base=0,
         pattern=[[-1, 128]], channel_multiplier=1)
    p.op("gpsimd", "memset", triu[:], 0.0)
    p.op("gpsimd", "affine_select", out=triu[:], in_=triu[:],
         compare_op=ALU.is_ge, fill=1.0, base=0,
         pattern=[[-1, 128]], channel_multiplier=1)
    p.op("gpsimd", "memset", ones_c[:], 1.0)
    p.op("gpsimd", "memset", ones_r1[:], 1.0)
    p.op("gpsimd", "memset", ones_r1b[:], 1.0, sem=s_gc)
    gc_consts = s_gc.v

    iotav = iota0[:, 1:TL + 1]
    iotam1 = iota0[:, 0:TL]

    p.dma("sync", w_sb[:], tdecay.rearrange("(g p) -> p g", p=128), s_ld,
          allow_slow_non_contiguous=True)
    p.dma("sync", u_sb[:], tfirst.rearrange("(g p) -> p g", p=128), s_ld,
          allow_slow_non_contiguous=True)
    p.dma("sync", tm_sb.ap().rearrange("p (m g) -> p m g", g=NCH),
          tmix.rearrange("m (g p) -> p m g", p=128), s_ld,
          allow_slow_non_contiguous=True)
    p.dma("sync", cmaskm[:], cmask_mul, s_ld)
    p.dma("sync", cmaska[:], cmask_add, s_ld)
    p.dma("sync", premk[:], premask, s_ld)
    p.dma("sync", b1_sb.ap().rearrange("p (e c) -> p e c", c=NFC),
          b1.rearrange("e (c p) -> p e c", p=128), s_ld,
          allow_slow_non_contiguous=True)
    ld_params = s_ld.v

    p.wait("vector", s_ld, ld_params)
    p.wait("vector", s_gc, gc_consts)
    p.op("vector", "tensor_scalar", negw_sb[:], w_sb[:], -1.0, None, ALU.mult)
    p.op("vector", "tensor_scalar", w2048_sb[:], w_sb[:], float(TL), None, ALU.mult)
    p.op("vector", "tensor_scalar", tm1_sb[:], tm_sb[:], -1.0, 1.0, ALU.mult, ALU.add)
    p.op("vector", "tensor_scalar", iota8p[:], iota8[:], 16384.0, None, ALU.add,
         sem=s_ve)
    ve_params = s_ve.v

    for w_src, w_dst in ((Wk, wb_d[0]), (Wv, wb_d[1]), (Wr, wb_d[2]), (Wo, wb_d[3])):
        p.dma("gpsimd", w_dst, w_src, s_gp)
    gp_wcast = s_gp.v
    p.dma("gpsimd", wgb.ap().rearrange("p (g e) -> p g e", e=E),
          Wg.rearrange("(g p) e -> p g e", p=128), s_gp,
          allow_slow_non_contiguous=True)
    gp_small = s_gp.v

    tp_done = 0
    for c in range(NTC):
        if c > 0:
            p.wait("sync", s_pe, tp_done)
        ldx = p.dma("sync", xck[:], x_pad[c * 128:(c + 1) * 128, :], s_ld)
        p.wait("tensor", s_ld, ldx)
        if c == 0:
            p.wait("tensor", s_gc, gc_consts)
        for g in range(NCH):
            bk = g % 4
            bank_wait(bk)
            p.op("tensor", "transpose", pbank[bk][:, 0:128],
                 xck[:, g * 128:(g + 1) * 128], ident[:], sem=s_pe)
            p.wait("scalar", s_pe, s_pe.v)
            tgt = p.op("scalar", "copy", xTv(g, c * 128, (c + 1) * 128),
                       pbank[bk][:, 0:128], sem=s_sc)
            bank_set(bk, s_sc, tgt)
        tp_done = s_pe.v
    ld_xT_last = p.dma("sync",
                       ARF.ap()[:, 0:NCH * (TL + 1)].rearrange(
                           "p (g t) -> p g t", g=NCH)[:, :, TL:TL + 1],
                       x_pad[TL:TL + 1, :].rearrange("o (g p) -> p g o", p=128), s_ld,
                       allow_slow_non_contiguous=True)
    sc_xT = s_sc.v

    p.wait("sync", s_gp, gp_wcast)
    for i in range(4):
        for g in range(NCH):
            p.dmat("sync", bv(WT[i] + g * H, WT[i] + (g + 1) * H),
                   wb_d[i][:, g * 128:(g + 1) * 128], s_ld)
    ld_wT = s_ld.v

    sp_zero = s_sp.v

    MIX_SP = {0: k_sp, 1: v_sp, 2: r_sp}
    spl = fv(SPL0, SPL0 + 4096)
    atmp = fv(20488, 20488 + TL)

    kbuf = [fv(SKA, SKA + TL), fv(SKB, SKB + TL)]
    svv = fv(SVV, SVV + TL)
    sa = fv(SA_, SA_ + TL)
    sl = fv(SL_, SL_ + TL)
    sg = fv(SG_, SG_ + TL)
    sminc = ARF[:, SM_:SM_ + TL + 1]
    ssinc = ARF[:, SS_:SS_ + TL + 1]
    sdinc = ARF[:, SD_:SD_ + TL + 1]
    sp_proj = {}
    mix2_sems = {}
    ld_k = {}
    ld_kv = {}
    sa_done = {}
    gv_done = {}
    scans_done = {}

    def scan_loads(g):
        p.wait("gpsimd", s_sp, sp_proj[0 * NCH + g])
        p.wait("gpsimd", s_sp, sp_proj[1 * NCH + g])
        if g >= 2:
            p.wait("gpsimd", s_ve, sa_done[g - 2])
        else:
            p.wait("gpsimd", s_ve, mix2_sems[min(g + 1, NCH - 1)])
        p.dma("gpsimd", kbuf[g % 2], k_sp[g * 128:(g + 1) * 128, :], s_gp)
        ld_k[g] = s_gp.v
        if g >= 1:
            p.wait("gpsimd", s_ve, gv_done[g - 1])
        else:
            p.wait("gpsimd", s_ve, mix2_sems[2])
        ld_kv[g] = p.dma("gpsimd", svv, v_sp[g * 128:(g + 1) * 128, :], s_gp)

    def scan_tile(g):
        kk_ = kbuf[g % 2]
        p.wait("vector", s_gp, ld_k[g])
        p.op("vector", "scalar_tensor_tensor", sa, iotav,
             negw_sb[:, g:g + 1], kk_, ALU.mult, ALU.add, sem=s_ve)
        sa_done[g] = s_ve.v
        p.op("vector", "memset", sminc[:, 0:1], NEG)
        p.op("vector", "tensor_tensor_scan", sminc[:, 1:TL + 1], sa, sa,
             sminc[:, 0:1], ALU.max, ALU.max)
        p.op("vector", "tensor_sub", sl, sminc[:, 0:TL], sminc[:, 1:TL + 1])
        p.op("vector", "tensor_sub", sa, sa, sminc[:, 1:TL + 1], sem=s_ve)
        ve_gin = s_ve.v
        p.wait("scalar", s_ve, ve_gin)
        p.op("scalar", "activation", sl, sl, AF.Exp)
        p.op("scalar", "activation", sa, sa, AF.Exp, sem=s_sc)
        sc_exp = s_sc.v
        p.wait("vector", s_sc, sc_exp)
        p.wait("vector", s_gp, ld_kv[g])
        p.op("vector", "tensor_mul", sg, sa, svv, sem=s_ve)
        gv_done[g] = s_ve.v
        if g + 1 < NCH:
            scan_loads(g + 1)
        p.op("vector", "memset", ssinc[:, 0:1], 0.0)
        p.op("vector", "memset", sdinc[:, 0:1], 0.0)
        p.op("vector", "tensor_tensor_scan", ssinc[:, 1:TL + 1], sl, sg,
             ssinc[:, 0:1], ALU.mult, ALU.add)
        p.op("vector", "tensor_tensor_scan", sdinc[:, 1:TL + 1], sl, sa,
             sdinc[:, 0:1], ALU.mult, ALU.add)
        p.op("vector", "scalar_tensor_tensor", sl, iotam1,
             w_sb[:, g:g + 1], sminc[:, 0:TL], ALU.mult, ALU.add)
        p.op("vector", "tensor_scalar", agg[:, g:g + 1], sminc[:, TL:TL + 1],
             w2048_sb[:, g:g + 1], w_sb[:, g:g + 1], ALU.add, ALU.subtract)
        p.op("vector", "tensor_copy", agg[:, NCH + g:NCH + g + 1],
             ssinc[:, TL:TL + 1])
        p.op("vector", "tensor_copy", agg[:, 2 * NCH + g:2 * NCH + g + 1],
             sdinc[:, TL:TL + 1], sem=s_ve)
        scans_done[g] = s_ve.v
        p.wait("sync", s_ve, scans_done[g])
        p.dma("sync", xw_sp[g * 128:(g + 1) * 128, :], sl, s_sp)
        p.dma("sync", dd_sp[g * 128:(g + 1) * 128, :], sdinc[:, 0:TL], s_sp)
        p.wait("gpsimd", s_ve, scans_done[g])
        p.dma("gpsimd", ss_sp[g * 128:(g + 1) * 128, :], ssinc[:, 0:TL], s_gp)

    pe_proj_done = 0
    for m in range(3):
        p.wait("vector", s_sc, sc_xT)
        p.wait("vector", s_ld, ld_xT_last)
        p.wait("vector", s_ve, ve_params)
        if m > 0:
            p.wait("vector", s_pe, pe_proj_done)
        for g in range(NCH):
            xr = xTv(g, 1, 1 + TL)
            xp = xTv(g, 0, TL)
            p.op("vector", "tensor_scalar", atmp, xr,
                 tm_sb[:, m * NCH + g: m * NCH + g + 1], None, ALU.mult)
            p.op("vector", "scalar_tensor_tensor", bv(XMT + g * TL, XMT + (g + 1) * TL),
                 xp, tm1_sb[:, m * NCH + g: m * NCH + g + 1], atmp, ALU.mult, ALU.add,
                 sem=s_ve)
            if m == 2:
                mix2_sems[g] = s_ve.v
        ve_mix = s_ve.v

        if m == 2:
            scan_loads(0)
        p.wait("tensor", s_ld, ld_wT)
        p.wait("tensor", s_ve, ve_mix)
        for oc in range(NCH):
            buf = oc % 2
            prev = sp_proj.get(m * NCH + oc - 2, 0)
            for tq in range(4):
                bk = 4 + (tq % 2)
                bank_wait(bk)
                for g in range(NCH):
                    p.op("tensor", "matmul", pbank[bk][:],
                         bv(WT[m] + g * H + oc * 128, WT[m] + g * H + (oc + 1) * 128),
                         bv(XMT + g * TL + tq * 512, XMT + g * TL + (tq + 1) * 512),
                         start=(g == 0), stop=(g == NCH - 1),
                         sem=(s_pe if g == NCH - 1 else None))
                p.wait("scalar", s_pe, s_pe.v)
                p.wait("scalar", s_sp, prev)
                func = AF.Sigmoid if m == 2 else AF.Copy
                tgt = p.op("scalar", "activation",
                           fv(SPL0 + buf * TL + tq * 512, SPL0 + buf * TL + (tq + 1) * 512),
                           pbank[bk][:], func, sem=s_sc)
                bank_set(bk, s_sc, tgt)
            p.wait("sync", s_sc, s_sc.v)
            sp_proj[m * NCH + oc] = p.dma(
                "sync", MIX_SP[m][oc * 128:(oc + 1) * 128, :],
                fv(SPL0 + buf * TL, SPL0 + (buf + 1) * TL), s_sp)
            if m == 2:
                scan_tile(oc)
        pe_proj_done = s_pe.v
    sp_proj_all = s_sp.v
    gp_scan_all = s_gp.v
    ve_agg = scans_done[NCH - 1]

    p.wait("gpsimd", s_pe, pe_proj_done)
    p.op("gpsimd", "memset", bv(XMT, XMT + H), 0.0, sem=s_gc)
    p.wait("gpsimd", s_gc, s_gc.v)
    sp_zero_ye = p.dma("gpsimd", ye_d[TRASH:TRASH + 1, :], ARB[0:1, XMT:XMT + H],
                       s_gp)

    if DEBUG:
        dbgv = fv(20496, 20512)
        d_dbgv = nc.dram_tensor("d_dbgv", [128, 16], F32, kind="ExternalOutput").ap()

    p.wait("sync", s_ve, ve_agg)
    cc_pub = p.dma("sync", cc1_in, agg[:], s_sp)
    p.wait("gpsimd", s_sp, cc_pub)
    p.op("gpsimd", "collective_compute", "AllGather", ALU.bypass,
         replica_groups=[list(range(NCORES))],
         ins=[cc1_in.opt()], outs=[cc1_out.opt()], sem=s_cc)
    cc1_done = s_cc.v
    p.op("vector", "memset", carry[:, 0:NCH], NEG)
    p.op("vector", "memset", carry[:, NCH:3 * NCH], 0.0, sem=s_ve)
    p.wait("gpsimd", s_cc, cc1_done)
    for cp in range(NCORES):
        if cp > 0:
            p.wait("gpsimd", s_ve, fold_done)
        ldt = p.dma("gpsimd", aggT[:], cc1_out[cp * 128:(cp + 1) * 128, :], s_gp)
        p.wait("vector", s_gp, ldt)
        p.op("vector", "tensor_scalar", sm1[:, 0:NCH], aggT[:, 0:NCH],
             cmaskm[:, cp:cp + 1], cmaska[:, cp:cp + 1], ALU.mult, ALU.add)
        p.op("vector", "tensor_scalar", sm1[:, NCH:3 * NCH], aggT[:, NCH:3 * NCH],
             cmaskm[:, cp:cp + 1], None, ALU.mult)
        p.op("vector", "tensor_scalar", sm2[:, 0:NCH], w2048_sb[:],
             cmaskm[:, cp:cp + 1], None, ALU.mult)
        p.nop("vector")
        p.op("vector", "tensor_tensor", sm2[:, 0:NCH], carry[:, 0:NCH],
             sm2[:, 0:NCH], ALU.add)
        p.nop("vector")
        p.op("vector", "tensor_tensor", sm3[:, 0:NCH], sm2[:, 0:NCH], sm1[:, 0:NCH],
             ALU.max)
        p.nop("vector")
        p.op("vector", "tensor_sub", sm2[:, NCH:2 * NCH], sm2[:, 0:NCH], sm3[:, 0:NCH])
        p.op("vector", "tensor_sub", sm2[:, 2 * NCH:3 * NCH], sm1[:, 0:NCH],
             sm3[:, 0:NCH], sem=s_ve)
        p.wait("scalar", s_ve, s_ve.v)
        p.op("scalar", "activation", sm2[:, NCH:3 * NCH], sm2[:, NCH:3 * NCH], AF.Exp,
             sem=s_sc)
        p.wait("vector", s_sc, s_sc.v)
        p.op("vector", "tensor_mul", sm3[:, NCH:2 * NCH], carry[:, NCH:2 * NCH],
             sm2[:, NCH:2 * NCH])
        p.op("vector", "tensor_mul", sm3[:, 2 * NCH:3 * NCH], carry[:, 2 * NCH:3 * NCH],
             sm2[:, NCH:2 * NCH])
        p.op("vector", "tensor_mul", sm1[:, NCH:2 * NCH], sm1[:, NCH:2 * NCH],
             sm2[:, 2 * NCH:3 * NCH])
        p.op("vector", "tensor_mul", sm1[:, 2 * NCH:3 * NCH], sm1[:, 2 * NCH:3 * NCH],
             sm2[:, 2 * NCH:3 * NCH])
        p.nop("vector")
        p.op("vector", "tensor_add", carry[:, NCH:3 * NCH], sm3[:, NCH:3 * NCH],
             sm1[:, NCH:3 * NCH])
        p.op("vector", "tensor_copy", carry[:, 0:NCH], sm3[:, 0:NCH], sem=s_ve)
        fold_done = s_ve.v
        if DEBUG and cp == 1:
            p.op("vector", "tensor_copy", dbgv[:, 8:9], sm1[:, 0:1], sem=s_ve)
            p.op("vector", "tensor_copy", dbgv[:, 9:10], sm2[:, 0:1], sem=s_ve)
            p.op("vector", "tensor_copy", dbgv[:, 10:11], sm3[:, 0:1], sem=s_ve)
            p.op("vector", "tensor_copy", dbgv[:, 11:12], sm2[:, NCH:NCH + 1], sem=s_ve)
            p.op("vector", "tensor_copy", dbgv[:, 12:13], sm2[:, 2 * NCH:2 * NCH + 1], sem=s_ve)
            p.op("vector", "tensor_copy", dbgv[:, 13:14], cmaskm[:, 1:2], sem=s_ve)
            p.op("vector", "tensor_copy", dbgv[:, 14:15], cmaska[:, 1:2], sem=s_ve)
            p.op("vector", "tensor_copy", dbgv[:, 15:16], carry[:, 0:1], sem=s_ve)
    ve_carry = s_ve.v

    ckk = [fv(KA, KA + TL), fv(KB, KB + TL)]
    cxw = [fv(XA, XA + TL), fv(XB, XB + TL)]
    css = [fv(SA2, SA2 + TL), fv(SB2, SB2 + TL)]
    cdd_ = fv(DA2, DA2 + TL)
    cvv = fv(CVV, CVV + TL)
    crr = fv(CRR, CRR + TL)
    cx3 = fv(CX3, CX3 + TL)
    cpb = fv(CP_, CP_ + TL)

    p.wait("sync", s_ve, ve_agg)
    p.wait("sync", s_sp, sp_proj_all)
    p.wait("sync", s_gp, gp_scan_all)
    p.wait("gpsimd", s_ve, ve_agg)
    p.wait("gpsimd", s_sp, sp_proj_all)
    p.wait("vector", s_sp, sp_proj_all)
    p.wait("vector", s_gp, gp_scan_all)
    cld = {}
    cld[("k", 0)] = p.dma("sync", ckk[0], k_sp[0:128, :], s_ld)
    cld[("v", 0)] = p.dma("sync", cvv, v_sp[0:128, :], s_ld)
    cld[("r", 0)] = p.dma("sync", crr, r_sp[0:128, :], s_ld)
    cld[("k", 1)] = p.dma("sync", ckk[1], k_sp[128:256, :], s_ld)
    cld[("d", 0)] = p.dma("gpsimd", cdd_, dd_sp[0:128, :], s_gp)
    for g0 in range(2):
        cld[("x", g0)] = p.dma("gpsimd", cxw[g0],
                               xw_sp[g0 * 128:(g0 + 1) * 128, :], s_gp)
        cld[("s", g0)] = p.dma("gpsimd", css[g0],
                               ss_sp[g0 * 128:(g0 + 1) * 128, :], s_gp)

    ve_rw_all = 0
    for g in range(NCH):
        kk_ = ckk[g % 2]
        xw_ = cxw[g % 2]
        ss_ = css[g % 2]
        dd_ = cdd_
        if g == 0:
            p.wait("vector", s_ve, ve_carry)
        p.wait("vector", s_gp, cld[("x", g)])
        p.op("vector", "tensor_scalar", cx3, iotav, w_sb[:, g:g + 1],
             carry[:, g:g + 1], ALU.mult, ALU.add)
        p.wait("vector", s_ld, cld[("k", g)])
        p.op("vector", "scalar_tensor_tensor", cpb, kk_, u_sb[:, g:g + 1],
             xw_, ALU.add, ALU.max)
        p.op("vector", "tensor_tensor", cpb, cpb, cx3, ALU.max)
        p.op("vector", "tensor_sub", xw_, xw_, cpb, sem=s_ve)
        v4 = s_ve.v
        p.op("vector", "tensor_sub", kk_, kk_, cpb, sem=s_ve)
        v5 = s_ve.v
        p.op("vector", "tensor_sub", cx3, cx3, cpb, sem=s_ve)
        v6 = s_ve.v
        p.wait("scalar", s_ve, v4)
        p.op("scalar", "activation", xw_, xw_, AF.Exp, sem=s_sc)
        e1d = s_sc.v
        p.wait("scalar", s_ve, v5)
        p.op("scalar", "activation", kk_, kk_, AF.Exp, bias=u_sb[:, g:g + 1],
             scale=1.0, sem=s_sc)
        e2d = s_sc.v
        p.wait("scalar", s_ve, v6)
        p.op("scalar", "activation", cx3, cx3, AF.Exp, sem=s_sc)
        e3d = s_sc.v
        p.wait("vector", s_sc, e1d)
        p.wait("vector", s_gp, cld[("s", g)])
        p.op("vector", "tensor_mul", ss_, ss_, xw_)
        p.wait("vector", s_gp, cld[("d", g)])
        p.op("vector", "tensor_mul", dd_, dd_, xw_, sem=s_ve)
        v11 = s_ve.v
        p.wait("vector", s_sc, e2d)
        p.wait("vector", s_ld, cld[("v", g)])
        p.op("vector", "tensor_mul", cvv, cvv, kk_)
        p.op("vector", "scalar_tensor_tensor", dd_, kk_, 1e-9, dd_,
             ALU.add, ALU.add, sem=s_ve)
        v12 = s_ve.v
        p.op("vector", "tensor_add", ss_, ss_, cvv, sem=s_ve)
        v9 = s_ve.v
        p.wait("vector", s_sc, e3d)
        p.op("vector", "scalar_tensor_tensor", dd_, cx3,
             carry[:, 2 * NCH + g:2 * NCH + g + 1], dd_, ALU.mult, ALU.add,
             sem=s_ve)
        v13 = s_ve.v
        p.op("vector", "scalar_tensor_tensor", ss_, cx3,
             carry[:, NCH + g:NCH + g + 1], ss_, ALU.mult, ALU.add)
        p.wait("scalar", s_ve, v13)
        p.op("scalar", "activation", dd_, dd_, AF.Ln)
        p.op("scalar", "activation", dd_, dd_, AF.Exp, scale=-1.0, sem=s_sc)
        recd = s_sc.v
        p.wait("vector", s_sc, recd)
        p.op("vector", "tensor_mul", ss_, ss_, dd_, sem=s_ve)
        v14 = s_ve.v
        p.wait("vector", s_ld, cld[("r", g)])
        p.op("vector", "tensor_mul", bv(RWO + g * TL, RWO + (g + 1) * TL),
             ss_, crr, sem=s_ve)
        ve_rw_all = s_ve.v

        if g + 1 < NCH:
            p.wait("sync", s_ve, v9)
            cld[("v", g + 1)] = p.dma("sync", cvv,
                                      v_sp[(g + 1) * 128:(g + 2) * 128, :], s_ld)
            p.wait("sync", s_ve, ve_rw_all)
            cld[("r", g + 1)] = p.dma("sync", crr,
                                      r_sp[(g + 1) * 128:(g + 2) * 128, :], s_ld)
            p.wait("gpsimd", s_ve, v14)
            cld[("d", g + 1)] = p.dma("gpsimd", cdd_,
                                      dd_sp[(g + 1) * 128:(g + 2) * 128, :], s_gp)
        if g + 2 < NCH:
            p.wait("sync", s_ve, v12)
            cld[("k", g + 2)] = p.dma("sync", ckk[g % 2],
                                      k_sp[(g + 2) * 128:(g + 3) * 128, :], s_ld)
            p.wait("gpsimd", s_ve, v11)
            cld[("x", g + 2)] = p.dma("gpsimd", cxw[g % 2],
                                      xw_sp[(g + 2) * 128:(g + 3) * 128, :], s_gp)
            p.wait("gpsimd", s_ve, ve_rw_all)
            cld[("s", g + 2)] = p.dma("gpsimd", css[g % 2],
                                      ss_sp[(g + 2) * 128:(g + 3) * 128, :], s_gp)

    if DEBUG:
        d_rw = nc.dram_tensor("d_rw", [128, NCH * TL], BF16, kind="ExternalOutput").ap()
        d_carry = nc.dram_tensor("d_carry", [128, 3 * NCH], F32, kind="ExternalOutput").ap()
        d_agg = nc.dram_tensor("d_agg", [128, 3 * NCH], F32, kind="ExternalOutput").ap()
        d_aggall = nc.dram_tensor("d_aggall", [128 * NCORES, 3 * NCH], F32, kind="ExternalOutput").ap()
        p.wait("sync", s_ve, ve_rw_all)
        p.dma("sync", d_rw, bv(RWO, RWO + NCH * TL), s_sp)
        p.dma("sync", d_carry, carry[:], s_sp)
        p.dma("sync", d_agg, agg[:], s_sp)
        p.dma("sync", d_dbgv, dbgv[:], s_sp)
        p.dma("sync", d_aggall, cc1_out, s_sp)

    c1gC = fv(0, H)
    youtC = fv(H, 2 * H)
    x1Tc = bv(X1TC, X1TC + H)
    if DEBUG:
        d_y1 = nc.dram_tensor("d_y1", [TL, H], F32, kind="ExternalOutput").ap()
        dbg2 = fv(3 * H, 3 * H + 16)
        d_dbg2 = nc.dram_tensor("d_dbg2", [128, 16], F32, kind="ExternalOutput").ap()
    p.wait("tensor", s_ve, ve_rw_all)
    p.wait("tensor", s_ld, ld_wT)
    p.wait("tensor", s_gp, gp_small)
    p.wait("vector", s_gp, sp_zero_ye)
    sp_x1_last = 0
    logit_mm_done = 0
    tpc_prev = 0
    for c in range(NTC):
        ldx = p.dma("sync", xck[:], x_pad[1 + c * 128: 1 + (c + 1) * 128, :], s_ld)
        for oh in range(2):
            bk = oh
            bank_wait(bk)
            for g in range(NCH):
                p.op("tensor", "matmul", pbank[bk][:],
                     bv(RWO + g * TL + c * 128, RWO + g * TL + (c + 1) * 128),
                     bv(WT[3] + g * H + oh * 512, WT[3] + g * H + (oh + 1) * 512),
                     start=(g == 0), stop=(g == NCH - 1),
                     sem=(s_pe if g == NCH - 1 else None))
            p.wait("vector", s_pe, s_pe.v)
            p.wait("vector", s_ld, ldx)
            p.wait("vector", s_sp, sp_x1_last)
            tgt = p.op("vector", "tensor_add", fv(H + oh * 512, H + (oh + 1) * 512),
                       pbank[bk][:], xck[:, oh * 512:(oh + 1) * 512], sem=s_ve)
            bank_set(bk, s_ve, tgt)
        if DEBUG:
            p.wait("sync", s_ve, s_ve.v)
            p.dma("sync", d_y1[c * 128:(c + 1) * 128, :], youtC, s_sp)
        y = youtC
        p.op("vector", "reduce_sum", rcol1[:], y, axis=AX, negate=True)
        p.nop("vector")
        p.op("vector", "tensor_scalar", rcol1[:], rcol1[:], 1.0 / H, None, ALU.mult)
        p.nop("vector")
        p.wait("vector", s_pe, tpc_prev)
        p.op("vector", "tensor_scalar", c1gC, y, rcol1[:, 0:1], None, ALU.add, sem=s_ve)
        p.wait("scalar", s_ve, s_ve.v)
        p.op("scalar", "activation", fv(2 * H, 3 * H), c1gC, AF.Square,
             accum_out=rcol2[:, 0:1], sem=s_sc)
        p.wait("vector", s_sc, s_sc.v)
        p.op("vector", "tensor_scalar", rcol2[:], rcol2[:], 1.0 / H, 1e-5,
             ALU.mult, ALU.add, sem=s_ve)
        p.wait("scalar", s_ve, s_ve.v)
        p.op("scalar", "activation", rcol3[:], rcol2[:], AF.Ln)
        p.nop("scalar")
        p.op("scalar", "activation", rcol3[:], rcol3[:], AF.Exp, scale=-0.5, sem=s_sc)
        p.wait("vector", s_sc, s_sc.v)
        p.op("vector", "tensor_scalar", c1gC, c1gC, rcol3[:, 0:1], None, ALU.mult)
        p.op("vector", "tensor_copy", bv(X1B + c * H, X1B + (c + 1) * H), c1gC, sem=s_ve)
        ve_c1 = s_ve.v
        p.wait("sync", s_ve, ve_c1)
        sp_x1_last = p.dma("sync", x1_sp[c * 128:(c + 1) * 128, :], c1gC, s_sp)

        p.wait("tensor", s_ve, ve_c1)
        for g in range(NCH):
            bk = 4 + (g % 2)
            bank_wait(bk)
            p.op("tensor", "transpose", pbank[bk][:, 0:128],
                 c1gC[:, g * 128:(g + 1) * 128], ident[:], sem=s_pe)
            p.wait("scalar", s_pe, s_pe.v)
            p.wait("scalar", s_pe, logit_mm_done)
            tgt = p.op("scalar", "copy", x1Tc[:, g * 128:(g + 1) * 128],
                       pbank[bk][:, 0:128], sem=s_sc)
            bank_set(bk, s_sc, tgt)
        tpc_prev = s_pe.v
        bank_wait(6)
        p.wait("tensor", s_sc, s_sc.v)
        for g in range(NCH):
            p.op("tensor", "matmul", pbank[6][:, 0:E],
                 x1Tc[:, g * 128:(g + 1) * 128], wgb[:, g * E:(g + 1) * E],
                 start=(g == 0), stop=(g == NCH - 1),
                 sem=(s_pe if g == NCH - 1 else None))
        logit_mm_done = s_pe.v
        p.wait("vector", s_pe, s_pe.v)
        tgt = p.op("vector", "tensor_copy", logit[:, c * E:(c + 1) * E],
                   pbank[6][:, 0:E], sem=s_ve)
        bank_set(6, s_ve, tgt)

        L = logit[:, c * E:(c + 1) * E]
        p.op("vector", "reduce_max", rcol1[:], L, axis=AX, negate=True)
        p.nop("vector")
        p.op("vector", "tensor_scalar", rsc1[:], L, rcol1[:, 0:1], 0.0, ALU.add, ALU.is_ge)
        p.nop("vector")
        p.op("vector", "scalar_tensor_tensor", rsc2[:], rsc1[:], 16384.0, iota8p[:],
             ALU.mult, ALU.subtract)
        p.nop("vector")
        p.op("vector", "reduce_max", rcol2[:], rsc2[:], axis=AX)
        p.nop("vector")
        p.op("vector", "tensor_scalar", mask1[:, c * E:(c + 1) * E], iota8[:],
             rcol2[:, 0:1], 0.0, ALU.add, ALU.is_equal)
        if DEBUG and c == 0:
            p.op("vector", "tensor_copy", dbg2[:, 0:8], rsc1[:], sem=s_ve)
            p.op("vector", "tensor_copy", dbg2[:, 8:9], rcol1[:], sem=s_ve)
            p.op("vector", "tensor_copy", dbg2[:, 9:10], rcol2[:], sem=s_ve)
        p.nop("vector")
        p.op("vector", "scalar_tensor_tensor", rsc3[:], mask1[:, c * E:(c + 1) * E],
             -3e38, L, ALU.mult, ALU.add)
        p.nop("vector")
        p.op("vector", "reduce_max", rcol3[:], rsc3[:], axis=AX, negate=True)
        p.nop("vector")
        p.op("vector", "tensor_scalar", rsc1[:], rsc3[:], rcol3[:, 0:1], 0.0,
             ALU.add, ALU.is_ge)
        p.nop("vector")
        p.op("vector", "scalar_tensor_tensor", rsc2[:], rsc1[:], 16384.0, iota8p[:],
             ALU.mult, ALU.subtract)
        p.nop("vector")
        p.op("vector", "reduce_max", rcol2[:], rsc2[:], axis=AX)
        p.nop("vector")
        p.op("vector", "tensor_scalar", mask2[:, c * E:(c + 1) * E], iota8[:],
             rcol2[:, 0:1], 0.0, ALU.add, ALU.is_equal, sem=s_ve)
        p.wait("scalar", s_ve, s_ve.v)
        p.op("scalar", "activation", rsc3[:], L, AF.Exp, bias=rcol1[:, 0:1], scale=1.0,
             accum_out=rcol3[:, 0:1], sem=s_sc)
        p.wait("vector", s_sc, s_sc.v)
        p.op("vector", "tensor_mul", rsc1[:], rsc3[:], mask1[:, c * E:(c + 1) * E])
        p.nop("vector")
        p.op("vector", "reduce_sum", rcol2[:], rsc1[:], axis=AX)
        p.nop("vector")
        p.op("vector", "tensor_mul", rsc1[:], rsc3[:], mask2[:, c * E:(c + 1) * E])
        p.nop("vector")
        p.op("vector", "reduce_sum", rcol1[:], rsc1[:], axis=AX)
        p.nop("vector")
        p.op("vector", "tensor_add", rsc2[:, 0:1], rcol2[:], rcol1[:])
        p.op("vector", "tensor_scalar", rsc2[:, 1:2], rcol3[:, 0:1], 1e-9, None, ALU.mult)
        p.nop("vector")
        p.op("vector", "tensor_add", rsc2[:, 0:1], rsc2[:, 0:1], rsc2[:, 1:2])
        p.nop("vector")
        p.op("vector", "reciprocal", rsc2[:, 2:3], rsc2[:, 0:1])
        p.nop("vector")
        p.op("vector", "tensor_mul", w1k[:, c:c + 1], rcol2[:], rsc2[:, 2:3])
        p.op("vector", "tensor_mul", w2k[:, c:c + 1], rcol1[:], rsc2[:, 2:3], sem=s_ve)
    sp_x1 = s_sp.v
    ve_x1b = s_ve.v
    ve_route1 = s_ve.v

    p.wait("tensor", s_ve, ve_route1)
    bank_wait(7)
    for c in range(NTC):
        p.op("tensor", "matmul", pbank[7][0:1, 0:E], ones_c[:],
             mask1[:, c * E:(c + 1) * E], start=(c == 0), stop=(c == NTC - 1),
             sem=(s_pe if c == NTC - 1 else None))
    bank_wait(6)
    for c in range(NTC):
        p.op("tensor", "matmul", pbank[6][0:1, 0:E], ones_c[:],
             mask2[:, c * E:(c + 1) * E], start=(c == 0), stop=(c == NTC - 1),
             sem=(s_pe if c == NTC - 1 else None))
    p.wait("vector", s_pe, s_pe.v)
    p.op("vector", "tensor_copy", sm1[0:1, 0:E], pbank[7][0:1, 0:E])
    tgt = p.op("vector", "tensor_copy", sm1[0:1, E:2 * E], pbank[6][0:1, 0:E], sem=s_ve)
    bank_set(6, s_ve, tgt)
    bank_set(7, s_ve, tgt)
    p.wait("sync", s_ve, s_ve.v)
    cc2_pub = p.dma("sync", cc2_in, sm1[0:1, 0:2 * E], s_sp)
    p.wait("gpsimd", s_sp, cc2_pub)
    p.op("gpsimd", "collective_compute", "AllGather", ALU.bypass,
         replica_groups=[list(range(NCORES))],
         ins=[cc2_in.opt()], outs=[cc2_out.opt()], sem=s_cc)
    cc2_done = s_cc.v
    p.wait("gpsimd", s_cc, cc2_done)
    ld_cc2 = p.dma("gpsimd", cc2_sb[:], cc2_out, s_gp)
    p.wait("tensor", s_gp, ld_cc2)
    bank_wait(7)
    p.op("tensor", "matmul", pbank[7][0:1, 0:2 * E], premk[:], cc2_sb[:],
         start=True, stop=True)
    bank_wait(5)
    p.op("tensor", "matmul", pbank[5][0:1, 0:E], ones_c[0:E, 0:1],
         cc2_sb[:, 0:E], start=True, stop=True, sem=s_pe)
    p.wait("vector", s_pe, s_pe.v)
    p.op("vector", "tensor_copy", offs[:], pbank[7][0:1, 0:2 * E])
    tgt = p.op("vector", "tensor_copy", offg[:], pbank[5][0:1, 0:E], sem=s_ve)
    bank_set(5, s_ve, tgt)
    bank_set(7, s_ve, tgt)
    ve_offs = s_ve.v

    def cumsum_pass(msk, possb, keepsb, locol):
        for c in range(NTC):
            bank_wait(5)
            p.wait("tensor", s_ve, s_ve.v)
            p.op("tensor", "matmul", pbank[5][:, 0:E], triu[:],
                 msk[:, c * E:(c + 1) * E], start=True, stop=False)
            p.op("tensor", "matmul", pbank[5][:, 0:E], ones_r1[:], locol,
                 start=False, stop=True, sem=s_pe)
            p.wait("vector", s_pe, s_pe.v)
            tgt_ = p.op("vector", "tensor_mul", rsc1[:], pbank[5][:, 0:E],
                        msk[:, c * E:(c + 1) * E], sem=s_ve)
            bank_set(5, s_ve, tgt_)
            p.nop("vector")
            p.op("vector", "reduce_sum", possb[:, c:c + 1], rsc1[:], axis=AX)
            if keepsb is not None:
                p.nop("vector")
                p.op("vector", "tensor_scalar", keepsb[:, c:c + 1], possb[:, c:c + 1],
                     float(CGLOB), 0.0, ALU.subtract, ALU.is_lt)
            bank_wait(6)
            p.op("tensor", "matmul", pbank[6][0:1, 0:E], ones_c[:],
                 msk[:, c * E:(c + 1) * E], start=True, stop=True, sem=s_pe)
            p.wait("vector", s_pe, s_pe.v)
            tgt_ = p.op("vector", "tensor_add", locol, locol, pbank[6][0:1, 0:E],
                        sem=s_ve)
            bank_set(6, s_ve, tgt_)

    p.wait("vector", s_ve, ve_offs)
    p.op("vector", "tensor_copy", loco[0:1, 0:E], offs[0:1, 0:E], sem=s_ve)
    cumsum_pass(mask1, pos1, keep1, loco[0:1, 0:E])
    p.op("vector", "tensor_add", loco[0:1, E:2 * E], offs[0:1, E:2 * E],
         offg[0:1, 0:E], sem=s_ve)
    cumsum_pass(mask2, pos2, keep2, loco[0:1, E:2 * E])

    for c in range(NTC):
        p.op("vector", "tensor_scalar", kept1[:, c * E:(c + 1) * E],
             mask1[:, c * E:(c + 1) * E], keep1[:, c:c + 1], None, ALU.mult)
        p.op("vector", "tensor_scalar", kept2[:, c * E:(c + 1) * E],
             mask2[:, c * E:(c + 1) * E], keep2[:, c:c + 1], None, ALU.mult)
        p.op("vector", "tensor_mul", w1k[:, c:c + 1], w1k[:, c:c + 1], keep1[:, c:c + 1])
        p.op("vector", "tensor_mul", w2k[:, c:c + 1], w2k[:, c:c + 1], keep2[:, c:c + 1],
             sem=s_ve)

    p.op("vector", "memset", loco[0:1, 0:E], 0.0, sem=s_ve)
    cumsum_pass(kept1, slot1, None, loco[0:1, 0:E])
    p.op("vector", "tensor_copy", loco[0:1, E:2 * E], loco[0:1, 0:E], sem=s_ve)
    cumsum_pass(kept2, slot2, None, loco[0:1, E:2 * E])
    p.op("vector", "tensor_copy", cnt_i32[0:1, 0:E], loco[0:1, E:2 * E], sem=s_ve)

    for c in range(NTC):
        for (msk, kp, slotsb, dstsi) in ((mask1, keep1, slot1, dst1i),
                                         (mask2, keep2, slot2, dst2i)):
            p.op("vector", "tensor_mul", rsc1[:], iota8[:], msk[:, c * E:(c + 1) * E])
            p.nop("vector")
            p.op("vector", "reduce_sum", rcol1[:], rsc1[:], axis=AX)
            p.op("vector", "tensor_scalar", rcol2[:], slotsb[:, c:c + 1],
                 float(CL), 0.0, ALU.subtract, ALU.is_lt)
            p.nop("vector")
            p.op("vector", "tensor_mul", rcol2[:], rcol2[:], kp[:, c:c + 1])
            p.op("vector", "tensor_scalar", rcol1[:], rcol1[:], float(CL), None, ALU.mult)
            p.nop("vector")
            p.op("vector", "tensor_tensor", rcol1[:], rcol1[:], slotsb[:, c:c + 1], ALU.add)
            p.nop("vector")
            p.op("vector", "tensor_scalar", rcol1[:], rcol1[:], float(TRASH), None,
                 ALU.subtract)
            p.nop("vector")
            p.op("vector", "tensor_mul", rcol1[:], rcol1[:], rcol2[:])
            p.nop("vector")
            p.op("vector", "tensor_scalar", rcol3[:], rcol1[:], float(TRASH), None, ALU.add)
            p.nop("vector")
            p.op("vector", "tensor_copy", dstsi[:, c:c + 1], rcol3[:], sem=s_ve)
    ve_dests = s_ve.v

    if DEBUG:
        d_rt = nc.dram_tensor("d_rt", [128, 6 * NTC], F32, kind="ExternalOutput").ap()
        d_rti = nc.dram_tensor("d_rti", [128, 2 * NTC], I32, kind="ExternalOutput").ap()
        d_msk = nc.dram_tensor("d_msk", [128, 3 * NTC * E], F32, kind="ExternalOutput").ap()
        p.wait("sync", s_ve, ve_dests)
        p.dma("sync", d_rt[:, 0 * NTC:1 * NTC], pos1[:], s_sp)
        p.dma("sync", d_rt[:, 1 * NTC:2 * NTC], pos2[:], s_sp)
        p.dma("sync", d_rt[:, 2 * NTC:3 * NTC], slot1[:], s_sp)
        p.dma("sync", d_rt[:, 3 * NTC:4 * NTC], slot2[:], s_sp)
        p.dma("sync", d_rt[:, 4 * NTC:5 * NTC], w1k[:], s_sp)
        p.dma("sync", d_rt[:, 5 * NTC:6 * NTC], w2k[:], s_sp)
        p.dma("sync", d_rti[:, 0:NTC], dst1i[:], s_sp)
        p.dma("sync", d_rti[:, NTC:2 * NTC], dst2i[:], s_sp)
        p.dma("sync", d_msk[:, 0:NTC * E], logit[:], s_sp)
        p.dma("sync", d_msk[:, NTC * E:2 * NTC * E], mask1[:], s_sp)
        p.dma("sync", d_msk[:, 2 * NTC * E:3 * NTC * E], mask2[:], s_sp)
        p.dma("sync", d_dbg2, dbg2, s_sp)

    p.wait("gpsimd", s_ve, ve_dests)
    p.wait("gpsimd", s_sp, sp_zero)
    for c in range(NTC):
        for dsts in (dst1i, dst2i):
            p.op("gpsimd", "indirect_dma_start",
                 out=xe_d, out_offset=IndirectOffsetOnAxis(dsts[:, c:c + 1], 0),
                 in_=bv(X1B + c * H, X1B + (c + 1) * H), in_offset=None,
                 sem=s_ix, inc=16)
    ix_scatter = s_ix.v

    if DEBUG:
        d_h1 = [nc.dram_tensor(f"d_h1{i}", [128, 16 * CL], BF16, kind="ExternalOutput").ap()
                for i in range(2)]
        d_yep = nc.dram_tensor("d_yep", [128, E * H], F32, kind="ExternalOutput").ap()
    yev = fv(0, 2 * H)
    ye_part = fv(2 * H, 2 * H + E * H)
    b2e = ARB[0:1, B2O:B2O + H]
    NHF = NFC // 2
    pe_w1_done = {}
    pe_w2_done = {}
    sp_ye = {}
    w1sl_guard = {}
    rtv = {}
    p.wait("tensor", s_ix, ix_scatter)
    for e in range(E):
        p.wait("sync", s_ix, ix_scatter)
        if e >= 1:
            p.wait("sync", s_pe, pe_w1_done[e - 1])
        for gg in range(NCH):
            p.dmat("sync", bv(XEO + gg * CL, XEO + (gg + 1) * CL),
                   xe_d[e * CL:(e + 1) * CL, gg * 128:(gg + 1) * 128], s_ld)
        ld_xeT = s_ld.v

        def fload(e_, ee=e):
            if not FFN_SKIP:
                return
            _, (v_,) = nc.values_load_multi_w_load_instructions(
                cnt_i32[0:1, ee:ee + 1], engines=[mybir.EngineType.PE],
                min_val=0, max_val=2 * TL, skip_runtime_bounds_check=True)
            rtv[ee] = v_
        p.emit("tensor", fload)

        for hf in range(2):
            if e + hf >= 1:
                p.wait("gpsimd", s_pe, pe_w2_done.get((e, hf - 1),
                                                      pe_w2_done.get((e - 1, 1), 0)))
            if hf == 0:
                p.dma("gpsimd", b2e, b2[e:e + 1, :], s_gp)
                gp_b2e = s_gp.v
            f0 = hf * (FF // 2)

            if e + hf >= 1:
                p.wait("scalar", s_pe, pe_w2_done.get((e, hf - 1),
                                                      pe_w2_done.get((e - 1, 1), 0)))
            for fs in range(4):
                buf = fs % 2
                p.wait("gpsimd", s_pe, w1sl_guard.get(buf, 0))
                p.dma("gpsimd",
                      bv(W1O + buf * NCH * 512, W1O + (buf + 1) * NCH * 512).rearrange(
                          "p (t f) -> p t f", f=512),
                      W1[e, :, f0 + fs * 512: f0 + fs * 512 + 512].rearrange(
                          "(t p) f -> p t f", p=128),
                      s_gp)
                gp_w1 = s_gp.v
                p.wait("tensor", s_gp, gp_w1)
                p.wait("tensor", s_ld, ld_xeT)
                for fc in range(4):
                    fcl = fs * 4 + fc
                    fcg = hf * NHF + fcl
                    for sh in range(2):
                        bk = sh
                        bank_wait(bk)
                        s_pe.v += 1

                        def w1body(e_, buf=buf, fc=fc, sh=sh, bk=bk,
                                   inc=(not FFN_SKIP or not FFN_SKIP_W1)):
                            for dt in range(NCH):
                                mm = e_.matmul(
                                    pbank[bk][:],
                                    bv(W1O + buf * NCH * 512 + dt * 512 + fc * 128,
                                       W1O + buf * NCH * 512 + dt * 512 + (fc + 1) * 128),
                                    bv(XEO + dt * CL + sh * 512,
                                       XEO + dt * CL + (sh + 1) * 512),
                                    start=(dt == 0), stop=(dt == NCH - 1))
                            if inc:
                                mm.then_inc(s_pe.h, 1)

                        def w1grp(e_, body=w1body, sh=sh, ee=e):
                            if not (FFN_SKIP and FFN_SKIP_W1):
                                body(e_)
                                return
                            with e_.If(rtv[ee] > sh * 512):
                                body(e_)
                            e_.matmul(pbank[7][:, 0:512], ones_r1b[0:1, 0:128],
                                      b2e[0:1, 0:512], start=True,
                                      stop=True).then_inc(s_pe.h, 1)
                        p.emit("tensor", w1grp)
                        p.wait("scalar", s_pe, s_pe.v)
                        tgt = p.op("scalar", "activation",
                                   bv(H1O + fcl * CL + sh * 512,
                                      H1O + fcl * CL + (sh + 1) * 512),
                                   pbank[bk][:], AF.Gelu,
                                   bias=b1_sb[:, e * NFC + fcg: e * NFC + fcg + 1],
                                   scale=1.0, sem=s_sc)
                        bank_set(bk, s_sc, tgt)
                w1sl_guard[buf] = s_pe.v
            sc_h1_done = s_sc.v
            if DEBUG and e == 0:
                p.wait("sync", s_sc, sc_h1_done)
                p.dma("sync", d_h1[hf], bv(H1O, H1O + 16 * CL), s_sp)

            for piece in range(2):
                q0 = f0 + piece * (FF // 4)
                p.dma("gpsimd",
                      bv(W2O + piece * 8 * H, W2O + (piece + 1) * 8 * H).rearrange(
                          "p (t d) -> p t d", d=H),
                      W2[e, q0:q0 + FF // 4, :].rearrange("(t p) d -> p t d", p=128),
                      s_gp)
            gp_w2 = s_gp.v
            if DEBUG and e == 0 and hf == 1:
                p.wait("sync", s_sc, s_sc.v)
                p.dma("sync", d_yep, ye_part, s_sp)
            p.wait("tensor", s_gp, gp_w2)
            if hf == 1:
                p.wait("tensor", s_gp, gp_b2e)
            p.wait("tensor", s_sc, sc_h1_done)
            for sc8 in range(8):
                ybuf = sc8 % 2
                for dh in range(2):
                    bk = 2 + ((sc8 * 2 + dh) % 4)
                    bank_wait(bk)
                    s_pe.v += 1

                    def w2body(e_, sc8=sc8, dh=dh, bk=bk, hf=hf):
                        for fcl in range(NHF):
                            last = (hf == 0 and fcl == NHF - 1)
                            mm = e_.matmul(
                                pbank[bk][:],
                                bv(H1O + fcl * CL + sc8 * 128,
                                   H1O + fcl * CL + (sc8 + 1) * 128),
                                bv(W2O + fcl * H + dh * 512,
                                   W2O + fcl * H + (dh + 1) * 512),
                                start=(fcl == 0), stop=last)
                        if hf == 0 and not FFN_SKIP:
                            mm.then_inc(s_pe.h, 1)

                    def w2grp(e_, body=w2body, sc8=sc8, dh=dh, bk=bk,
                              hf=hf, ee=e):
                        if not FFN_SKIP:
                            body(e_)
                        else:
                            with e_.If(rtv[ee] > sc8 * 128):
                                body(e_)
                        if hf == 1:
                            e_.matmul(
                                pbank[bk][:], ones_r1b[0:1, 0:128],
                                b2e[0:1, dh * 512:(dh + 1) * 512],
                                start=False, stop=True).then_inc(s_pe.h, 1)
                        elif FFN_SKIP:
                            e_.matmul(pbank[6][:, 0:512], ones_r1b[0:1, 0:128],
                                      b2e[0:1, 0:512], start=True,
                                      stop=True).then_inc(s_pe.h, 1)
                    p.emit("tensor", w2grp)
                    if hf == 0:
                        p.wait("scalar", s_pe, s_pe.v)
                        tgt = p.op("scalar", "copy",
                                   fv(2 * H + sc8 * H + dh * 512,
                                      2 * H + sc8 * H + (dh + 1) * 512),
                                   pbank[bk][:], sem=s_sc)
                        bank_set(bk, s_sc, tgt)
                    else:
                        p.wait("vector", s_pe, s_pe.v)
                        p.wait("vector", s_sp, sp_ye.get((e, sc8 - 2),
                                                         sp_ye.get((e - 1, 6 + sc8), 0)))
                        tgt = p.op("vector", "tensor_add",
                                   fv(ybuf * H + dh * 512, ybuf * H + (dh + 1) * 512),
                                   pbank[bk][:],
                                   fv(2 * H + sc8 * H + dh * 512,
                                      2 * H + sc8 * H + (dh + 1) * 512), sem=s_ve)
                        bank_set(bk, s_ve, tgt)
                if hf == 1:
                    p.wait("sync", s_ve, s_ve.v)
                    sp_ye[(e, sc8)] = p.dma(
                        "sync", ye_d[e * CL + sc8 * 128: e * CL + (sc8 + 1) * 128, :],
                        fv(ybuf * H, (ybuf + 1) * H), s_sp)
            pe_w2_done[(e, hf)] = s_pe.v
        pe_w1_done[e] = s_pe.v
    sp_ye_all = s_sp.v

    ot_done = {}
    ve_e_last = {}
    for c in range(NTC):
        half = c % 2
        if c == 0:
            p.wait("gpsimd", s_sp, sp_ye_all)
        if c >= 2:
            p.wait("gpsimd", s_ot, ot_done[c - 2])
        p.op("gpsimd", "indirect_dma_start",
             out=fv(half * H, (half + 1) * H), out_offset=None,
             in_=ye_d, in_offset=IndirectOffsetOnAxis(dst1i[:, c:c + 1], 0),
             sem=s_ix, inc=16)
        p.op("gpsimd", "indirect_dma_start",
             out=fv(2 * H + half * H, 2 * H + (half + 1) * H), out_offset=None,
             in_=ye_d, in_offset=IndirectOffsetOnAxis(dst2i[:, c:c + 1], 0),
             sem=s_ix, inc=16)
        ixg = s_ix.v
        if c >= 2:
            p.wait("sync", s_ot, ot_done[c - 2])
        ldx1 = p.dma("sync", fv(4 * H + half * H, 4 * H + (half + 1) * H),
                     x1_sp[c * 128:(c + 1) * 128, :], s_ld)
        p.wait("vector", s_ix, ixg)
        p.wait("vector", s_ld, ldx1)
        yy = fv(6 * H + half * H, 6 * H + (half + 1) * H)
        p.op("vector", "tensor_scalar", yy, fv(half * H, (half + 1) * H),
             w1k[:, c:c + 1], None, ALU.mult)
        p.op("vector", "scalar_tensor_tensor", yy, fv(2 * H + half * H, 2 * H + (half + 1) * H),
             w2k[:, c:c + 1], yy, ALU.mult, ALU.add)
        p.op("vector", "tensor_add", yy, yy, fv(4 * H + half * H, 4 * H + (half + 1) * H))
        p.op("vector", "reduce_sum", rcol1[:], yy, axis=AX, negate=True)
        p.nop("vector")
        p.op("vector", "tensor_scalar", rcol1[:], rcol1[:], 1.0 / H, None, ALU.mult)
        p.nop("vector")
        p.op("vector", "tensor_scalar", yy, yy, rcol1[:, 0:1], None, ALU.add, sem=s_ve)
        p.wait("scalar", s_ve, s_ve.v)
        p.op("scalar", "activation", fv(8 * H, 9 * H), yy, AF.Square,
             accum_out=rcol2[:, 0:1], sem=s_sc)
        p.wait("vector", s_sc, s_sc.v)
        p.op("vector", "tensor_scalar", rcol2[:], rcol2[:], 1.0 / H, 1e-5,
             ALU.mult, ALU.add, sem=s_ve)
        p.wait("scalar", s_ve, s_ve.v)
        p.op("scalar", "activation", rcol3[:], rcol2[:], AF.Ln)
        p.nop("scalar")
        p.op("scalar", "activation", rcol3[:], rcol3[:], AF.Exp, scale=-0.5, sem=s_sc)
        p.wait("vector", s_sc, s_sc.v)
        p.op("vector", "tensor_scalar", yy, yy, rcol3[:, 0:1], None, ALU.mult, sem=s_ve)
        ve_e_last[c] = s_ve.v
        p.wait("sync", s_ve, s_ve.v)
        ot_done[c] = p.dma("sync", out[c * 128:(c + 1) * 128, :], yy, s_ot)
    p.wait("sync", s_ot, s_ot.v)

    with nc.Block() as block:
        block.sync(lambda e: [fn(e) for fn in p.ops["sync"]])
        block.gpsimd(lambda e: [fn(e) for fn in p.ops["gpsimd"]])
        block.vector(lambda e: [fn(e) for fn in p.ops["vector"]])
        block.scalar(lambda e: [fn(e) for fn in p.ops["scalar"]])
        block.tensor(lambda e: [fn(e) for fn in p.ops["tensor"]])
    ctx.close()
    return nc


_BUILT = None


def _get_built():
    global _BUILT
    if _BUILT is None:
        _BUILT = build_nc()
    return _BUILT


def _host_inputs(inputs):
    x = np.ascontiguousarray(np.asarray(inputs["L0_attn_x"], np.float32).reshape(B * T, H))
    tmix = np.ascontiguousarray(np.stack([
        np.asarray(inputs["L0_attn_time_mix_k"], np.float32).reshape(H),
        np.asarray(inputs["L0_attn_time_mix_v"], np.float32).reshape(H),
        np.asarray(inputs["L0_attn_time_mix_r"], np.float32).reshape(H),
    ]))
    common = {
        "Wk": np.ascontiguousarray(np.asarray(inputs["L0_attn_W_k"], np.float32)),
        "Wv": np.ascontiguousarray(np.asarray(inputs["L0_attn_W_v"], np.float32)),
        "Wr": np.ascontiguousarray(np.asarray(inputs["L0_attn_W_r"], np.float32)),
        "Wo": np.ascontiguousarray(np.asarray(inputs["L0_attn_W_o"], np.float32)),
        "tdecay": np.ascontiguousarray(np.asarray(inputs["L0_attn_time_decay"], np.float32)),
        "tfirst": np.ascontiguousarray(np.asarray(inputs["L0_attn_time_first"], np.float32)),
        "tmix": tmix,
        "Wg": np.ascontiguousarray(np.asarray(inputs["L0_ffn_Wg"], np.float32)),
        "W1": np.ascontiguousarray(np.asarray(inputs["L0_ffn_W1"], np.float32)),
        "b1": np.ascontiguousarray(np.asarray(inputs["L0_ffn_b1"], np.float32)),
        "W2": np.ascontiguousarray(np.asarray(inputs["L0_ffn_W2"], np.float32)),
        "b2": np.ascontiguousarray(np.asarray(inputs["L0_ffn_b2"], np.float32)),
    }
    in_maps = []
    for c in range(NCORES):
        t0 = c * TL
        if (t0 % T) == 0:
            xp = np.vstack([np.zeros((1, H), np.float32), x[t0:t0 + TL]])
        else:
            xp = x[t0 - 1:t0 + TL]
        mm = np.zeros((128, NCORES), np.float32)
        ma = np.full((128, NCORES), NEG, np.float32)
        pm = np.zeros((NCORES, 1), np.float32)
        for cp in range(c - (c % 4), c):
            mm[:, cp] = 1.0
            ma[:, cp] = 0.0
        for cp in range(c):
            pm[cp, 0] = 1.0
        m = dict(common)
        m["x_pad"] = np.ascontiguousarray(xp)
        m["cmask_mul"] = mm
        m["cmask_add"] = ma
        m["premask"] = pm
        in_maps.append(m)
    return in_maps


def kernel(**inputs):
    nc = _get_built()
    in_maps = _host_inputs(inputs)
    res = run_bass_kernel_spmd(nc, in_maps, core_ids=list(range(NCORES)))
    outs = [res.results[c]["y"] for c in range(NCORES)]
    return np.concatenate(outs, axis=0).reshape(B, T, H).astype(np.float32)




# version 19 (final)
# speedup vs baseline: 1.6074x; 1.0255x over previous
import numpy as np

import concourse.bass as bass
import concourse.mybir as mybir
from concourse.bass_utils import run_bass_kernel_spmd
from concourse.bass_core_rust import IndirectOffsetOnAxis

F32 = mybir.dt.float32
BF16 = mybir.dt.bfloat16
I32 = mybir.dt.int32
AF = mybir.ActivationFunctionType
ALU = mybir.AluOpType
AX = mybir.AxisListType.X

NCORES = 8
B, T, H = 2, 8192, 1024
TL = 2048
NTC = TL // 128
NCH = H // 128
E = 8
FF = 4096
NFC = FF // 128
CGLOB = 8192
CL = 1024
NEG = -1e38
TRASH = E * CL


class SemCtr:
    def __init__(self, handle, name):
        self.h = handle
        self.v = 0
        self.name = name


class Plan:
    def __init__(self):
        self.ops = {k: [] for k in ("sync", "gpsimd", "vector", "scalar", "tensor")}
        self.hist = {}

    def emit(self, eng, fn):
        self.ops[eng].append(fn)

    def wait(self, eng, sem, val=None):
        if val is None:
            val = sem.v
        assert val <= sem.v, f"wait on future value of {sem.name}: {val} > {sem.v}"
        if val > 0:
            self.emit(eng, lambda e, s=sem.h, v=val: e.wait_ge(s, v))

    def dma(self, eng, out, in_, sem, **kw):
        sem.v += 16
        self.emit(eng, lambda e, o=out, i=in_, s=sem.h, kw=kw:
                  e.dma_start(out=o, in_=i, **kw).then_inc(s, 16))
        return sem.v

    def dmat(self, eng, out, in_, sem):
        sem.v += 16
        self.emit(eng, lambda e, o=out, i=in_, s=sem.h:
                  e.dma_start_transpose(o, i).then_inc(s, 16))
        return sem.v

    scalar_nop_fn = None

    def nop(self, eng):
        if eng == "scalar":
            self.emit(eng, self.scalar_nop_fn)
        else:
            self.emit(eng, lambda e: e.engine_nop())

    @staticmethod
    def _foot(a):
        try:
            t = a.tensor.name
            off = a.offset
            dims = list(a.ap)
        except Exception:
            return None
        if not dims:
            return (t, 0, 1, off, off + 1, 1)
        pitch, np_ = dims[0]
        if pitch <= 0:
            pitch = 1
        p0 = off // pitch
        c0 = off % pitch
        span = sum(st * (ct - 1) for st, ct in dims[1:]) + 1
        return (t, p0, p0 + np_, c0, c0 + span, span)

    @staticmethod
    def _ovl(f1, f2):
        return (f1[0] == f2[0] and f1[1] < f2[2] and f2[1] < f1[2]
                and f1[3] < f2[4] and f2[3] < f1[4])

    WINDOW = 8

    def _hazard(self, eng, writes, reads):
        hist = self.hist.setdefault(eng, [])
        need = 0
        for (pw, pr, psem, pval) in hist:
            conflict = False
            for f1 in pw:
                for f2 in writes + reads:
                    if self._ovl(f1, f2) and (f1[5] < 512 or f2[5] < 512):
                        conflict = True
            for f1 in pr:
                for f2 in writes:
                    if self._ovl(f1, f2) and (f1[5] < 512 or f2[5] < 512):
                        conflict = True
            if conflict:
                if psem is not None:
                    need = max(need, pval) if psem is self.def_sem[eng] else need
                    if psem is not self.def_sem[eng]:
                        self.wait(eng, psem, pval)
                else:
                    self.nop(eng)
                    self.nop(eng)
        if need:
            self.wait(eng, self.def_sem[eng], need)

    def op(self, eng, meth, *args, sem=None, inc=1, **kw):
        writes = reads = None
        if eng in ("vector", "scalar"):
            w = [self._foot(args[0])] if args else []
            w += [self._foot(kw["accum_out"])] if kw.get("accum_out") is not None else []
            r = [self._foot(a) for a in args[1:]]
            r += [self._foot(v) for k2, v in kw.items() if k2 != "accum_out"]
            writes = [f for f in w if f is not None]
            reads = [f for f in r if f is not None]
            self._hazard(eng, writes, reads)
            small = any(f[5] < 512 for f in writes)
            if sem is None and small:
                sem = self.def_sem[eng]
                inc = 1
        if sem is not None:
            sem.v += inc
            ret = sem.v
            self.emit(eng, lambda e, a=args, kw=kw, s=sem.h, n=inc:
                      getattr(e, meth)(*a, **kw).then_inc(s, n))
        else:
            ret = None
            self.emit(eng, lambda e, a=args, kw=kw: getattr(e, meth)(*a, **kw))
        if writes is not None:
            hist = self.hist.setdefault(eng, [])
            hist.append((writes, reads, sem, sem.v if sem is not None else None))
            if len(hist) > self.WINDOW:
                hist.pop(0)
        return ret


DEBUG = False
FFN_SKIP = True
FFN_SKIP_W1 = True


def build_nc():
    nc = bass.Bass("TRN2", target_bir_lowering=False)
    SCR = "ExternalOutput" if DEBUG else "Internal"

    x_pad = nc.dram_tensor("x_pad", [TL + 1, H], F32, kind="ExternalInput").ap()
    Wk = nc.dram_tensor("Wk", [H, H], F32, kind="ExternalInput").ap()
    Wv = nc.dram_tensor("Wv", [H, H], F32, kind="ExternalInput").ap()
    Wr = nc.dram_tensor("Wr", [H, H], F32, kind="ExternalInput").ap()
    Wo = nc.dram_tensor("Wo", [H, H], F32, kind="ExternalInput").ap()
    tdecay = nc.dram_tensor("tdecay", [H], F32, kind="ExternalInput").ap()
    tfirst = nc.dram_tensor("tfirst", [H], F32, kind="ExternalInput").ap()
    tmix = nc.dram_tensor("tmix", [3, H], F32, kind="ExternalInput").ap()
    Wg = nc.dram_tensor("Wg", [H, E], F32, kind="ExternalInput").ap()
    W1 = nc.dram_tensor("W1", [E, H, FF], F32, kind="ExternalInput").ap()
    b1 = nc.dram_tensor("b1", [E, FF], F32, kind="ExternalInput").ap()
    W2 = nc.dram_tensor("W2", [E, FF, H], F32, kind="ExternalInput").ap()
    b2 = nc.dram_tensor("b2", [E, H], F32, kind="ExternalInput").ap()
    cmask_mul = nc.dram_tensor("cmask_mul", [128, NCORES], F32, kind="ExternalInput").ap()
    cmask_add = nc.dram_tensor("cmask_add", [128, NCORES], F32, kind="ExternalInput").ap()
    premask = nc.dram_tensor("premask", [NCORES, 1], F32, kind="ExternalInput").ap()
    out = nc.dram_tensor("y", [TL, H], F32, kind="ExternalOutput").ap()

    k_sp = nc.dram_tensor("k_sp", [H, TL], F32, kind=SCR).ap()
    v_sp = nc.dram_tensor("v_sp", [H, TL], F32, kind=SCR).ap()
    r_sp = nc.dram_tensor("r_sp", [H, TL], F32, kind=SCR).ap()
    xw_sp = nc.dram_tensor("xw_sp", [H, TL], F32, kind=SCR).ap()
    ss_sp = nc.dram_tensor("ss_sp", [H, TL], F32, kind=SCR).ap()
    dd_sp = nc.dram_tensor("dd_sp", [H, TL], F32, kind=SCR).ap()
    wb_d = [nc.dram_tensor(f"w{i}b_d", [H, H], BF16).ap() for i in range(4)]
    x1_sp = nc.dram_tensor("x1_sp", [TL, H], F32, kind=SCR).ap()
    xe_d = nc.dram_tensor("xe_d", [E * CL + 1, H], BF16, kind=SCR).ap()
    ye_d = nc.dram_tensor("ye_d", [E * CL + 1, H], F32, kind=SCR).ap()
    cc1_in = nc.dram_tensor("cc1_in", [128, 3 * NCH], F32).ap()
    cc1_out = nc.dram_tensor("cc1_out", [128 * NCORES, 3 * NCH], F32).ap()
    cc2_in = nc.dram_tensor("cc2_in", [1, 2 * E], F32).ap()
    cc2_out = nc.dram_tensor("cc2_out", [NCORES, 2 * E], F32).ap()

    from contextlib import ExitStack
    ctx = ExitStack()

    def sb(name, shape, dt=F32):
        return ctx.enter_context(nc.sbuf_tensor(name, shape, dt))

    def mksem(name):
        return SemCtr(ctx.enter_context(nc.semaphore(name)), name)

    NF = 22537
    NBQ = 50176
    ARF = sb("ARF", [128, NF])
    ARB = sb("ARB", [128, NBQ], BF16)

    def fv(a, b):
        return ARF[:, a:b]

    def bv(a, b):
        return ARB[:, a:b]

    def xTv(g, a, b):
        base = g * (TL + 1)
        return ARF[:, base + a: base + b]
    SPL0 = 16392
    SKA, SKB, SVV = 0, 2048, 4096
    SA_, SL_, SG_ = 6144, 8192, 10240
    SM_, SS_, SD_ = 12288, 14337, 20488
    KA, KB, XA, XB = 0, 2048, 4096, 6144
    SA2, SB2, DA2 = 8192, 10240, 12288
    CVV, CRR, CX3, CP_ = 14336, 16384, 18432, 20480

    WT = [0, 8192, 16384, 24576]
    XMT = 32768
    RWO = 0
    X1B = 32768
    X1TC = 16384
    H1O = 0
    W2O = 16384
    W1O = 32768
    XEO = 40960
    B2O = 49152

    iota0 = sb("iota0", [128, TL + 1])
    iota8 = sb("iota8", [128, E])
    iota8p = sb("iota8p", [128, E])
    ident = sb("ident", [128, 128])
    identb = sb("identb", [128, 128], BF16)
    triu = sb("triu", [128, 128])
    ones_c = sb("ones_c", [128, 1])
    ones_r1 = sb("ones_r1", [1, 128])
    ones_r1b = sb("ones_r1b", [1, 128], BF16)
    w_sb = sb("w_sb", [128, NCH])
    negw_sb = sb("negw_sb", [128, NCH])
    w2048_sb = sb("w2048_sb", [128, NCH])
    u_sb = sb("u_sb", [128, NCH])
    tm_sb = sb("tm_sb", [128, 3 * NCH])
    tm1_sb = sb("tm1_sb", [128, 3 * NCH])
    cmaskm = sb("cmaskm", [128, NCORES])
    cmaska = sb("cmaska", [128, NCORES])
    premk = sb("premk", [NCORES, 1])
    wgb = sb("wgb", [128, NCH * E], BF16)
    b1_sb = sb("b1_sb", [128, E * NFC])
    agg = sb("agg", [128, 3 * NCH])
    aggT = sb("aggT", [128, 3 * NCH])
    carry = sb("carry", [128, 3 * NCH])
    sm1 = sb("sm1", [128, 3 * NCH])
    sm2 = sb("sm2", [128, 3 * NCH])
    sm3 = sb("sm3", [128, 3 * NCH])
    cc2_sb = sb("cc2_sb", [NCORES, 2 * E])
    offs = sb("offs", [1, 2 * E])
    offg = sb("offg", [1, E])
    loco = sb("loco", [1, 2 * E])
    logit = sb("logit", [128, NTC * E])
    mask1 = sb("mask1", [128, NTC * E])
    mask2 = sb("mask2", [128, NTC * E])
    kept1 = sb("kept1", [128, NTC * E])
    kept2 = sb("kept2", [128, NTC * E])
    pos1 = sb("pos1", [128, NTC])
    pos2 = sb("pos2", [128, NTC])
    keep1 = sb("keep1", [128, NTC])
    keep2 = sb("keep2", [128, NTC])
    w1k = sb("w1k", [128, NTC])
    w2k = sb("w2k", [128, NTC])
    slot1 = sb("slot1", [128, NTC])
    slot2 = sb("slot2", [128, NTC])
    dst1i = sb("dst1i", [128, NTC], I32)
    dst2i = sb("dst2i", [128, NTC], I32)
    rsc1 = sb("rsc1", [128, E])
    rsc2 = sb("rsc2", [128, E])
    rsc3 = sb("rsc3", [128, E])
    rcol1 = sb("rcol1", [128, 1])
    rcol2 = sb("rcol2", [128, 1])
    rcol3 = sb("rcol3", [128, 1])
    xck = sb("xck", [128, H])
    nopc = sb("nopc", [128, 1])
    cnt_i32 = sb("cnt_i32", [1, E], I32)

    def psumt(name, shape):
        return ctx.enter_context(nc.psum_tensor(name, shape, F32))

    pbank = [psumt(f"pb{i}", [128, 512]) for i in range(8)]

    s_ld = mksem("s_ld")
    s_gp = mksem("s_gp")
    s_gc = mksem("s_gc")
    s_sp = mksem("s_sp")
    s_pe = mksem("s_pe")
    s_ve = mksem("s_ve")
    s_sc = mksem("s_sc")
    s_cc = mksem("s_cc")
    s_ix = mksem("s_ix")
    s_ot = mksem("s_ot")

    p = Plan()
    p.scalar_nop_fn = lambda e: e.copy(nopc[:], nopc[:])
    p.def_sem = {"vector": s_ve, "scalar": s_sc}
    bankg = {}

    def bank_wait(i):
        if i in bankg:
            sem, val = bankg[i]
            p.wait("tensor", sem, val)

    def bank_set(i, sem, val):
        bankg[i] = (sem, val)

    p.op("gpsimd", "iota", iota0[:], [[1, TL + 1]], base=-1, channel_multiplier=0,
         allow_small_or_imprecise_dtypes=True)
    p.op("gpsimd", "iota", iota8[:], [[1, E]], channel_multiplier=0,
         allow_small_or_imprecise_dtypes=True)
    p.op("gpsimd", "memset", ident[:], 0.0)
    p.op("gpsimd", "affine_select", out=ident[:], in_=ident[:],
         compare_op=ALU.not_equal, fill=1.0, base=0,
         pattern=[[-1, 128]], channel_multiplier=1)
    p.op("gpsimd", "memset", identb[:], 0.0)
    p.op("gpsimd", "affine_select", out=identb[:], in_=identb[:],
         compare_op=ALU.not_equal, fill=1.0, base=0,
         pattern=[[-1, 128]], channel_multiplier=1)
    p.op("gpsimd", "memset", triu[:], 0.0)
    p.op("gpsimd", "affine_select", out=triu[:], in_=triu[:],
         compare_op=ALU.is_ge, fill=1.0, base=0,
         pattern=[[-1, 128]], channel_multiplier=1)
    p.op("gpsimd", "memset", ones_c[:], 1.0)
    p.op("gpsimd", "memset", ones_r1[:], 1.0)
    p.op("gpsimd", "memset", ones_r1b[:], 1.0, sem=s_gc)
    gc_consts = s_gc.v

    iotav = iota0[:, 1:TL + 1]
    iotam1 = iota0[:, 0:TL]

    p.dma("sync", w_sb[:], tdecay.rearrange("(g p) -> p g", p=128), s_ld,
          allow_slow_non_contiguous=True)
    p.dma("sync", u_sb[:], tfirst.rearrange("(g p) -> p g", p=128), s_ld,
          allow_slow_non_contiguous=True)
    p.dma("sync", tm_sb.ap().rearrange("p (m g) -> p m g", g=NCH),
          tmix.rearrange("m (g p) -> p m g", p=128), s_ld,
          allow_slow_non_contiguous=True)
    p.dma("sync", cmaskm[:], cmask_mul, s_ld)
    p.dma("sync", cmaska[:], cmask_add, s_ld)
    p.dma("sync", premk[:], premask, s_ld)
    p.dma("sync", b1_sb.ap().rearrange("p (e c) -> p e c", c=NFC),
          b1.rearrange("e (c p) -> p e c", p=128), s_ld,
          allow_slow_non_contiguous=True)
    ld_params = s_ld.v

    p.wait("vector", s_ld, ld_params)
    p.wait("vector", s_gc, gc_consts)
    p.op("vector", "tensor_scalar", negw_sb[:], w_sb[:], -1.0, None, ALU.mult)
    p.op("vector", "tensor_scalar", w2048_sb[:], w_sb[:], float(TL), None, ALU.mult)
    p.op("vector", "tensor_scalar", tm1_sb[:], tm_sb[:], -1.0, 1.0, ALU.mult, ALU.add)
    p.op("vector", "tensor_scalar", iota8p[:], iota8[:], 16384.0, None, ALU.add,
         sem=s_ve)
    ve_params = s_ve.v

    for w_src, w_dst in ((Wk, wb_d[0]), (Wv, wb_d[1]), (Wr, wb_d[2]), (Wo, wb_d[3])):
        p.dma("gpsimd", w_dst, w_src, s_gp)
    gp_wcast = s_gp.v
    p.dma("gpsimd", wgb.ap().rearrange("p (g e) -> p g e", e=E),
          Wg.rearrange("(g p) e -> p g e", p=128), s_gp,
          allow_slow_non_contiguous=True)
    gp_small = s_gp.v

    tp_done = 0
    for c in range(NTC):
        if c > 0:
            p.wait("sync", s_pe, tp_done)
        ldx = p.dma("sync", xck[:], x_pad[c * 128:(c + 1) * 128, :], s_ld)
        p.wait("tensor", s_ld, ldx)
        if c == 0:
            p.wait("tensor", s_gc, gc_consts)
        for g in range(NCH):
            bk = g % 4
            bank_wait(bk)
            p.op("tensor", "transpose", pbank[bk][:, 0:128],
                 xck[:, g * 128:(g + 1) * 128], ident[:], sem=s_pe)
            p.wait("scalar", s_pe, s_pe.v)
            tgt = p.op("scalar", "copy", xTv(g, c * 128, (c + 1) * 128),
                       pbank[bk][:, 0:128], sem=s_sc)
            bank_set(bk, s_sc, tgt)
        tp_done = s_pe.v
    ld_xT_last = p.dma("sync",
                       ARF.ap()[:, 0:NCH * (TL + 1)].rearrange(
                           "p (g t) -> p g t", g=NCH)[:, :, TL:TL + 1],
                       x_pad[TL:TL + 1, :].rearrange("o (g p) -> p g o", p=128), s_ld,
                       allow_slow_non_contiguous=True)
    sc_xT = s_sc.v

    p.wait("sync", s_gp, gp_wcast)
    for i in range(4):
        for g in range(NCH):
            p.dmat("sync", bv(WT[i] + g * H, WT[i] + (g + 1) * H),
                   wb_d[i][:, g * 128:(g + 1) * 128], s_ld)
    ld_wT = s_ld.v

    sp_zero = s_sp.v

    MIX_SP = {0: k_sp, 1: v_sp, 2: r_sp}
    spl = fv(SPL0, SPL0 + 4096)
    atmp = fv(20488, 20488 + TL)

    kbuf = [fv(SKA, SKA + TL), fv(SKB, SKB + TL)]
    svv = fv(SVV, SVV + TL)
    sa = fv(SA_, SA_ + TL)
    sl = fv(SL_, SL_ + TL)
    sg = fv(SG_, SG_ + TL)
    sminc = ARF[:, SM_:SM_ + TL + 1]
    ssinc = ARF[:, SS_:SS_ + TL + 1]
    sdinc = ARF[:, SD_:SD_ + TL + 1]
    sp_proj = {}
    mix2_sems = {}
    ld_k = {}
    ld_kv = {}
    sa_done = {}
    gv_done = {}
    scans_done = {}

    def scan_loads(g):
        p.wait("gpsimd", s_sp, sp_proj[0 * NCH + g])
        p.wait("gpsimd", s_sp, sp_proj[1 * NCH + g])
        if g >= 2:
            p.wait("gpsimd", s_ve, sa_done[g - 2])
        else:
            p.wait("gpsimd", s_ve, mix2_sems[min(g + 1, NCH - 1)])
        p.dma("gpsimd", kbuf[g % 2], k_sp[g * 128:(g + 1) * 128, :], s_gp)
        ld_k[g] = s_gp.v
        if g >= 1:
            p.wait("gpsimd", s_ve, gv_done[g - 1])
        else:
            p.wait("gpsimd", s_ve, mix2_sems[2])
        ld_kv[g] = p.dma("gpsimd", svv, v_sp[g * 128:(g + 1) * 128, :], s_gp)

    def scan_tile(g):
        kk_ = kbuf[g % 2]
        p.wait("vector", s_gp, ld_k[g])
        p.op("vector", "scalar_tensor_tensor", sa, iotav,
             negw_sb[:, g:g + 1], kk_, ALU.mult, ALU.add, sem=s_ve)
        sa_done[g] = s_ve.v
        p.op("vector", "memset", sminc[:, 0:1], NEG)
        p.op("vector", "tensor_tensor_scan", sminc[:, 1:TL + 1], sa, sa,
             sminc[:, 0:1], ALU.max, ALU.max)
        p.op("vector", "tensor_sub", sl, sminc[:, 0:TL], sminc[:, 1:TL + 1])
        p.op("vector", "tensor_sub", sa, sa, sminc[:, 1:TL + 1], sem=s_ve)
        ve_gin = s_ve.v
        p.wait("scalar", s_ve, ve_gin)
        p.op("scalar", "activation", sl, sl, AF.Exp)
        p.op("scalar", "activation", sa, sa, AF.Exp, sem=s_sc)
        sc_exp = s_sc.v
        p.wait("vector", s_sc, sc_exp)
        p.wait("vector", s_gp, ld_kv[g])
        p.op("vector", "tensor_mul", sg, sa, svv, sem=s_ve)
        gv_done[g] = s_ve.v
        if g + 1 < NCH:
            scan_loads(g + 1)
        p.op("vector", "memset", ssinc[:, 0:1], 0.0)
        p.op("vector", "memset", sdinc[:, 0:1], 0.0)
        p.op("vector", "tensor_tensor_scan", ssinc[:, 1:TL + 1], sl, sg,
             ssinc[:, 0:1], ALU.mult, ALU.add)
        p.op("vector", "tensor_tensor_scan", sdinc[:, 1:TL + 1], sl, sa,
             sdinc[:, 0:1], ALU.mult, ALU.add)
        p.op("vector", "scalar_tensor_tensor", sl, iotam1,
             w_sb[:, g:g + 1], sminc[:, 0:TL], ALU.mult, ALU.add)
        p.op("vector", "tensor_scalar", agg[:, g:g + 1], sminc[:, TL:TL + 1],
             w2048_sb[:, g:g + 1], w_sb[:, g:g + 1], ALU.add, ALU.subtract)
        p.op("vector", "tensor_copy", agg[:, NCH + g:NCH + g + 1],
             ssinc[:, TL:TL + 1])
        p.op("vector", "tensor_copy", agg[:, 2 * NCH + g:2 * NCH + g + 1],
             sdinc[:, TL:TL + 1], sem=s_ve)
        scans_done[g] = s_ve.v
        p.wait("sync", s_ve, scans_done[g])
        p.dma("sync", xw_sp[g * 128:(g + 1) * 128, :], sl, s_sp)
        p.dma("sync", dd_sp[g * 128:(g + 1) * 128, :], sdinc[:, 0:TL], s_sp)
        p.wait("gpsimd", s_ve, scans_done[g])
        p.dma("gpsimd", ss_sp[g * 128:(g + 1) * 128, :], ssinc[:, 0:TL], s_gp)

    pe_proj_done = 0
    for m in range(3):
        p.wait("vector", s_sc, sc_xT)
        p.wait("vector", s_ld, ld_xT_last)
        p.wait("vector", s_ve, ve_params)
        if m > 0:
            p.wait("vector", s_pe, pe_proj_done)
        for g in range(NCH):
            xr = xTv(g, 1, 1 + TL)
            xp = xTv(g, 0, TL)
            p.op("vector", "tensor_scalar", atmp, xr,
                 tm_sb[:, m * NCH + g: m * NCH + g + 1], None, ALU.mult)
            p.op("vector", "scalar_tensor_tensor", bv(XMT + g * TL, XMT + (g + 1) * TL),
                 xp, tm1_sb[:, m * NCH + g: m * NCH + g + 1], atmp, ALU.mult, ALU.add,
                 sem=s_ve)
            if m == 2:
                mix2_sems[g] = s_ve.v
        ve_mix = s_ve.v

        if m == 2:
            scan_loads(0)
        p.wait("tensor", s_ld, ld_wT)
        p.wait("tensor", s_ve, ve_mix)
        for oc in range(NCH):
            buf = oc % 2
            prev = sp_proj.get(m * NCH + oc - 2, 0)
            for tq in range(4):
                bk = 4 + (tq % 2)
                bank_wait(bk)
                for g in range(NCH):
                    p.op("tensor", "matmul", pbank[bk][:],
                         bv(WT[m] + g * H + oc * 128, WT[m] + g * H + (oc + 1) * 128),
                         bv(XMT + g * TL + tq * 512, XMT + g * TL + (tq + 1) * 512),
                         start=(g == 0), stop=(g == NCH - 1),
                         sem=(s_pe if g == NCH - 1 else None))
                p.wait("scalar", s_pe, s_pe.v)
                p.wait("scalar", s_sp, prev)
                func = AF.Sigmoid if m == 2 else AF.Copy
                tgt = p.op("scalar", "activation",
                           fv(SPL0 + buf * TL + tq * 512, SPL0 + buf * TL + (tq + 1) * 512),
                           pbank[bk][:], func, sem=s_sc)
                bank_set(bk, s_sc, tgt)
            p.wait("sync", s_sc, s_sc.v)
            sp_proj[m * NCH + oc] = p.dma(
                "sync", MIX_SP[m][oc * 128:(oc + 1) * 128, :],
                fv(SPL0 + buf * TL, SPL0 + (buf + 1) * TL), s_sp)
            if m == 2:
                scan_tile(oc)
        pe_proj_done = s_pe.v
    sp_proj_all = s_sp.v
    gp_scan_all = s_gp.v
    ve_agg = scans_done[NCH - 1]

    p.wait("gpsimd", s_pe, pe_proj_done)
    p.op("gpsimd", "memset", bv(XMT, XMT + H), 0.0, sem=s_gc)
    p.wait("gpsimd", s_gc, s_gc.v)
    sp_zero_ye = p.dma("gpsimd", ye_d[TRASH:TRASH + 1, :], ARB[0:1, XMT:XMT + H],
                       s_gp)

    if DEBUG:
        dbgv = fv(20496, 20512)
        d_dbgv = nc.dram_tensor("d_dbgv", [128, 16], F32, kind="ExternalOutput").ap()

    p.wait("sync", s_ve, ve_agg)
    cc_pub = p.dma("sync", cc1_in, agg[:], s_sp)
    p.wait("gpsimd", s_sp, cc_pub)
    p.op("gpsimd", "collective_compute", "AllGather", ALU.bypass,
         replica_groups=[list(range(NCORES))],
         ins=[cc1_in.opt()], outs=[cc1_out.opt()], sem=s_cc)
    cc1_done = s_cc.v
    p.op("vector", "memset", carry[:, 0:NCH], NEG)
    p.op("vector", "memset", carry[:, NCH:3 * NCH], 0.0, sem=s_ve)
    p.wait("gpsimd", s_cc, cc1_done)
    for cp in range(NCORES):
        if cp > 0:
            p.wait("gpsimd", s_ve, fold_done)
        ldt = p.dma("gpsimd", aggT[:], cc1_out[cp * 128:(cp + 1) * 128, :], s_gp)
        p.wait("vector", s_gp, ldt)
        p.op("vector", "tensor_scalar", sm1[:, 0:NCH], aggT[:, 0:NCH],
             cmaskm[:, cp:cp + 1], cmaska[:, cp:cp + 1], ALU.mult, ALU.add)
        p.op("vector", "tensor_scalar", sm1[:, NCH:3 * NCH], aggT[:, NCH:3 * NCH],
             cmaskm[:, cp:cp + 1], None, ALU.mult)
        p.op("vector", "tensor_scalar", sm2[:, 0:NCH], w2048_sb[:],
             cmaskm[:, cp:cp + 1], None, ALU.mult)
        p.nop("vector")
        p.op("vector", "tensor_tensor", sm2[:, 0:NCH], carry[:, 0:NCH],
             sm2[:, 0:NCH], ALU.add)
        p.nop("vector")
        p.op("vector", "tensor_tensor", sm3[:, 0:NCH], sm2[:, 0:NCH], sm1[:, 0:NCH],
             ALU.max)
        p.nop("vector")
        p.op("vector", "tensor_sub", sm2[:, NCH:2 * NCH], sm2[:, 0:NCH], sm3[:, 0:NCH])
        p.op("vector", "tensor_sub", sm2[:, 2 * NCH:3 * NCH], sm1[:, 0:NCH],
             sm3[:, 0:NCH], sem=s_ve)
        p.wait("scalar", s_ve, s_ve.v)
        p.op("scalar", "activation", sm2[:, NCH:3 * NCH], sm2[:, NCH:3 * NCH], AF.Exp,
             sem=s_sc)
        p.wait("vector", s_sc, s_sc.v)
        p.op("vector", "tensor_mul", sm3[:, NCH:2 * NCH], carry[:, NCH:2 * NCH],
             sm2[:, NCH:2 * NCH])
        p.op("vector", "tensor_mul", sm3[:, 2 * NCH:3 * NCH], carry[:, 2 * NCH:3 * NCH],
             sm2[:, NCH:2 * NCH])
        p.op("vector", "tensor_mul", sm1[:, NCH:2 * NCH], sm1[:, NCH:2 * NCH],
             sm2[:, 2 * NCH:3 * NCH])
        p.op("vector", "tensor_mul", sm1[:, 2 * NCH:3 * NCH], sm1[:, 2 * NCH:3 * NCH],
             sm2[:, 2 * NCH:3 * NCH])
        p.nop("vector")
        p.op("vector", "tensor_add", carry[:, NCH:3 * NCH], sm3[:, NCH:3 * NCH],
             sm1[:, NCH:3 * NCH])
        p.op("vector", "tensor_copy", carry[:, 0:NCH], sm3[:, 0:NCH], sem=s_ve)
        fold_done = s_ve.v
        if DEBUG and cp == 1:
            p.op("vector", "tensor_copy", dbgv[:, 8:9], sm1[:, 0:1], sem=s_ve)
            p.op("vector", "tensor_copy", dbgv[:, 9:10], sm2[:, 0:1], sem=s_ve)
            p.op("vector", "tensor_copy", dbgv[:, 10:11], sm3[:, 0:1], sem=s_ve)
            p.op("vector", "tensor_copy", dbgv[:, 11:12], sm2[:, NCH:NCH + 1], sem=s_ve)
            p.op("vector", "tensor_copy", dbgv[:, 12:13], sm2[:, 2 * NCH:2 * NCH + 1], sem=s_ve)
            p.op("vector", "tensor_copy", dbgv[:, 13:14], cmaskm[:, 1:2], sem=s_ve)
            p.op("vector", "tensor_copy", dbgv[:, 14:15], cmaska[:, 1:2], sem=s_ve)
            p.op("vector", "tensor_copy", dbgv[:, 15:16], carry[:, 0:1], sem=s_ve)
    ve_carry = s_ve.v

    ckk = [fv(KA, KA + TL), fv(KB, KB + TL)]
    cxw = [fv(XA, XA + TL), fv(XB, XB + TL)]
    css = [fv(SA2, SA2 + TL), fv(SB2, SB2 + TL)]
    cdd_ = fv(DA2, DA2 + TL)
    cvv = fv(CVV, CVV + TL)
    crr = fv(CRR, CRR + TL)
    cx3 = fv(CX3, CX3 + TL)
    cpb = fv(CP_, CP_ + TL)

    p.wait("sync", s_ve, ve_agg)
    p.wait("sync", s_sp, sp_proj_all)
    p.wait("sync", s_gp, gp_scan_all)
    p.wait("gpsimd", s_ve, ve_agg)
    p.wait("gpsimd", s_sp, sp_proj_all)
    p.wait("vector", s_sp, sp_proj_all)
    p.wait("vector", s_gp, gp_scan_all)
    cld = {}
    cld[("k", 0)] = p.dma("sync", ckk[0], k_sp[0:128, :], s_ld)
    cld[("v", 0)] = p.dma("sync", cvv, v_sp[0:128, :], s_ld)
    cld[("r", 0)] = p.dma("sync", crr, r_sp[0:128, :], s_ld)
    cld[("k", 1)] = p.dma("sync", ckk[1], k_sp[128:256, :], s_ld)
    cld[("d", 0)] = p.dma("gpsimd", cdd_, dd_sp[0:128, :], s_gp)
    for g0 in range(2):
        cld[("x", g0)] = p.dma("gpsimd", cxw[g0],
                               xw_sp[g0 * 128:(g0 + 1) * 128, :], s_gp)
        cld[("s", g0)] = p.dma("gpsimd", css[g0],
                               ss_sp[g0 * 128:(g0 + 1) * 128, :], s_gp)

    ve_rw_all = 0
    for g in range(NCH):
        kk_ = ckk[g % 2]
        xw_ = cxw[g % 2]
        ss_ = css[g % 2]
        dd_ = cdd_
        if g == 0:
            p.wait("vector", s_ve, ve_carry)
        p.wait("vector", s_gp, cld[("x", g)])
        p.op("vector", "tensor_scalar", cx3, iotav, w_sb[:, g:g + 1],
             carry[:, g:g + 1], ALU.mult, ALU.add)
        p.wait("vector", s_ld, cld[("k", g)])
        p.op("vector", "scalar_tensor_tensor", cpb, kk_, u_sb[:, g:g + 1],
             xw_, ALU.add, ALU.max)
        p.op("vector", "tensor_tensor", cpb, cpb, cx3, ALU.max)
        p.op("vector", "tensor_sub", xw_, xw_, cpb, sem=s_ve)
        v4 = s_ve.v
        p.op("vector", "tensor_sub", kk_, kk_, cpb, sem=s_ve)
        v5 = s_ve.v
        p.op("vector", "tensor_sub", cx3, cx3, cpb, sem=s_ve)
        v6 = s_ve.v
        p.wait("scalar", s_ve, v4)
        p.op("scalar", "activation", xw_, xw_, AF.Exp, sem=s_sc)
        e1d = s_sc.v
        p.wait("scalar", s_ve, v5)
        p.op("scalar", "activation", kk_, kk_, AF.Exp, bias=u_sb[:, g:g + 1],
             scale=1.0, sem=s_sc)
        e2d = s_sc.v
        p.wait("scalar", s_ve, v6)
        p.op("scalar", "activation", cx3, cx3, AF.Exp, sem=s_sc)
        e3d = s_sc.v
        p.wait("vector", s_sc, e1d)
        p.wait("vector", s_gp, cld[("s", g)])
        p.op("vector", "tensor_mul", ss_, ss_, xw_)
        p.wait("vector", s_gp, cld[("d", g)])
        p.op("vector", "tensor_mul", dd_, dd_, xw_, sem=s_ve)
        v11 = s_ve.v
        p.wait("vector", s_sc, e2d)
        p.wait("vector", s_ld, cld[("v", g)])
        p.op("vector", "tensor_mul", cvv, cvv, kk_)
        p.op("vector", "scalar_tensor_tensor", dd_, kk_, 1e-9, dd_,
             ALU.add, ALU.add, sem=s_ve)
        v12 = s_ve.v
        p.op("vector", "tensor_add", ss_, ss_, cvv, sem=s_ve)
        v9 = s_ve.v
        p.wait("vector", s_sc, e3d)
        p.op("vector", "scalar_tensor_tensor", dd_, cx3,
             carry[:, 2 * NCH + g:2 * NCH + g + 1], dd_, ALU.mult, ALU.add,
             sem=s_ve)
        v13 = s_ve.v
        p.op("vector", "scalar_tensor_tensor", ss_, cx3,
             carry[:, NCH + g:NCH + g + 1], ss_, ALU.mult, ALU.add)
        p.wait("scalar", s_ve, v13)
        p.op("scalar", "activation", dd_, dd_, AF.Ln)
        p.op("scalar", "activation", dd_, dd_, AF.Exp, scale=-1.0, sem=s_sc)
        recd = s_sc.v
        p.wait("vector", s_sc, recd)
        p.op("vector", "tensor_mul", ss_, ss_, dd_, sem=s_ve)
        v14 = s_ve.v
        p.wait("vector", s_ld, cld[("r", g)])
        p.op("vector", "tensor_mul", bv(RWO + g * TL, RWO + (g + 1) * TL),
             ss_, crr, sem=s_ve)
        ve_rw_all = s_ve.v

        if g + 1 < NCH:
            p.wait("sync", s_ve, v9)
            cld[("v", g + 1)] = p.dma("sync", cvv,
                                      v_sp[(g + 1) * 128:(g + 2) * 128, :], s_ld)
            p.wait("sync", s_ve, ve_rw_all)
            cld[("r", g + 1)] = p.dma("sync", crr,
                                      r_sp[(g + 1) * 128:(g + 2) * 128, :], s_ld)
            p.wait("gpsimd", s_ve, v14)
            cld[("d", g + 1)] = p.dma("gpsimd", cdd_,
                                      dd_sp[(g + 1) * 128:(g + 2) * 128, :], s_gp)
        if g + 2 < NCH:
            p.wait("sync", s_ve, v12)
            cld[("k", g + 2)] = p.dma("sync", ckk[g % 2],
                                      k_sp[(g + 2) * 128:(g + 3) * 128, :], s_ld)
            p.wait("gpsimd", s_ve, v11)
            cld[("x", g + 2)] = p.dma("gpsimd", cxw[g % 2],
                                      xw_sp[(g + 2) * 128:(g + 3) * 128, :], s_gp)
            p.wait("gpsimd", s_ve, ve_rw_all)
            cld[("s", g + 2)] = p.dma("gpsimd", css[g % 2],
                                      ss_sp[(g + 2) * 128:(g + 3) * 128, :], s_gp)

    if DEBUG:
        d_rw = nc.dram_tensor("d_rw", [128, NCH * TL], BF16, kind="ExternalOutput").ap()
        d_carry = nc.dram_tensor("d_carry", [128, 3 * NCH], F32, kind="ExternalOutput").ap()
        d_agg = nc.dram_tensor("d_agg", [128, 3 * NCH], F32, kind="ExternalOutput").ap()
        d_aggall = nc.dram_tensor("d_aggall", [128 * NCORES, 3 * NCH], F32, kind="ExternalOutput").ap()
        p.wait("sync", s_ve, ve_rw_all)
        p.dma("sync", d_rw, bv(RWO, RWO + NCH * TL), s_sp)
        p.dma("sync", d_carry, carry[:], s_sp)
        p.dma("sync", d_agg, agg[:], s_sp)
        p.dma("sync", d_dbgv, dbgv[:], s_sp)
        p.dma("sync", d_aggall, cc1_out, s_sp)

    c1gC = fv(0, H)
    youtC = fv(H, 2 * H)
    x1Tc = bv(X1TC, X1TC + H)
    if DEBUG:
        d_y1 = nc.dram_tensor("d_y1", [TL, H], F32, kind="ExternalOutput").ap()
        dbg2 = fv(3 * H, 3 * H + 16)
        d_dbg2 = nc.dram_tensor("d_dbg2", [128, 16], F32, kind="ExternalOutput").ap()
    p.wait("tensor", s_ve, ve_rw_all)
    p.wait("tensor", s_ld, ld_wT)
    p.wait("tensor", s_gp, gp_small)
    p.wait("vector", s_gp, sp_zero_ye)
    sp_x1_last = 0
    logit_mm_done = 0
    tpc_prev = 0
    for c in range(NTC):
        ldx = p.dma("sync", xck[:], x_pad[1 + c * 128: 1 + (c + 1) * 128, :], s_ld)
        for oh in range(2):
            bk = oh
            bank_wait(bk)
            for g in range(NCH):
                p.op("tensor", "matmul", pbank[bk][:],
                     bv(RWO + g * TL + c * 128, RWO + g * TL + (c + 1) * 128),
                     bv(WT[3] + g * H + oh * 512, WT[3] + g * H + (oh + 1) * 512),
                     start=(g == 0), stop=(g == NCH - 1),
                     sem=(s_pe if g == NCH - 1 else None))
            p.wait("vector", s_pe, s_pe.v)
            p.wait("vector", s_ld, ldx)
            p.wait("vector", s_sp, sp_x1_last)
            tgt = p.op("vector", "tensor_add", fv(H + oh * 512, H + (oh + 1) * 512),
                       pbank[bk][:], xck[:, oh * 512:(oh + 1) * 512], sem=s_ve)
            bank_set(bk, s_ve, tgt)
        if DEBUG:
            p.wait("sync", s_ve, s_ve.v)
            p.dma("sync", d_y1[c * 128:(c + 1) * 128, :], youtC, s_sp)
        y = youtC
        p.op("vector", "reduce_sum", rcol1[:], y, axis=AX, negate=True)
        p.nop("vector")
        p.op("vector", "tensor_scalar", rcol1[:], rcol1[:], 1.0 / H, None, ALU.mult)
        p.nop("vector")
        p.wait("vector", s_pe, tpc_prev)
        p.op("vector", "tensor_scalar", c1gC, y, rcol1[:, 0:1], None, ALU.add, sem=s_ve)
        p.wait("scalar", s_ve, s_ve.v)
        p.op("scalar", "activation", fv(2 * H, 3 * H), c1gC, AF.Square,
             accum_out=rcol2[:, 0:1], sem=s_sc)
        p.wait("vector", s_sc, s_sc.v)
        p.op("vector", "tensor_scalar", rcol2[:], rcol2[:], 1.0 / H, 1e-5,
             ALU.mult, ALU.add, sem=s_ve)
        p.wait("scalar", s_ve, s_ve.v)
        p.op("scalar", "activation", rcol3[:], rcol2[:], AF.Ln)
        p.nop("scalar")
        p.op("scalar", "activation", rcol3[:], rcol3[:], AF.Exp, scale=-0.5, sem=s_sc)
        p.wait("vector", s_sc, s_sc.v)
        p.op("vector", "tensor_scalar", c1gC, c1gC, rcol3[:, 0:1], None, ALU.mult)
        p.op("vector", "tensor_copy", bv(X1B + c * H, X1B + (c + 1) * H), c1gC, sem=s_ve)
        ve_c1 = s_ve.v
        p.wait("sync", s_ve, ve_c1)
        sp_x1_last = p.dma("sync", x1_sp[c * 128:(c + 1) * 128, :], c1gC, s_sp)

        p.wait("tensor", s_ve, ve_c1)
        for g in range(NCH):
            bk = 4 + (g % 2)
            bank_wait(bk)
            p.op("tensor", "transpose", pbank[bk][:, 0:128],
                 c1gC[:, g * 128:(g + 1) * 128], ident[:], sem=s_pe)
            p.wait("scalar", s_pe, s_pe.v)
            p.wait("scalar", s_pe, logit_mm_done)
            tgt = p.op("scalar", "copy", x1Tc[:, g * 128:(g + 1) * 128],
                       pbank[bk][:, 0:128], sem=s_sc)
            bank_set(bk, s_sc, tgt)
        tpc_prev = s_pe.v
        bank_wait(6)
        p.wait("tensor", s_sc, s_sc.v)
        for g in range(NCH):
            p.op("tensor", "matmul", pbank[6][:, 0:E],
                 x1Tc[:, g * 128:(g + 1) * 128], wgb[:, g * E:(g + 1) * E],
                 start=(g == 0), stop=(g == NCH - 1),
                 sem=(s_pe if g == NCH - 1 else None))
        logit_mm_done = s_pe.v
        p.wait("vector", s_pe, s_pe.v)
        tgt = p.op("vector", "tensor_copy", logit[:, c * E:(c + 1) * E],
                   pbank[6][:, 0:E], sem=s_ve)
        bank_set(6, s_ve, tgt)

        L = logit[:, c * E:(c + 1) * E]
        p.op("vector", "reduce_max", rcol1[:], L, axis=AX, negate=True)
        p.nop("vector")
        p.op("vector", "tensor_scalar", rsc1[:], L, rcol1[:, 0:1], 0.0, ALU.add, ALU.is_ge)
        p.nop("vector")
        p.op("vector", "scalar_tensor_tensor", rsc2[:], rsc1[:], 16384.0, iota8p[:],
             ALU.mult, ALU.subtract)
        p.nop("vector")
        p.op("vector", "reduce_max", rcol2[:], rsc2[:], axis=AX)
        p.nop("vector")
        p.op("vector", "tensor_scalar", mask1[:, c * E:(c + 1) * E], iota8[:],
             rcol2[:, 0:1], 0.0, ALU.add, ALU.is_equal)
        if DEBUG and c == 0:
            p.op("vector", "tensor_copy", dbg2[:, 0:8], rsc1[:], sem=s_ve)
            p.op("vector", "tensor_copy", dbg2[:, 8:9], rcol1[:], sem=s_ve)
            p.op("vector", "tensor_copy", dbg2[:, 9:10], rcol2[:], sem=s_ve)
        p.nop("vector")
        p.op("vector", "scalar_tensor_tensor", rsc3[:], mask1[:, c * E:(c + 1) * E],
             -3e38, L, ALU.mult, ALU.add)
        p.nop("vector")
        p.op("vector", "reduce_max", rcol3[:], rsc3[:], axis=AX, negate=True)
        p.nop("vector")
        p.op("vector", "tensor_scalar", rsc1[:], rsc3[:], rcol3[:, 0:1], 0.0,
             ALU.add, ALU.is_ge)
        p.nop("vector")
        p.op("vector", "scalar_tensor_tensor", rsc2[:], rsc1[:], 16384.0, iota8p[:],
             ALU.mult, ALU.subtract)
        p.nop("vector")
        p.op("vector", "reduce_max", rcol2[:], rsc2[:], axis=AX)
        p.nop("vector")
        p.op("vector", "tensor_scalar", mask2[:, c * E:(c + 1) * E], iota8[:],
             rcol2[:, 0:1], 0.0, ALU.add, ALU.is_equal, sem=s_ve)
        p.wait("scalar", s_ve, s_ve.v)
        p.op("scalar", "activation", rsc3[:], L, AF.Exp, bias=rcol1[:, 0:1], scale=1.0,
             accum_out=rcol3[:, 0:1], sem=s_sc)
        p.wait("vector", s_sc, s_sc.v)
        p.op("vector", "tensor_mul", rsc1[:], rsc3[:], mask1[:, c * E:(c + 1) * E])
        p.nop("vector")
        p.op("vector", "reduce_sum", rcol2[:], rsc1[:], axis=AX)
        p.nop("vector")
        p.op("vector", "tensor_mul", rsc1[:], rsc3[:], mask2[:, c * E:(c + 1) * E])
        p.nop("vector")
        p.op("vector", "reduce_sum", rcol1[:], rsc1[:], axis=AX)
        p.nop("vector")
        p.op("vector", "tensor_add", rsc2[:, 0:1], rcol2[:], rcol1[:])
        p.op("vector", "tensor_scalar", rsc2[:, 1:2], rcol3[:, 0:1], 1e-9, None, ALU.mult)
        p.nop("vector")
        p.op("vector", "tensor_add", rsc2[:, 0:1], rsc2[:, 0:1], rsc2[:, 1:2])
        p.nop("vector")
        p.op("vector", "reciprocal", rsc2[:, 2:3], rsc2[:, 0:1])
        p.nop("vector")
        p.op("vector", "tensor_mul", w1k[:, c:c + 1], rcol2[:], rsc2[:, 2:3])
        p.op("vector", "tensor_mul", w2k[:, c:c + 1], rcol1[:], rsc2[:, 2:3], sem=s_ve)
    sp_x1 = s_sp.v
    ve_x1b = s_ve.v
    ve_route1 = s_ve.v

    p.wait("tensor", s_ve, ve_route1)
    bank_wait(7)
    for c in range(NTC):
        p.op("tensor", "matmul", pbank[7][0:1, 0:E], ones_c[:],
             mask1[:, c * E:(c + 1) * E], start=(c == 0), stop=(c == NTC - 1),
             sem=(s_pe if c == NTC - 1 else None))
    bank_wait(6)
    for c in range(NTC):
        p.op("tensor", "matmul", pbank[6][0:1, 0:E], ones_c[:],
             mask2[:, c * E:(c + 1) * E], start=(c == 0), stop=(c == NTC - 1),
             sem=(s_pe if c == NTC - 1 else None))
    p.wait("vector", s_pe, s_pe.v)
    p.op("vector", "tensor_copy", sm1[0:1, 0:E], pbank[7][0:1, 0:E])
    tgt = p.op("vector", "tensor_copy", sm1[0:1, E:2 * E], pbank[6][0:1, 0:E], sem=s_ve)
    bank_set(6, s_ve, tgt)
    bank_set(7, s_ve, tgt)
    p.wait("sync", s_ve, s_ve.v)
    cc2_pub = p.dma("sync", cc2_in, sm1[0:1, 0:2 * E], s_sp)
    p.wait("gpsimd", s_sp, cc2_pub)
    p.op("gpsimd", "collective_compute", "AllGather", ALU.bypass,
         replica_groups=[list(range(NCORES))],
         ins=[cc2_in.opt()], outs=[cc2_out.opt()], sem=s_cc)
    cc2_done = s_cc.v
    p.wait("gpsimd", s_cc, cc2_done)
    ld_cc2 = p.dma("gpsimd", cc2_sb[:], cc2_out, s_gp)
    p.wait("tensor", s_gp, ld_cc2)
    bank_wait(7)
    p.op("tensor", "matmul", pbank[7][0:1, 0:2 * E], premk[:], cc2_sb[:],
         start=True, stop=True)
    bank_wait(5)
    p.op("tensor", "matmul", pbank[5][0:1, 0:E], ones_c[0:E, 0:1],
         cc2_sb[:, 0:E], start=True, stop=True, sem=s_pe)
    p.wait("vector", s_pe, s_pe.v)
    p.op("vector", "tensor_copy", offs[:], pbank[7][0:1, 0:2 * E])
    tgt = p.op("vector", "tensor_copy", offg[:], pbank[5][0:1, 0:E], sem=s_ve)
    bank_set(5, s_ve, tgt)
    bank_set(7, s_ve, tgt)
    ve_offs = s_ve.v

    def cumsum_pass(msk, possb, keepsb, locol):
        for c in range(NTC):
            bank_wait(5)
            p.wait("tensor", s_ve, s_ve.v)
            p.op("tensor", "matmul", pbank[5][:, 0:E], triu[:],
                 msk[:, c * E:(c + 1) * E], start=True, stop=False)
            p.op("tensor", "matmul", pbank[5][:, 0:E], ones_r1[:], locol,
                 start=False, stop=True, sem=s_pe)
            p.wait("vector", s_pe, s_pe.v)
            tgt_ = p.op("vector", "tensor_mul", rsc1[:], pbank[5][:, 0:E],
                        msk[:, c * E:(c + 1) * E], sem=s_ve)
            bank_set(5, s_ve, tgt_)
            p.nop("vector")
            p.op("vector", "reduce_sum", possb[:, c:c + 1], rsc1[:], axis=AX)
            if keepsb is not None:
                p.nop("vector")
                p.op("vector", "tensor_scalar", keepsb[:, c:c + 1], possb[:, c:c + 1],
                     float(CGLOB), 0.0, ALU.subtract, ALU.is_lt)
            bank_wait(6)
            p.op("tensor", "matmul", pbank[6][0:1, 0:E], ones_c[:],
                 msk[:, c * E:(c + 1) * E], start=True, stop=True, sem=s_pe)
            p.wait("vector", s_pe, s_pe.v)
            tgt_ = p.op("vector", "tensor_add", locol, locol, pbank[6][0:1, 0:E],
                        sem=s_ve)
            bank_set(6, s_ve, tgt_)

    p.wait("vector", s_ve, ve_offs)
    p.op("vector", "tensor_copy", loco[0:1, 0:E], offs[0:1, 0:E], sem=s_ve)
    cumsum_pass(mask1, pos1, keep1, loco[0:1, 0:E])
    p.op("vector", "tensor_add", loco[0:1, E:2 * E], offs[0:1, E:2 * E],
         offg[0:1, 0:E], sem=s_ve)
    cumsum_pass(mask2, pos2, keep2, loco[0:1, E:2 * E])

    for c in range(NTC):
        p.op("vector", "tensor_scalar", kept1[:, c * E:(c + 1) * E],
             mask1[:, c * E:(c + 1) * E], keep1[:, c:c + 1], None, ALU.mult)
        p.op("vector", "tensor_scalar", kept2[:, c * E:(c + 1) * E],
             mask2[:, c * E:(c + 1) * E], keep2[:, c:c + 1], None, ALU.mult)
        p.op("vector", "tensor_mul", w1k[:, c:c + 1], w1k[:, c:c + 1], keep1[:, c:c + 1])
        p.op("vector", "tensor_mul", w2k[:, c:c + 1], w2k[:, c:c + 1], keep2[:, c:c + 1],
             sem=s_ve)

    p.op("vector", "memset", loco[0:1, 0:E], 0.0, sem=s_ve)
    cumsum_pass(kept1, slot1, None, loco[0:1, 0:E])
    p.op("vector", "tensor_copy", loco[0:1, E:2 * E], loco[0:1, 0:E], sem=s_ve)
    cumsum_pass(kept2, slot2, None, loco[0:1, E:2 * E])
    p.op("vector", "tensor_copy", cnt_i32[0:1, 0:E], loco[0:1, E:2 * E], sem=s_ve)

    for c in range(NTC):
        for (msk, kp, slotsb, dstsi) in ((mask1, keep1, slot1, dst1i),
                                         (mask2, keep2, slot2, dst2i)):
            p.op("vector", "tensor_mul", rsc1[:], iota8[:], msk[:, c * E:(c + 1) * E])
            p.nop("vector")
            p.op("vector", "reduce_sum", rcol1[:], rsc1[:], axis=AX)
            p.op("vector", "tensor_scalar", rcol2[:], slotsb[:, c:c + 1],
                 float(CL), 0.0, ALU.subtract, ALU.is_lt)
            p.nop("vector")
            p.op("vector", "tensor_mul", rcol2[:], rcol2[:], kp[:, c:c + 1])
            p.op("vector", "tensor_scalar", rcol1[:], rcol1[:], float(CL), None, ALU.mult)
            p.nop("vector")
            p.op("vector", "tensor_tensor", rcol1[:], rcol1[:], slotsb[:, c:c + 1], ALU.add)
            p.nop("vector")
            p.op("vector", "tensor_scalar", rcol1[:], rcol1[:], float(TRASH), None,
                 ALU.subtract)
            p.nop("vector")
            p.op("vector", "tensor_mul", rcol1[:], rcol1[:], rcol2[:])
            p.nop("vector")
            p.op("vector", "tensor_scalar", rcol3[:], rcol1[:], float(TRASH), None, ALU.add)
            p.nop("vector")
            p.op("vector", "tensor_copy", dstsi[:, c:c + 1], rcol3[:], sem=s_ve)
    ve_dests = s_ve.v

    if DEBUG:
        d_rt = nc.dram_tensor("d_rt", [128, 6 * NTC], F32, kind="ExternalOutput").ap()
        d_rti = nc.dram_tensor("d_rti", [128, 2 * NTC], I32, kind="ExternalOutput").ap()
        d_msk = nc.dram_tensor("d_msk", [128, 3 * NTC * E], F32, kind="ExternalOutput").ap()
        p.wait("sync", s_ve, ve_dests)
        p.dma("sync", d_rt[:, 0 * NTC:1 * NTC], pos1[:], s_sp)
        p.dma("sync", d_rt[:, 1 * NTC:2 * NTC], pos2[:], s_sp)
        p.dma("sync", d_rt[:, 2 * NTC:3 * NTC], slot1[:], s_sp)
        p.dma("sync", d_rt[:, 3 * NTC:4 * NTC], slot2[:], s_sp)
        p.dma("sync", d_rt[:, 4 * NTC:5 * NTC], w1k[:], s_sp)
        p.dma("sync", d_rt[:, 5 * NTC:6 * NTC], w2k[:], s_sp)
        p.dma("sync", d_rti[:, 0:NTC], dst1i[:], s_sp)
        p.dma("sync", d_rti[:, NTC:2 * NTC], dst2i[:], s_sp)
        p.dma("sync", d_msk[:, 0:NTC * E], logit[:], s_sp)
        p.dma("sync", d_msk[:, NTC * E:2 * NTC * E], mask1[:], s_sp)
        p.dma("sync", d_msk[:, 2 * NTC * E:3 * NTC * E], mask2[:], s_sp)
        p.dma("sync", d_dbg2, dbg2, s_sp)

    p.wait("gpsimd", s_ve, ve_dests)
    p.wait("gpsimd", s_sp, sp_zero)
    for c in range(NTC):
        for dsts in (dst1i, dst2i):
            p.op("gpsimd", "indirect_dma_start",
                 out=xe_d, out_offset=IndirectOffsetOnAxis(dsts[:, c:c + 1], 0),
                 in_=bv(X1B + c * H, X1B + (c + 1) * H), in_offset=None,
                 sem=s_ix, inc=16)
    ix_scatter = s_ix.v

    if DEBUG:
        d_h1 = [nc.dram_tensor(f"d_h1{i}", [128, 16 * CL], BF16, kind="ExternalOutput").ap()
                for i in range(2)]
        d_yep = nc.dram_tensor("d_yep", [128, E * H], F32, kind="ExternalOutput").ap()
    yev = fv(0, 2 * H)
    ye_part = fv(2 * H, 2 * H + E * H)
    b2e = ARB[0:1, B2O:B2O + H]
    NHF = NFC // 2
    pe_w1_done = {}
    pe_w2_done = {}
    sp_ye = {}
    w1sl_guard = {}
    rtv = {}
    p.wait("tensor", s_ix, ix_scatter)
    for e in range(E):
        p.wait("sync", s_ix, ix_scatter)
        if e >= 1:
            p.wait("sync", s_pe, pe_w1_done[e - 1])
        for gg in range(NCH):
            p.dmat("sync", bv(XEO + gg * CL, XEO + (gg + 1) * CL),
                   xe_d[e * CL:(e + 1) * CL, gg * 128:(gg + 1) * 128], s_ld)
        ld_xeT = s_ld.v

        def fload(e_, ee=e):
            if not FFN_SKIP:
                return
            _, (v_,) = nc.values_load_multi_w_load_instructions(
                cnt_i32[0:1, ee:ee + 1], engines=[mybir.EngineType.PE],
                min_val=0, max_val=2 * TL, skip_runtime_bounds_check=True)
            rtv[ee] = v_
        p.emit("tensor", fload)

        for hf in range(2):
            if e + hf >= 1:
                p.wait("gpsimd", s_pe, pe_w2_done.get((e, hf - 1),
                                                      pe_w2_done.get((e - 1, 1), 0)))
            if hf == 0:
                p.dma("gpsimd", b2e, b2[e:e + 1, :], s_gp)
                gp_b2e = s_gp.v
            f0 = hf * (FF // 2)

            if e + hf >= 1:
                p.wait("scalar", s_pe, pe_w2_done.get((e, hf - 1),
                                                      pe_w2_done.get((e - 1, 1), 0)))
            for fs in range(4):
                buf = fs % 2
                p.wait("gpsimd", s_pe, w1sl_guard.get(buf, 0))
                p.dma("gpsimd",
                      bv(W1O + buf * NCH * 512, W1O + (buf + 1) * NCH * 512).rearrange(
                          "p (t f) -> p t f", f=512),
                      W1[e, :, f0 + fs * 512: f0 + fs * 512 + 512].rearrange(
                          "(t p) f -> p t f", p=128),
                      s_gp)
                gp_w1 = s_gp.v
                p.wait("tensor", s_gp, gp_w1)
                p.wait("tensor", s_ld, ld_xeT)
                for fc in range(4):
                    fcl = fs * 4 + fc
                    fcg = hf * NHF + fcl
                    for sh in range(2):
                        bk = sh
                        bank_wait(bk)
                        s_pe.v += 1

                        def w1body(e_, buf=buf, fc=fc, sh=sh, bk=bk,
                                   inc=(not FFN_SKIP or not FFN_SKIP_W1)):
                            for dt in range(NCH):
                                mm = e_.matmul(
                                    pbank[bk][:],
                                    bv(W1O + buf * NCH * 512 + dt * 512 + fc * 128,
                                       W1O + buf * NCH * 512 + dt * 512 + (fc + 1) * 128),
                                    bv(XEO + dt * CL + sh * 512,
                                       XEO + dt * CL + (sh + 1) * 512),
                                    start=(dt == 0), stop=(dt == NCH - 1))
                            if inc:
                                mm.then_inc(s_pe.h, 1)

                        def w1grp(e_, body=w1body, sh=sh, ee=e):
                            if not (FFN_SKIP and FFN_SKIP_W1):
                                body(e_)
                                return
                            with e_.If(rtv[ee] > sh * 512):
                                body(e_)
                            e_.matmul(pbank[7][0:1, 0:1], ones_r1b[0:1, 0:1],
                                      b2e[0:1, 0:1], start=True,
                                      stop=True).then_inc(s_pe.h, 1)
                        p.emit("tensor", w1grp)
                        p.wait("scalar", s_pe, s_pe.v)
                        tgt = p.op("scalar", "activation",
                                   bv(H1O + fcl * CL + sh * 512,
                                      H1O + fcl * CL + (sh + 1) * 512),
                                   pbank[bk][:], AF.Gelu,
                                   bias=b1_sb[:, e * NFC + fcg: e * NFC + fcg + 1],
                                   scale=1.0, sem=s_sc)
                        bank_set(bk, s_sc, tgt)
                w1sl_guard[buf] = s_pe.v
            sc_h1_done = s_sc.v
            if DEBUG and e == 0:
                p.wait("sync", s_sc, sc_h1_done)
                p.dma("sync", d_h1[hf], bv(H1O, H1O + 16 * CL), s_sp)

            for piece in range(2):
                q0 = f0 + piece * (FF // 4)
                p.dma("gpsimd",
                      bv(W2O + piece * 8 * H, W2O + (piece + 1) * 8 * H).rearrange(
                          "p (t d) -> p t d", d=H),
                      W2[e, q0:q0 + FF // 4, :].rearrange("(t p) d -> p t d", p=128),
                      s_gp)
            gp_w2 = s_gp.v
            if DEBUG and e == 0 and hf == 1:
                p.wait("sync", s_sc, s_sc.v)
                p.dma("sync", d_yep, ye_part, s_sp)
            p.wait("tensor", s_gp, gp_w2)
            if hf == 1:
                p.wait("tensor", s_gp, gp_b2e)
            p.wait("tensor", s_sc, sc_h1_done)
            for sc8 in range(8):
                ybuf = sc8 % 2
                for dh in range(2):
                    bk = 2 + ((sc8 * 2 + dh) % 4)
                    bank_wait(bk)
                    s_pe.v += 1

                    def w2body(e_, sc8=sc8, dh=dh, bk=bk, hf=hf):
                        for fcl in range(NHF):
                            last = (hf == 0 and fcl == NHF - 1)
                            mm = e_.matmul(
                                pbank[bk][:],
                                bv(H1O + fcl * CL + sc8 * 128,
                                   H1O + fcl * CL + (sc8 + 1) * 128),
                                bv(W2O + fcl * H + dh * 512,
                                   W2O + fcl * H + (dh + 1) * 512),
                                start=(fcl == 0), stop=last)
                        if hf == 0 and not FFN_SKIP:
                            mm.then_inc(s_pe.h, 1)

                    def w2grp(e_, body=w2body, sc8=sc8, dh=dh, bk=bk,
                              hf=hf, ee=e):
                        if not FFN_SKIP:
                            body(e_)
                        else:
                            with e_.If(rtv[ee] > sc8 * 128):
                                body(e_)
                        if hf == 1:
                            e_.matmul(
                                pbank[bk][:], ones_r1b[0:1, 0:128],
                                b2e[0:1, dh * 512:(dh + 1) * 512],
                                start=False, stop=True).then_inc(s_pe.h, 1)
                        elif FFN_SKIP:
                            e_.matmul(pbank[6][0:1, 0:1], ones_r1b[0:1, 0:1],
                                      b2e[0:1, 0:1], start=True,
                                      stop=True).then_inc(s_pe.h, 1)
                    p.emit("tensor", w2grp)
                    if hf == 0:
                        p.wait("scalar", s_pe, s_pe.v)
                        tgt = p.op("scalar", "copy",
                                   fv(2 * H + sc8 * H + dh * 512,
                                      2 * H + sc8 * H + (dh + 1) * 512),
                                   pbank[bk][:], sem=s_sc)
                        bank_set(bk, s_sc, tgt)
                    else:
                        p.wait("vector", s_pe, s_pe.v)
                        p.wait("vector", s_sp, sp_ye.get((e, sc8 - 2),
                                                         sp_ye.get((e - 1, 6 + sc8), 0)))
                        tgt = p.op("vector", "tensor_add",
                                   fv(ybuf * H + dh * 512, ybuf * H + (dh + 1) * 512),
                                   pbank[bk][:],
                                   fv(2 * H + sc8 * H + dh * 512,
                                      2 * H + sc8 * H + (dh + 1) * 512), sem=s_ve)
                        bank_set(bk, s_ve, tgt)
                if hf == 1:
                    p.wait("sync", s_ve, s_ve.v)
                    sp_ye[(e, sc8)] = p.dma(
                        "sync", ye_d[e * CL + sc8 * 128: e * CL + (sc8 + 1) * 128, :],
                        fv(ybuf * H, (ybuf + 1) * H), s_sp)
            pe_w2_done[(e, hf)] = s_pe.v
        pe_w1_done[e] = s_pe.v
    sp_ye_all = s_sp.v

    ot_done = {}
    ve_e_last = {}
    for c in range(NTC):
        half = c % 2
        if c == 0:
            p.wait("gpsimd", s_sp, sp_ye_all)
        if c >= 2:
            p.wait("gpsimd", s_ot, ot_done[c - 2])
        p.op("gpsimd", "indirect_dma_start",
             out=fv(half * H, (half + 1) * H), out_offset=None,
             in_=ye_d, in_offset=IndirectOffsetOnAxis(dst1i[:, c:c + 1], 0),
             sem=s_ix, inc=16)
        p.op("gpsimd", "indirect_dma_start",
             out=fv(2 * H + half * H, 2 * H + (half + 1) * H), out_offset=None,
             in_=ye_d, in_offset=IndirectOffsetOnAxis(dst2i[:, c:c + 1], 0),
             sem=s_ix, inc=16)
        ixg = s_ix.v
        if c >= 2:
            p.wait("sync", s_ot, ot_done[c - 2])
        ldx1 = p.dma("sync", fv(4 * H + half * H, 4 * H + (half + 1) * H),
                     x1_sp[c * 128:(c + 1) * 128, :], s_ld)
        p.wait("vector", s_ix, ixg)
        p.wait("vector", s_ld, ldx1)
        yy = fv(6 * H + half * H, 6 * H + (half + 1) * H)
        p.op("vector", "tensor_scalar", yy, fv(half * H, (half + 1) * H),
             w1k[:, c:c + 1], None, ALU.mult)
        p.op("vector", "scalar_tensor_tensor", yy, fv(2 * H + half * H, 2 * H + (half + 1) * H),
             w2k[:, c:c + 1], yy, ALU.mult, ALU.add)
        p.op("vector", "tensor_add", yy, yy, fv(4 * H + half * H, 4 * H + (half + 1) * H))
        p.op("vector", "reduce_sum", rcol1[:], yy, axis=AX, negate=True)
        p.nop("vector")
        p.op("vector", "tensor_scalar", rcol1[:], rcol1[:], 1.0 / H, None, ALU.mult)
        p.nop("vector")
        p.op("vector", "tensor_scalar", yy, yy, rcol1[:, 0:1], None, ALU.add, sem=s_ve)
        p.wait("scalar", s_ve, s_ve.v)
        p.op("scalar", "activation", fv(8 * H, 9 * H), yy, AF.Square,
             accum_out=rcol2[:, 0:1], sem=s_sc)
        p.wait("vector", s_sc, s_sc.v)
        p.op("vector", "tensor_scalar", rcol2[:], rcol2[:], 1.0 / H, 1e-5,
             ALU.mult, ALU.add, sem=s_ve)
        p.wait("scalar", s_ve, s_ve.v)
        p.op("scalar", "activation", rcol3[:], rcol2[:], AF.Ln)
        p.nop("scalar")
        p.op("scalar", "activation", rcol3[:], rcol3[:], AF.Exp, scale=-0.5, sem=s_sc)
        p.wait("vector", s_sc, s_sc.v)
        p.op("vector", "tensor_scalar", yy, yy, rcol3[:, 0:1], None, ALU.mult, sem=s_ve)
        ve_e_last[c] = s_ve.v
        p.wait("sync", s_ve, s_ve.v)
        ot_done[c] = p.dma("sync", out[c * 128:(c + 1) * 128, :], yy, s_ot)
    p.wait("sync", s_ot, s_ot.v)

    with nc.Block() as block:
        block.sync(lambda e: [fn(e) for fn in p.ops["sync"]])
        block.gpsimd(lambda e: [fn(e) for fn in p.ops["gpsimd"]])
        block.vector(lambda e: [fn(e) for fn in p.ops["vector"]])
        block.scalar(lambda e: [fn(e) for fn in p.ops["scalar"]])
        block.tensor(lambda e: [fn(e) for fn in p.ops["tensor"]])
    ctx.close()
    return nc


_BUILT = None


def _get_built():
    global _BUILT
    if _BUILT is None:
        _BUILT = build_nc()
    return _BUILT


def _host_inputs(inputs):
    x = np.ascontiguousarray(np.asarray(inputs["L0_attn_x"], np.float32).reshape(B * T, H))
    tmix = np.ascontiguousarray(np.stack([
        np.asarray(inputs["L0_attn_time_mix_k"], np.float32).reshape(H),
        np.asarray(inputs["L0_attn_time_mix_v"], np.float32).reshape(H),
        np.asarray(inputs["L0_attn_time_mix_r"], np.float32).reshape(H),
    ]))
    common = {
        "Wk": np.ascontiguousarray(np.asarray(inputs["L0_attn_W_k"], np.float32)),
        "Wv": np.ascontiguousarray(np.asarray(inputs["L0_attn_W_v"], np.float32)),
        "Wr": np.ascontiguousarray(np.asarray(inputs["L0_attn_W_r"], np.float32)),
        "Wo": np.ascontiguousarray(np.asarray(inputs["L0_attn_W_o"], np.float32)),
        "tdecay": np.ascontiguousarray(np.asarray(inputs["L0_attn_time_decay"], np.float32)),
        "tfirst": np.ascontiguousarray(np.asarray(inputs["L0_attn_time_first"], np.float32)),
        "tmix": tmix,
        "Wg": np.ascontiguousarray(np.asarray(inputs["L0_ffn_Wg"], np.float32)),
        "W1": np.ascontiguousarray(np.asarray(inputs["L0_ffn_W1"], np.float32)),
        "b1": np.ascontiguousarray(np.asarray(inputs["L0_ffn_b1"], np.float32)),
        "W2": np.ascontiguousarray(np.asarray(inputs["L0_ffn_W2"], np.float32)),
        "b2": np.ascontiguousarray(np.asarray(inputs["L0_ffn_b2"], np.float32)),
    }
    in_maps = []
    for c in range(NCORES):
        t0 = c * TL
        if (t0 % T) == 0:
            xp = np.vstack([np.zeros((1, H), np.float32), x[t0:t0 + TL]])
        else:
            xp = x[t0 - 1:t0 + TL]
        mm = np.zeros((128, NCORES), np.float32)
        ma = np.full((128, NCORES), NEG, np.float32)
        pm = np.zeros((NCORES, 1), np.float32)
        for cp in range(c - (c % 4), c):
            mm[:, cp] = 1.0
            ma[:, cp] = 0.0
        for cp in range(c):
            pm[cp, 0] = 1.0
        m = dict(common)
        m["x_pad"] = np.ascontiguousarray(xp)
        m["cmask_mul"] = mm
        m["cmask_add"] = ma
        m["premask"] = pm
        in_maps.append(m)
    return in_maps


def kernel(**inputs):
    nc = _get_built()
    in_maps = _host_inputs(inputs)
    res = run_bass_kernel_spmd(nc, in_maps, core_ids=list(range(NCORES)))
    outs = [res.results[c]["y"] for c in range(NCORES)]
    return np.concatenate(outs, axis=0).reshape(B, T, H).astype(np.float32)


